# version 03
# baseline (speedup 1.0000x reference)
import numpy as np
from contextlib import ExitStack

import concourse.bass as bass
import concourse.mybir as mybir
from concourse.bass_utils import run_bass_kernel_spmd

try:
    import jax
    jax.config.update("jax_compilation_cache_dir", "/tmp/jaxcache")
    jax.config.update("jax_persistent_cache_min_entry_size_bytes", -1)
    jax.config.update("jax_persistent_cache_min_compile_time_secs", 0.0)
except Exception:
    pass

F32 = mybir.dt.float32
BF16 = mybir.dt.bfloat16
AF = mybir.ActivationFunctionType
ALU = mybir.AluOpType
AXX = mybir.AxisListType.X

T = 1024
NT = 8
D = 2048
DC = 16
H = 8192
HC = 64
NH = 16
S = 4096
HALO = 256
GK = 16
KW = 1664
KT = 13
OWN0 = 256
GCOL = 1536
STAB = 16.0
MSK = -1.0e30
EPS1 = 1e-5
EPS2 = 1e-8
SCL = float(1.0 / np.sqrt(np.float32(128.0)))
INV_D = 1.0 / 2048.0
BIGS = 1.0e38

STOP_AFTER = None
DEBUG_SCR = False


def build(stop_after=None):
    nc = bass.Bass("TRN2", target_bir_lowering=False, detect_race_conditions=False)
    st = ExitStack()
    e = st.enter_context

    def inp(name, shape, dt=F32):
        return nc.dram_tensor(name, list(shape), dt, kind="ExternalInput").ap()

    x_in = inp("x", (T, D))
    wg_in = inp("wg", (HC, 128, DC, 128))
    wu_in = inp("wu", (HC, 128, DC, 128))
    wd_in = inp("wd", (DC, 128, HC, 128))
    wq_in = inp("wq", (4, 128, DC, 512))
    wk_in = inp("wk", (4, 128, DC, 512))
    wv_in = inp("wv", (4, 128, DC, 512))
    wo_in = inp("wo", (DC, 128, DC, 128))
    m01_in = inp("m01", (128, 8, 2, 128), BF16)
    eb_in = inp("eb", (128, 8, 6))
    hsel_in = inp("hsel", (128, 12))
    gbl_in = inp("gbl", (128, 16))
    id_in = inp("ident", (128, 128))

    out_t = nc.dram_tensor("out", [T, D], F32, kind="ExternalOutput").ap()

    def scr(name, shape, dt=F32):
        kind = "ExternalOutput" if DEBUG_SCR else "Internal"
        return nc.dram_tensor(name, list(shape), dt, kind=kind).ap()

    x1_scr = scr("x1_scr", (T, D))
    x2_scr = scr("x2_scr", (T, D))
    ln1_scr = scr("ln1_scr", (NT, 128, DC, 128), BF16)
    sgn_g = scr("sgn_g", (HC, 128, DC, 128), BF16)
    sgn_u = scr("sgn_u", (HC, 128, DC, 128), BF16)
    sgn_d = scr("sgn_d", (DC, 128, HC, 128), BF16)
    ccA_i = scr("ccA_i", (128, DC, HALO), BF16)
    ccA_o = scr("ccA_o", (2, 128, DC, HALO), BF16)
    ccB_i = scr("ccB_i", (128, DC, HALO), BF16)
    ccB_o = scr("ccB_o", (2, 128, DC, HALO), BF16)
    ccC_i = scr("ccC_i", (128, DC, GK), BF16)
    ccC_o = scr("ccC_o", (4, 128, DC, GK), BF16)
    ccD_i = scr("ccD_i", (128, 512))
    ccD_o = scr("ccD_o", (4, 128, 512))

    class Ctr:
        def __init__(self, name):
            self.sem = e(nc.semaphore(name))
            self.n = 0

    c_sy = Ctr("c_sy")
    c_gp = Ctr("c_gp")
    c_t = Ctr("c_t")
    c_v = Ctr("c_v")
    c_s = Ctr("c_s")
    c_cc = Ctr("c_cc")

    R = dict(side="right")
    ident = e(nc.sbuf_tensor("identS", [128, 128], F32, **R))
    m01 = e(nc.sbuf_tensor("m01S", [128, 8, 2, 128], BF16, **R))
    ebias = e(nc.sbuf_tensor("ebiasS", [128, 8, 6], F32, **R))
    hsel = e(nc.sbuf_tensor("hselS", [128, 12], F32, **R))
    gbl = e(nc.sbuf_tensor("gblS", [128, 16], F32, **R))
    ones_col = e(nc.sbuf_tensor("ones_col", [128, 1], BF16, **R))
    ones_row = e(nc.sbuf_tensor("ones_row", [1, 128], F32, **R))
    neg16 = e(nc.sbuf_tensor("neg16", [128, 1], F32, **R))
    stats = e(nc.sbuf_tensor("stats", [128, 16], F32, **R))
    gsc = e(nc.sbuf_tensor("gsc", [128, DC, GK], BF16, **R))

    block = e(nc.Block())
    psum_f = [e(nc.psum_tensor(f"pb{i}", [128, 512], F32)) for i in range(8)]

    @block.sync
    def _(sy):
        for dst, src in ((ident, id_in), (m01, m01_in), (ebias, eb_in),
                         (hsel, hsel_in), (gbl, gbl_in)):
            sy.dma_start(out=dst[:], in_=src).then_inc(c_sy.sem, 16)
    c_sy.n += 5 * 16
    consts_ld = c_sy.n

    @block.vector
    def _(v):
        v.memset(ones_col[:], 1.0)
        v.memset(ones_row[:], 1.0)
        v.memset(neg16[:], -STAB).then_inc(c_v.sem, 1)
    c_v.n += 1
    consts_v = c_v.n

    @block.scalar
    def _(s):
        s.wait_ge(c_v.sem, consts_v)
        s.wait_ge(c_sy.sem, consts_ld)

    @block.tensor
    def _(t):
        t.wait_ge(c_v.sem, consts_v)
        t.wait_ge(c_sy.sem, consts_ld)

    wfree = {}

    def wait_free(engine, key):
        for ctr, val in wfree.get(key, ()):
            engine.wait_ge(ctr.sem, val)

    def set_free(key, *deps):
        wfree[key] = list(deps)


    def finish():
        @block.sync
        def _(sy):
            sy.wait_ge(c_sy.sem, c_sy.n)
        @block.gpsimd
        def _(g):
            g.wait_ge(c_gp.sem, c_gp.n)

    def vstep(fn):
        @block.vector
        def _(v):
            fn(v).then_inc(c_v.sem, 1)
        c_v.n += 1
        return c_v.n

    def emit_ln(src, dst, eps, tmp_sq, src_s=None, src_v=None, src_sy=None,
                dst_t=None):
        def _w(eng):
            if src_s is not None:
                eng.wait_ge(c_s.sem, src_s)
            if src_v is not None:
                eng.wait_ge(c_v.sem, src_v)
            if src_sy is not None:
                eng.wait_ge(c_sy.sem, src_sy)

        def f(v):
            _w(v)
            return v.reduce_sum(stats[:, 0:1], src, axis=AXX)
        va = vstep(f)

        @block.scalar
        def _(s):
            _w(s)
            s.activation(out=tmp_sq, in_=src, func=AF.Square).then_inc(c_s.sem, 1)
        c_s.n += 1
        sqdone = c_s.n

        def f(v):
            v.wait_ge(c_s.sem, sqdone)
            return v.reduce_sum(stats[:, 2:3], tmp_sq, axis=AXX)
        vb = vstep(f)

        def f(v):
            v.wait_ge(c_v.sem, vb)
            v.tensor_scalar_mul(stats[:, 1:2], stats[:, 0:1], INV_D)
            v.tensor_scalar_mul(stats[:, 3:4], stats[:, 2:3], INV_D)
            return v.tensor_mul(stats[:, 4:5], stats[:, 0:1], stats[:, 0:1])
        vc = vstep(f)

        def f(v):
            v.wait_ge(c_v.sem, vc)
            return v.scalar_tensor_tensor(stats[:, 5:6], stats[:, 4:5],
                                          -INV_D * INV_D, stats[:, 3:4],
                                          ALU.mult, ALU.add)
        vd = vstep(f)

        def f(v):
            v.wait_ge(c_v.sem, vd)
            return v.tensor_scalar_add(stats[:, 6:7], stats[:, 5:6], float(eps))
        ve = vstep(f)

        @block.scalar
        def _(s):
            s.wait_ge(c_v.sem, ve)
            s.activation(out=stats[:, 7:8], in_=stats[:, 6:7], func=AF.Sqrt).then_inc(c_s.sem, 1)
        c_s.n += 1
        sddone = c_s.n

        def f(v):
            v.wait_ge(c_s.sem, sddone)
            return v.reciprocal(stats[:, 8:9], stats[:, 7:8])
        vf = vstep(f)

        def f(v):
            v.wait_ge(c_v.sem, vf)
            return v.scalar_tensor_tensor(stats[:, 9:10], stats[:, 1:2], -1.0,
                                          stats[:, 8:9], ALU.mult, ALU.mult)
        vg = vstep(f)

        @block.scalar
        def _(s):
            s.wait_ge(c_v.sem, vg)
            if dst_t is not None:
                s.wait_ge(c_t.sem, dst_t)
            s.activation(out=dst, in_=src, func=AF.Identity,
                         bias=stats[:, 9:10], scale=stats[:, 8:9]).then_inc(c_s.sem, 1)
        c_s.n += 1
        return c_s.n

    def emit_quant(z, y, a, cc, src_s):
        MAGIC = 12582912.0

        def f(v):
            v.wait_ge(c_s.sem, src_s)
            return v.tensor_reduce(stats[:, 10:11], z, axis=AXX, op=ALU.max,
                                   apply_absolute_value=True)
        qa = vstep(f)

        def f(v):
            v.wait_ge(c_v.sem, qa)
            return v.reciprocal(stats[:, 11:12], stats[:, 10:11])
        qb = vstep(f)

        def f(v):
            v.wait_ge(c_v.sem, qb)
            v.tensor_scalar_mul(stats[:, 12:13], stats[:, 11:12], 127.0)
            return v.tensor_scalar_mul(stats[:, 13:14], stats[:, 10:11],
                                       1.0 / 127.0)
        qc = vstep(f)

        def f(v):
            v.wait_ge(c_v.sem, qc)
            v.tensor_scalar_mul(y, z, stats[:, 12:13])
            v.tensor_scalar(a, y, MAGIC, -MAGIC, ALU.add, ALU.add)
            v.tensor_scalar(a, a, 127.0, -128.0, ALU.min, ALU.max)
            return v.tensor_scalar_mul(a, a, stats[:, 13:14])
        return vstep(f)

    def emit_transposes(dst_fn, src, n, src_v=None, src_s=None, src_sy=None):
        sready = c_s.n
        last_ev = {0: None, 1: None}
        for i in range(n):
            bk = 6 + (i % 2)
            pb = psum_f[bk]

            @block.tensor
            def _(t, i=i, pb=pb, bk=bk):
                if src_v is not None:
                    t.wait_ge(c_v.sem, src_v)
                if src_s is not None:
                    t.wait_ge(c_s.sem, src_s)
                if src_sy is not None:
                    t.wait_ge(c_sy.sem, src_sy)
                if i < 2:
                    wait_free(t, ("bank", bk))
                else:
                    t.wait_ge(c_s.sem, sready + i - 1)
                t.transpose(pb[:, 0:128], src[:, i * 128:(i + 1) * 128],
                            ident[:]).then_inc(c_t.sem, 1)
            c_t.n += 1
            trd = c_t.n

            @block.scalar
            def _(s, i=i, pb=pb, trd=trd):
                s.wait_ge(c_t.sem, trd)
                s.copy(out=dst_fn(i), in_=pb[:, 0:128]).then_inc(c_s.sem, 1)
            c_s.n += 1
            last_ev[i % 2] = c_s.n
        for par in (0, 1):
            if last_ev[par] is not None:
                set_free(("bank", 6 + par), (c_s, last_ev[par]))
        return c_s.n

    L = dict(side="left")

    def emit_entry(f, src_dram):
        sub = ExitStack()
        A = sub.enter_context(nc.sbuf_tensor(f["tag"] + "A", [128, D], F32, **L))
        DD = sub.enter_context(nc.sbuf_tensor(f["tag"] + "DD", [128, D], F32, **L))
        B, C = f["B"], f["C"]
        prev_tile_s = c_s.n
        for it in range(NT):
            @block.sync
            def _(sy, it=it, prev_tile_s=prev_tile_s):
                sy.wait_ge(c_s.sem, prev_tile_s)
                sy.dma_start(out=A[:], in_=src_dram[it * 128:(it + 1) * 128, :]
                             ).then_inc(c_sy.sem, 16)
            c_sy.n += 16
            ld = c_sy.n

            l1 = emit_ln(A[:], B[:], EPS1, C[:], src_sy=ld)
            zs = emit_ln(B[:], A[:], EPS2, C[:], src_s=l1)
            emit_quant(A[:], B[:], DD[:], C[:], zs)
            prev_tile_s = emit_transposes(
                lambda dc, it=it: f["xq"][:, dc, it * 128:(it + 1) * 128],
                DD[:], DC, src_v=c_v.n)
        sub.close()
        return c_s.n

    def emit_stage12(f, grp, first_pass, xq_rdy):
        t0 = grp * 512
        mm12 = {}
        pmul = {}
        for jc in range(HC):
            b = jc % 2

            @block.gpsimd
            def _(g, jc=jc, b=b):
                wait_free(g, ("gu", b))
                if first_pass:
                    g.dma_start(out=f["wg"][b][:], in_=wg_in[jc]).then_inc(c_gp.sem, 16)
                    g.dma_start(out=f["wu"][b][:], in_=wu_in[jc]).then_inc(c_gp.sem, 16)
                else:
                    g.dma_start(out=f["wg"][b][:], in_=sgn_g[jc]).then_inc(c_gp.sem, 16)
                    g.dma_start(out=f["wu"][b][:], in_=sgn_u[jc]).then_inc(c_gp.sem, 16)
            c_gp.n += 32
            wld = c_gp.n

            if first_pass:
                @block.vector
                def _(v, b=b, wld=wld):
                    v.wait_ge(c_gp.sem, wld)
                    v.tensor_scalar(f["wg"][b][:], f["wg"][b][:], BIGS, None, ALU.mult)
                    v.tensor_scalar(f["wg"][b][:], f["wg"][b][:], -1.0, 1.0,
                                    ALU.max, ALU.min).then_inc(c_v.sem, 1)
                c_v.n += 1
                gsd = c_v.n

                @block.scalar
                def _(s, b=b, wld=wld):
                    s.wait_ge(c_gp.sem, wld)
                    s.sign(out=f["wu"][b][:], in_=f["wu"][b][:]).then_inc(c_s.sem, 1)
                c_s.n += 1
                usd = c_s.n

                @block.sync
                def _(sy, jc=jc, b=b, gsd=gsd, usd=usd):
                    sy.wait_ge(c_v.sem, gsd)
                    sy.dma_start(out=sgn_g[jc], in_=f["wg"][b][:]).then_inc(c_sy.sem, 16)
                    sy.wait_ge(c_s.sem, usd)
                    sy.dma_start(out=sgn_u[jc], in_=f["wu"][b][:]).then_inc(c_sy.sem, 16)
                c_sy.n += 32
                scr_wr = c_sy.n
            else:
                gsd = usd = scr_wr = None

            pg, pu = psum_f[0 + 2 * b], psum_f[1 + 2 * b]

            @block.tensor
            def _(t, jc=jc, b=b, wld=wld, gsd=gsd, usd=usd, pg=pg, pu=pu):
                if gsd is not None:
                    t.wait_ge(c_v.sem, gsd)
                    t.wait_ge(c_s.sem, usd)
                else:
                    t.wait_ge(c_gp.sem, wld)
                if jc == 0:
                    t.wait_ge(c_s.sem, xq_rdy)
                if jc >= 2:
                    t.wait_ge(c_v.sem, pmul[jc - 2])
                else:
                    wait_free(t, ("bank", 0 + 2 * b))
                    wait_free(t, ("bank", 1 + 2 * b))
                for dc in range(DC):
                    mm = t.matmul(pg[:], f["wg"][b][:, dc, :],
                                  f["xq"][:, dc, t0:t0 + 512],
                                  start=(dc == 0), stop=(dc == DC - 1))
                mm.then_inc(c_t.sem, 1)
                for dc in range(DC):
                    mm = t.matmul(pu[:], f["wu"][b][:, dc, :],
                                  f["xq"][:, dc, t0:t0 + 512],
                                  start=(dc == 0), stop=(dc == DC - 1))
                mm.then_inc(c_t.sem, 1)
            c_t.n += 2
            mm12[jc] = c_t.n
            deps = [(c_t, mm12[jc])]
            if first_pass:
                deps.append((c_sy, scr_wr))
            set_free(("gu", b), *deps)

            @block.scalar
            def _(s, pg=pg, pu=pu, mmd=mm12[jc]):
                s.wait_ge(c_t.sem, mmd - 1)
                s.activation(out=f["g"][:], in_=pg[:], func=AF.Silu).then_inc(c_s.sem, 1)
                s.wait_ge(c_t.sem, mmd)
                s.copy(out=f["u"][:], in_=pu[:]).then_inc(c_s.sem, 1)
            c_s.n += 2
            evd = c_s.n

            @block.vector
            def _(v, jc=jc, evd=evd):
                v.wait_ge(c_s.sem, evd)
                v.tensor_mul(f["p"][:, jc, :], f["g"][:], f["u"][:]).then_inc(c_v.sem, 1)
            c_v.n += 1
            pmul[jc] = c_v.n
        set_free(("bank", 0), (c_s, c_s.n - 2))
        set_free(("bank", 1), (c_s, c_s.n - 2))
        set_free(("bank", 2), (c_s, c_s.n))
        set_free(("bank", 3), (c_s, c_s.n))
        return pmul[HC - 1]

    def emit_stage3(f, app, grp, first_pass, src_res, dst_res, p_rdy):
        t0 = grp * 512
        x1rd = [0] * 4
        barrier_s = c_s.n
        barrier_v = c_v.n
        for tt in range(4):
            @block.sync
            def _(sy, tt=tt):
                sy.wait_ge(c_s.sem, barrier_s)
                sy.wait_ge(c_v.sem, barrier_v)
                r0 = t0 + tt * 128
                sy.dma_start(out=f["x1g"][:, tt, :],
                             in_=src_res[r0:r0 + 128, :]).then_inc(c_sy.sem, 16)
            c_sy.n += 16
            x1rd[tt] = c_sy.n

        s3ev = {}
        xadd = {}
        for ddc in range(DC):
            b = ddc % 2

            @block.gpsimd
            def _(g, ddc=ddc, b=b):
                wait_free(g, ("wd", b))
                if first_pass:
                    g.dma_start(out=f["wd"][b][:], in_=wd_in[ddc]).then_inc(c_gp.sem, 16)
                else:
                    g.dma_start(out=f["wd"][b][:], in_=sgn_d[ddc]).then_inc(c_gp.sem, 16)
            c_gp.n += 16
            wld = c_gp.n

            if first_pass:
                @block.vector
                def _(v, b=b, wld=wld):
                    v.wait_ge(c_gp.sem, wld)
                    v.tensor_scalar(f["wd"][b][:, 0:32, :], f["wd"][b][:, 0:32, :],
                                    BIGS, None, ALU.mult)
                    v.tensor_scalar(f["wd"][b][:, 0:32, :], f["wd"][b][:, 0:32, :],
                                    -1.0, 1.0, ALU.max, ALU.min).then_inc(c_v.sem, 1)
                c_v.n += 1
                dsv = c_v.n

                @block.scalar
                def _(s, b=b, wld=wld):
                    s.wait_ge(c_gp.sem, wld)
                    s.sign(out=f["wd"][b][:, 32:64, :],
                           in_=f["wd"][b][:, 32:64, :]).then_inc(c_s.sem, 1)
                c_s.n += 1
                dss = c_s.n

                @block.sync
                def _(sy, ddc=ddc, b=b, dsv=dsv, dss=dss):
                    sy.wait_ge(c_v.sem, dsv)
                    sy.wait_ge(c_s.sem, dss)
                    sy.dma_start(out=sgn_d[ddc], in_=f["wd"][b][:]).then_inc(c_sy.sem, 16)
                c_sy.n += 16
                scr3_wr = c_sy.n
            else:
                dsv = dss = scr3_wr = None

            po = psum_f[0 + b]

            @block.tensor
            def _(t, ddc=ddc, b=b, wld=wld, dsv=dsv, dss=dss, po=po):
                if dsv is not None:
                    t.wait_ge(c_v.sem, dsv)
                    t.wait_ge(c_s.sem, dss)
                else:
                    t.wait_ge(c_gp.sem, wld)
                if ddc == 0:
                    t.wait_ge(c_v.sem, p_rdy)
                if ddc >= 2:
                    t.wait_ge(c_s.sem, s3ev[ddc - 2])
                else:
                    wait_free(t, ("bank", b))
                for jc in range(HC):
                    mm = t.matmul(po[:], f["wd"][b][:, jc, :], f["p"][:, jc, :],
                                  start=(jc == 0), stop=(jc == HC - 1))
                mm.then_inc(c_t.sem, 1)
            c_t.n += 1
            mmd = c_t.n
            deps = [(c_t, mmd)]
            if first_pass:
                deps.append((c_sy, scr3_wr))
            set_free(("wd", b), *deps)

            @block.scalar
            def _(s, po=po, mmd=mmd):
                s.wait_ge(c_t.sem, mmd)
                s.mul(out=f["ct"][:], in_=po[:], mul=0.5).then_inc(c_s.sem, 1)
            c_s.n += 1
            s3ev[ddc] = c_s.n
            ctd = c_s.n

            for tt in range(4):
                pb = psum_f[6 + (tt % 2)]

                @block.tensor
                def _(t, ddc=ddc, tt=tt, pb=pb, ctd=ctd):
                    t.wait_ge(c_s.sem, ctd)
                    t.wait_ge(c_v.sem, xadd.get((ddc - 1, tt), 0))
                    t.wait_ge(c_v.sem, xadd.get((ddc, tt - 2), 0))
                    if ddc == 0 and tt < 2:
                        wait_free(t, ("bank", 6 + tt % 2))
                    t.transpose(pb[:, 0:128], f["ct"][:, tt * 128:(tt + 1) * 128],
                                ident[:]).then_inc(c_t.sem, 1)
                c_t.n += 1
                trd = c_t.n

                @block.vector
                def _(v, ddc=ddc, tt=tt, pb=pb, trd=trd):
                    v.wait_ge(c_t.sem, trd)
                    v.wait_ge(c_sy.sem, x1rd[tt])
                    v.tensor_add(f["x1g"][:, tt, ddc * 128:(ddc + 1) * 128],
                                 pb[:, 0:128],
                                 f["x1g"][:, tt, ddc * 128:(ddc + 1) * 128]
                                 ).then_inc(c_v.sem, 1)
                c_v.n += 1
                xadd[(ddc, tt)] = c_v.n
        set_free(("bank", 6), (c_v, xadd[(DC - 1, 2)]))
        set_free(("bank", 7), (c_v, xadd[(DC - 1, 3)]))
        set_free(("bank", 0), (c_s, s3ev[DC - 2]))
        set_free(("bank", 1), (c_s, s3ev[DC - 1]))

        for tt in range(4):
            done = xadd[(DC - 1, tt)]

            @block.sync
            def _(sy, tt=tt, done=done):
                sy.wait_ge(c_v.sem, done)
                r0 = t0 + tt * 128
                sy.dma_start(out=dst_res[r0:r0 + 128, :],
                             in_=f["x1g"][:, tt, :]).then_inc(c_sy.sem, 16)
            c_sy.n += 16

            if app == 1:
                lnd = emit_ln(f["x1g"][:, tt, :], f["B"][:], EPS1, f["C"][:],
                              src_v=done, dst_t=c_t.n)
                lnstg_wr = wfree.get(("lnstg",))
                emit_transposes(
                    lambda dc: f["lnstg"][:, dc, :], f["B"][:], DC, src_s=lnd,
                    src_sy=(lnstg_wr[0][1] if lnstg_wr else None))
                gt = grp * 4 + tt

                @block.sync
                def _(sy, gt=gt, lst=c_s.n):
                    sy.wait_ge(c_s.sem, lst)
                    sy.dma_start(out=ln1_scr[gt], in_=f["lnstg"][:]
                                 ).then_inc(c_sy.sem, 16)
                c_sy.n += 16
                set_free(("lnstg",), (c_sy, c_sy.n))

    def emit_ffn(app):
        fs = ExitStack()
        fe = fs.enter_context
        tag = f"f{app}"
        f = {"tag": tag}
        f["p"] = fe(nc.sbuf_tensor(tag + "p", [128, HC, 512], BF16, **L))
        f["xq"] = fe(nc.sbuf_tensor(tag + "xq", [128, DC, T], BF16, **L))
        f["B"] = fe(nc.sbuf_tensor(tag + "B", [128, D], F32, **L))
        f["C"] = fe(nc.sbuf_tensor(tag + "C", [128, D], F32, **L))
        f["lnstg"] = fe(nc.sbuf_tensor(tag + "lnstg", [128, DC, 128], BF16, **L))
        f["wg"] = [fe(nc.sbuf_tensor(f"{tag}wg{b}", [128, DC, 128], BF16, **L))
                   for b in range(2)]
        f["wu"] = [fe(nc.sbuf_tensor(f"{tag}wu{b}", [128, DC, 128], BF16, **L))
                   for b in range(2)]
        f["g"] = fe(nc.sbuf_tensor(tag + "g", [128, 512], BF16, **L))
        f["u"] = fe(nc.sbuf_tensor(tag + "u", [128, 512], BF16, **L))
        f["ct"] = fe(nc.sbuf_tensor(tag + "ct", [128, 512], F32, **L))
        f["x1g"] = fe(nc.sbuf_tensor(tag + "x1g", [128, 4, D], F32, **L))

        src = x_in if app == 1 else x2_scr
        dst = x1_scr if app == 1 else out_t
        if DEBUG_SCR and app == 1:
            xq_dbg = scr("xq_dbg", (128, DC, T), BF16)
            st_dbg = scr("st_dbg", (128, 16))
            id_dbg = scr("id_dbg", (128, 128))
        for b in range(2):
            set_free(("gu", b), (c_v, c_v.n), (c_t, c_t.n))
            set_free(("wd", b), (c_v, c_v.n), (c_t, c_t.n))
        xq_rdy = emit_entry(f, src)
        if DEBUG_SCR and app == 1:
            @block.sync
            def _(sy):
                sy.wait_ge(c_s.sem, xq_rdy)
                sy.dma_start(out=xq_dbg, in_=f["xq"][:]).then_inc(c_sy.sem, 16)
                sy.dma_start(out=st_dbg, in_=stats[:]).then_inc(c_sy.sem, 16)
                sy.dma_start(out=id_dbg, in_=ident[:]).then_inc(c_sy.sem, 16)
            c_sy.n += 48

        sub = ExitStack()
        f["wd"] = [sub.enter_context(
            nc.sbuf_tensor(f"{tag}wd{b}", [128, HC, 128], BF16, **L))
            for b in range(2)]
        for grp in (0, 1):
            first = (app == 1 and grp == 0)
            p_rdy = emit_stage12(f, grp, first, xq_rdy)
            emit_stage3(f, app, grp, first, src_res=src, dst_res=dst, p_rdy=p_rdy)
        sub.close()
        fs.close()

    emit_ffn(1)
    if stop_after == "ffn1":
        finish()
        return nc, st

    oscf = ExitStack()
    osc = oscf.enter_context(nc.sbuf_tensor("osc", [128, NH, T], BF16, **L))
    oLg = oscf.enter_context(nc.sbuf_tensor("oLg", [128, DC * GK], F32, **L))
    lgf = oscf.enter_context(nc.sbuf_tensor("lgf", [1, NH * GK], F32, **L))
    gmrg = oscf.enter_context(nc.sbuf_tensor("gmrg", [128, 4, 512], F32, **L))
    lnf = ExitStack()
    ln1T = lnf.enter_context(nc.sbuf_tensor("ln1T", [128, DC, KW], BF16, **L))

    @block.vector
    def _(v):
        v.memset(ln1T[:, :, 1552:1664], 0.0).then_inc(c_v.sem, 1)
    c_v.n += 1

    @block.sync
    def _(sy):
        for tt in range(NT):
            sy.dma_start(out=ln1T[:, :, OWN0 + tt * 128:OWN0 + (tt + 1) * 128],
                         in_=ln1_scr[tt]).then_inc(c_sy.sem, 16)
    c_sy.n += NT * 16
    ln1_own = c_sy.n

    exs = ExitStack()
    xe = exs.enter_context
    exA = xe(nc.sbuf_tensor("exA", [128, DC, HALO], BF16, **L))
    exB = xe(nc.sbuf_tensor("exB", [128, DC, HALO], BF16, **L))
    exO = xe(nc.sbuf_tensor("exO", [128, 2, DC, HALO], BF16, **L))
    exO2 = xe(nc.sbuf_tensor("exO2", [128, 2, DC, HALO], BF16, **L))
    exC = xe(nc.sbuf_tensor("exC", [128, 4, DC, GK], BF16, **L))

    @block.vector
    def _(v):
        v.wait_ge(c_sy.sem, ln1_own)
        v.wait_ge(c_sy.sem, consts_ld)
        eL = ln1T[:, :, OWN0:OWN0 + HALO]
        eR = ln1T[:, :, OWN0 + T - HALO:OWN0 + T]
        v.tensor_scalar_mul(exA[:], eL, hsel[:, 8:9])
        v.scalar_tensor_tensor(exA[:], eR, hsel[:, 9:10], exA[:], ALU.mult, ALU.add)
        v.tensor_scalar_mul(exB[:], eL, hsel[:, 10:11])
        v.scalar_tensor_tensor(exB[:], eR, hsel[:, 11:12], exB[:], ALU.mult,
                               ALU.add).then_inc(c_v.sem, 1)
    c_v.n += 1
    exrdy = c_v.n

    @block.gpsimd
    def _(g):
        g.wait_ge(c_v.sem, exrdy)
        base = c_gp.n
        g.dma_start(out=ccA_i, in_=exA[:]).then_inc(c_gp.sem, 16)
        g.dma_start(out=ccB_i, in_=exB[:]).then_inc(c_gp.sem, 16)
        g.dma_start(out=ccC_i, in_=ln1T[:, :, OWN0:OWN0 + GK]).then_inc(c_gp.sem, 16)
        g.wait_ge(c_gp.sem, base + 48)
        g.collective_compute(
            "AllGather", ALU.bypass,
            replica_groups=[[0, 1], [2, 3], [4, 5], [6, 7]],
            ins=[ccA_i.opt()], outs=[ccA_o.opt()]).then_inc(c_cc.sem, 1)
        g.collective_compute(
            "AllGather", ALU.bypass,
            replica_groups=[[0, 3], [1, 2], [4, 7], [5, 6]],
            ins=[ccB_i.opt()], outs=[ccB_o.opt()]).then_inc(c_cc.sem, 1)
        g.collective_compute(
            "AllGather", ALU.bypass,
            replica_groups=[[0, 1, 2, 3], [4, 5, 6, 7]],
            ins=[ccC_i.opt()], outs=[ccC_o.opt()]).then_inc(c_cc.sem, 1)
        g.wait_ge(c_cc.sem, c_cc.n + 3)
        for sl in range(2):
            g.dma_start(out=exO[:, sl], in_=ccA_o[sl]).then_inc(c_gp.sem, 16)
            g.dma_start(out=exO2[:, sl], in_=ccB_o[sl]).then_inc(c_gp.sem, 16)
        g.dma_start(out=exC[:, 0], in_=ccC_o[0]).then_inc(c_gp.sem, 16)
    c_gp.n += 128
    c_cc.n += 3
    exld = c_gp.n

    @block.vector
    def _(v):
        v.wait_ge(c_gp.sem, exld)
        hl = ln1T[:, :, 0:HALO]
        v.tensor_scalar_mul(hl, exO[:, 0], hsel[:, 0:1])
        v.scalar_tensor_tensor(hl, exO[:, 1], hsel[:, 1:2], hl, ALU.mult, ALU.add)
        v.scalar_tensor_tensor(hl, exO2[:, 0], hsel[:, 2:3], hl, ALU.mult, ALU.add)
        v.scalar_tensor_tensor(hl, exO2[:, 1], hsel[:, 3:4], hl, ALU.mult, ALU.add)
        hr = ln1T[:, :, OWN0 + T:OWN0 + T + HALO]
        v.tensor_scalar_mul(hr, exO[:, 0], hsel[:, 4:5])
        v.scalar_tensor_tensor(hr, exO[:, 1], hsel[:, 5:6], hr, ALU.mult, ALU.add)
        v.scalar_tensor_tensor(hr, exO2[:, 0], hsel[:, 6:7], hr, ALU.mult, ALU.add)
        v.scalar_tensor_tensor(hr, exO2[:, 1], hsel[:, 7:8], hr, ALU.mult, ALU.add)
        v.tensor_copy(ln1T[:, :, GCOL:GCOL + GK], exC[:, 0]).then_inc(c_v.sem, 1)
    c_v.n += 1
    halo_done = c_v.n

    exs.close()
    if stop_after == "exchange":
        finish()
        return nc, st

    ast = ExitStack()
    ae = ast.enter_context
    kw_sb = ae(nc.sbuf_tensor("kw_sb", [128, 4, KW], BF16, **L))
    vw_sb = ae(nc.sbuf_tensor("vw_sb", [128, KT, 512], BF16, **L))
    qw_sb = ae(nc.sbuf_tensor("qw_sb", [128, 4, T], BF16, **L))
    qg_sb = ae(nc.sbuf_tensor("qg_sb", [128, 4, 128], BF16, **L))
    wat = [ae(nc.sbuf_tensor(f"wat{b}", [128, DC, 512], BF16, **L))
           for b in range(2)]
    pT_sb = ae(nc.sbuf_tensor("pT_sb", [128, 8, 128], BF16, **L))
    lr_sb = ae(nc.sbuf_tensor("lr_sb", [1, 128], F32, **L))
    lrep_sb = ae(nc.sbuf_tensor("lrep_sb", [128, 128], F32, **L))

    @block.vector
    def _(v):
        v.memset(qg_sb[:], 0.0).then_inc(c_v.sem, 1)
    c_v.n += 1
    qg_zero = c_v.n

    def proj_wave(w, which, wbuf, wdma):
        if which == "k":
            for oc in range(4):
                for ci, ts in enumerate(range(0, KW, 512)):
                    tn = min(512, KW - ts)
                    bk = 4 + (oc * 4 + ci) % 2
                    pb = psum_f[bk]

                    @block.tensor
                    def _(t, oc=oc, ts=ts, tn=tn, pb=pb, bk=bk):
                        t.wait_ge(c_gp.sem, wdma)
                        t.wait_ge(c_v.sem, halo_done)
                        wait_free(t, ("bank", bk))
                        for dc in range(DC):
                            mm = t.matmul(pb[:, 0:tn],
                                          wbuf[:, dc, oc * 128:(oc + 1) * 128],
                                          ln1T[:, dc, ts:ts + tn],
                                          start=(dc == 0), stop=(dc == DC - 1))
                        mm.then_inc(c_t.sem, 1)
                    c_t.n += 1
                    mmd = c_t.n

                    @block.scalar
                    def _(s, oc=oc, ts=ts, tn=tn, pb=pb, mmd=mmd):
                        s.wait_ge(c_t.sem, mmd)
                        s.copy(out=kw_sb[:, oc, ts:ts + tn],
                               in_=pb[:, 0:tn]).then_inc(c_s.sem, 1)
                    c_s.n += 1
                    set_free(("bank", bk), (c_s, c_s.n))
        elif which == "v":
            for tt in range(KT):
                bk = 4 + tt % 2
                pb = psum_f[bk]

                @block.tensor
                def _(t, tt=tt, pb=pb, bk=bk):
                    t.wait_ge(c_gp.sem, wdma)
                    t.wait_ge(c_v.sem, halo_done)
                    wait_free(t, ("bank", bk))
                    for dc in range(DC):
                        mm = t.matmul(pb[:], ln1T[:, dc, tt * 128:(tt + 1) * 128],
                                      wbuf[:, dc, :],
                                      start=(dc == 0), stop=(dc == DC - 1))
                    mm.then_inc(c_t.sem, 1)
                c_t.n += 1
                mmd = c_t.n

                @block.scalar
                def _(s, tt=tt, pb=pb, mmd=mmd):
                    s.wait_ge(c_t.sem, mmd)
                    s.copy(out=vw_sb[:, tt, :], in_=pb[:]).then_inc(c_s.sem, 1)
                c_s.n += 1
                set_free(("bank", bk), (c_s, c_s.n))
        else:
            for oc in range(4):
                for th in range(3):
                    bk = 4 + th % 2
                    pb = psum_f[bk]

                    @block.tensor
                    def _(t, oc=oc, th=th, pb=pb, bk=bk):
                        t.wait_ge(c_gp.sem, wdma)
                        if th == 2:
                            t.wait_ge(c_v.sem, qg_zero)
                        wait_free(t, ("bank", bk))
                        for dc in range(DC):
                            if th < 2:
                                rhs = ln1T[:, dc, OWN0 + th * 512:OWN0 + th * 512 + 512]
                                dst = pb[:]
                            else:
                                rhs = ln1T[:, dc, GCOL:GCOL + GK]
                                dst = pb[:, 0:GK]
                            mm = t.matmul(dst, wbuf[:, dc, oc * 128:(oc + 1) * 128],
                                          rhs, start=(dc == 0), stop=(dc == DC - 1))
                        mm.then_inc(c_t.sem, 1)
                    c_t.n += 1
                    mmd = c_t.n

                    @block.scalar
                    def _(s, oc=oc, th=th, pb=pb, mmd=mmd):
                        s.wait_ge(c_t.sem, mmd)
                        if th < 2:
                            s.copy(out=qw_sb[:, oc, th * 512:th * 512 + 512],
                                   in_=pb[:]).then_inc(c_s.sem, 1)
                        else:
                            s.copy(out=qg_sb[:, oc, 0:GK],
                                   in_=pb[:, 0:GK]).then_inc(c_s.sem, 1)
                    c_s.n += 1
                    set_free(("bank", bk), (c_s, c_s.n))

    def attend(w, hh, qt):
        glob = (qt == 8)
        nkt = 8 if glob else 6
        po, pl = psum_f[2], psum_f[3]
        h = w * 4 + hh

        expd = [0] * nkt
        for i in range(nkt):
            pb = psum_f[i % 2]

            @block.tensor
            def _(t, i=i, pb=pb):
                if i >= 2:
                    d = expd[i - 2]
                    if isinstance(d, tuple):
                        t.wait_ge(c_v.sem, d[1])
                    else:
                        t.wait_ge(c_s.sem, d)
                else:
                    wait_free(t, ("bank", i % 2))
                if glob:
                    kcol, nk = OWN0 + i * 128, 128
                    rhs = qg_sb[:, hh, :]
                else:
                    kcol = GCOL if i == 5 else 128 * qt + 128 * i
                    nk = GK if i == 5 else 128
                    rhs = qw_sb[:, hh, qt * 128:(qt + 1) * 128]
                mm = t.matmul(pb[0:nk, 0:128], kw_sb[:, hh, kcol:kcol + nk],
                              rhs, start=True, stop=True)
                mm.then_inc(c_t.sem, 1)
            c_t.n += 1
            smm = c_t.n

            @block.scalar
            def _(s, i=i, pb=pb, smm=smm):
                s.wait_ge(c_t.sem, smm)
                nk = GK if (not glob and i == 5) else 128
                bias = neg16[0:nk, :] if glob else ebias[0:nk, qt, i:i + 1]
                s.activation(out=pT_sb[0:nk, i, :], in_=pb[0:nk, 0:128],
                             func=AF.Exp, bias=bias, scale=SCL).then_inc(c_s.sem, 1)
            c_s.n += 1
            expd[i] = c_s.n
            set_free(("bank", i % 2), (c_s, c_s.n))

            if (not glob) and i in (0, 4):
                @block.vector
                def _(v, i=i, ed=expd[i]):
                    v.wait_ge(c_s.sem, ed)
                    v.wait_ge(c_sy.sem, consts_ld)
                    v.tensor_mul(pT_sb[:, i, :], pT_sb[:, i, :],
                                 m01[:, qt, (0 if i == 0 else 1), :]
                                 ).then_inc(c_v.sem, 1)
                c_v.n += 1
                expd[i] = ("v", c_v.n)

        for i in range(nkt):
            dep = expd[i]

            @block.tensor
            def _(t, i=i, dep=dep):
                if isinstance(dep, tuple):
                    t.wait_ge(c_v.sem, dep[1])
                else:
                    t.wait_ge(c_s.sem, dep)
                if i == 0:
                    wait_free(t, ("bank", 2))
                    wait_free(t, ("bank", 3))
                if glob:
                    nk, vt = 128, 2 + i
                else:
                    nk = GK if i == 5 else 128
                    vt = 12 if i == 5 else qt + i
                mm = t.matmul(pl[0:1, 0:128], ones_col[0:nk, :],
                              pT_sb[0:nk, i, :], start=(i == 0), stop=(i == nkt - 1))
                mm.then_inc(c_t.sem, 1)
                mm = t.matmul(po[:, 0:128], vw_sb[0:nk, vt, hh * 128:hh * 128 + 128],
                              pT_sb[0:nk, i, :], start=(i == 0), stop=(i == nkt - 1))
                mm.then_inc(c_t.sem, 1)
            c_t.n += 2
        avd = c_t.n

        if glob:
            @block.vector
            def _(v, h=h):
                v.wait_ge(c_t.sem, avd)
                v.tensor_copy(oLg[:, h * GK:(h + 1) * GK], po[:, 0:GK])
                v.tensor_copy(lgf[0:1, h * GK:(h + 1) * GK],
                              pl[0:1, 0:GK]).then_inc(c_v.sem, 1)
            c_v.n += 1
            set_free(("bank", 2), (c_v, c_v.n))
            set_free(("bank", 3), (c_v, c_v.n))
            return c_v.n

        @block.vector
        def _(v):
            v.wait_ge(c_t.sem, avd)
            v.reciprocal(lr_sb[:], pl[0:1, 0:128]).then_inc(c_v.sem, 1)
        c_v.n += 1
        lrd = c_v.n
        set_free(("bank", 3), (c_v, lrd))

        @block.tensor
        def _(t):
            t.wait_ge(c_v.sem, lrd)
            wait_free(t, ("bank", 4))
            wait_free(t, ("lrep",))
            t.matmul(psum_f[4][:, 0:128], ones_row[:], lr_sb[:],
                     start=True, stop=True).then_inc(c_t.sem, 1)
        c_t.n += 1
        repd = c_t.n

        @block.scalar
        def _(s):
            s.wait_ge(c_t.sem, repd)
            s.copy(out=lrep_sb[:], in_=psum_f[4][:, 0:128]).then_inc(c_s.sem, 1)
        c_s.n += 1
        repc = c_s.n
        set_free(("bank", 4), (c_s, repc))

        @block.vector
        def _(v, h=h):
            v.wait_ge(c_s.sem, repc)
            v.tensor_mul(osc[:, h, qt * 128:(qt + 1) * 128], po[:, 0:128],
                         lrep_sb[:]).then_inc(c_v.sem, 1)
        c_v.n += 1
        set_free(("bank", 2), (c_v, c_v.n))
        set_free(("lrep",), (c_v, c_v.n))
        return c_v.n

    for b in range(2):
        set_free(("wat", b), (c_v, halo_done), (c_t, c_t.n))
    for w in range(4):
        for which, win, b in (("k", wk_in, 0), ("v", wv_in, 1), ("q", wq_in, 0)):
            @block.gpsimd
            def _(g, w=w, win=win, b=b):
                wait_free(g, ("wat", b))
                g.dma_start(out=wat[b][:], in_=win[w]).then_inc(c_gp.sem, 16)
            c_gp.n += 16
            proj_wave(w, which, wat[b], c_gp.n)
            set_free(("wat", b), (c_t, c_t.n))
        for hh in range(4):
            for qt in range(8):
                attend(w, hh, qt)
            attend(w, hh, 8)

    ast.close()
    lnf.close()
    attn_done_v = c_v.n
    if stop_after == "attn":
        finish()
        return nc, st

    @block.vector
    def _(v):
        v.wait_ge(c_gp.sem, c_gp.n)
        v.memset(gmrg[:, 0, 256:512], 0.0).then_inc(c_v.sem, 1)
    c_v.n += 1
    zpad = c_v.n

    @block.gpsimd
    def _(g):
        g.wait_ge(c_v.sem, attn_done_v)
        g.wait_ge(c_v.sem, zpad)
        base = c_gp.n
        g.dma_start(out=ccD_i[0:128, 0:256], in_=oLg[:]).then_inc(c_gp.sem, 16)
        g.dma_start(out=ccD_i[0:128, 256:512],
                    in_=gmrg[:, 0, 256:512]).then_inc(c_gp.sem, 16)
        g.dma_start(out=ccD_i[0:1, 256:512], in_=lgf[:]).then_inc(c_gp.sem, 16)
        g.wait_ge(c_gp.sem, base + 48)
        g.collective_compute(
            "AllGather", ALU.bypass,
            replica_groups=[[0, 1, 2, 3], [4, 5, 6, 7]],
            ins=[ccD_i.opt()], outs=[ccD_o.opt()]).then_inc(c_cc.sem, 1)
        g.wait_ge(c_cc.sem, c_cc.n + 1)
        for sl in range(4):
            g.dma_start(out=gmrg[:, sl], in_=ccD_o[sl]).then_inc(c_gp.sem, 16)
    c_gp.n += 112
    c_cc.n += 1
    mrg_ld = c_gp.n

    def f(v):
        v.wait_ge(c_gp.sem, mrg_ld)
        v.tensor_add(gmrg[:, 0, :], gmrg[:, 0, :], gmrg[:, 1, :])
        return v.tensor_add(gmrg[:, 2, :], gmrg[:, 2, :], gmrg[:, 3, :])
    m1 = vstep(f)

    def f(v):
        v.wait_ge(c_v.sem, m1)
        return v.tensor_add(gmrg[:, 0, :], gmrg[:, 0, :], gmrg[:, 2, :])
    m2 = vstep(f)

    def f(v):
        v.wait_ge(c_v.sem, m2)
        return v.reciprocal(gmrg[0:1, 1, 256:512], gmrg[0:1, 0, 256:512])
    mrgd = vstep(f)

    gsc_dep = {}
    for hq in range(NH):
        bk = 4 + hq % 2
        pb = psum_f[bk]

        @block.tensor
        def _(t, hq=hq, pb=pb, bk=bk):
            t.wait_ge(c_v.sem, mrgd)
            wait_free(t, ("bank", bk))
            t.matmul(pb[:, 0:GK], ones_row[:],
                     gmrg[0:1, 1, 256 + hq * GK:256 + (hq + 1) * GK],
                     start=True, stop=True).then_inc(c_t.sem, 1)
        c_t.n += 1
        rpd = c_t.n

        @block.vector
        def _(v, hq=hq, pb=pb, rpd=rpd):
            v.wait_ge(c_t.sem, rpd)
            v.tensor_mul(gsc[:, hq, :], gmrg[:, 0, hq * GK:(hq + 1) * GK],
                         pb[:, 0:GK]).then_inc(c_v.sem, 1)
        c_v.n += 1
        gsc_dep[hq] = c_v.n
        set_free(("bank", bk), (c_v, c_v.n))
    gsc_all = c_v.n

    p6 = ExitStack()
    p6e = p6.enter_context
    x2t_sb = p6e(nc.sbuf_tensor("x2t_sb", [128, NT, D], F32, **L))
    x1t_sb = p6e(nc.sbuf_tensor("x1t_sb", [128, NT, D], F32, **L))
    wo_sb = [p6e(nc.sbuf_tensor(f"wo_sb{b}", [128, DC, 128], BF16, **L))
             for b in range(2)]
    cta_sb = p6e(nc.sbuf_tensor("cta_sb", [128, T], F32, **L))
    gtmp_sb = p6e(nc.sbuf_tensor("gtmp_sb", [128, GK], F32, **L))

    for b in range(2):
        set_free(("wo", b), (c_v, gsc_all), (c_t, c_t.n))
    x1_ld = [0] * NT
    for tt in range(NT):
        @block.sync
        def _(sy, tt=tt):
            sy.wait_ge(c_v.sem, gsc_all)
            sy.dma_start(out=x1t_sb[:, tt, :],
                         in_=x1_scr[tt * 128:(tt + 1) * 128, :]).then_inc(c_sy.sem, 16)
        c_sy.n += 16
        x1_ld[tt] = c_sy.n

    aoev = {}
    x2d = {}
    bldd = {}
    for ddc in range(DC):
        b = ddc % 2

        @block.gpsimd
        def _(g, ddc=ddc, b=b):
            wait_free(g, ("wo", b))
            g.dma_start(out=wo_sb[b][:], in_=wo_in[ddc]).then_inc(c_gp.sem, 16)
        c_gp.n += 16
        wld = c_gp.n

        pa, pb2, pg16 = psum_f[0 + b], psum_f[2 + b], psum_f[4]

        @block.tensor
        def _(t, ddc=ddc, b=b, wld=wld, pa=pa, pb2=pb2, pg16=pg16):
            t.wait_ge(c_gp.sem, wld)
            t.wait_ge(c_v.sem, gsc_all)
            if ddc < 2:
                wait_free(t, ("bank", 0 + b))
                wait_free(t, ("bank", 2 + b))
            if ddc == 0:
                wait_free(t, ("bank", 4))
            t.wait_ge(c_s.sem, aoev.get(ddc - 2, 0))
            t.wait_ge(c_v.sem, bldd.get(ddc - 1, 0))
            for th in range(2):
                dst = pa if th == 0 else pb2
                for dvc in range(DC):
                    mm = t.matmul(dst[:], wo_sb[b][:, dvc, :],
                                  osc[:, dvc, th * 512:th * 512 + 512],
                                  start=(dvc == 0), stop=(dvc == DC - 1))
                mm.then_inc(c_t.sem, 1)
            for dvc in range(DC):
                mm = t.matmul(pg16[:, 0:GK], wo_sb[b][:, dvc, :], gsc[:, dvc, :],
                              start=(dvc == 0), stop=(dvc == DC - 1))
            mm.then_inc(c_t.sem, 1)
        c_t.n += 3
        mmd = c_t.n
        set_free(("wo", b), (c_t, mmd))

        @block.scalar
        def _(s, pa=pa, pb2=pb2, mmd=mmd):
            s.wait_ge(c_t.sem, mmd - 1)
            s.copy(out=cta_sb[:, 0:512], in_=pa[:]).then_inc(c_s.sem, 1)
            s.copy(out=cta_sb[:, 512:1024], in_=pb2[:]).then_inc(c_s.sem, 1)
        c_s.n += 2
        ctd = c_s.n
        aoev[ddc] = c_s.n

        def f(v, ddc=ddc, pg16=pg16, ctd=ctd, mmd=mmd):
            v.wait_ge(c_s.sem, ctd)
            v.wait_ge(c_t.sem, mmd)
            return v.tensor_sub(gtmp_sb[:], pg16[:, 0:GK], cta_sb[:, 0:GK])
        b1 = vstep(f)

        def f(v, b1=b1):
            v.wait_ge(c_v.sem, b1)
            return v.tensor_mul(gtmp_sb[:], gtmp_sb[:], gbl[:])
        b2 = vstep(f)

        def f(v, b2=b2):
            v.wait_ge(c_v.sem, b2)
            return v.tensor_add(cta_sb[:, 0:GK], cta_sb[:, 0:GK], gtmp_sb[:])
        bldd[ddc] = vstep(f)
        bld = bldd[ddc]

        for tt in range(NT):
            pbt = psum_f[6 + tt % 2]

            @block.tensor
            def _(t, ddc=ddc, tt=tt, pbt=pbt, bld=bld):
                t.wait_ge(c_v.sem, bld)
                t.wait_ge(c_v.sem, x2d.get((ddc - 1, tt), 0))
                t.wait_ge(c_v.sem, x2d.get((ddc, tt - 2), 0))
                if ddc == 0 and tt < 2:
                    wait_free(t, ("bank", 6 + tt % 2))
                t.transpose(pbt[:, 0:128], cta_sb[:, tt * 128:(tt + 1) * 128],
                            ident[:]).then_inc(c_t.sem, 1)
            c_t.n += 1
            trd = c_t.n

            @block.vector
            def _(v, ddc=ddc, tt=tt, pbt=pbt, trd=trd):
                v.wait_ge(c_t.sem, trd)
                v.wait_ge(c_sy.sem, x1_ld[tt])
                v.tensor_add(x2t_sb[:, tt, ddc * 128:(ddc + 1) * 128],
                             pbt[:, 0:128],
                             x1t_sb[:, tt, ddc * 128:(ddc + 1) * 128]
                             ).then_inc(c_v.sem, 1)
            c_v.n += 1
            x2d[(ddc, tt)] = c_v.n
    set_free(("bank", 6), (c_v, x2d[(DC - 1, NT - 2)]))
    set_free(("bank", 7), (c_v, x2d[(DC - 1, NT - 1)]))
    set_free(("bank", 0), (c_s, aoev[DC - 2]))
    set_free(("bank", 1), (c_s, aoev[DC - 1]))
    set_free(("bank", 2), (c_s, aoev[DC - 2]))
    set_free(("bank", 3), (c_s, aoev[DC - 1]))
    set_free(("bank", 4), (c_v, bldd[DC - 1]))

    for tt in range(NT):
        done = x2d[(DC - 1, tt)]

        @block.sync
        def _(sy, tt=tt, done=done):
            sy.wait_ge(c_v.sem, done)
            sy.dma_start(out=x2_scr[tt * 128:(tt + 1) * 128, :],
                         in_=x2t_sb[:, tt, :]).then_inc(c_sy.sem, 16)
        c_sy.n += 16

    p6.close()
    oscf.close()
    if stop_after == "p6":
        finish()
        return nc, st

    emit_ffn(2)

    finish()
    return nc, st


def _host_masks(core):
    S0 = 1024 * (core % 4)
    m01 = np.zeros((8, 2, 128, 128), np.float32)
    eb = np.zeros((8, 6, 128), np.float32)
    for qt in range(8):
        for ii, i in enumerate((0, 4)):
            kb = 128 * qt + 128 * i + np.arange(128)[:, None]
            kg = S0 - 256 + kb
            qg = S0 + 128 * qt + np.arange(128)[None, :]
            ok = (kg >= 0) & (kg < S) & ((np.abs(qg - kg) < 256) | (kg < GK))
            m01[qt, ii] = np.where(ok, 1.0, 0.0)
        for i in range(5):
            kg0 = S0 - 256 + 128 * qt + 128 * i
            valid = (kg0 >= 0) and (kg0 + 127 < S)
            eb[qt, i, :] = -STAB if valid else MSK
        eb[qt, 5, :] = MSK
        dbl = (S0 == 0) and (qt <= 2)
        eb[qt, 5, 0:GK] = MSK if dbl else -STAB
    c4 = core % 4
    hs = np.zeros(12, np.float32)
    if c4 % 2 == 0:
        hs[9] = 1.0
    else:
        hs[8] = 1.0
    if c4 == 1:
        hs[11] = 1.0
    else:
        hs[10] = 1.0
    if c4 == 1:
        hs[0] = 1.0
    elif c4 == 2:
        hs[2] = 1.0
    elif c4 == 3:
        hs[0] = 1.0
    if c4 == 0:
        hs[5] = 1.0
    elif c4 == 1:
        hs[7] = 1.0
    elif c4 == 2:
        hs[5] = 1.0
    hsel = np.broadcast_to(hs, (128, 12)).copy()
    gbl = np.full((128, GK), 1.0 if c4 == 0 else 0.0, np.float32)
    return m01, eb, hsel, gbl


def _prep_inputs(x, Wq, Wk, Wv, Wo, W_g, W_u, W_d):
    import ml_dtypes
    x = np.ascontiguousarray(np.asarray(x, np.float32))
    Wq, Wk, Wv, Wo, W_g, W_u, W_d = (
        np.asarray(w, np.float32) for w in (Wq, Wk, Wv, Wo, W_g, W_u, W_d))

    wg = np.ascontiguousarray(W_g.reshape(HC, 128, DC, 128).transpose(0, 3, 2, 1))
    wu = np.ascontiguousarray(W_u.reshape(HC, 128, DC, 128).transpose(0, 3, 2, 1))
    wd = np.ascontiguousarray(W_d.reshape(DC, 128, HC, 128).transpose(0, 3, 2, 1))
    wq = np.ascontiguousarray(Wq.reshape(DC, 128, 4, 512).transpose(2, 1, 0, 3))
    wk = np.ascontiguousarray(Wk.reshape(DC, 128, 4, 512).transpose(2, 1, 0, 3))
    wv = np.ascontiguousarray(Wv.reshape(DC, 128, 4, 512).transpose(2, 1, 0, 3))
    wo = np.ascontiguousarray(Wo.reshape(DC, 128, DC, 128).transpose(2, 1, 0, 3))
    ident = np.eye(128, dtype=np.float32)

    in_maps = []
    for core in range(8):
        b, c4 = core // 4, core % 4
        m01, eb, hsel, gbl = _host_masks(core)
        in_maps.append({
            "x": np.ascontiguousarray(x[b, c4 * T:(c4 + 1) * T, :]),
            "wg": wg, "wu": wu, "wd": wd,
            "wq": wq, "wk": wk, "wv": wv, "wo": wo,
            "m01": np.ascontiguousarray(
                m01.transpose(2, 0, 1, 3)).astype(ml_dtypes.bfloat16),
            "eb": np.ascontiguousarray(eb.transpose(2, 0, 1)),
            "hsel": hsel, "gbl": gbl, "ident": ident,
        })
    return in_maps


def kernel(L0_attn_hidden_states, L0_attn_Wq, L0_attn_Wk, L0_attn_Wv,
           L0_attn_Wo, L0_ffn_W_g, L0_ffn_W_u, L0_ffn_W_d):
    in_maps = _prep_inputs(L0_attn_hidden_states, L0_attn_Wq, L0_attn_Wk,
                           L0_attn_Wv, L0_attn_Wo, L0_ffn_W_g, L0_ffn_W_u,
                           L0_ffn_W_d)
    nc, st = build(STOP_AFTER)
    st.close()
    res = run_bass_kernel_spmd(nc, in_maps, core_ids=list(range(8)))
    out = np.empty((2, S, D), np.float32)
    for core in range(8):
        b, c4 = core // 4, core % 4
        out[b, c4 * T:(c4 + 1) * T, :] = res.results[core]["out"]
    return out


# version 04
# speedup vs baseline: 1.0572x; 1.0572x over previous
import numpy as np
from contextlib import ExitStack

import concourse.bass as bass
import concourse.mybir as mybir
from concourse.bass_utils import run_bass_kernel_spmd

try:
    import jax
    jax.config.update("jax_compilation_cache_dir", "/tmp/jaxcache")
    jax.config.update("jax_persistent_cache_min_entry_size_bytes", -1)
    jax.config.update("jax_persistent_cache_min_compile_time_secs", 0.0)
except Exception:
    pass

F32 = mybir.dt.float32
BF16 = mybir.dt.bfloat16
AF = mybir.ActivationFunctionType
ALU = mybir.AluOpType
AXX = mybir.AxisListType.X

T = 1024
NT = 8
D = 2048
DC = 16
H = 8192
HC = 64
NH = 16
S = 4096
HALO = 256
GK = 16
KW = 1664
KT = 13
OWN0 = 256
GCOL = 1536
STAB = 16.0
MSK = -1.0e30
EPS1 = 1e-5
EPS2 = 1e-8
SCL = float(1.0 / np.sqrt(np.float32(128.0)))
INV_D = 1.0 / 2048.0
BIGS = 1.0e38

STOP_AFTER = None
DEBUG_SCR = False


def build(stop_after=None):
    nc = bass.Bass("TRN2", target_bir_lowering=False, detect_race_conditions=False)
    st = ExitStack()
    e = st.enter_context

    def inp(name, shape, dt=F32):
        return nc.dram_tensor(name, list(shape), dt, kind="ExternalInput").ap()

    x_in = inp("x", (T, D))
    wg_in = inp("wg", (HC, 128, DC, 128))
    wu_in = inp("wu", (HC, 128, DC, 128))
    wd_in = inp("wd", (DC, 128, HC, 128))
    wq_in = inp("wq", (4, 128, DC, 512))
    wk_in = inp("wk", (4, 128, DC, 512))
    wv_in = inp("wv", (4, 128, DC, 512))
    wo_in = inp("wo", (DC, 128, DC, 128))
    m01_in = inp("m01", (128, 8, 2, 128), BF16)
    eb_in = inp("eb", (128, 8, 6))
    hsel_in = inp("hsel", (128, 12))
    gbl_in = inp("gbl", (128, 16))
    id_in = inp("ident", (128, 128))

    out_t = nc.dram_tensor("out", [T, D], F32, kind="ExternalOutput").ap()

    def scr(name, shape, dt=F32):
        kind = "ExternalOutput" if DEBUG_SCR else "Internal"
        return nc.dram_tensor(name, list(shape), dt, kind=kind).ap()

    x1_scr = scr("x1_scr", (T, D))
    x2_scr = scr("x2_scr", (T, D))
    ln1_scr = scr("ln1_scr", (NT, 128, DC, 128), BF16)
    sgn_g = scr("sgn_g", (HC, 128, DC, 128), BF16)
    sgn_u = scr("sgn_u", (HC, 128, DC, 128), BF16)
    sgn_d = scr("sgn_d", (DC, 128, HC, 128), BF16)
    ccA_i = scr("ccA_i", (128, DC, HALO), BF16)
    ccA_o = scr("ccA_o", (2, 128, DC, HALO), BF16)
    ccB_i = scr("ccB_i", (128, DC, HALO), BF16)
    ccB_o = scr("ccB_o", (2, 128, DC, HALO), BF16)
    ccC_i = scr("ccC_i", (128, DC, GK), BF16)
    ccC_o = scr("ccC_o", (4, 128, DC, GK), BF16)
    ccD_i = scr("ccD_i", (128, 512))
    ccD_o = scr("ccD_o", (4, 128, 512))

    class Ctr:
        def __init__(self, name):
            self.sem = e(nc.semaphore(name))
            self.n = 0

    c_sy = Ctr("c_sy")
    c_gp = Ctr("c_gp")
    c_t = Ctr("c_t")
    c_v = Ctr("c_v")
    c_s = Ctr("c_s")
    c_cc = Ctr("c_cc")

    R = dict(side="right")
    ident = e(nc.sbuf_tensor("identS", [128, 128], F32, **R))
    m01 = e(nc.sbuf_tensor("m01S", [128, 8, 2, 128], BF16, **R))
    ebias = e(nc.sbuf_tensor("ebiasS", [128, 8, 6], F32, **R))
    hsel = e(nc.sbuf_tensor("hselS", [128, 12], F32, **R))
    gbl = e(nc.sbuf_tensor("gblS", [128, 16], F32, **R))
    ones_col = e(nc.sbuf_tensor("ones_col", [128, 1], BF16, **R))
    ones_row = e(nc.sbuf_tensor("ones_row", [1, 128], F32, **R))
    neg16 = e(nc.sbuf_tensor("neg16", [128, 1], F32, **R))
    stats = e(nc.sbuf_tensor("stats", [128, 16], F32, **R))
    bnscr = e(nc.sbuf_tensor("bnscr", [128, 24], F32, **R))
    gsc = e(nc.sbuf_tensor("gsc", [128, DC, GK], BF16, **R))

    block = e(nc.Block())
    psum_f = [e(nc.psum_tensor(f"pb{i}", [128, 512], F32)) for i in range(8)]

    @block.sync
    def _(sy):
        for dst, src in ((ident, id_in), (m01, m01_in), (ebias, eb_in),
                         (hsel, hsel_in), (gbl, gbl_in)):
            sy.dma_start(out=dst[:], in_=src).then_inc(c_sy.sem, 16)
    c_sy.n += 5 * 16
    consts_ld = c_sy.n

    @block.vector
    def _(v):
        v.memset(ones_col[:], 1.0)
        v.memset(ones_row[:], 1.0)
        v.memset(neg16[:], -STAB).then_inc(c_v.sem, 1)
    c_v.n += 1
    consts_v = c_v.n

    @block.scalar
    def _(s):
        s.wait_ge(c_v.sem, consts_v)
        s.wait_ge(c_sy.sem, consts_ld)

    @block.tensor
    def _(t):
        t.wait_ge(c_v.sem, consts_v)
        t.wait_ge(c_sy.sem, consts_ld)

    wfree = {}

    def wait_free(engine, key):
        for ctr, val in wfree.get(key, ()):
            engine.wait_ge(ctr.sem, val)

    def set_free(key, *deps):
        wfree[key] = list(deps)


    def finish():
        @block.sync
        def _(sy):
            sy.wait_ge(c_sy.sem, c_sy.n)
        @block.gpsimd
        def _(g):
            g.wait_ge(c_gp.sem, c_gp.n)

    def vstep(fn):
        @block.vector
        def _(v):
            fn(v).then_inc(c_v.sem, 1)
        c_v.n += 1
        return c_v.n

    def emit_ln(src, dst, eps, tmp_sq, src_s=None, src_v=None, src_sy=None,
                dst_t=None):
        def _w(eng):
            if src_s is not None:
                eng.wait_ge(c_s.sem, src_s)
            if src_v is not None:
                eng.wait_ge(c_v.sem, src_v)
            if src_sy is not None:
                eng.wait_ge(c_sy.sem, src_sy)

        def f(v):
            _w(v)
            for k in range(4):
                r = v.bn_stats(bnscr[:, 6 * k:6 * k + 6],
                               src[:, 512 * k:512 * k + 512])
            return r
        va = vstep(f)

        def f(v):
            v.wait_ge(c_v.sem, va)
            return v.bn_aggr(stats[:, 0:2], bnscr[:, 0:24])
        vb = vstep(f)

        def f(v):
            v.wait_ge(c_v.sem, vb)
            return v.tensor_scalar_add(stats[:, 6:7], stats[:, 1:2], float(eps))
        ve = vstep(f)

        @block.scalar
        def _(s):
            s.wait_ge(c_v.sem, ve)
            s.activation(out=stats[:, 7:8], in_=stats[:, 6:7], func=AF.Sqrt).then_inc(c_s.sem, 1)
        c_s.n += 1
        sddone = c_s.n

        def f(v):
            v.wait_ge(c_s.sem, sddone)
            return v.reciprocal(stats[:, 8:9], stats[:, 7:8])
        vf = vstep(f)

        def f(v):
            v.wait_ge(c_v.sem, vf)
            return v.scalar_tensor_tensor(stats[:, 9:10], stats[:, 0:1], -1.0,
                                          stats[:, 8:9], ALU.mult, ALU.mult)
        vg = vstep(f)

        def f(v):
            v.wait_ge(c_v.sem, vg)
            if dst_t is not None:
                v.wait_ge(c_t.sem, dst_t)
            return v.tensor_scalar(dst, src, stats[:, 8:9], stats[:, 9:10],
                                   ALU.mult, ALU.add)
        return vstep(f)

    def emit_quant(z, y, a, cc, src_s):
        MAGIC = 12582912.0

        def f(v):
            v.wait_ge(c_v.sem, src_s)
            return v.tensor_reduce(stats[:, 10:11], z, axis=AXX, op=ALU.max,
                                   apply_absolute_value=True)
        qa = vstep(f)

        def f(v):
            v.wait_ge(c_v.sem, qa)
            return v.reciprocal(stats[:, 11:12], stats[:, 10:11])
        qb = vstep(f)

        def f(v):
            v.wait_ge(c_v.sem, qb)
            v.tensor_scalar_mul(stats[:, 12:13], stats[:, 11:12], 127.0)
            return v.tensor_scalar_mul(stats[:, 13:14], stats[:, 10:11],
                                       1.0 / 127.0)
        qc = vstep(f)

        def f(v):
            v.wait_ge(c_v.sem, qc)
            v.tensor_scalar_mul(y, z, stats[:, 12:13])
            v.tensor_scalar(a, y, MAGIC, -MAGIC, ALU.add, ALU.add)
            v.tensor_scalar(a, a, 127.0, -128.0, ALU.min, ALU.max)
            return v.tensor_scalar_mul(a, a, stats[:, 13:14])
        return vstep(f)

    def emit_transposes(dst_fn, src, n, src_v=None, src_s=None, src_sy=None):
        sready = c_s.n
        last_ev = {0: None, 1: None}
        for i in range(n):
            bk = 6 + (i % 2)
            pb = psum_f[bk]

            @block.tensor
            def _(t, i=i, pb=pb, bk=bk):
                if src_v is not None:
                    t.wait_ge(c_v.sem, src_v)
                if src_s is not None:
                    t.wait_ge(c_s.sem, src_s)
                if src_sy is not None:
                    t.wait_ge(c_sy.sem, src_sy)
                if i < 2:
                    wait_free(t, ("bank", bk))
                else:
                    t.wait_ge(c_s.sem, sready + i - 1)
                t.transpose(pb[:, 0:128], src[:, i * 128:(i + 1) * 128],
                            ident[:]).then_inc(c_t.sem, 1)
            c_t.n += 1
            trd = c_t.n

            @block.scalar
            def _(s, i=i, pb=pb, trd=trd):
                s.wait_ge(c_t.sem, trd)
                s.copy(out=dst_fn(i), in_=pb[:, 0:128]).then_inc(c_s.sem, 1)
            c_s.n += 1
            last_ev[i % 2] = c_s.n
        for par in (0, 1):
            if last_ev[par] is not None:
                set_free(("bank", 6 + par), (c_s, last_ev[par]))
        return c_s.n

    L = dict(side="left")

    def emit_entry(f, src_dram):
        sub = ExitStack()
        A = sub.enter_context(nc.sbuf_tensor(f["tag"] + "A", [128, D], F32, **L))
        DD = sub.enter_context(nc.sbuf_tensor(f["tag"] + "DD", [128, D], F32, **L))
        B, C = f["B"], f["C"]
        prev_tile_s = c_s.n
        for it in range(NT):
            @block.sync
            def _(sy, it=it, prev_tile_s=prev_tile_s):
                sy.wait_ge(c_s.sem, prev_tile_s)
                sy.dma_start(out=A[:], in_=src_dram[it * 128:(it + 1) * 128, :]
                             ).then_inc(c_sy.sem, 16)
            c_sy.n += 16
            ld = c_sy.n

            l1 = emit_ln(A[:], B[:], EPS1, C[:], src_sy=ld)
            zs = emit_ln(B[:], A[:], EPS2, C[:], src_v=l1)
            emit_quant(A[:], B[:], DD[:], C[:], zs)
            prev_tile_s = emit_transposes(
                lambda dc, it=it: f["xq"][:, dc, it * 128:(it + 1) * 128],
                DD[:], DC, src_v=c_v.n)
        sub.close()
        return c_s.n

    def emit_stage12(f, grp, first_pass, xq_rdy):
        t0 = grp * 512
        mm12 = {}
        pmul = {}
        for jc in range(HC):
            b = jc % 2

            @block.gpsimd
            def _(g, jc=jc, b=b):
                wait_free(g, ("gu", b))
                if first_pass:
                    g.dma_start(out=f["wg"][b][:], in_=wg_in[jc]).then_inc(c_gp.sem, 16)
                    g.dma_start(out=f["wu"][b][:], in_=wu_in[jc]).then_inc(c_gp.sem, 16)
                else:
                    g.dma_start(out=f["wg"][b][:], in_=sgn_g[jc]).then_inc(c_gp.sem, 16)
                    g.dma_start(out=f["wu"][b][:], in_=sgn_u[jc]).then_inc(c_gp.sem, 16)
            c_gp.n += 32
            wld = c_gp.n

            if first_pass:
                @block.vector
                def _(v, b=b, wld=wld):
                    v.wait_ge(c_gp.sem, wld)
                    v.tensor_scalar(f["wg"][b][:], f["wg"][b][:], BIGS, None, ALU.mult)
                    v.tensor_scalar(f["wg"][b][:], f["wg"][b][:], -1.0, 1.0,
                                    ALU.max, ALU.min).then_inc(c_v.sem, 1)
                c_v.n += 1
                gsd = c_v.n

                @block.vector
                def _(v, b=b, wld=wld):
                    v.wait_ge(c_gp.sem, wld)
                    v.tensor_scalar(f["wu"][b][:], f["wu"][b][:], BIGS, None, ALU.mult)
                    v.tensor_scalar(f["wu"][b][:], f["wu"][b][:], -1.0, 1.0,
                                    ALU.max, ALU.min).then_inc(c_v.sem, 1)
                c_v.n += 1
                usd = c_v.n

                @block.sync
                def _(sy, jc=jc, b=b, gsd=gsd, usd=usd):
                    sy.wait_ge(c_v.sem, usd)
                    sy.dma_start(out=sgn_g[jc], in_=f["wg"][b][:]).then_inc(c_sy.sem, 16)
                    sy.dma_start(out=sgn_u[jc], in_=f["wu"][b][:]).then_inc(c_sy.sem, 16)
                c_sy.n += 32
                scr_wr = c_sy.n
            else:
                gsd = usd = scr_wr = None

            pg, pu = psum_f[0 + 2 * b], psum_f[1 + 2 * b]

            @block.tensor
            def _(t, jc=jc, b=b, wld=wld, gsd=gsd, usd=usd, pg=pg, pu=pu):
                if gsd is not None:
                    t.wait_ge(c_v.sem, usd)
                else:
                    t.wait_ge(c_gp.sem, wld)
                if jc == 0:
                    t.wait_ge(c_s.sem, xq_rdy)
                if jc >= 2:
                    t.wait_ge(c_v.sem, pmul[jc - 2])
                else:
                    wait_free(t, ("bank", 0 + 2 * b))
                    wait_free(t, ("bank", 1 + 2 * b))
                for dc in range(DC):
                    mm = t.matmul(pg[:], f["wg"][b][:, dc, :],
                                  f["xq"][:, dc, t0:t0 + 512],
                                  start=(dc == 0), stop=(dc == DC - 1))
                mm.then_inc(c_t.sem, 1)
                for dc in range(DC):
                    mm = t.matmul(pu[:], f["wu"][b][:, dc, :],
                                  f["xq"][:, dc, t0:t0 + 512],
                                  start=(dc == 0), stop=(dc == DC - 1))
                mm.then_inc(c_t.sem, 1)
            c_t.n += 2
            mm12[jc] = c_t.n
            deps = [(c_t, mm12[jc])]
            if first_pass:
                deps.append((c_sy, scr_wr))
            set_free(("gu", b), *deps)

            @block.scalar
            def _(s, pg=pg, mmd=mm12[jc]):
                s.wait_ge(c_t.sem, mmd - 1)
                s.activation(out=f["g"][:], in_=pg[:], func=AF.Silu).then_inc(c_s.sem, 1)
            c_s.n += 1
            evd = c_s.n

            ucp = c_v.n + 1

            @block.vector
            def _(v, jc=jc, pu=pu, evd=evd, mmd=mm12[jc], ucp=ucp):
                v.wait_ge(c_t.sem, mmd)
                v.tensor_copy(f["u"][:], pu[:]).then_inc(c_v.sem, 1)
                v.wait_ge(c_v.sem, ucp)
                v.wait_ge(c_s.sem, evd)
                v.tensor_mul(f["p"][:, jc, :], f["g"][:], f["u"][:]).then_inc(c_v.sem, 1)
            c_v.n += 2
            pmul[jc] = c_v.n
        set_free(("bank", 0), (c_s, c_s.n), (c_v, c_v.n))
        set_free(("bank", 1), (c_s, c_s.n), (c_v, c_v.n))
        set_free(("bank", 2), (c_s, c_s.n), (c_v, c_v.n))
        set_free(("bank", 3), (c_s, c_s.n), (c_v, c_v.n))
        return pmul[HC - 1]

    def emit_stage3(f, app, grp, first_pass, src_res, dst_res, p_rdy):
        t0 = grp * 512
        x1rd = [0] * 4
        barrier_s = c_s.n
        barrier_v = c_v.n
        for tt in range(4):
            @block.sync
            def _(sy, tt=tt):
                sy.wait_ge(c_s.sem, barrier_s)
                sy.wait_ge(c_v.sem, barrier_v)
                r0 = t0 + tt * 128
                sy.dma_start(out=f["x1g"][:, tt, :],
                             in_=src_res[r0:r0 + 128, :]).then_inc(c_sy.sem, 16)
            c_sy.n += 16
            x1rd[tt] = c_sy.n

        s3ev = {}
        xadd = {}
        for ddc in range(DC):
            b = ddc % 2

            @block.gpsimd
            def _(g, ddc=ddc, b=b):
                wait_free(g, ("wd", b))
                if first_pass:
                    g.dma_start(out=f["wd"][b][:], in_=wd_in[ddc]).then_inc(c_gp.sem, 16)
                else:
                    g.dma_start(out=f["wd"][b][:], in_=sgn_d[ddc]).then_inc(c_gp.sem, 16)
            c_gp.n += 16
            wld = c_gp.n

            if first_pass:
                @block.vector
                def _(v, b=b, wld=wld):
                    v.wait_ge(c_gp.sem, wld)
                    v.tensor_scalar(f["wd"][b][:], f["wd"][b][:],
                                    BIGS, None, ALU.mult)
                    v.tensor_scalar(f["wd"][b][:], f["wd"][b][:],
                                    -1.0, 1.0, ALU.max, ALU.min).then_inc(c_v.sem, 1)
                c_v.n += 1
                dsv = c_v.n
                dss = None

                @block.sync
                def _(sy, ddc=ddc, b=b, dsv=dsv):
                    sy.wait_ge(c_v.sem, dsv)
                    sy.dma_start(out=sgn_d[ddc], in_=f["wd"][b][:]).then_inc(c_sy.sem, 16)
                c_sy.n += 16
                scr3_wr = c_sy.n
            else:
                dsv = dss = scr3_wr = None

            po = psum_f[0 + b]

            @block.tensor
            def _(t, ddc=ddc, b=b, wld=wld, dsv=dsv, po=po):
                if dsv is not None:
                    t.wait_ge(c_v.sem, dsv)
                else:
                    t.wait_ge(c_gp.sem, wld)
                if ddc == 0:
                    t.wait_ge(c_v.sem, p_rdy)
                if ddc >= 2:
                    t.wait_ge(c_s.sem, s3ev[ddc - 2])
                else:
                    wait_free(t, ("bank", b))
                for jc in range(HC):
                    mm = t.matmul(po[:], f["wd"][b][:, jc, :], f["p"][:, jc, :],
                                  start=(jc == 0), stop=(jc == HC - 1))
                mm.then_inc(c_t.sem, 1)
            c_t.n += 1
            mmd = c_t.n
            deps = [(c_t, mmd)]
            if first_pass:
                deps.append((c_sy, scr3_wr))
            set_free(("wd", b), *deps)

            @block.scalar
            def _(s, po=po, mmd=mmd):
                s.wait_ge(c_t.sem, mmd)
                s.mul(out=f["ct"][:], in_=po[:], mul=0.5).then_inc(c_s.sem, 1)
            c_s.n += 1
            s3ev[ddc] = c_s.n
            ctd = c_s.n

            for tt in range(4):
                pb = psum_f[6 + (tt % 2)]

                @block.tensor
                def _(t, ddc=ddc, tt=tt, pb=pb, ctd=ctd):
                    t.wait_ge(c_s.sem, ctd)
                    t.wait_ge(c_v.sem, xadd.get((ddc - 1, tt), 0))
                    t.wait_ge(c_v.sem, xadd.get((ddc, tt - 2), 0))
                    if ddc == 0 and tt < 2:
                        wait_free(t, ("bank", 6 + tt % 2))
                    t.transpose(pb[:, 0:128], f["ct"][:, tt * 128:(tt + 1) * 128],
                                ident[:]).then_inc(c_t.sem, 1)
                c_t.n += 1
                trd = c_t.n

                @block.vector
                def _(v, ddc=ddc, tt=tt, pb=pb, trd=trd):
                    v.wait_ge(c_t.sem, trd)
                    v.wait_ge(c_sy.sem, x1rd[tt])
                    v.tensor_add(f["x1g"][:, tt, ddc * 128:(ddc + 1) * 128],
                                 pb[:, 0:128],
                                 f["x1g"][:, tt, ddc * 128:(ddc + 1) * 128]
                                 ).then_inc(c_v.sem, 1)
                c_v.n += 1
                xadd[(ddc, tt)] = c_v.n
        set_free(("bank", 6), (c_v, xadd[(DC - 1, 2)]))
        set_free(("bank", 7), (c_v, xadd[(DC - 1, 3)]))
        set_free(("bank", 0), (c_s, s3ev[DC - 2]))
        set_free(("bank", 1), (c_s, s3ev[DC - 1]))

        for tt in range(4):
            done = xadd[(DC - 1, tt)]

            @block.sync
            def _(sy, tt=tt, done=done):
                sy.wait_ge(c_v.sem, done)
                r0 = t0 + tt * 128
                sy.dma_start(out=dst_res[r0:r0 + 128, :],
                             in_=f["x1g"][:, tt, :]).then_inc(c_sy.sem, 16)
            c_sy.n += 16

            if app == 1:
                lnd = emit_ln(f["x1g"][:, tt, :], f["B"][:], EPS1, f["C"][:],
                              src_v=done, dst_t=c_t.n)
                lnstg_wr = wfree.get(("lnstg",))
                emit_transposes(
                    lambda dc: f["lnstg"][:, dc, :], f["B"][:], DC, src_v=lnd,
                    src_sy=(lnstg_wr[0][1] if lnstg_wr else None))
                gt = grp * 4 + tt

                @block.sync
                def _(sy, gt=gt, lst=c_s.n):
                    sy.wait_ge(c_s.sem, lst)
                    sy.dma_start(out=ln1_scr[gt], in_=f["lnstg"][:]
                                 ).then_inc(c_sy.sem, 16)
                c_sy.n += 16
                set_free(("lnstg",), (c_sy, c_sy.n))

    def emit_ffn(app):
        fs = ExitStack()
        fe = fs.enter_context
        tag = f"f{app}"
        f = {"tag": tag}
        f["p"] = fe(nc.sbuf_tensor(tag + "p", [128, HC, 512], BF16, **L))
        f["xq"] = fe(nc.sbuf_tensor(tag + "xq", [128, DC, T], BF16, **L))
        f["B"] = fe(nc.sbuf_tensor(tag + "B", [128, D], F32, **L))
        f["C"] = fe(nc.sbuf_tensor(tag + "C", [128, D], F32, **L))
        f["lnstg"] = fe(nc.sbuf_tensor(tag + "lnstg", [128, DC, 128], BF16, **L))
        f["wg"] = [fe(nc.sbuf_tensor(f"{tag}wg{b}", [128, DC, 128], BF16, **L))
                   for b in range(2)]
        f["wu"] = [fe(nc.sbuf_tensor(f"{tag}wu{b}", [128, DC, 128], BF16, **L))
                   for b in range(2)]
        f["g"] = fe(nc.sbuf_tensor(tag + "g", [128, 512], BF16, **L))
        f["u"] = fe(nc.sbuf_tensor(tag + "u", [128, 512], BF16, **L))
        f["ct"] = fe(nc.sbuf_tensor(tag + "ct", [128, 512], F32, **L))
        f["x1g"] = fe(nc.sbuf_tensor(tag + "x1g", [128, 4, D], F32, **L))

        src = x_in if app == 1 else x2_scr
        dst = x1_scr if app == 1 else out_t
        if DEBUG_SCR and app == 1:
            xq_dbg = scr("xq_dbg", (128, DC, T), BF16)
            st_dbg = scr("st_dbg", (128, 16))
            id_dbg = scr("id_dbg", (128, 128))
        for b in range(2):
            set_free(("gu", b), (c_v, c_v.n), (c_t, c_t.n))
            set_free(("wd", b), (c_v, c_v.n), (c_t, c_t.n))
        xq_rdy = emit_entry(f, src)
        if DEBUG_SCR and app == 1:
            @block.sync
            def _(sy):
                sy.wait_ge(c_s.sem, xq_rdy)
                sy.dma_start(out=xq_dbg, in_=f["xq"][:]).then_inc(c_sy.sem, 16)
                sy.dma_start(out=st_dbg, in_=stats[:]).then_inc(c_sy.sem, 16)
                sy.dma_start(out=id_dbg, in_=ident[:]).then_inc(c_sy.sem, 16)
            c_sy.n += 48

        sub = ExitStack()
        f["wd"] = [sub.enter_context(
            nc.sbuf_tensor(f"{tag}wd{b}", [128, HC, 128], BF16, **L))
            for b in range(2)]
        for grp in (0, 1):
            first = (app == 1 and grp == 0)
            p_rdy = emit_stage12(f, grp, first, xq_rdy)
            emit_stage3(f, app, grp, first, src_res=src, dst_res=dst, p_rdy=p_rdy)
        sub.close()
        fs.close()

    emit_ffn(1)
    if stop_after == "ffn1":
        finish()
        return nc, st

    oscf = ExitStack()
    osc = oscf.enter_context(nc.sbuf_tensor("osc", [128, NH, T], BF16, **L))
    oLg = oscf.enter_context(nc.sbuf_tensor("oLg", [128, DC * GK], F32, **L))
    lgf = oscf.enter_context(nc.sbuf_tensor("lgf", [1, NH * GK], F32, **L))
    gmrg = oscf.enter_context(nc.sbuf_tensor("gmrg", [128, 4, 512], F32, **L))
    lnf = ExitStack()
    ln1T = lnf.enter_context(nc.sbuf_tensor("ln1T", [128, DC, KW], BF16, **L))

    @block.vector
    def _(v):
        v.memset(ln1T[:, :, 1552:1664], 0.0).then_inc(c_v.sem, 1)
    c_v.n += 1

    @block.sync
    def _(sy):
        for tt in range(NT):
            sy.dma_start(out=ln1T[:, :, OWN0 + tt * 128:OWN0 + (tt + 1) * 128],
                         in_=ln1_scr[tt]).then_inc(c_sy.sem, 16)
    c_sy.n += NT * 16
    ln1_own = c_sy.n

    exs = ExitStack()
    xe = exs.enter_context
    exA = xe(nc.sbuf_tensor("exA", [128, DC, HALO], BF16, **L))
    exB = xe(nc.sbuf_tensor("exB", [128, DC, HALO], BF16, **L))
    exO = xe(nc.sbuf_tensor("exO", [128, 2, DC, HALO], BF16, **L))
    exO2 = xe(nc.sbuf_tensor("exO2", [128, 2, DC, HALO], BF16, **L))
    exC = xe(nc.sbuf_tensor("exC", [128, 4, DC, GK], BF16, **L))

    @block.vector
    def _(v):
        v.wait_ge(c_sy.sem, ln1_own)
        v.wait_ge(c_sy.sem, consts_ld)
        eL = ln1T[:, :, OWN0:OWN0 + HALO]
        eR = ln1T[:, :, OWN0 + T - HALO:OWN0 + T]
        v.tensor_scalar_mul(exA[:], eL, hsel[:, 8:9])
        v.scalar_tensor_tensor(exA[:], eR, hsel[:, 9:10], exA[:], ALU.mult, ALU.add)
        v.tensor_scalar_mul(exB[:], eL, hsel[:, 10:11])
        v.scalar_tensor_tensor(exB[:], eR, hsel[:, 11:12], exB[:], ALU.mult,
                               ALU.add).then_inc(c_v.sem, 1)
    c_v.n += 1
    exrdy = c_v.n

    @block.gpsimd
    def _(g):
        g.wait_ge(c_v.sem, exrdy)
        base = c_gp.n
        g.dma_start(out=ccA_i, in_=exA[:]).then_inc(c_gp.sem, 16)
        g.dma_start(out=ccB_i, in_=exB[:]).then_inc(c_gp.sem, 16)
        g.dma_start(out=ccC_i, in_=ln1T[:, :, OWN0:OWN0 + GK]).then_inc(c_gp.sem, 16)
        g.wait_ge(c_gp.sem, base + 48)
        g.collective_compute(
            "AllGather", ALU.bypass,
            replica_groups=[[0, 1], [2, 3], [4, 5], [6, 7]],
            ins=[ccA_i.opt()], outs=[ccA_o.opt()]).then_inc(c_cc.sem, 1)
        g.collective_compute(
            "AllGather", ALU.bypass,
            replica_groups=[[0, 3], [1, 2], [4, 7], [5, 6]],
            ins=[ccB_i.opt()], outs=[ccB_o.opt()]).then_inc(c_cc.sem, 1)
        g.collective_compute(
            "AllGather", ALU.bypass,
            replica_groups=[[0, 1, 2, 3], [4, 5, 6, 7]],
            ins=[ccC_i.opt()], outs=[ccC_o.opt()]).then_inc(c_cc.sem, 1)
        g.wait_ge(c_cc.sem, c_cc.n + 3)
        for sl in range(2):
            g.dma_start(out=exO[:, sl], in_=ccA_o[sl]).then_inc(c_gp.sem, 16)
            g.dma_start(out=exO2[:, sl], in_=ccB_o[sl]).then_inc(c_gp.sem, 16)
        g.dma_start(out=exC[:, 0], in_=ccC_o[0]).then_inc(c_gp.sem, 16)
    c_gp.n += 128
    c_cc.n += 3
    exld = c_gp.n

    @block.vector
    def _(v):
        v.wait_ge(c_gp.sem, exld)
        hl = ln1T[:, :, 0:HALO]
        v.tensor_scalar_mul(hl, exO[:, 0], hsel[:, 0:1])
        v.scalar_tensor_tensor(hl, exO[:, 1], hsel[:, 1:2], hl, ALU.mult, ALU.add)
        v.scalar_tensor_tensor(hl, exO2[:, 0], hsel[:, 2:3], hl, ALU.mult, ALU.add)
        v.scalar_tensor_tensor(hl, exO2[:, 1], hsel[:, 3:4], hl, ALU.mult, ALU.add)
        hr = ln1T[:, :, OWN0 + T:OWN0 + T + HALO]
        v.tensor_scalar_mul(hr, exO[:, 0], hsel[:, 4:5])
        v.scalar_tensor_tensor(hr, exO[:, 1], hsel[:, 5:6], hr, ALU.mult, ALU.add)
        v.scalar_tensor_tensor(hr, exO2[:, 0], hsel[:, 6:7], hr, ALU.mult, ALU.add)
        v.scalar_tensor_tensor(hr, exO2[:, 1], hsel[:, 7:8], hr, ALU.mult, ALU.add)
        v.tensor_copy(ln1T[:, :, GCOL:GCOL + GK], exC[:, 0]).then_inc(c_v.sem, 1)
    c_v.n += 1
    halo_done = c_v.n

    exs.close()
    if stop_after == "exchange":
        finish()
        return nc, st

    ast = ExitStack()
    ae = ast.enter_context
    kw_sb = ae(nc.sbuf_tensor("kw_sb", [128, 4, KW], BF16, **L))
    vw_sb = ae(nc.sbuf_tensor("vw_sb", [128, KT, 512], BF16, **L))
    qw_sb = ae(nc.sbuf_tensor("qw_sb", [128, 4, T], BF16, **L))
    qg_sb = ae(nc.sbuf_tensor("qg_sb", [128, 4, 128], BF16, **L))
    wat = [ae(nc.sbuf_tensor(f"wat{b}", [128, DC, 512], BF16, **L))
           for b in range(2)]
    pT_sb = ae(nc.sbuf_tensor("pT_sb", [128, 8, 128], BF16, **L))
    lr_sb = ae(nc.sbuf_tensor("lr_sb", [1, 128], F32, **L))
    lrep_sb = ae(nc.sbuf_tensor("lrep_sb", [128, 128], F32, **L))

    @block.vector
    def _(v):
        v.memset(qg_sb[:], 0.0).then_inc(c_v.sem, 1)
    c_v.n += 1
    qg_zero = c_v.n

    def proj_wave(w, which, wbuf, wdma):
        if which == "k":
            for oc in range(4):
                for ci, ts in enumerate(range(0, KW, 512)):
                    tn = min(512, KW - ts)
                    bk = 4 + (oc * 4 + ci) % 2
                    pb = psum_f[bk]

                    @block.tensor
                    def _(t, oc=oc, ts=ts, tn=tn, pb=pb, bk=bk):
                        t.wait_ge(c_gp.sem, wdma)
                        t.wait_ge(c_v.sem, halo_done)
                        wait_free(t, ("bank", bk))
                        for dc in range(DC):
                            mm = t.matmul(pb[:, 0:tn],
                                          wbuf[:, dc, oc * 128:(oc + 1) * 128],
                                          ln1T[:, dc, ts:ts + tn],
                                          start=(dc == 0), stop=(dc == DC - 1))
                        mm.then_inc(c_t.sem, 1)
                    c_t.n += 1
                    mmd = c_t.n

                    @block.scalar
                    def _(s, oc=oc, ts=ts, tn=tn, pb=pb, mmd=mmd):
                        s.wait_ge(c_t.sem, mmd)
                        s.copy(out=kw_sb[:, oc, ts:ts + tn],
                               in_=pb[:, 0:tn]).then_inc(c_s.sem, 1)
                    c_s.n += 1
                    set_free(("bank", bk), (c_s, c_s.n))
        elif which == "v":
            for tt in range(KT):
                bk = 4 + tt % 2
                pb = psum_f[bk]

                @block.tensor
                def _(t, tt=tt, pb=pb, bk=bk):
                    t.wait_ge(c_gp.sem, wdma)
                    t.wait_ge(c_v.sem, halo_done)
                    wait_free(t, ("bank", bk))
                    for dc in range(DC):
                        mm = t.matmul(pb[:], ln1T[:, dc, tt * 128:(tt + 1) * 128],
                                      wbuf[:, dc, :],
                                      start=(dc == 0), stop=(dc == DC - 1))
                    mm.then_inc(c_t.sem, 1)
                c_t.n += 1
                mmd = c_t.n

                @block.scalar
                def _(s, tt=tt, pb=pb, mmd=mmd):
                    s.wait_ge(c_t.sem, mmd)
                    s.copy(out=vw_sb[:, tt, :], in_=pb[:]).then_inc(c_s.sem, 1)
                c_s.n += 1
                set_free(("bank", bk), (c_s, c_s.n))
        else:
            for oc in range(4):
                for th in range(3):
                    bk = 4 + th % 2
                    pb = psum_f[bk]

                    @block.tensor
                    def _(t, oc=oc, th=th, pb=pb, bk=bk):
                        t.wait_ge(c_gp.sem, wdma)
                        if th == 2:
                            t.wait_ge(c_v.sem, qg_zero)
                        wait_free(t, ("bank", bk))
                        for dc in range(DC):
                            if th < 2:
                                rhs = ln1T[:, dc, OWN0 + th * 512:OWN0 + th * 512 + 512]
                                dst = pb[:]
                            else:
                                rhs = ln1T[:, dc, GCOL:GCOL + GK]
                                dst = pb[:, 0:GK]
                            mm = t.matmul(dst, wbuf[:, dc, oc * 128:(oc + 1) * 128],
                                          rhs, start=(dc == 0), stop=(dc == DC - 1))
                        mm.then_inc(c_t.sem, 1)
                    c_t.n += 1
                    mmd = c_t.n

                    @block.scalar
                    def _(s, oc=oc, th=th, pb=pb, mmd=mmd):
                        s.wait_ge(c_t.sem, mmd)
                        if th < 2:
                            s.copy(out=qw_sb[:, oc, th * 512:th * 512 + 512],
                                   in_=pb[:]).then_inc(c_s.sem, 1)
                        else:
                            s.copy(out=qg_sb[:, oc, 0:GK],
                                   in_=pb[:, 0:GK]).then_inc(c_s.sem, 1)
                    c_s.n += 1
                    set_free(("bank", bk), (c_s, c_s.n))

    def attend(w, hh, qt):
        glob = (qt == 8)
        nkt = 8 if glob else 6
        po, pl = psum_f[2], psum_f[3]
        h = w * 4 + hh

        expd = [0] * nkt
        for i in range(nkt):
            pb = psum_f[i % 2]

            @block.tensor
            def _(t, i=i, pb=pb):
                if i >= 2:
                    d = expd[i - 2]
                    if isinstance(d, tuple):
                        t.wait_ge(c_v.sem, d[1])
                    else:
                        t.wait_ge(c_s.sem, d)
                else:
                    wait_free(t, ("bank", i % 2))
                if glob:
                    kcol, nk = OWN0 + i * 128, 128
                    rhs = qg_sb[:, hh, :]
                else:
                    kcol = GCOL if i == 5 else 128 * qt + 128 * i
                    nk = GK if i == 5 else 128
                    rhs = qw_sb[:, hh, qt * 128:(qt + 1) * 128]
                mm = t.matmul(pb[0:nk, 0:128], kw_sb[:, hh, kcol:kcol + nk],
                              rhs, start=True, stop=True)
                mm.then_inc(c_t.sem, 1)
            c_t.n += 1
            smm = c_t.n

            @block.scalar
            def _(s, i=i, pb=pb, smm=smm):
                s.wait_ge(c_t.sem, smm)
                nk = GK if (not glob and i == 5) else 128
                bias = neg16[0:nk, :] if glob else ebias[0:nk, qt, i:i + 1]
                s.activation(out=pT_sb[0:nk, i, :], in_=pb[0:nk, 0:128],
                             func=AF.Exp, bias=bias, scale=SCL).then_inc(c_s.sem, 1)
            c_s.n += 1
            expd[i] = c_s.n
            set_free(("bank", i % 2), (c_s, c_s.n))

            if (not glob) and i in (0, 4):
                @block.vector
                def _(v, i=i, ed=expd[i]):
                    v.wait_ge(c_s.sem, ed)
                    v.wait_ge(c_sy.sem, consts_ld)
                    v.tensor_mul(pT_sb[:, i, :], pT_sb[:, i, :],
                                 m01[:, qt, (0 if i == 0 else 1), :]
                                 ).then_inc(c_v.sem, 1)
                c_v.n += 1
                expd[i] = ("v", c_v.n)

        for i in range(nkt):
            dep = expd[i]

            @block.tensor
            def _(t, i=i, dep=dep):
                if isinstance(dep, tuple):
                    t.wait_ge(c_v.sem, dep[1])
                else:
                    t.wait_ge(c_s.sem, dep)
                if i == 0:
                    wait_free(t, ("bank", 2))
                    wait_free(t, ("bank", 3))
                if glob:
                    nk, vt = 128, 2 + i
                else:
                    nk = GK if i == 5 else 128
                    vt = 12 if i == 5 else qt + i
                mm = t.matmul(pl[0:1, 0:128], ones_col[0:nk, :],
                              pT_sb[0:nk, i, :], start=(i == 0), stop=(i == nkt - 1))
                mm.then_inc(c_t.sem, 1)
                mm = t.matmul(po[:, 0:128], vw_sb[0:nk, vt, hh * 128:hh * 128 + 128],
                              pT_sb[0:nk, i, :], start=(i == 0), stop=(i == nkt - 1))
                mm.then_inc(c_t.sem, 1)
            c_t.n += 2
        avd = c_t.n

        if glob:
            @block.vector
            def _(v, h=h):
                v.wait_ge(c_t.sem, avd)
                v.tensor_copy(oLg[:, h * GK:(h + 1) * GK], po[:, 0:GK])
                v.tensor_copy(lgf[0:1, h * GK:(h + 1) * GK],
                              pl[0:1, 0:GK]).then_inc(c_v.sem, 1)
            c_v.n += 1
            set_free(("bank", 2), (c_v, c_v.n))
            set_free(("bank", 3), (c_v, c_v.n))
            return c_v.n

        @block.vector
        def _(v):
            v.wait_ge(c_t.sem, avd)
            v.reciprocal(lr_sb[:], pl[0:1, 0:128]).then_inc(c_v.sem, 1)
        c_v.n += 1
        lrd = c_v.n
        set_free(("bank", 3), (c_v, lrd))

        @block.tensor
        def _(t):
            t.wait_ge(c_v.sem, lrd)
            wait_free(t, ("bank", 4))
            wait_free(t, ("lrep",))
            t.matmul(psum_f[4][:, 0:128], ones_row[:], lr_sb[:],
                     start=True, stop=True).then_inc(c_t.sem, 1)
        c_t.n += 1
        repd = c_t.n

        lcp = c_v.n + 1

        @block.vector
        def _(v, h=h, lcp=lcp):
            v.wait_ge(c_t.sem, repd)
            v.tensor_copy(lrep_sb[:], psum_f[4][:, 0:128]).then_inc(c_v.sem, 1)
            v.wait_ge(c_v.sem, lcp)
            v.tensor_mul(osc[:, h, qt * 128:(qt + 1) * 128], po[:, 0:128],
                         lrep_sb[:]).then_inc(c_v.sem, 1)
        c_v.n += 2
        repc = c_v.n
        set_free(("bank", 4), (c_v, repc))
        set_free(("bank", 2), (c_v, c_v.n))
        set_free(("lrep",), (c_v, c_v.n))
        return c_v.n

    for b in range(2):
        set_free(("wat", b), (c_v, halo_done), (c_t, c_t.n))
    for w in range(4):
        for which, win, b in (("k", wk_in, 0), ("v", wv_in, 1), ("q", wq_in, 0)):
            @block.gpsimd
            def _(g, w=w, win=win, b=b):
                wait_free(g, ("wat", b))
                g.dma_start(out=wat[b][:], in_=win[w]).then_inc(c_gp.sem, 16)
            c_gp.n += 16
            proj_wave(w, which, wat[b], c_gp.n)
            set_free(("wat", b), (c_t, c_t.n))
        for hh in range(4):
            for qt in range(8):
                attend(w, hh, qt)
            attend(w, hh, 8)

    ast.close()
    lnf.close()
    attn_done_v = c_v.n
    if stop_after == "attn":
        finish()
        return nc, st

    @block.vector
    def _(v):
        v.wait_ge(c_gp.sem, c_gp.n)
        v.memset(gmrg[:, 0, 256:512], 0.0).then_inc(c_v.sem, 1)
    c_v.n += 1
    zpad = c_v.n

    @block.gpsimd
    def _(g):
        g.wait_ge(c_v.sem, attn_done_v)
        g.wait_ge(c_v.sem, zpad)
        base = c_gp.n
        g.dma_start(out=ccD_i[0:128, 0:256], in_=oLg[:]).then_inc(c_gp.sem, 16)
        g.dma_start(out=ccD_i[0:128, 256:512],
                    in_=gmrg[:, 0, 256:512]).then_inc(c_gp.sem, 16)
        g.dma_start(out=ccD_i[0:1, 256:512], in_=lgf[:]).then_inc(c_gp.sem, 16)
        g.wait_ge(c_gp.sem, base + 48)
        g.collective_compute(
            "AllGather", ALU.bypass,
            replica_groups=[[0, 1, 2, 3], [4, 5, 6, 7]],
            ins=[ccD_i.opt()], outs=[ccD_o.opt()]).then_inc(c_cc.sem, 1)
        g.wait_ge(c_cc.sem, c_cc.n + 1)
        for sl in range(4):
            g.dma_start(out=gmrg[:, sl], in_=ccD_o[sl]).then_inc(c_gp.sem, 16)
    c_gp.n += 112
    c_cc.n += 1
    mrg_ld = c_gp.n

    def f(v):
        v.wait_ge(c_gp.sem, mrg_ld)
        v.tensor_add(gmrg[:, 0, :], gmrg[:, 0, :], gmrg[:, 1, :])
        return v.tensor_add(gmrg[:, 2, :], gmrg[:, 2, :], gmrg[:, 3, :])
    m1 = vstep(f)

    def f(v):
        v.wait_ge(c_v.sem, m1)
        return v.tensor_add(gmrg[:, 0, :], gmrg[:, 0, :], gmrg[:, 2, :])
    m2 = vstep(f)

    def f(v):
        v.wait_ge(c_v.sem, m2)
        return v.reciprocal(gmrg[0:1, 1, 256:512], gmrg[0:1, 0, 256:512])
    mrgd = vstep(f)

    gsc_dep = {}
    for hq in range(NH):
        bk = 4 + hq % 2
        pb = psum_f[bk]

        @block.tensor
        def _(t, hq=hq, pb=pb, bk=bk):
            t.wait_ge(c_v.sem, mrgd)
            wait_free(t, ("bank", bk))
            t.matmul(pb[:, 0:GK], ones_row[:],
                     gmrg[0:1, 1, 256 + hq * GK:256 + (hq + 1) * GK],
                     start=True, stop=True).then_inc(c_t.sem, 1)
        c_t.n += 1
        rpd = c_t.n

        @block.vector
        def _(v, hq=hq, pb=pb, rpd=rpd):
            v.wait_ge(c_t.sem, rpd)
            v.tensor_mul(gsc[:, hq, :], gmrg[:, 0, hq * GK:(hq + 1) * GK],
                         pb[:, 0:GK]).then_inc(c_v.sem, 1)
        c_v.n += 1
        gsc_dep[hq] = c_v.n
        set_free(("bank", bk), (c_v, c_v.n))
    gsc_all = c_v.n

    p6 = ExitStack()
    p6e = p6.enter_context
    x2t_sb = p6e(nc.sbuf_tensor("x2t_sb", [128, NT, D], F32, **L))
    x1t_sb = p6e(nc.sbuf_tensor("x1t_sb", [128, NT, D], F32, **L))
    wo_sb = [p6e(nc.sbuf_tensor(f"wo_sb{b}", [128, DC, 128], BF16, **L))
             for b in range(2)]
    cta_sb = p6e(nc.sbuf_tensor("cta_sb", [128, T], F32, **L))
    gtmp_sb = p6e(nc.sbuf_tensor("gtmp_sb", [128, GK], F32, **L))

    for b in range(2):
        set_free(("wo", b), (c_v, gsc_all), (c_t, c_t.n))
    x1_ld = [0] * NT
    for tt in range(NT):
        @block.sync
        def _(sy, tt=tt):
            sy.wait_ge(c_v.sem, gsc_all)
            sy.dma_start(out=x1t_sb[:, tt, :],
                         in_=x1_scr[tt * 128:(tt + 1) * 128, :]).then_inc(c_sy.sem, 16)
        c_sy.n += 16
        x1_ld[tt] = c_sy.n

    aoev = {}
    x2d = {}
    bldd = {}
    for ddc in range(DC):
        b = ddc % 2

        @block.gpsimd
        def _(g, ddc=ddc, b=b):
            wait_free(g, ("wo", b))
            g.dma_start(out=wo_sb[b][:], in_=wo_in[ddc]).then_inc(c_gp.sem, 16)
        c_gp.n += 16
        wld = c_gp.n

        pa, pb2, pg16 = psum_f[0 + b], psum_f[2 + b], psum_f[4]

        @block.tensor
        def _(t, ddc=ddc, b=b, wld=wld, pa=pa, pb2=pb2, pg16=pg16):
            t.wait_ge(c_gp.sem, wld)
            t.wait_ge(c_v.sem, gsc_all)
            if ddc < 2:
                wait_free(t, ("bank", 0 + b))
                wait_free(t, ("bank", 2 + b))
            if ddc == 0:
                wait_free(t, ("bank", 4))
            t.wait_ge(c_s.sem, aoev.get(ddc - 2, 0))
            t.wait_ge(c_v.sem, bldd.get(ddc - 1, 0))
            for th in range(2):
                dst = pa if th == 0 else pb2
                for dvc in range(DC):
                    mm = t.matmul(dst[:], wo_sb[b][:, dvc, :],
                                  osc[:, dvc, th * 512:th * 512 + 512],
                                  start=(dvc == 0), stop=(dvc == DC - 1))
                mm.then_inc(c_t.sem, 1)
            for dvc in range(DC):
                mm = t.matmul(pg16[:, 0:GK], wo_sb[b][:, dvc, :], gsc[:, dvc, :],
                              start=(dvc == 0), stop=(dvc == DC - 1))
            mm.then_inc(c_t.sem, 1)
        c_t.n += 3
        mmd = c_t.n
        set_free(("wo", b), (c_t, mmd))

        @block.scalar
        def _(s, pa=pa, pb2=pb2, mmd=mmd):
            s.wait_ge(c_t.sem, mmd - 1)
            s.copy(out=cta_sb[:, 0:512], in_=pa[:]).then_inc(c_s.sem, 1)
            s.copy(out=cta_sb[:, 512:1024], in_=pb2[:]).then_inc(c_s.sem, 1)
        c_s.n += 2
        ctd = c_s.n
        aoev[ddc] = c_s.n

        def f(v, ddc=ddc, pg16=pg16, ctd=ctd, mmd=mmd):
            v.wait_ge(c_s.sem, ctd)
            v.wait_ge(c_t.sem, mmd)
            return v.tensor_sub(gtmp_sb[:], pg16[:, 0:GK], cta_sb[:, 0:GK])
        b1 = vstep(f)

        def f(v, b1=b1):
            v.wait_ge(c_v.sem, b1)
            return v.tensor_mul(gtmp_sb[:], gtmp_sb[:], gbl[:])
        b2 = vstep(f)

        def f(v, b2=b2):
            v.wait_ge(c_v.sem, b2)
            return v.tensor_add(cta_sb[:, 0:GK], cta_sb[:, 0:GK], gtmp_sb[:])
        bldd[ddc] = vstep(f)
        bld = bldd[ddc]

        for tt in range(NT):
            pbt = psum_f[6 + tt % 2]

            @block.tensor
            def _(t, ddc=ddc, tt=tt, pbt=pbt, bld=bld):
                t.wait_ge(c_v.sem, bld)
                t.wait_ge(c_v.sem, x2d.get((ddc - 1, tt), 0))
                t.wait_ge(c_v.sem, x2d.get((ddc, tt - 2), 0))
                if ddc == 0 and tt < 2:
                    wait_free(t, ("bank", 6 + tt % 2))
                t.transpose(pbt[:, 0:128], cta_sb[:, tt * 128:(tt + 1) * 128],
                            ident[:]).then_inc(c_t.sem, 1)
            c_t.n += 1
            trd = c_t.n

            @block.vector
            def _(v, ddc=ddc, tt=tt, pbt=pbt, trd=trd):
                v.wait_ge(c_t.sem, trd)
                v.wait_ge(c_sy.sem, x1_ld[tt])
                v.tensor_add(x2t_sb[:, tt, ddc * 128:(ddc + 1) * 128],
                             pbt[:, 0:128],
                             x1t_sb[:, tt, ddc * 128:(ddc + 1) * 128]
                             ).then_inc(c_v.sem, 1)
            c_v.n += 1
            x2d[(ddc, tt)] = c_v.n
    set_free(("bank", 6), (c_v, x2d[(DC - 1, NT - 2)]))
    set_free(("bank", 7), (c_v, x2d[(DC - 1, NT - 1)]))
    set_free(("bank", 0), (c_s, aoev[DC - 2]))
    set_free(("bank", 1), (c_s, aoev[DC - 1]))
    set_free(("bank", 2), (c_s, aoev[DC - 2]))
    set_free(("bank", 3), (c_s, aoev[DC - 1]))
    set_free(("bank", 4), (c_v, bldd[DC - 1]))

    for tt in range(NT):
        done = x2d[(DC - 1, tt)]

        @block.sync
        def _(sy, tt=tt, done=done):
            sy.wait_ge(c_v.sem, done)
            sy.dma_start(out=x2_scr[tt * 128:(tt + 1) * 128, :],
                         in_=x2t_sb[:, tt, :]).then_inc(c_sy.sem, 16)
        c_sy.n += 16

    p6.close()
    oscf.close()
    if stop_after == "p6":
        finish()
        return nc, st

    emit_ffn(2)

    finish()
    return nc, st


def _host_masks(core):
    S0 = 1024 * (core % 4)
    m01 = np.zeros((8, 2, 128, 128), np.float32)
    eb = np.zeros((8, 6, 128), np.float32)
    for qt in range(8):
        for ii, i in enumerate((0, 4)):
            kb = 128 * qt + 128 * i + np.arange(128)[:, None]
            kg = S0 - 256 + kb
            qg = S0 + 128 * qt + np.arange(128)[None, :]
            ok = (kg >= 0) & (kg < S) & ((np.abs(qg - kg) < 256) | (kg < GK))
            m01[qt, ii] = np.where(ok, 1.0, 0.0)
        for i in range(5):
            kg0 = S0 - 256 + 128 * qt + 128 * i
            valid = (kg0 >= 0) and (kg0 + 127 < S)
            eb[qt, i, :] = -STAB if valid else MSK
        eb[qt, 5, :] = MSK
        dbl = (S0 == 0) and (qt <= 2)
        eb[qt, 5, 0:GK] = MSK if dbl else -STAB
    c4 = core % 4
    hs = np.zeros(12, np.float32)
    if c4 % 2 == 0:
        hs[9] = 1.0
    else:
        hs[8] = 1.0
    if c4 == 1:
        hs[11] = 1.0
    else:
        hs[10] = 1.0
    if c4 == 1:
        hs[0] = 1.0
    elif c4 == 2:
        hs[2] = 1.0
    elif c4 == 3:
        hs[0] = 1.0
    if c4 == 0:
        hs[5] = 1.0
    elif c4 == 1:
        hs[7] = 1.0
    elif c4 == 2:
        hs[5] = 1.0
    hsel = np.broadcast_to(hs, (128, 12)).copy()
    gbl = np.full((128, GK), 1.0 if c4 == 0 else 0.0, np.float32)
    return m01, eb, hsel, gbl


def _prep_inputs(x, Wq, Wk, Wv, Wo, W_g, W_u, W_d):
    import ml_dtypes
    x = np.ascontiguousarray(np.asarray(x, np.float32))
    Wq, Wk, Wv, Wo, W_g, W_u, W_d = (
        np.asarray(w, np.float32) for w in (Wq, Wk, Wv, Wo, W_g, W_u, W_d))

    wg = np.ascontiguousarray(W_g.reshape(HC, 128, DC, 128).transpose(0, 3, 2, 1))
    wu = np.ascontiguousarray(W_u.reshape(HC, 128, DC, 128).transpose(0, 3, 2, 1))
    wd = np.ascontiguousarray(W_d.reshape(DC, 128, HC, 128).transpose(0, 3, 2, 1))
    wq = np.ascontiguousarray(Wq.reshape(DC, 128, 4, 512).transpose(2, 1, 0, 3))
    wk = np.ascontiguousarray(Wk.reshape(DC, 128, 4, 512).transpose(2, 1, 0, 3))
    wv = np.ascontiguousarray(Wv.reshape(DC, 128, 4, 512).transpose(2, 1, 0, 3))
    wo = np.ascontiguousarray(Wo.reshape(DC, 128, DC, 128).transpose(2, 1, 0, 3))
    ident = np.eye(128, dtype=np.float32)

    in_maps = []
    for core in range(8):
        b, c4 = core // 4, core % 4
        m01, eb, hsel, gbl = _host_masks(core)
        in_maps.append({
            "x": np.ascontiguousarray(x[b, c4 * T:(c4 + 1) * T, :]),
            "wg": wg, "wu": wu, "wd": wd,
            "wq": wq, "wk": wk, "wv": wv, "wo": wo,
            "m01": np.ascontiguousarray(
                m01.transpose(2, 0, 1, 3)).astype(ml_dtypes.bfloat16),
            "eb": np.ascontiguousarray(eb.transpose(2, 0, 1)),
            "hsel": hsel, "gbl": gbl, "ident": ident,
        })
    return in_maps


def kernel(L0_attn_hidden_states, L0_attn_Wq, L0_attn_Wk, L0_attn_Wv,
           L0_attn_Wo, L0_ffn_W_g, L0_ffn_W_u, L0_ffn_W_d):
    in_maps = _prep_inputs(L0_attn_hidden_states, L0_attn_Wq, L0_attn_Wk,
                           L0_attn_Wv, L0_attn_Wo, L0_ffn_W_g, L0_ffn_W_u,
                           L0_ffn_W_d)
    nc, st = build(STOP_AFTER)
    st.close()
    res = run_bass_kernel_spmd(nc, in_maps, core_ids=list(range(8)))
    out = np.empty((2, S, D), np.float32)
    for core in range(8):
        b, c4 = core // 4, core % 4
        out[b, c4 * T:(c4 + 1) * T, :] = res.results[core]["out"]
    return out


# version 05
# speedup vs baseline: 1.3055x; 1.2349x over previous
import numpy as np
from contextlib import ExitStack

import concourse.bass as bass
import concourse.mybir as mybir
from concourse.bass_utils import run_bass_kernel_spmd

try:
    import jax
    jax.config.update("jax_compilation_cache_dir", "/tmp/jaxcache")
    jax.config.update("jax_persistent_cache_min_entry_size_bytes", -1)
    jax.config.update("jax_persistent_cache_min_compile_time_secs", 0.0)
except Exception:
    pass

F32 = mybir.dt.float32
BF16 = mybir.dt.bfloat16
AF = mybir.ActivationFunctionType
ALU = mybir.AluOpType
AXX = mybir.AxisListType.X

T = 1024
NT = 8
D = 2048
DC = 16
H = 8192
HC = 64
NH = 16
S = 4096
HALO = 256
GK = 16
KW = 1664
KT = 13
OWN0 = 256
GCOL = 1536
STAB = 16.0
MSK = -1.0e30
EPS1 = 1e-5
EPS2 = 1e-8
SCL = float(1.0 / np.sqrt(np.float32(128.0)))
INV_D = 1.0 / 2048.0
BIGS = 1.0e38

STOP_AFTER = None
DEBUG_SCR = False


def build(stop_after=None):
    nc = bass.Bass("TRN2", target_bir_lowering=False, detect_race_conditions=False)
    st = ExitStack()
    e = st.enter_context

    def inp(name, shape, dt=F32):
        return nc.dram_tensor(name, list(shape), dt, kind="ExternalInput").ap()

    x_in = inp("x", (T, D))
    wg_in = inp("wg", (HC, 128, DC, 128))
    wu_in = inp("wu", (HC, 128, DC, 128))
    wd_in = inp("wd", (DC, 128, HC, 128))
    wq_in = inp("wq", (4, 128, DC, 512))
    wk_in = inp("wk", (4, 128, DC, 512))
    wv_in = inp("wv", (4, 128, DC, 512))
    wo_in = inp("wo", (DC, 128, DC, 128))
    m01_in = inp("m01", (128, 8, 2, 128), BF16)
    eb_in = inp("eb", (128, 8, 6))
    hsel_in = inp("hsel", (128, 12))
    gbl_in = inp("gbl", (128, 16))
    id_in = inp("ident", (128, 128))

    out_t = nc.dram_tensor("out", [T, D], F32, kind="ExternalOutput").ap()

    def scr(name, shape, dt=F32):
        kind = "ExternalOutput" if DEBUG_SCR else "Internal"
        return nc.dram_tensor(name, list(shape), dt, kind=kind).ap()

    x1_scr = scr("x1_scr", (T, D))
    x2_scr = scr("x2_scr", (T, D))
    ln1_scr = scr("ln1_scr", (NT, 128, DC, 128), BF16)
    sgn_g = scr("sgn_g", (HC, 128, DC, 128), BF16)
    sgn_u = scr("sgn_u", (HC, 128, DC, 128), BF16)
    sgn_d = scr("sgn_d", (DC, 128, HC, 128), BF16)
    ccA_i = scr("ccA_i", (128, DC, HALO), BF16)
    ccA_o = scr("ccA_o", (2, 128, DC, HALO), BF16)
    ccB_i = scr("ccB_i", (128, DC, HALO), BF16)
    ccB_o = scr("ccB_o", (2, 128, DC, HALO), BF16)
    ccC_i = scr("ccC_i", (128, DC, GK), BF16)
    ccC_o = scr("ccC_o", (4, 128, DC, GK), BF16)
    ccD_i = scr("ccD_i", (128, 512))
    ccD_o = scr("ccD_o", (4, 128, 512))

    class Ctr:
        def __init__(self, name):
            self.sem = e(nc.semaphore(name))
            self.n = 0

    c_sy = Ctr("c_sy")
    c_gp = Ctr("c_gp")
    c_t = Ctr("c_t")
    c_v = Ctr("c_v")
    c_s = Ctr("c_s")
    c_cc = Ctr("c_cc")

    R = dict(side="right")
    ident = e(nc.sbuf_tensor("identS", [128, 128], F32, **R))
    m01 = e(nc.sbuf_tensor("m01S", [128, 8, 2, 128], BF16, **R))
    ebias = e(nc.sbuf_tensor("ebiasS", [128, 8, 6], F32, **R))
    hsel = e(nc.sbuf_tensor("hselS", [128, 12], F32, **R))
    gbl = e(nc.sbuf_tensor("gblS", [128, 16], F32, **R))
    ones_col = e(nc.sbuf_tensor("ones_col", [128, 1], BF16, **R))
    ones_row = e(nc.sbuf_tensor("ones_row", [1, 128], F32, **R))
    neg16 = e(nc.sbuf_tensor("neg16", [128, 1], F32, **R))
    stats = e(nc.sbuf_tensor("stats", [128, 16], F32, **R))
    bnscr = e(nc.sbuf_tensor("bnscr", [128, 24], F32, **R))
    gsc = e(nc.sbuf_tensor("gsc", [128, DC, GK], BF16, **R))

    block = e(nc.Block())

    pend = {k: [] for k in ("tensor", "vector", "scalar", "gpsimd", "sync")}

    class _DQ:
        def __getattr__(self, eng):
            def dec(fn):
                pend[eng].append(fn)
                return fn
            return dec

    blockq = _DQ()

    def flush():
        for eng in ("tensor", "vector", "scalar", "gpsimd", "sync"):
            fns = pend[eng]
            pend[eng] = []
            if fns:
                getattr(block, eng)(lambda e, fns=fns: [f(e) for f in fns])

    def wdeps(eng, deps):
        for c_, v_ in deps:
            eng.wait_ge(c_.sem, v_)

    psum_f = [e(nc.psum_tensor(f"pb{i}", [128, 512], F32)) for i in range(8)]

    @blockq.sync
    def _(sy):
        for dst, src in ((ident, id_in), (m01, m01_in), (ebias, eb_in),
                         (hsel, hsel_in), (gbl, gbl_in)):
            sy.dma_start(out=dst[:], in_=src).then_inc(c_sy.sem, 16)
    c_sy.n += 5 * 16
    consts_ld = c_sy.n

    @blockq.vector
    def _(v):
        v.memset(ones_col[:], 1.0)
        v.memset(ones_row[:], 1.0)
        v.memset(neg16[:], -STAB).then_inc(c_v.sem, 1)
    c_v.n += 1
    consts_v = c_v.n

    @blockq.scalar
    def _(s):
        s.wait_ge(c_v.sem, consts_v)
        s.wait_ge(c_sy.sem, consts_ld)

    @blockq.tensor
    def _(t):
        t.wait_ge(c_v.sem, consts_v)
        t.wait_ge(c_sy.sem, consts_ld)

    wfree = {}

    def wait_free(engine, key):
        for ctr, val in wfree.get(key, ()):
            engine.wait_ge(ctr.sem, val)

    def wf(*keys):
        out = []
        for k in keys:
            out.extend(wfree.get(k, ()))
        return out

    def set_free(key, *deps):
        wfree[key] = list(deps)


    def finish():
        @blockq.sync
        def _(sy):
            sy.wait_ge(c_sy.sem, c_sy.n)
        @blockq.gpsimd
        def _(g):
            g.wait_ge(c_gp.sem, c_gp.n)

    def vstep(fn):
        @blockq.vector
        def _(v):
            fn(v).then_inc(c_v.sem, 1)
        c_v.n += 1
        return c_v.n

    def emit_ln(src, dst, eps, tmp_sq, src_s=None, src_v=None, src_sy=None,
                dst_t=None):
        def _w(eng):
            if src_s is not None:
                eng.wait_ge(c_s.sem, src_s)
            if src_v is not None:
                eng.wait_ge(c_v.sem, src_v)
            if src_sy is not None:
                eng.wait_ge(c_sy.sem, src_sy)

        def f(v):
            _w(v)
            for k in range(4):
                r = v.bn_stats(bnscr[:, 6 * k:6 * k + 6],
                               src[:, 512 * k:512 * k + 512])
            return r
        va = vstep(f)

        def f(v):
            v.wait_ge(c_v.sem, va)
            return v.bn_aggr(stats[:, 0:2], bnscr[:, 0:24])
        vb = vstep(f)

        def f(v):
            v.wait_ge(c_v.sem, vb)
            return v.tensor_scalar_add(stats[:, 6:7], stats[:, 1:2], float(eps))
        ve = vstep(f)

        @blockq.scalar
        def _(s):
            s.wait_ge(c_v.sem, ve)
            s.activation(out=stats[:, 7:8], in_=stats[:, 6:7], func=AF.Sqrt).then_inc(c_s.sem, 1)
        c_s.n += 1
        sddone = c_s.n

        def f(v):
            v.wait_ge(c_s.sem, sddone)
            return v.reciprocal(stats[:, 8:9], stats[:, 7:8])
        vf = vstep(f)

        def f(v):
            v.wait_ge(c_v.sem, vf)
            return v.scalar_tensor_tensor(stats[:, 9:10], stats[:, 0:1], -1.0,
                                          stats[:, 8:9], ALU.mult, ALU.mult)
        vg = vstep(f)

        def f(v):
            v.wait_ge(c_v.sem, vg)
            if dst_t is not None:
                v.wait_ge(c_t.sem, dst_t)
            return v.tensor_scalar(dst, src, stats[:, 8:9], stats[:, 9:10],
                                   ALU.mult, ALU.add)
        return vstep(f)

    def emit_quant(z, y, a, cc, src_s):
        MAGIC = 12582912.0

        def f(v):
            v.wait_ge(c_v.sem, src_s)
            return v.tensor_reduce(stats[:, 10:11], z, axis=AXX, op=ALU.max,
                                   apply_absolute_value=True)
        qa = vstep(f)

        def f(v):
            v.wait_ge(c_v.sem, qa)
            return v.reciprocal(stats[:, 11:12], stats[:, 10:11])
        qb = vstep(f)

        def f(v):
            v.wait_ge(c_v.sem, qb)
            v.tensor_scalar_mul(stats[:, 12:13], stats[:, 11:12], 127.0)
            return v.tensor_scalar_mul(stats[:, 13:14], stats[:, 10:11],
                                       1.0 / 127.0)
        qc = vstep(f)

        def f(v):
            v.wait_ge(c_v.sem, qc)
            v.tensor_scalar_mul(y, z, stats[:, 12:13])
            v.tensor_scalar(a, y, MAGIC, -MAGIC, ALU.add, ALU.add)
            v.tensor_scalar(a, a, 127.0, -128.0, ALU.min, ALU.max)
            return v.tensor_scalar_mul(a, a, stats[:, 13:14])
        return vstep(f)

    def emit_transposes(dst_fn, src, n, src_v=None, src_s=None, src_sy=None):
        sready = c_s.n
        last_ev = {0: None, 1: None}
        for i in range(n):
            bk = 6 + (i % 2)
            pb = psum_f[bk]
            _fdv = wf(("bank", bk)) if i < 2 else None

            @blockq.tensor
            def _(t, i=i, pb=pb, bk=bk, _fd=_fdv):
                if src_v is not None:
                    t.wait_ge(c_v.sem, src_v)
                if src_s is not None:
                    t.wait_ge(c_s.sem, src_s)
                if src_sy is not None:
                    t.wait_ge(c_sy.sem, src_sy)
                if _fd is not None:
                    wdeps(t, _fd)
                else:
                    t.wait_ge(c_s.sem, sready + i - 1)
                t.transpose(pb[:, 0:128], src[:, i * 128:(i + 1) * 128],
                            ident[:]).then_inc(c_t.sem, 1)
            c_t.n += 1
            trd = c_t.n

            @blockq.scalar
            def _(s, i=i, pb=pb, trd=trd):
                s.wait_ge(c_t.sem, trd)
                s.copy(out=dst_fn(i), in_=pb[:, 0:128]).then_inc(c_s.sem, 1)
            c_s.n += 1
            last_ev[i % 2] = c_s.n
        for par in (0, 1):
            if last_ev[par] is not None:
                set_free(("bank", 6 + par), (c_s, last_ev[par]))
        return c_s.n

    L = dict(side="left")

    def emit_entry(f, src_dram):
        sub = ExitStack()
        A = sub.enter_context(nc.sbuf_tensor(f["tag"] + "A", [128, D], F32, **L))
        DD = sub.enter_context(nc.sbuf_tensor(f["tag"] + "DD", [128, D], F32, **L))
        B, C = f["B"], f["C"]
        prev_tile_s = c_s.n
        for it in range(NT):
            @blockq.sync
            def _(sy, it=it, prev_tile_s=prev_tile_s):
                sy.wait_ge(c_s.sem, prev_tile_s)
                sy.dma_start(out=A[:], in_=src_dram[it * 128:(it + 1) * 128, :]
                             ).then_inc(c_sy.sem, 16)
            c_sy.n += 16
            ld = c_sy.n

            l1 = emit_ln(A[:], B[:], EPS1, C[:], src_sy=ld)
            zs = emit_ln(B[:], A[:], EPS2, C[:], src_v=l1)
            emit_quant(A[:], B[:], DD[:], C[:], zs)
            prev_tile_s = emit_transposes(
                lambda dc, it=it: f["xq"][:, dc, it * 128:(it + 1) * 128],
                DD[:], DC, src_v=c_v.n)
        sub.close()
        return c_s.n

    def emit_stage12(f, grp, first_pass, xq_rdy):
        t0 = grp * 512
        mm12 = {}
        pmul = {}
        for jc in range(HC):
            b = jc % 2

            _fgu = wf(("gu", b))

            @blockq.gpsimd
            def _(g, jc=jc, b=b, _fgu=_fgu):
                wdeps(g, _fgu)
                if first_pass:
                    g.dma_start(out=f["wg"][b][:], in_=wg_in[jc]).then_inc(c_gp.sem, 16)
                    g.dma_start(out=f["wu"][b][:], in_=wu_in[jc]).then_inc(c_gp.sem, 16)
                else:
                    g.dma_start(out=f["wg"][b][:], in_=sgn_g[jc]).then_inc(c_gp.sem, 16)
                    g.dma_start(out=f["wu"][b][:], in_=sgn_u[jc]).then_inc(c_gp.sem, 16)
            c_gp.n += 32
            wld = c_gp.n

            if first_pass:
                @blockq.vector
                def _(v, b=b, wld=wld):
                    v.wait_ge(c_gp.sem, wld)
                    v.tensor_scalar(f["wg"][b][:], f["wg"][b][:], BIGS, None, ALU.mult)
                    v.tensor_scalar(f["wg"][b][:], f["wg"][b][:], -1.0, 1.0,
                                    ALU.max, ALU.min).then_inc(c_v.sem, 1)
                c_v.n += 1
                gsd = c_v.n

                @blockq.vector
                def _(v, b=b, wld=wld):
                    v.wait_ge(c_gp.sem, wld)
                    v.tensor_scalar(f["wu"][b][:], f["wu"][b][:], BIGS, None, ALU.mult)
                    v.tensor_scalar(f["wu"][b][:], f["wu"][b][:], -1.0, 1.0,
                                    ALU.max, ALU.min).then_inc(c_v.sem, 1)
                c_v.n += 1
                usd = c_v.n

                @blockq.sync
                def _(sy, jc=jc, b=b, gsd=gsd, usd=usd):
                    sy.wait_ge(c_v.sem, usd)
                    sy.dma_start(out=sgn_g[jc], in_=f["wg"][b][:]).then_inc(c_sy.sem, 16)
                    sy.dma_start(out=sgn_u[jc], in_=f["wu"][b][:]).then_inc(c_sy.sem, 16)
                c_sy.n += 32
                scr_wr = c_sy.n
            else:
                gsd = usd = scr_wr = None

            pg, pu = psum_f[0 + 2 * b], psum_f[1 + 2 * b]
            _fbk = wf(("bank", 0 + 2 * b), ("bank", 1 + 2 * b)) if jc < 2 else None

            @blockq.tensor
            def _(t, jc=jc, b=b, wld=wld, gsd=gsd, usd=usd, pg=pg, pu=pu,
                  _fbk=_fbk):
                if gsd is not None:
                    t.wait_ge(c_v.sem, usd)
                else:
                    t.wait_ge(c_gp.sem, wld)
                if jc == 0:
                    t.wait_ge(c_s.sem, xq_rdy)
                if jc >= 2:
                    t.wait_ge(c_v.sem, pmul[jc - 2])
                else:
                    wdeps(t, _fbk)
                for dc in range(DC):
                    mm = t.matmul(pg[:], f["wg"][b][:, dc, :],
                                  f["xq"][:, dc, t0:t0 + 512],
                                  start=(dc == 0), stop=(dc == DC - 1))
                mm.then_inc(c_t.sem, 1)
                for dc in range(DC):
                    mm = t.matmul(pu[:], f["wu"][b][:, dc, :],
                                  f["xq"][:, dc, t0:t0 + 512],
                                  start=(dc == 0), stop=(dc == DC - 1))
                mm.then_inc(c_t.sem, 1)
            c_t.n += 2
            mm12[jc] = c_t.n
            deps = [(c_t, mm12[jc])]
            if first_pass:
                deps.append((c_sy, scr_wr))
            set_free(("gu", b), *deps)

            @blockq.scalar
            def _(s, pg=pg, mmd=mm12[jc]):
                s.wait_ge(c_t.sem, mmd - 1)
                s.activation(out=f["g"][:], in_=pg[:], func=AF.Silu).then_inc(c_s.sem, 1)
            c_s.n += 1
            evd = c_s.n

            ucp = c_v.n + 1

            @blockq.vector
            def _(v, jc=jc, pu=pu, evd=evd, mmd=mm12[jc], ucp=ucp):
                v.wait_ge(c_t.sem, mmd)
                v.tensor_copy(f["u"][:], pu[:]).then_inc(c_v.sem, 1)
                v.wait_ge(c_v.sem, ucp)
                v.wait_ge(c_s.sem, evd)
                v.tensor_mul(f["p"][:, jc, :], f["g"][:], f["u"][:]).then_inc(c_v.sem, 1)
            c_v.n += 2
            pmul[jc] = c_v.n
        set_free(("bank", 0), (c_s, c_s.n), (c_v, c_v.n))
        set_free(("bank", 1), (c_s, c_s.n), (c_v, c_v.n))
        set_free(("bank", 2), (c_s, c_s.n), (c_v, c_v.n))
        set_free(("bank", 3), (c_s, c_s.n), (c_v, c_v.n))
        return pmul[HC - 1]

    def emit_stage3(f, app, grp, first_pass, src_res, dst_res, p_rdy):
        t0 = grp * 512
        x1rd = [0] * 4
        barrier_s = c_s.n
        barrier_v = c_v.n
        for tt in range(4):
            @blockq.sync
            def _(sy, tt=tt):
                sy.wait_ge(c_s.sem, barrier_s)
                sy.wait_ge(c_v.sem, barrier_v)
                r0 = t0 + tt * 128
                sy.dma_start(out=f["x1g"][:, tt, :],
                             in_=src_res[r0:r0 + 128, :]).then_inc(c_sy.sem, 16)
            c_sy.n += 16
            x1rd[tt] = c_sy.n

        s3ev = {}
        xadd = {}
        for ddc in range(DC):
            b = ddc % 2

            _fwd = wf(("wd", b))

            @blockq.gpsimd
            def _(g, ddc=ddc, b=b, _fwd=_fwd):
                wdeps(g, _fwd)
                if first_pass:
                    g.dma_start(out=f["wd"][b][:], in_=wd_in[ddc]).then_inc(c_gp.sem, 16)
                else:
                    g.dma_start(out=f["wd"][b][:], in_=sgn_d[ddc]).then_inc(c_gp.sem, 16)
            c_gp.n += 16
            wld = c_gp.n

            if first_pass:
                @blockq.vector
                def _(v, b=b, wld=wld):
                    v.wait_ge(c_gp.sem, wld)
                    v.tensor_scalar(f["wd"][b][:], f["wd"][b][:],
                                    BIGS, None, ALU.mult)
                    v.tensor_scalar(f["wd"][b][:], f["wd"][b][:],
                                    -1.0, 1.0, ALU.max, ALU.min).then_inc(c_v.sem, 1)
                c_v.n += 1
                dsv = c_v.n
                dss = None

                @blockq.sync
                def _(sy, ddc=ddc, b=b, dsv=dsv):
                    sy.wait_ge(c_v.sem, dsv)
                    sy.dma_start(out=sgn_d[ddc], in_=f["wd"][b][:]).then_inc(c_sy.sem, 16)
                c_sy.n += 16
                scr3_wr = c_sy.n
            else:
                dsv = dss = scr3_wr = None

            po = psum_f[0 + b]
            _fb3 = wf(("bank", b)) if ddc < 2 else None

            @blockq.tensor
            def _(t, ddc=ddc, b=b, wld=wld, dsv=dsv, po=po, _fb3=_fb3):
                if dsv is not None:
                    t.wait_ge(c_v.sem, dsv)
                else:
                    t.wait_ge(c_gp.sem, wld)
                if ddc == 0:
                    t.wait_ge(c_v.sem, p_rdy)
                if ddc >= 2:
                    t.wait_ge(c_s.sem, s3ev[ddc - 2])
                else:
                    wdeps(t, _fb3)
                for jc in range(HC):
                    mm = t.matmul(po[:], f["wd"][b][:, jc, :], f["p"][:, jc, :],
                                  start=(jc == 0), stop=(jc == HC - 1))
                mm.then_inc(c_t.sem, 1)
            c_t.n += 1
            mmd = c_t.n
            deps = [(c_t, mmd)]
            if first_pass:
                deps.append((c_sy, scr3_wr))
            set_free(("wd", b), *deps)

            @blockq.scalar
            def _(s, po=po, mmd=mmd):
                s.wait_ge(c_t.sem, mmd)
                s.mul(out=f["ct"][:], in_=po[:], mul=0.5).then_inc(c_s.sem, 1)
            c_s.n += 1
            s3ev[ddc] = c_s.n
            ctd = c_s.n

            for tt in range(4):
                pb = psum_f[6 + (tt % 2)]
                _fb67 = wf(("bank", 6 + tt % 2)) if (ddc == 0 and tt < 2) else None

                @blockq.tensor
                def _(t, ddc=ddc, tt=tt, pb=pb, ctd=ctd, _fb67=_fb67):
                    t.wait_ge(c_s.sem, ctd)
                    t.wait_ge(c_v.sem, xadd.get((ddc - 1, tt), 0))
                    t.wait_ge(c_v.sem, xadd.get((ddc, tt - 2), 0))
                    if _fb67 is not None:
                        wdeps(t, _fb67)
                    t.transpose(pb[:, 0:128], f["ct"][:, tt * 128:(tt + 1) * 128],
                                ident[:]).then_inc(c_t.sem, 1)
                c_t.n += 1
                trd = c_t.n

                @blockq.vector
                def _(v, ddc=ddc, tt=tt, pb=pb, trd=trd):
                    v.wait_ge(c_t.sem, trd)
                    v.wait_ge(c_sy.sem, x1rd[tt])
                    v.tensor_add(f["x1g"][:, tt, ddc * 128:(ddc + 1) * 128],
                                 pb[:, 0:128],
                                 f["x1g"][:, tt, ddc * 128:(ddc + 1) * 128]
                                 ).then_inc(c_v.sem, 1)
                c_v.n += 1
                xadd[(ddc, tt)] = c_v.n
        set_free(("bank", 6), (c_v, xadd[(DC - 1, 2)]))
        set_free(("bank", 7), (c_v, xadd[(DC - 1, 3)]))
        set_free(("bank", 0), (c_s, s3ev[DC - 2]))
        set_free(("bank", 1), (c_s, s3ev[DC - 1]))

        for tt in range(4):
            done = xadd[(DC - 1, tt)]

            @blockq.sync
            def _(sy, tt=tt, done=done):
                sy.wait_ge(c_v.sem, done)
                r0 = t0 + tt * 128
                sy.dma_start(out=dst_res[r0:r0 + 128, :],
                             in_=f["x1g"][:, tt, :]).then_inc(c_sy.sem, 16)
            c_sy.n += 16

            if app == 1:
                lnd = emit_ln(f["x1g"][:, tt, :], f["B"][:], EPS1, f["C"][:],
                              src_v=done, dst_t=c_t.n)
                lnstg_wr = wfree.get(("lnstg",))
                emit_transposes(
                    lambda dc: f["lnstg"][:, dc, :], f["B"][:], DC, src_v=lnd,
                    src_sy=(lnstg_wr[0][1] if lnstg_wr else None))
                gt = grp * 4 + tt

                @blockq.sync
                def _(sy, gt=gt, lst=c_s.n):
                    sy.wait_ge(c_s.sem, lst)
                    sy.dma_start(out=ln1_scr[gt], in_=f["lnstg"][:]
                                 ).then_inc(c_sy.sem, 16)
                c_sy.n += 16
                set_free(("lnstg",), (c_sy, c_sy.n))

    def emit_ffn(app):
        fs = ExitStack()
        fe = fs.enter_context
        tag = f"f{app}"
        f = {"tag": tag}
        f["p"] = fe(nc.sbuf_tensor(tag + "p", [128, HC, 512], BF16, **L))
        f["xq"] = fe(nc.sbuf_tensor(tag + "xq", [128, DC, T], BF16, **L))
        f["B"] = fe(nc.sbuf_tensor(tag + "B", [128, D], F32, **L))
        f["C"] = fe(nc.sbuf_tensor(tag + "C", [128, D], F32, **L))
        f["lnstg"] = fe(nc.sbuf_tensor(tag + "lnstg", [128, DC, 128], BF16, **L))
        f["wg"] = [fe(nc.sbuf_tensor(f"{tag}wg{b}", [128, DC, 128], BF16, **L))
                   for b in range(2)]
        f["wu"] = [fe(nc.sbuf_tensor(f"{tag}wu{b}", [128, DC, 128], BF16, **L))
                   for b in range(2)]
        f["g"] = fe(nc.sbuf_tensor(tag + "g", [128, 512], BF16, **L))
        f["u"] = fe(nc.sbuf_tensor(tag + "u", [128, 512], BF16, **L))
        f["ct"] = fe(nc.sbuf_tensor(tag + "ct", [128, 512], F32, **L))
        f["x1g"] = fe(nc.sbuf_tensor(tag + "x1g", [128, 4, D], F32, **L))

        src = x_in if app == 1 else x2_scr
        dst = x1_scr if app == 1 else out_t
        if DEBUG_SCR and app == 1:
            xq_dbg = scr("xq_dbg", (128, DC, T), BF16)
            st_dbg = scr("st_dbg", (128, 16))
            id_dbg = scr("id_dbg", (128, 128))
        for b in range(2):
            set_free(("gu", b), (c_v, c_v.n), (c_t, c_t.n))
            set_free(("wd", b), (c_v, c_v.n), (c_t, c_t.n))
        xq_rdy = emit_entry(f, src)
        if DEBUG_SCR and app == 1:
            @blockq.sync
            def _(sy):
                sy.wait_ge(c_s.sem, xq_rdy)
                sy.dma_start(out=xq_dbg, in_=f["xq"][:]).then_inc(c_sy.sem, 16)
                sy.dma_start(out=st_dbg, in_=stats[:]).then_inc(c_sy.sem, 16)
                sy.dma_start(out=id_dbg, in_=ident[:]).then_inc(c_sy.sem, 16)
            c_sy.n += 48

        sub = ExitStack()
        f["wd"] = [sub.enter_context(
            nc.sbuf_tensor(f"{tag}wd{b}", [128, HC, 128], BF16, **L))
            for b in range(2)]
        for grp in (0, 1):
            first = (app == 1 and grp == 0)
            p_rdy = emit_stage12(f, grp, first, xq_rdy)
            emit_stage3(f, app, grp, first, src_res=src, dst_res=dst, p_rdy=p_rdy)
        sub.close()
        fs.close()

    emit_ffn(1)
    if stop_after == "ffn1":
        finish()
        flush()
        return nc, st

    oscf = ExitStack()
    osc = oscf.enter_context(nc.sbuf_tensor("osc", [128, NH, T], BF16, **L))
    oLg = oscf.enter_context(nc.sbuf_tensor("oLg", [128, DC * GK], F32, **L))
    lgf = oscf.enter_context(nc.sbuf_tensor("lgf", [1, NH * GK], F32, **L))
    gmrg = oscf.enter_context(nc.sbuf_tensor("gmrg", [128, 4, 512], F32, **L))
    lnf = ExitStack()
    ln1T = lnf.enter_context(nc.sbuf_tensor("ln1T", [128, DC, KW], BF16, **L))

    @blockq.vector
    def _(v):
        v.memset(ln1T[:, :, 1552:1664], 0.0).then_inc(c_v.sem, 1)
    c_v.n += 1

    @blockq.sync
    def _(sy):
        for tt in range(NT):
            sy.dma_start(out=ln1T[:, :, OWN0 + tt * 128:OWN0 + (tt + 1) * 128],
                         in_=ln1_scr[tt]).then_inc(c_sy.sem, 16)
    c_sy.n += NT * 16
    ln1_own = c_sy.n

    exs = ExitStack()
    xe = exs.enter_context
    exA = xe(nc.sbuf_tensor("exA", [128, DC, HALO], BF16, **L))
    exB = xe(nc.sbuf_tensor("exB", [128, DC, HALO], BF16, **L))
    exO = xe(nc.sbuf_tensor("exO", [128, 2, DC, HALO], BF16, **L))
    exO2 = xe(nc.sbuf_tensor("exO2", [128, 2, DC, HALO], BF16, **L))
    exC = xe(nc.sbuf_tensor("exC", [128, 4, DC, GK], BF16, **L))

    @blockq.vector
    def _(v):
        v.wait_ge(c_sy.sem, ln1_own)
        v.wait_ge(c_sy.sem, consts_ld)
        eL = ln1T[:, :, OWN0:OWN0 + HALO]
        eR = ln1T[:, :, OWN0 + T - HALO:OWN0 + T]
        v.tensor_scalar_mul(exA[:], eL, hsel[:, 8:9])
        v.scalar_tensor_tensor(exA[:], eR, hsel[:, 9:10], exA[:], ALU.mult, ALU.add)
        v.tensor_scalar_mul(exB[:], eL, hsel[:, 10:11])
        v.scalar_tensor_tensor(exB[:], eR, hsel[:, 11:12], exB[:], ALU.mult,
                               ALU.add).then_inc(c_v.sem, 1)
    c_v.n += 1
    exrdy = c_v.n

    _exbase = c_gp.n
    _exccw = c_cc.n + 3

    @blockq.gpsimd
    def _(g, base=_exbase, ccw=_exccw):
        g.wait_ge(c_v.sem, exrdy)
        g.dma_start(out=ccA_i, in_=exA[:]).then_inc(c_gp.sem, 16)
        g.dma_start(out=ccB_i, in_=exB[:]).then_inc(c_gp.sem, 16)
        g.dma_start(out=ccC_i, in_=ln1T[:, :, OWN0:OWN0 + GK]).then_inc(c_gp.sem, 16)
        g.wait_ge(c_gp.sem, base + 48)
        g.collective_compute(
            "AllGather", ALU.bypass,
            replica_groups=[[0, 1], [2, 3], [4, 5], [6, 7]],
            ins=[ccA_i.opt()], outs=[ccA_o.opt()]).then_inc(c_cc.sem, 1)
        g.collective_compute(
            "AllGather", ALU.bypass,
            replica_groups=[[0, 3], [1, 2], [4, 7], [5, 6]],
            ins=[ccB_i.opt()], outs=[ccB_o.opt()]).then_inc(c_cc.sem, 1)
        g.collective_compute(
            "AllGather", ALU.bypass,
            replica_groups=[[0, 1, 2, 3], [4, 5, 6, 7]],
            ins=[ccC_i.opt()], outs=[ccC_o.opt()]).then_inc(c_cc.sem, 1)
        g.wait_ge(c_cc.sem, ccw)
        for sl in range(2):
            g.dma_start(out=exO[:, sl], in_=ccA_o[sl]).then_inc(c_gp.sem, 16)
            g.dma_start(out=exO2[:, sl], in_=ccB_o[sl]).then_inc(c_gp.sem, 16)
        g.dma_start(out=exC[:, 0], in_=ccC_o[0]).then_inc(c_gp.sem, 16)
    c_gp.n += 128
    c_cc.n += 3
    exld = c_gp.n

    @blockq.vector
    def _(v):
        v.wait_ge(c_gp.sem, exld)
        hl = ln1T[:, :, 0:HALO]
        v.tensor_scalar_mul(hl, exO[:, 0], hsel[:, 0:1])
        v.scalar_tensor_tensor(hl, exO[:, 1], hsel[:, 1:2], hl, ALU.mult, ALU.add)
        v.scalar_tensor_tensor(hl, exO2[:, 0], hsel[:, 2:3], hl, ALU.mult, ALU.add)
        v.scalar_tensor_tensor(hl, exO2[:, 1], hsel[:, 3:4], hl, ALU.mult, ALU.add)
        hr = ln1T[:, :, OWN0 + T:OWN0 + T + HALO]
        v.tensor_scalar_mul(hr, exO[:, 0], hsel[:, 4:5])
        v.scalar_tensor_tensor(hr, exO[:, 1], hsel[:, 5:6], hr, ALU.mult, ALU.add)
        v.scalar_tensor_tensor(hr, exO2[:, 0], hsel[:, 6:7], hr, ALU.mult, ALU.add)
        v.scalar_tensor_tensor(hr, exO2[:, 1], hsel[:, 7:8], hr, ALU.mult, ALU.add)
        v.tensor_copy(ln1T[:, :, GCOL:GCOL + GK], exC[:, 0]).then_inc(c_v.sem, 1)
    c_v.n += 1
    halo_done = c_v.n

    exs.close()
    if stop_after == "exchange":
        finish()
        flush()
        return nc, st

    ast = ExitStack()
    ae = ast.enter_context
    kw_sb = ae(nc.sbuf_tensor("kw_sb", [128, 4, KW], BF16, **L))
    vw_sb = ae(nc.sbuf_tensor("vw_sb", [128, KT, 512], BF16, **L))
    qw_sb = ae(nc.sbuf_tensor("qw_sb", [128, 4, T], BF16, **L))
    qg_sb = ae(nc.sbuf_tensor("qg_sb", [128, 4, 128], BF16, **L))
    wat = [ae(nc.sbuf_tensor(f"wat{b}", [128, DC, 512], BF16, **L))
           for b in range(2)]
    pT_sb = ae(nc.sbuf_tensor("pT_sb", [128, 8, 128], BF16, **L))
    lr_sb = ae(nc.sbuf_tensor("lr_sb", [1, 128], F32, **L))
    lrep_sb = ae(nc.sbuf_tensor("lrep_sb", [128, 128], F32, **L))

    @blockq.vector
    def _(v):
        v.memset(qg_sb[:], 0.0).then_inc(c_v.sem, 1)
    c_v.n += 1
    qg_zero = c_v.n

    def proj_wave(w, which, wbuf, wdma):
        if which == "k":
            for oc in range(4):
                for ci, ts in enumerate(range(0, KW, 512)):
                    tn = min(512, KW - ts)
                    bk = 4 + (oc * 4 + ci) % 2
                    pb = psum_f[bk]
                    _fpk = wf(("bank", bk))

                    @blockq.tensor
                    def _(t, oc=oc, ts=ts, tn=tn, pb=pb, bk=bk, _fpk=_fpk):
                        t.wait_ge(c_gp.sem, wdma)
                        t.wait_ge(c_v.sem, halo_done)
                        wdeps(t, _fpk)
                        for dc in range(DC):
                            mm = t.matmul(pb[:, 0:tn],
                                          wbuf[:, dc, oc * 128:(oc + 1) * 128],
                                          ln1T[:, dc, ts:ts + tn],
                                          start=(dc == 0), stop=(dc == DC - 1))
                        mm.then_inc(c_t.sem, 1)
                    c_t.n += 1
                    mmd = c_t.n

                    @blockq.scalar
                    def _(s, oc=oc, ts=ts, tn=tn, pb=pb, mmd=mmd):
                        s.wait_ge(c_t.sem, mmd)
                        s.copy(out=kw_sb[:, oc, ts:ts + tn],
                               in_=pb[:, 0:tn]).then_inc(c_s.sem, 1)
                    c_s.n += 1
                    set_free(("bank", bk), (c_s, c_s.n))
        elif which == "v":
            for tt in range(KT):
                bk = 4 + tt % 2
                pb = psum_f[bk]
                _fpv = wf(("bank", bk))

                @blockq.tensor
                def _(t, tt=tt, pb=pb, bk=bk, _fpv=_fpv):
                    t.wait_ge(c_gp.sem, wdma)
                    t.wait_ge(c_v.sem, halo_done)
                    wdeps(t, _fpv)
                    for dc in range(DC):
                        mm = t.matmul(pb[:], ln1T[:, dc, tt * 128:(tt + 1) * 128],
                                      wbuf[:, dc, :],
                                      start=(dc == 0), stop=(dc == DC - 1))
                    mm.then_inc(c_t.sem, 1)
                c_t.n += 1
                mmd = c_t.n

                @blockq.scalar
                def _(s, tt=tt, pb=pb, mmd=mmd):
                    s.wait_ge(c_t.sem, mmd)
                    s.copy(out=vw_sb[:, tt, :], in_=pb[:]).then_inc(c_s.sem, 1)
                c_s.n += 1
                set_free(("bank", bk), (c_s, c_s.n))
        else:
            for oc in range(4):
                for th in range(3):
                    bk = 4 + th % 2
                    pb = psum_f[bk]
                    _fpq = wf(("bank", bk))

                    @blockq.tensor
                    def _(t, oc=oc, th=th, pb=pb, bk=bk, _fpq=_fpq):
                        t.wait_ge(c_gp.sem, wdma)
                        if th == 2:
                            t.wait_ge(c_v.sem, qg_zero)
                        wdeps(t, _fpq)
                        for dc in range(DC):
                            if th < 2:
                                rhs = ln1T[:, dc, OWN0 + th * 512:OWN0 + th * 512 + 512]
                                dst = pb[:]
                            else:
                                rhs = ln1T[:, dc, GCOL:GCOL + GK]
                                dst = pb[:, 0:GK]
                            mm = t.matmul(dst, wbuf[:, dc, oc * 128:(oc + 1) * 128],
                                          rhs, start=(dc == 0), stop=(dc == DC - 1))
                        mm.then_inc(c_t.sem, 1)
                    c_t.n += 1
                    mmd = c_t.n

                    @blockq.scalar
                    def _(s, oc=oc, th=th, pb=pb, mmd=mmd):
                        s.wait_ge(c_t.sem, mmd)
                        if th < 2:
                            s.copy(out=qw_sb[:, oc, th * 512:th * 512 + 512],
                                   in_=pb[:]).then_inc(c_s.sem, 1)
                        else:
                            s.copy(out=qg_sb[:, oc, 0:GK],
                                   in_=pb[:, 0:GK]).then_inc(c_s.sem, 1)
                    c_s.n += 1
                    set_free(("bank", bk), (c_s, c_s.n))

    def attend(w, hh, qt):
        glob = (qt == 8)
        nkt = 8 if glob else 6
        po, pl = psum_f[2], psum_f[3]
        h = w * 4 + hh

        expd = [0] * nkt
        for i in range(nkt):
            pb = psum_f[i % 2]
            _fsb = wf(("bank", i % 2)) if i < 2 else None

            @blockq.tensor
            def _(t, i=i, pb=pb, _fsb=_fsb):
                if i >= 2:
                    d = expd[i - 2]
                    if isinstance(d, tuple):
                        t.wait_ge(c_v.sem, d[1])
                    else:
                        t.wait_ge(c_s.sem, d)
                else:
                    wdeps(t, _fsb)
                if glob:
                    kcol, nk = OWN0 + i * 128, 128
                    rhs = qg_sb[:, hh, :]
                else:
                    kcol = GCOL if i == 5 else 128 * qt + 128 * i
                    nk = GK if i == 5 else 128
                    rhs = qw_sb[:, hh, qt * 128:(qt + 1) * 128]
                mm = t.matmul(pb[0:nk, 0:128], kw_sb[:, hh, kcol:kcol + nk],
                              rhs, start=True, stop=True)
                mm.then_inc(c_t.sem, 1)
            c_t.n += 1
            smm = c_t.n

            @blockq.scalar
            def _(s, i=i, pb=pb, smm=smm):
                s.wait_ge(c_t.sem, smm)
                nk = GK if (not glob and i == 5) else 128
                bias = neg16[0:nk, :] if glob else ebias[0:nk, qt, i:i + 1]
                s.activation(out=pT_sb[0:nk, i, :], in_=pb[0:nk, 0:128],
                             func=AF.Exp, bias=bias, scale=SCL).then_inc(c_s.sem, 1)
            c_s.n += 1
            expd[i] = c_s.n
            set_free(("bank", i % 2), (c_s, c_s.n))

            if (not glob) and i in (0, 4):
                @blockq.vector
                def _(v, i=i, ed=expd[i]):
                    v.wait_ge(c_s.sem, ed)
                    v.wait_ge(c_sy.sem, consts_ld)
                    v.tensor_mul(pT_sb[:, i, :], pT_sb[:, i, :],
                                 m01[:, qt, (0 if i == 0 else 1), :]
                                 ).then_inc(c_v.sem, 1)
                c_v.n += 1
                expd[i] = ("v", c_v.n)

        _fol = wf(("bank", 2), ("bank", 3))
        for i in range(nkt):
            dep = expd[i]

            @blockq.tensor
            def _(t, i=i, dep=dep, _fol=_fol):
                if isinstance(dep, tuple):
                    t.wait_ge(c_v.sem, dep[1])
                else:
                    t.wait_ge(c_s.sem, dep)
                if i == 0:
                    wdeps(t, _fol)
                if glob:
                    nk, vt = 128, 2 + i
                else:
                    nk = GK if i == 5 else 128
                    vt = 12 if i == 5 else qt + i
                mm = t.matmul(pl[0:1, 0:128], ones_col[0:nk, :],
                              pT_sb[0:nk, i, :], start=(i == 0), stop=(i == nkt - 1))
                mm.then_inc(c_t.sem, 1)
                mm = t.matmul(po[:, 0:128], vw_sb[0:nk, vt, hh * 128:hh * 128 + 128],
                              pT_sb[0:nk, i, :], start=(i == 0), stop=(i == nkt - 1))
                mm.then_inc(c_t.sem, 1)
            c_t.n += 2
        avd = c_t.n

        if glob:
            @blockq.vector
            def _(v, h=h):
                v.wait_ge(c_t.sem, avd)
                v.tensor_copy(oLg[:, h * GK:(h + 1) * GK], po[:, 0:GK])
                v.tensor_copy(lgf[0:1, h * GK:(h + 1) * GK],
                              pl[0:1, 0:GK]).then_inc(c_v.sem, 1)
            c_v.n += 1
            set_free(("bank", 2), (c_v, c_v.n))
            set_free(("bank", 3), (c_v, c_v.n))
            return c_v.n

        @blockq.vector
        def _(v):
            v.wait_ge(c_t.sem, avd)
            v.reciprocal(lr_sb[:], pl[0:1, 0:128]).then_inc(c_v.sem, 1)
        c_v.n += 1
        lrd = c_v.n
        set_free(("bank", 3), (c_v, lrd))

        _frep = wf(("bank", 4), ("lrep",))

        @blockq.tensor
        def _(t, _frep=_frep):
            t.wait_ge(c_v.sem, lrd)
            wdeps(t, _frep)
            t.matmul(psum_f[4][:, 0:128], ones_row[:], lr_sb[:],
                     start=True, stop=True).then_inc(c_t.sem, 1)
        c_t.n += 1
        repd = c_t.n

        lcp = c_v.n + 1

        @blockq.vector
        def _(v, h=h, lcp=lcp):
            v.wait_ge(c_t.sem, repd)
            v.tensor_copy(lrep_sb[:], psum_f[4][:, 0:128]).then_inc(c_v.sem, 1)
            v.wait_ge(c_v.sem, lcp)
            v.tensor_mul(osc[:, h, qt * 128:(qt + 1) * 128], po[:, 0:128],
                         lrep_sb[:]).then_inc(c_v.sem, 1)
        c_v.n += 2
        repc = c_v.n
        set_free(("bank", 4), (c_v, repc))
        set_free(("bank", 2), (c_v, c_v.n))
        set_free(("lrep",), (c_v, c_v.n))
        return c_v.n

    for b in range(2):
        set_free(("wat", b), (c_v, halo_done), (c_t, c_t.n))
    for w in range(4):
        for which, win, b in (("k", wk_in, 0), ("v", wv_in, 1), ("q", wq_in, 0)):
            _fwat = wf(("wat", b))

            @blockq.gpsimd
            def _(g, w=w, win=win, b=b, _fwat=_fwat):
                wdeps(g, _fwat)
                g.dma_start(out=wat[b][:], in_=win[w]).then_inc(c_gp.sem, 16)
            c_gp.n += 16
            proj_wave(w, which, wat[b], c_gp.n)
            set_free(("wat", b), (c_t, c_t.n))
        for hh in range(4):
            for qt in range(8):
                attend(w, hh, qt)
            attend(w, hh, 8)

    ast.close()
    lnf.close()
    attn_done_v = c_v.n
    if stop_after == "attn":
        finish()
        flush()
        return nc, st

    _gmidle = c_gp.n

    @blockq.vector
    def _(v, gmidle=_gmidle):
        v.wait_ge(c_gp.sem, gmidle)
        v.memset(gmrg[:, 0, 256:512], 0.0).then_inc(c_v.sem, 1)
    c_v.n += 1
    zpad = c_v.n

    _mgbase = c_gp.n
    _mgccw = c_cc.n + 1

    @blockq.gpsimd
    def _(g, base=_mgbase, ccw=_mgccw):
        g.wait_ge(c_v.sem, attn_done_v)
        g.wait_ge(c_v.sem, zpad)
        g.dma_start(out=ccD_i[0:128, 0:256], in_=oLg[:]).then_inc(c_gp.sem, 16)
        g.dma_start(out=ccD_i[0:128, 256:512],
                    in_=gmrg[:, 0, 256:512]).then_inc(c_gp.sem, 16)
        g.dma_start(out=ccD_i[0:1, 256:512], in_=lgf[:]).then_inc(c_gp.sem, 16)
        g.wait_ge(c_gp.sem, base + 48)
        g.collective_compute(
            "AllGather", ALU.bypass,
            replica_groups=[[0, 1, 2, 3], [4, 5, 6, 7]],
            ins=[ccD_i.opt()], outs=[ccD_o.opt()]).then_inc(c_cc.sem, 1)
        g.wait_ge(c_cc.sem, ccw)
        for sl in range(4):
            g.dma_start(out=gmrg[:, sl], in_=ccD_o[sl]).then_inc(c_gp.sem, 16)
    c_gp.n += 112
    c_cc.n += 1
    mrg_ld = c_gp.n

    def f(v):
        v.wait_ge(c_gp.sem, mrg_ld)
        v.tensor_add(gmrg[:, 0, :], gmrg[:, 0, :], gmrg[:, 1, :])
        return v.tensor_add(gmrg[:, 2, :], gmrg[:, 2, :], gmrg[:, 3, :])
    m1 = vstep(f)

    def f(v):
        v.wait_ge(c_v.sem, m1)
        return v.tensor_add(gmrg[:, 0, :], gmrg[:, 0, :], gmrg[:, 2, :])
    m2 = vstep(f)

    def f(v):
        v.wait_ge(c_v.sem, m2)
        return v.reciprocal(gmrg[0:1, 1, 256:512], gmrg[0:1, 0, 256:512])
    mrgd = vstep(f)

    gsc_dep = {}
    for hq in range(NH):
        bk = 4 + hq % 2
        pb = psum_f[bk]

        _fg5 = wf(("bank", bk))

        @blockq.tensor
        def _(t, hq=hq, pb=pb, bk=bk, _fg5=_fg5):
            t.wait_ge(c_v.sem, mrgd)
            wdeps(t, _fg5)
            t.matmul(pb[:, 0:GK], ones_row[:],
                     gmrg[0:1, 1, 256 + hq * GK:256 + (hq + 1) * GK],
                     start=True, stop=True).then_inc(c_t.sem, 1)
        c_t.n += 1
        rpd = c_t.n

        @blockq.vector
        def _(v, hq=hq, pb=pb, rpd=rpd):
            v.wait_ge(c_t.sem, rpd)
            v.tensor_mul(gsc[:, hq, :], gmrg[:, 0, hq * GK:(hq + 1) * GK],
                         pb[:, 0:GK]).then_inc(c_v.sem, 1)
        c_v.n += 1
        gsc_dep[hq] = c_v.n
        set_free(("bank", bk), (c_v, c_v.n))
    gsc_all = c_v.n

    p6 = ExitStack()
    p6e = p6.enter_context
    x2t_sb = p6e(nc.sbuf_tensor("x2t_sb", [128, NT, D], F32, **L))
    x1t_sb = p6e(nc.sbuf_tensor("x1t_sb", [128, NT, D], F32, **L))
    wo_sb = [p6e(nc.sbuf_tensor(f"wo_sb{b}", [128, DC, 128], BF16, **L))
             for b in range(2)]
    cta_sb = p6e(nc.sbuf_tensor("cta_sb", [128, T], F32, **L))
    gtmp_sb = p6e(nc.sbuf_tensor("gtmp_sb", [128, GK], F32, **L))

    for b in range(2):
        set_free(("wo", b), (c_v, gsc_all), (c_t, c_t.n))
    x1_ld = [0] * NT
    for tt in range(NT):
        @blockq.sync
        def _(sy, tt=tt):
            sy.wait_ge(c_v.sem, gsc_all)
            sy.dma_start(out=x1t_sb[:, tt, :],
                         in_=x1_scr[tt * 128:(tt + 1) * 128, :]).then_inc(c_sy.sem, 16)
        c_sy.n += 16
        x1_ld[tt] = c_sy.n

    aoev = {}
    x2d = {}
    bldd = {}
    for ddc in range(DC):
        b = ddc % 2

        _fwo = wf(("wo", b))

        @blockq.gpsimd
        def _(g, ddc=ddc, b=b, _fwo=_fwo):
            wdeps(g, _fwo)
            g.dma_start(out=wo_sb[b][:], in_=wo_in[ddc]).then_inc(c_gp.sem, 16)
        c_gp.n += 16
        wld = c_gp.n

        pa, pb2, pg16 = psum_f[0 + b], psum_f[2 + b], psum_f[4]

        _fp6 = (wf(("bank", 0 + b), ("bank", 2 + b)) if ddc < 2 else []) + \
               (wf(("bank", 4)) if ddc == 0 else [])

        @blockq.tensor
        def _(t, ddc=ddc, b=b, wld=wld, pa=pa, pb2=pb2, pg16=pg16, _fp6=_fp6):
            t.wait_ge(c_gp.sem, wld)
            t.wait_ge(c_v.sem, gsc_all)
            wdeps(t, _fp6)
            t.wait_ge(c_s.sem, aoev.get(ddc - 2, 0))
            t.wait_ge(c_v.sem, bldd.get(ddc - 1, 0))
            for th in range(2):
                dst = pa if th == 0 else pb2
                for dvc in range(DC):
                    mm = t.matmul(dst[:], wo_sb[b][:, dvc, :],
                                  osc[:, dvc, th * 512:th * 512 + 512],
                                  start=(dvc == 0), stop=(dvc == DC - 1))
                mm.then_inc(c_t.sem, 1)
            for dvc in range(DC):
                mm = t.matmul(pg16[:, 0:GK], wo_sb[b][:, dvc, :], gsc[:, dvc, :],
                              start=(dvc == 0), stop=(dvc == DC - 1))
            mm.then_inc(c_t.sem, 1)
        c_t.n += 3
        mmd = c_t.n
        set_free(("wo", b), (c_t, mmd))

        @blockq.scalar
        def _(s, pa=pa, pb2=pb2, mmd=mmd):
            s.wait_ge(c_t.sem, mmd - 1)
            s.copy(out=cta_sb[:, 0:512], in_=pa[:]).then_inc(c_s.sem, 1)
            s.copy(out=cta_sb[:, 512:1024], in_=pb2[:]).then_inc(c_s.sem, 1)
        c_s.n += 2
        ctd = c_s.n
        aoev[ddc] = c_s.n

        def f(v, ddc=ddc, pg16=pg16, ctd=ctd, mmd=mmd):
            v.wait_ge(c_s.sem, ctd)
            v.wait_ge(c_t.sem, mmd)
            return v.tensor_sub(gtmp_sb[:], pg16[:, 0:GK], cta_sb[:, 0:GK])
        b1 = vstep(f)

        def f(v, b1=b1):
            v.wait_ge(c_v.sem, b1)
            return v.tensor_mul(gtmp_sb[:], gtmp_sb[:], gbl[:])
        b2 = vstep(f)

        def f(v, b2=b2):
            v.wait_ge(c_v.sem, b2)
            return v.tensor_add(cta_sb[:, 0:GK], cta_sb[:, 0:GK], gtmp_sb[:])
        bldd[ddc] = vstep(f)
        bld = bldd[ddc]

        for tt in range(NT):
            pbt = psum_f[6 + tt % 2]
            _fx67 = wf(("bank", 6 + tt % 2)) if (ddc == 0 and tt < 2) else None

            @blockq.tensor
            def _(t, ddc=ddc, tt=tt, pbt=pbt, bld=bld, _fx67=_fx67):
                t.wait_ge(c_v.sem, bld)
                t.wait_ge(c_v.sem, x2d.get((ddc - 1, tt), 0))
                t.wait_ge(c_v.sem, x2d.get((ddc, tt - 2), 0))
                if _fx67 is not None:
                    wdeps(t, _fx67)
                t.transpose(pbt[:, 0:128], cta_sb[:, tt * 128:(tt + 1) * 128],
                            ident[:]).then_inc(c_t.sem, 1)
            c_t.n += 1
            trd = c_t.n

            @blockq.vector
            def _(v, ddc=ddc, tt=tt, pbt=pbt, trd=trd):
                v.wait_ge(c_t.sem, trd)
                v.wait_ge(c_sy.sem, x1_ld[tt])
                v.tensor_add(x2t_sb[:, tt, ddc * 128:(ddc + 1) * 128],
                             pbt[:, 0:128],
                             x1t_sb[:, tt, ddc * 128:(ddc + 1) * 128]
                             ).then_inc(c_v.sem, 1)
            c_v.n += 1
            x2d[(ddc, tt)] = c_v.n
    set_free(("bank", 6), (c_v, x2d[(DC - 1, NT - 2)]))
    set_free(("bank", 7), (c_v, x2d[(DC - 1, NT - 1)]))
    set_free(("bank", 0), (c_s, aoev[DC - 2]))
    set_free(("bank", 1), (c_s, aoev[DC - 1]))
    set_free(("bank", 2), (c_s, aoev[DC - 2]))
    set_free(("bank", 3), (c_s, aoev[DC - 1]))
    set_free(("bank", 4), (c_v, bldd[DC - 1]))

    for tt in range(NT):
        done = x2d[(DC - 1, tt)]

        @blockq.sync
        def _(sy, tt=tt, done=done):
            sy.wait_ge(c_v.sem, done)
            sy.dma_start(out=x2_scr[tt * 128:(tt + 1) * 128, :],
                         in_=x2t_sb[:, tt, :]).then_inc(c_sy.sem, 16)
        c_sy.n += 16

    p6.close()
    oscf.close()
    if stop_after == "p6":
        finish()
        flush()
        return nc, st

    emit_ffn(2)

    finish()
    flush()
    return nc, st


def _host_masks(core):
    S0 = 1024 * (core % 4)
    m01 = np.zeros((8, 2, 128, 128), np.float32)
    eb = np.zeros((8, 6, 128), np.float32)
    for qt in range(8):
        for ii, i in enumerate((0, 4)):
            kb = 128 * qt + 128 * i + np.arange(128)[:, None]
            kg = S0 - 256 + kb
            qg = S0 + 128 * qt + np.arange(128)[None, :]
            ok = (kg >= 0) & (kg < S) & ((np.abs(qg - kg) < 256) | (kg < GK))
            m01[qt, ii] = np.where(ok, 1.0, 0.0)
        for i in range(5):
            kg0 = S0 - 256 + 128 * qt + 128 * i
            valid = (kg0 >= 0) and (kg0 + 127 < S)
            eb[qt, i, :] = -STAB if valid else MSK
        eb[qt, 5, :] = MSK
        dbl = (S0 == 0) and (qt <= 2)
        eb[qt, 5, 0:GK] = MSK if dbl else -STAB
    c4 = core % 4
    hs = np.zeros(12, np.float32)
    if c4 % 2 == 0:
        hs[9] = 1.0
    else:
        hs[8] = 1.0
    if c4 == 1:
        hs[11] = 1.0
    else:
        hs[10] = 1.0
    if c4 == 1:
        hs[0] = 1.0
    elif c4 == 2:
        hs[2] = 1.0
    elif c4 == 3:
        hs[0] = 1.0
    if c4 == 0:
        hs[5] = 1.0
    elif c4 == 1:
        hs[7] = 1.0
    elif c4 == 2:
        hs[5] = 1.0
    hsel = np.broadcast_to(hs, (128, 12)).copy()
    gbl = np.full((128, GK), 1.0 if c4 == 0 else 0.0, np.float32)
    return m01, eb, hsel, gbl


def _prep_inputs(x, Wq, Wk, Wv, Wo, W_g, W_u, W_d):
    import ml_dtypes
    x = np.ascontiguousarray(np.asarray(x, np.float32))
    Wq, Wk, Wv, Wo, W_g, W_u, W_d = (
        np.asarray(w, np.float32) for w in (Wq, Wk, Wv, Wo, W_g, W_u, W_d))

    wg = np.ascontiguousarray(W_g.reshape(HC, 128, DC, 128).transpose(0, 3, 2, 1))
    wu = np.ascontiguousarray(W_u.reshape(HC, 128, DC, 128).transpose(0, 3, 2, 1))
    wd = np.ascontiguousarray(W_d.reshape(DC, 128, HC, 128).transpose(0, 3, 2, 1))
    wq = np.ascontiguousarray(Wq.reshape(DC, 128, 4, 512).transpose(2, 1, 0, 3))
    wk = np.ascontiguousarray(Wk.reshape(DC, 128, 4, 512).transpose(2, 1, 0, 3))
    wv = np.ascontiguousarray(Wv.reshape(DC, 128, 4, 512).transpose(2, 1, 0, 3))
    wo = np.ascontiguousarray(Wo.reshape(DC, 128, DC, 128).transpose(2, 1, 0, 3))
    ident = np.eye(128, dtype=np.float32)

    in_maps = []
    for core in range(8):
        b, c4 = core // 4, core % 4
        m01, eb, hsel, gbl = _host_masks(core)
        in_maps.append({
            "x": np.ascontiguousarray(x[b, c4 * T:(c4 + 1) * T, :]),
            "wg": wg, "wu": wu, "wd": wd,
            "wq": wq, "wk": wk, "wv": wv, "wo": wo,
            "m01": np.ascontiguousarray(
                m01.transpose(2, 0, 1, 3)).astype(ml_dtypes.bfloat16),
            "eb": np.ascontiguousarray(eb.transpose(2, 0, 1)),
            "hsel": hsel, "gbl": gbl, "ident": ident,
        })
    return in_maps


def kernel(L0_attn_hidden_states, L0_attn_Wq, L0_attn_Wk, L0_attn_Wv,
           L0_attn_Wo, L0_ffn_W_g, L0_ffn_W_u, L0_ffn_W_d):
    in_maps = _prep_inputs(L0_attn_hidden_states, L0_attn_Wq, L0_attn_Wk,
                           L0_attn_Wv, L0_attn_Wo, L0_ffn_W_g, L0_ffn_W_u,
                           L0_ffn_W_d)
    nc, st = build(STOP_AFTER)
    st.close()
    res = run_bass_kernel_spmd(nc, in_maps, core_ids=list(range(8)))
    out = np.empty((2, S, D), np.float32)
    for core in range(8):
        b, c4 = core // 4, core % 4
        out[b, c4 * T:(c4 + 1) * T, :] = res.results[core]["out"]
    return out


# version 07
# speedup vs baseline: 1.3236x; 1.0138x over previous
import numpy as np
from contextlib import ExitStack

import concourse.bass as bass
import concourse.mybir as mybir
from concourse.bass_utils import run_bass_kernel_spmd

try:
    import jax
    jax.config.update("jax_compilation_cache_dir", "/tmp/jaxcache")
    jax.config.update("jax_persistent_cache_min_entry_size_bytes", -1)
    jax.config.update("jax_persistent_cache_min_compile_time_secs", 0.0)
except Exception:
    pass

F32 = mybir.dt.float32
BF16 = mybir.dt.bfloat16
AF = mybir.ActivationFunctionType
ALU = mybir.AluOpType
AXX = mybir.AxisListType.X

T = 1024
NT = 8
D = 2048
DC = 16
H = 8192
HC = 64
NH = 16
S = 4096
HALO = 256
GK = 16
KW = 1664
KT = 13
OWN0 = 256
GCOL = 1536
STAB = 16.0
MSK = -1.0e30
EPS1 = 1e-5
EPS2 = 1e-8
SCL = float(1.0 / np.sqrt(np.float32(128.0)))
INV_D = 1.0 / 2048.0
BIGS = 1.0e38

TMAP = (0, 1, 6, 7, 2, 3, 4, 5)

STOP_AFTER = None
DEBUG_SCR = False


def build(stop_after=None):
    nc = bass.Bass("TRN2", target_bir_lowering=False, detect_race_conditions=False)
    st = ExitStack()
    e = st.enter_context

    def inp(name, shape, dt=F32):
        return nc.dram_tensor(name, list(shape), dt, kind="ExternalInput").ap()

    x_in = inp("x", (T, D))
    wg_in = inp("wg", (HC, 128, DC, 128))
    wu_in = inp("wu", (HC, 128, DC, 128))
    wd_in = inp("wd", (DC, 128, HC, 128))
    wq_in = inp("wq", (4, 128, DC, 512))
    wk_in = inp("wk", (4, 128, DC, 512))
    wv_in = inp("wv", (4, 128, DC, 512))
    wo_in = inp("wo", (DC, 128, DC, 128))
    m01_in = inp("m01", (128, 8, 2, 128), BF16)
    eb_in = inp("eb", (128, 8, 6))
    hsel_in = inp("hsel", (128, 12))
    gbl_in = inp("gbl", (128, 16))
    id_in = inp("ident", (128, 128))

    out_t = nc.dram_tensor("out", [T, D], F32, kind="ExternalOutput").ap()

    def scr(name, shape, dt=F32):
        kind = "ExternalOutput" if DEBUG_SCR else "Internal"
        return nc.dram_tensor(name, list(shape), dt, kind=kind).ap()

    x1_scr = scr("x1_scr", (T, D))
    x2_scr = scr("x2_scr", (T, D))
    ln1_scr = scr("ln1_scr", (NT, 128, DC, 128), BF16)
    sgn_g = scr("sgn_g", (HC, 128, DC, 128), BF16)
    sgn_u = scr("sgn_u", (HC, 128, DC, 128), BF16)
    sgn_d = scr("sgn_d", (DC, 128, HC, 128), BF16)
    ccA_i = scr("ccA_i", (128, DC, HALO), BF16)
    ccA_o = scr("ccA_o", (2, 128, DC, HALO), BF16)
    ccB_i = scr("ccB_i", (128, DC, HALO), BF16)
    ccB_o = scr("ccB_o", (2, 128, DC, HALO), BF16)
    ccC_i = scr("ccC_i", (128, DC, GK), BF16)
    ccC_o = scr("ccC_o", (4, 128, DC, GK), BF16)
    ccD_i = scr("ccD_i", (128, 512))
    ccD_o = scr("ccD_o", (4, 128, 512))

    class Ctr:
        def __init__(self, name):
            self.sem = e(nc.semaphore(name))
            self.n = 0

    c_sy = Ctr("c_sy")
    c_gp = Ctr("c_gp")
    c_t = Ctr("c_t")
    c_v = Ctr("c_v")
    c_s = Ctr("c_s")
    c_cc = Ctr("c_cc")

    R = dict(side="right")
    ident = e(nc.sbuf_tensor("identS", [128, 128], F32, **R))
    m01 = e(nc.sbuf_tensor("m01S", [128, 8, 2, 128], BF16, **R))
    ebias = e(nc.sbuf_tensor("ebiasS", [128, 8, 6], F32, **R))
    hsel = e(nc.sbuf_tensor("hselS", [128, 12], F32, **R))
    gbl = e(nc.sbuf_tensor("gblS", [128, 16], F32, **R))
    ones_col = e(nc.sbuf_tensor("ones_col", [128, 1], BF16, **R))
    ones_row = e(nc.sbuf_tensor("ones_row", [1, 128], F32, **R))
    neg16 = e(nc.sbuf_tensor("neg16", [128, 1], F32, **R))
    stats = e(nc.sbuf_tensor("stats", [128, 32], F32, **R))
    bnscr = e(nc.sbuf_tensor("bnscr", [128, 48], F32, **R))
    gsc = e(nc.sbuf_tensor("gsc", [128, DC, GK], BF16, **R))

    block = e(nc.Block())

    pend = {k: [] for k in ("tensor", "vector", "scalar", "gpsimd", "sync")}

    class _DQ:
        def __getattr__(self, eng):
            def dec(fn):
                pend[eng].append(fn)
                return fn
            return dec

    blockq = _DQ()

    def flush():
        for eng in ("tensor", "vector", "scalar", "gpsimd", "sync"):
            fns = pend[eng]
            pend[eng] = []
            if fns:
                getattr(block, eng)(lambda e, fns=fns: [f(e) for f in fns])

    def wdeps(eng, deps):
        for c_, v_ in deps:
            eng.wait_ge(c_.sem, v_)

    psum_f = [e(nc.psum_tensor(f"pb{i}", [128, 512], F32)) for i in range(8)]

    @blockq.sync
    def _(sy):
        for dst, src in ((ident, id_in), (m01, m01_in), (ebias, eb_in),
                         (hsel, hsel_in), (gbl, gbl_in)):
            sy.dma_start(out=dst[:], in_=src).then_inc(c_sy.sem, 16)
    c_sy.n += 5 * 16
    consts_ld = c_sy.n

    @blockq.vector
    def _(v):
        v.memset(ones_col[:], 1.0)
        v.memset(ones_row[:], 1.0)
        v.memset(neg16[:], -STAB).then_inc(c_v.sem, 1)
    c_v.n += 1
    consts_v = c_v.n

    @blockq.scalar
    def _(s):
        s.wait_ge(c_v.sem, consts_v)
        s.wait_ge(c_sy.sem, consts_ld)

    @blockq.tensor
    def _(t):
        t.wait_ge(c_v.sem, consts_v)
        t.wait_ge(c_sy.sem, consts_ld)

    wfree = {}

    def wait_free(engine, key):
        for ctr, val in wfree.get(key, ()):
            engine.wait_ge(ctr.sem, val)

    def wf(*keys):
        out = []
        for k in keys:
            out.extend(wfree.get(k, ()))
        return out

    def set_free(key, *deps):
        wfree[key] = list(deps)


    def finish():
        @blockq.sync
        def _(sy):
            sy.wait_ge(c_sy.sem, c_sy.n)
        @blockq.gpsimd
        def _(g):
            g.wait_ge(c_gp.sem, c_gp.n)

    def vstep(fn):
        @blockq.vector
        def _(v):
            fn(v).then_inc(c_v.sem, 1)
        c_v.n += 1
        return c_v.n

    def gen_ln(src, dst, eps, so=0, bo=0, src_s=None, src_v=None, src_sy=None,
               dst_t=None, out=None):
        def _w(eng):
            if src_s is not None:
                eng.wait_ge(c_s.sem, src_s)
            if src_v is not None:
                eng.wait_ge(c_v.sem, src_v)
            if src_sy is not None:
                eng.wait_ge(c_sy.sem, src_sy)

        def f(v):
            _w(v)
            for k in range(4):
                r = v.bn_stats(bnscr[:, bo + 6 * k:bo + 6 * k + 6],
                               src[:, 512 * k:512 * k + 512])
            return r
        va = vstep(f)
        yield

        def f(v):
            v.wait_ge(c_v.sem, va)
            return v.bn_aggr(stats[:, so:so + 2], bnscr[:, bo:bo + 24])
        vb = vstep(f)
        yield

        def f(v):
            v.wait_ge(c_v.sem, vb)
            return v.tensor_scalar_add(stats[:, so + 6:so + 7],
                                       stats[:, so + 1:so + 2], float(eps))
        ve = vstep(f)
        yield

        @blockq.scalar
        def _(s):
            s.wait_ge(c_v.sem, ve)
            s.activation(out=stats[:, so + 7:so + 8], in_=stats[:, so + 6:so + 7],
                         func=AF.Sqrt).then_inc(c_s.sem, 1)
        c_s.n += 1
        sddone = c_s.n
        yield

        def f(v):
            v.wait_ge(c_s.sem, sddone)
            return v.reciprocal(stats[:, so + 8:so + 9], stats[:, so + 7:so + 8])
        vf = vstep(f)
        yield

        def f(v):
            v.wait_ge(c_v.sem, vf)
            return v.scalar_tensor_tensor(stats[:, so + 9:so + 10],
                                          stats[:, so:so + 1], -1.0,
                                          stats[:, so + 8:so + 9],
                                          ALU.mult, ALU.mult)
        vg = vstep(f)
        yield

        def f(v):
            v.wait_ge(c_v.sem, vg)
            if dst_t is not None:
                v.wait_ge(c_t.sem, dst_t)
            return v.tensor_scalar(dst, src, stats[:, so + 8:so + 9],
                                   stats[:, so + 9:so + 10], ALU.mult, ALU.add)
        r = vstep(f)
        if out is not None:
            out.append(r)

    def emit_ln(src, dst, eps, tmp_sq=None, src_s=None, src_v=None,
                src_sy=None, dst_t=None):
        out = []
        for _ in gen_ln(src, dst, eps, src_s=src_s, src_v=src_v,
                        src_sy=src_sy, dst_t=dst_t, out=out):
            pass
        return out[0]

    def gen_quant(z, y, a, so, src_v, out=None):
        MAGIC = 12582912.0

        def f(v):
            v.wait_ge(c_v.sem, src_v)
            return v.tensor_reduce(stats[:, so + 10:so + 11], z, axis=AXX,
                                   op=ALU.max, apply_absolute_value=True)
        qa = vstep(f)
        yield

        def f(v):
            v.wait_ge(c_v.sem, qa)
            return v.reciprocal(stats[:, so + 11:so + 12], stats[:, so + 10:so + 11])
        qb = vstep(f)
        yield

        def f(v):
            v.wait_ge(c_v.sem, qb)
            v.tensor_scalar_mul(stats[:, so + 12:so + 13], stats[:, so + 11:so + 12], 127.0)
            return v.tensor_scalar_mul(stats[:, so + 13:so + 14],
                                       stats[:, so + 10:so + 11], 1.0 / 127.0)
        qc = vstep(f)
        yield

        def f(v):
            v.wait_ge(c_v.sem, qc)
            v.tensor_scalar_mul(y, z, stats[:, so + 12:so + 13])
            v.tensor_scalar(a, y, MAGIC, -MAGIC, ALU.add, ALU.add)
            v.tensor_scalar(a, a, 127.0, -128.0, ALU.min, ALU.max)
            return v.tensor_scalar_mul(a, a, stats[:, so + 13:so + 14])
        r = vstep(f)
        if out is not None:
            out.append(r)

    def emit_transposes(dst_fn, src, n, src_v=None, src_s=None, src_sy=None):
        sready = c_s.n
        last_ev = {0: None, 1: None}
        for i in range(n):
            bk = 6 + (i % 2)
            pb = psum_f[bk]
            _fdv = wf(("bank", bk)) if i < 2 else None

            @blockq.tensor
            def _(t, i=i, pb=pb, bk=bk, _fd=_fdv):
                if src_v is not None:
                    t.wait_ge(c_v.sem, src_v)
                if src_s is not None:
                    t.wait_ge(c_s.sem, src_s)
                if src_sy is not None:
                    t.wait_ge(c_sy.sem, src_sy)
                if _fd is not None:
                    wdeps(t, _fd)
                else:
                    t.wait_ge(c_s.sem, sready + i - 1)
                t.transpose(pb[:, 0:128], src[:, i * 128:(i + 1) * 128],
                            ident[:]).then_inc(c_t.sem, 1)
            c_t.n += 1
            trd = c_t.n

            @blockq.scalar
            def _(s, i=i, pb=pb, trd=trd):
                s.wait_ge(c_t.sem, trd)
                s.copy(out=dst_fn(i), in_=pb[:, 0:128]).then_inc(c_s.sem, 1)
            c_s.n += 1
            last_ev[i % 2] = c_s.n
        for par in (0, 1):
            if last_ev[par] is not None:
                set_free(("bank", 6 + par), (c_s, last_ev[par]))
        return c_s.n

    L = dict(side="left")

    def emit_entry(f, src_dram):
        sub = ExitStack()
        Ab = [sub.enter_context(nc.sbuf_tensor(f"{f['tag']}A{p}", [128, D], F32, **L))
              for p in range(2)]
        Bb = [sub.enter_context(nc.sbuf_tensor(f"{f['tag']}B{p}", [128, D], F32, **L))
              for p in range(2)]
        buf_free = [0, 0]

        def tile_gen(lt):
            p = lt % 2
            A, B = Ab[p], Bb[p]
            rt = TMAP[lt]

            @blockq.sync
            def _(sy, free_s=buf_free[p], rt=rt):
                sy.wait_ge(c_s.sem, free_s)
                sy.dma_start(out=A[:], in_=src_dram[rt * 128:(rt + 1) * 128, :]
                             ).then_inc(c_sy.sem, 16)
            c_sy.n += 16
            ld = c_sy.n
            yield

            o1 = []
            yield from gen_ln(A[:], B[:], EPS1, so=16 * p, bo=24 * p,
                              src_sy=ld, out=o1)
            o2 = []
            yield from gen_ln(B[:], A[:], EPS2, so=16 * p, bo=24 * p,
                              src_v=o1[0], out=o2)
            o3 = []
            yield from gen_quant(A[:], B[:], A[:], 16 * p, o2[0], out=o3)
            buf_free[p] = emit_transposes(
                lambda dc, lt=lt: f["xq"][:, dc, lt * 128:(lt + 1) * 128],
                A[:], DC, src_v=o3[0])

        gens = [tile_gen(0), tile_gen(1)]
        nxt = 2
        while gens:
            done = []
            for g in gens:
                if next(g, "END") == "END":
                    done.append(g)
            for g in done:
                gens.remove(g)
                if nxt < NT:
                    gens.append(tile_gen(nxt))
                    nxt += 1
        sub.close()
        return c_s.n

    def emit_stage12(f, grp, first_pass, xq_rdy):
        t0 = grp * 512
        mm12 = {}
        pmul = {}
        for jc in range(HC):
            b = jc % 2

            _fgu = wf(("gu", b))

            @blockq.gpsimd
            def _(g, jc=jc, b=b, _fgu=_fgu):
                wdeps(g, _fgu)
                if first_pass:
                    g.dma_start(out=f["wg"][b][:], in_=wg_in[jc]).then_inc(c_gp.sem, 16)
                    g.dma_start(out=f["wu"][b][:], in_=wu_in[jc]).then_inc(c_gp.sem, 16)
                else:
                    g.dma_start(out=f["wg"][b][:], in_=sgn_g[jc]).then_inc(c_gp.sem, 16)
                    g.dma_start(out=f["wu"][b][:], in_=sgn_u[jc]).then_inc(c_gp.sem, 16)
            c_gp.n += 32
            wld = c_gp.n

            if first_pass:
                @blockq.vector
                def _(v, b=b, wld=wld):
                    v.wait_ge(c_gp.sem, wld)
                    v.tensor_scalar(f["wg"][b][:], f["wg"][b][:], BIGS, None, ALU.mult)
                    v.tensor_scalar(f["wg"][b][:], f["wg"][b][:], -1.0, 1.0,
                                    ALU.max, ALU.min).then_inc(c_v.sem, 1)
                c_v.n += 1
                gsd = c_v.n

                @blockq.vector
                def _(v, b=b, wld=wld):
                    v.wait_ge(c_gp.sem, wld)
                    v.tensor_scalar(f["wu"][b][:], f["wu"][b][:], BIGS, None, ALU.mult)
                    v.tensor_scalar(f["wu"][b][:], f["wu"][b][:], -1.0, 1.0,
                                    ALU.max, ALU.min).then_inc(c_v.sem, 1)
                c_v.n += 1
                usd = c_v.n

                @blockq.sync
                def _(sy, jc=jc, b=b, gsd=gsd, usd=usd):
                    sy.wait_ge(c_v.sem, usd)
                    sy.dma_start(out=sgn_g[jc], in_=f["wg"][b][:]).then_inc(c_sy.sem, 16)
                    sy.dma_start(out=sgn_u[jc], in_=f["wu"][b][:]).then_inc(c_sy.sem, 16)
                c_sy.n += 32
                scr_wr = c_sy.n
            else:
                gsd = usd = scr_wr = None

            pg, pu = psum_f[0 + 2 * b], psum_f[1 + 2 * b]
            _fbk = wf(("bank", 0 + 2 * b), ("bank", 1 + 2 * b)) if jc < 2 else None

            @blockq.tensor
            def _(t, jc=jc, b=b, wld=wld, gsd=gsd, usd=usd, pg=pg, pu=pu,
                  _fbk=_fbk):
                if gsd is not None:
                    t.wait_ge(c_v.sem, usd)
                else:
                    t.wait_ge(c_gp.sem, wld)
                if jc == 0:
                    t.wait_ge(c_s.sem, xq_rdy)
                if jc >= 2:
                    t.wait_ge(c_v.sem, pmul[jc - 2])
                else:
                    wdeps(t, _fbk)
                for dc in range(DC):
                    mm = t.matmul(pg[:], f["wg"][b][:, dc, :],
                                  f["xq"][:, dc, t0:t0 + 512],
                                  start=(dc == 0), stop=(dc == DC - 1))
                mm.then_inc(c_t.sem, 1)
                for dc in range(DC):
                    mm = t.matmul(pu[:], f["wu"][b][:, dc, :],
                                  f["xq"][:, dc, t0:t0 + 512],
                                  start=(dc == 0), stop=(dc == DC - 1))
                mm.then_inc(c_t.sem, 1)
            c_t.n += 2
            mm12[jc] = c_t.n
            deps = [(c_t, mm12[jc])]
            if first_pass:
                deps.append((c_sy, scr_wr))
            set_free(("gu", b), *deps)

            @blockq.scalar
            def _(s, pg=pg, mmd=mm12[jc]):
                s.wait_ge(c_t.sem, mmd - 1)
                s.activation(out=f["g"][:], in_=pg[:], func=AF.Silu).then_inc(c_s.sem, 1)
            c_s.n += 1
            evd = c_s.n

            ucp = c_v.n + 1

            @blockq.vector
            def _(v, jc=jc, pu=pu, evd=evd, mmd=mm12[jc], ucp=ucp):
                v.wait_ge(c_t.sem, mmd)
                v.tensor_copy(f["u"][:], pu[:]).then_inc(c_v.sem, 1)
                v.wait_ge(c_v.sem, ucp)
                v.wait_ge(c_s.sem, evd)
                v.tensor_mul(f["p"][:, jc, :], f["g"][:], f["u"][:]).then_inc(c_v.sem, 1)
            c_v.n += 2
            pmul[jc] = c_v.n
        set_free(("bank", 0), (c_s, c_s.n), (c_v, c_v.n))
        set_free(("bank", 1), (c_s, c_s.n), (c_v, c_v.n))
        set_free(("bank", 2), (c_s, c_s.n), (c_v, c_v.n))
        set_free(("bank", 3), (c_s, c_s.n), (c_v, c_v.n))
        return pmul[HC - 1]

    def emit_stage3(f, app, grp, first_pass, src_res, dst_res, p_rdy):
        t0 = grp * 512
        x1rd = [0] * 4
        barrier_s = c_s.n
        barrier_v = c_v.n
        for tt in range(4):
            @blockq.sync
            def _(sy, tt=tt):
                sy.wait_ge(c_s.sem, barrier_s)
                sy.wait_ge(c_v.sem, barrier_v)
                r0 = TMAP[grp * 4 + tt] * 128
                sy.dma_start(out=f["x1g"][:, tt, :],
                             in_=src_res[r0:r0 + 128, :]).then_inc(c_sy.sem, 16)
            c_sy.n += 16
            x1rd[tt] = c_sy.n

        s3ev = {}
        xadd = {}
        for ddc in range(DC):
            b = ddc % 2

            _fwd = wf(("wd", b))

            @blockq.gpsimd
            def _(g, ddc=ddc, b=b, _fwd=_fwd):
                wdeps(g, _fwd)
                if first_pass:
                    g.dma_start(out=f["wd"][b][:], in_=wd_in[ddc]).then_inc(c_gp.sem, 16)
                else:
                    g.dma_start(out=f["wd"][b][:], in_=sgn_d[ddc]).then_inc(c_gp.sem, 16)
            c_gp.n += 16
            wld = c_gp.n

            if first_pass:
                @blockq.vector
                def _(v, b=b, wld=wld):
                    v.wait_ge(c_gp.sem, wld)
                    v.tensor_scalar(f["wd"][b][:], f["wd"][b][:],
                                    BIGS, None, ALU.mult)
                    v.tensor_scalar(f["wd"][b][:], f["wd"][b][:],
                                    -1.0, 1.0, ALU.max, ALU.min).then_inc(c_v.sem, 1)
                c_v.n += 1
                dsv = c_v.n
                dss = None

                @blockq.sync
                def _(sy, ddc=ddc, b=b, dsv=dsv):
                    sy.wait_ge(c_v.sem, dsv)
                    sy.dma_start(out=sgn_d[ddc], in_=f["wd"][b][:]).then_inc(c_sy.sem, 16)
                c_sy.n += 16
                scr3_wr = c_sy.n
            else:
                dsv = dss = scr3_wr = None

            po = psum_f[0 + b]
            _fb3 = wf(("bank", b)) if ddc < 2 else None

            @blockq.tensor
            def _(t, ddc=ddc, b=b, wld=wld, dsv=dsv, po=po, _fb3=_fb3):
                if dsv is not None:
                    t.wait_ge(c_v.sem, dsv)
                else:
                    t.wait_ge(c_gp.sem, wld)
                if ddc == 0:
                    t.wait_ge(c_v.sem, p_rdy)
                if ddc >= 2:
                    t.wait_ge(c_s.sem, s3ev[ddc - 2])
                else:
                    wdeps(t, _fb3)
                for jc in range(HC):
                    mm = t.matmul(po[:], f["wd"][b][:, jc, :], f["p"][:, jc, :],
                                  start=(jc == 0), stop=(jc == HC - 1))
                mm.then_inc(c_t.sem, 1)
            c_t.n += 1
            mmd = c_t.n
            deps = [(c_t, mmd)]
            if first_pass:
                deps.append((c_sy, scr3_wr))
            set_free(("wd", b), *deps)

            @blockq.scalar
            def _(s, po=po, mmd=mmd):
                s.wait_ge(c_t.sem, mmd)
                s.mul(out=f["ct"][:], in_=po[:], mul=0.5).then_inc(c_s.sem, 1)
            c_s.n += 1
            s3ev[ddc] = c_s.n
            ctd = c_s.n

            for tt in range(4):
                pb = psum_f[6 + (tt % 2)]
                _fb67 = wf(("bank", 6 + tt % 2)) if (ddc == 0 and tt < 2) else None

                @blockq.tensor
                def _(t, ddc=ddc, tt=tt, pb=pb, ctd=ctd, _fb67=_fb67):
                    t.wait_ge(c_s.sem, ctd)
                    t.wait_ge(c_v.sem, xadd.get((ddc - 1, tt), 0))
                    t.wait_ge(c_v.sem, xadd.get((ddc, tt - 2), 0))
                    if _fb67 is not None:
                        wdeps(t, _fb67)
                    t.transpose(pb[:, 0:128], f["ct"][:, tt * 128:(tt + 1) * 128],
                                ident[:]).then_inc(c_t.sem, 1)
                c_t.n += 1
                trd = c_t.n

                @blockq.vector
                def _(v, ddc=ddc, tt=tt, pb=pb, trd=trd):
                    v.wait_ge(c_t.sem, trd)
                    v.wait_ge(c_sy.sem, x1rd[tt])
                    v.tensor_add(f["x1g"][:, tt, ddc * 128:(ddc + 1) * 128],
                                 pb[:, 0:128],
                                 f["x1g"][:, tt, ddc * 128:(ddc + 1) * 128]
                                 ).then_inc(c_v.sem, 1)
                c_v.n += 1
                xadd[(ddc, tt)] = c_v.n
        set_free(("bank", 6), (c_v, xadd[(DC - 1, 2)]))
        set_free(("bank", 7), (c_v, xadd[(DC - 1, 3)]))
        set_free(("bank", 0), (c_s, s3ev[DC - 2]))
        set_free(("bank", 1), (c_s, s3ev[DC - 1]))

        for tt in range(4):
            done = xadd[(DC - 1, tt)]

            @blockq.sync
            def _(sy, tt=tt, done=done):
                sy.wait_ge(c_v.sem, done)
                r0 = TMAP[grp * 4 + tt] * 128
                sy.dma_start(out=dst_res[r0:r0 + 128, :],
                             in_=f["x1g"][:, tt, :]).then_inc(c_sy.sem, 16)
            c_sy.n += 16

            if app == 1:
                lnd = emit_ln(f["x1g"][:, tt, :], f["B"][:], EPS1,
                              src_v=done, dst_t=c_t.n)
                lnstg_wr = wfree.get(("lnstg",))
                emit_transposes(
                    lambda dc: f["lnstg"][:, dc, :], f["B"][:], DC, src_v=lnd,
                    src_sy=(lnstg_wr[0][1] if lnstg_wr else None))
                gt = TMAP[grp * 4 + tt]

                @blockq.sync
                def _(sy, gt=gt, lst=c_s.n):
                    sy.wait_ge(c_s.sem, lst)
                    sy.dma_start(out=ln1_scr[gt], in_=f["lnstg"][:]
                                 ).then_inc(c_sy.sem, 16)
                c_sy.n += 16
                set_free(("lnstg",), (c_sy, c_sy.n))

    def emit_ffn(app):
        fs = ExitStack()
        fe = fs.enter_context
        tag = f"f{app}"
        f = {"tag": tag}
        f["p"] = fe(nc.sbuf_tensor(tag + "p", [128, HC, 512], BF16, **L))
        f["xq"] = fe(nc.sbuf_tensor(tag + "xq", [128, DC, T], BF16, **L))
        f["B"] = fe(nc.sbuf_tensor(tag + "B", [128, D], F32, **L))
        f["lnstg"] = fe(nc.sbuf_tensor(tag + "lnstg", [128, DC, 128], BF16, **L))
        f["wg"] = [fe(nc.sbuf_tensor(f"{tag}wg{b}", [128, DC, 128], BF16, **L))
                   for b in range(2)]
        f["wu"] = [fe(nc.sbuf_tensor(f"{tag}wu{b}", [128, DC, 128], BF16, **L))
                   for b in range(2)]
        f["g"] = fe(nc.sbuf_tensor(tag + "g", [128, 512], BF16, **L))
        f["u"] = fe(nc.sbuf_tensor(tag + "u", [128, 512], BF16, **L))
        f["ct"] = fe(nc.sbuf_tensor(tag + "ct", [128, 512], F32, **L))
        f["x1g"] = fe(nc.sbuf_tensor(tag + "x1g", [128, 4, D], F32, **L))

        src = x_in if app == 1 else x2_scr
        dst = x1_scr if app == 1 else out_t
        if DEBUG_SCR and app == 1:
            xq_dbg = scr("xq_dbg", (128, DC, T), BF16)
            st_dbg = scr("st_dbg", (128, 16))
            id_dbg = scr("id_dbg", (128, 128))
        for b in range(2):
            set_free(("gu", b), (c_v, c_v.n), (c_t, c_t.n))
            set_free(("wd", b), (c_v, c_v.n), (c_t, c_t.n))

        def exchange_send():
            wd0, wd1 = f["wd"][0], f["wd"][1]
            eready_s = c_sy.n
            fdw = wf(("wd", 0)) + wf(("wd", 1))

            @blockq.gpsimd
            def _(g, eready_s=eready_s, fdw=fdw):
                wdeps(g, fdw)
                g.wait_ge(c_sy.sem, eready_s)
                g.dma_start(out=wd0[:, 32:48, :], in_=ln1_scr[0]).then_inc(c_gp.sem, 16)
                g.dma_start(out=wd0[:, 48:64, :], in_=ln1_scr[1]).then_inc(c_gp.sem, 16)
                g.dma_start(out=wd1[:, 32:48, :], in_=ln1_scr[6]).then_inc(c_gp.sem, 16)
                g.dma_start(out=wd1[:, 48:64, :], in_=ln1_scr[7]).then_inc(c_gp.sem, 16)
            c_gp.n += 64
            eld = c_gp.n

            @blockq.vector
            def _(v, eld=eld):
                v.wait_ge(c_gp.sem, eld)
                v.wait_ge(c_sy.sem, consts_ld)
                for k in range(2):
                    eL = wd0[:, 32 + 16 * k:48 + 16 * k, :]
                    eR = wd1[:, 32 + 16 * k:48 + 16 * k, :]
                    sA = wd0[:, 16 * k:16 * k + 16, :]
                    sB = wd1[:, 16 * k:16 * k + 16, :]
                    v.tensor_scalar_mul(sA, eL, hsel[:, 8:9])
                    v.scalar_tensor_tensor(sA, eR, hsel[:, 9:10], sA,
                                           ALU.mult, ALU.add)
                    v.tensor_scalar_mul(sB, eL, hsel[:, 10:11])
                    r = v.scalar_tensor_tensor(sB, eR, hsel[:, 11:12], sB,
                                               ALU.mult, ALU.add)
                r.then_inc(c_v.sem, 1)
            c_v.n += 1
            bld = c_v.n

            @blockq.gpsimd
            def _(g, bld=bld):
                g.wait_ge(c_v.sem, bld)
                g.dma_start(out=ccA_i[:, :, 0:128], in_=wd0[:, 0:16, :]).then_inc(c_gp.sem, 16)
                g.dma_start(out=ccA_i[:, :, 128:256], in_=wd0[:, 16:32, :]).then_inc(c_gp.sem, 16)
                g.dma_start(out=ccB_i[:, :, 0:128], in_=wd1[:, 0:16, :]).then_inc(c_gp.sem, 16)
                g.dma_start(out=ccB_i[:, :, 128:256], in_=wd1[:, 16:32, :]).then_inc(c_gp.sem, 16)
                g.dma_start(out=ccC_i, in_=ln1_scr[0][:, :, 0:16]).then_inc(c_gp.sem, 16)
            c_gp.n += 80
            stg = c_gp.n

            @blockq.gpsimd
            def _(g, stg=stg):
                g.wait_ge(c_gp.sem, stg)
                g.collective_compute(
                    "AllGather", ALU.bypass,
                    replica_groups=[[0, 1], [2, 3], [4, 5], [6, 7]],
                    ins=[ccA_i.opt()], outs=[ccA_o.opt()]).then_inc(c_cc.sem, 1)
                g.collective_compute(
                    "AllGather", ALU.bypass,
                    replica_groups=[[0, 3], [1, 2], [4, 7], [5, 6]],
                    ins=[ccB_i.opt()], outs=[ccB_o.opt()]).then_inc(c_cc.sem, 1)
                g.collective_compute(
                    "AllGather", ALU.bypass,
                    replica_groups=[[0, 1, 2, 3], [4, 5, 6, 7]],
                    ins=[ccC_i.opt()], outs=[ccC_o.opt()]).then_inc(c_cc.sem, 1)
            c_cc.n += 3
            set_free(("wd", 0), (c_gp, stg))
            set_free(("wd", 1), (c_gp, stg))

        xq_rdy = emit_entry(f, src)
        if DEBUG_SCR and app == 1:
            @blockq.sync
            def _(sy):
                sy.wait_ge(c_s.sem, xq_rdy)
                sy.dma_start(out=xq_dbg, in_=f["xq"][:]).then_inc(c_sy.sem, 16)
                sy.dma_start(out=st_dbg, in_=stats[:]).then_inc(c_sy.sem, 16)
                sy.dma_start(out=id_dbg, in_=ident[:]).then_inc(c_sy.sem, 16)
            c_sy.n += 48

        sub = ExitStack()
        f["wd"] = [sub.enter_context(
            nc.sbuf_tensor(f"{tag}wd{b}", [128, HC, 128], BF16, **L))
            for b in range(2)]
        for grp in (0, 1):
            first = (app == 1 and grp == 0)
            p_rdy = emit_stage12(f, grp, first, xq_rdy)
            emit_stage3(f, app, grp, first, src_res=src, dst_res=dst, p_rdy=p_rdy)
            if app == 1 and grp == 1 and stop_after != "ffn1":
                exchange_send()
        sub.close()
        fs.close()

    emit_ffn(1)
    if stop_after == "ffn1":
        finish()
        flush()
        return nc, st

    oscf = ExitStack()
    osc = oscf.enter_context(nc.sbuf_tensor("osc", [128, NH, T], BF16, **L))
    oLg = oscf.enter_context(nc.sbuf_tensor("oLg", [128, DC * GK], F32, **L))
    lgf = oscf.enter_context(nc.sbuf_tensor("lgf", [1, NH * GK], F32, **L))
    gmrg = oscf.enter_context(nc.sbuf_tensor("gmrg", [128, 4, 512], F32, **L))
    lnf = ExitStack()
    ln1T = lnf.enter_context(nc.sbuf_tensor("ln1T", [128, DC, KW], BF16, **L))

    @blockq.vector
    def _(v):
        v.memset(ln1T[:, :, 1552:1664], 0.0).then_inc(c_v.sem, 1)
    c_v.n += 1

    @blockq.sync
    def _(sy):
        for tt in range(NT):
            sy.dma_start(out=ln1T[:, :, OWN0 + tt * 128:OWN0 + (tt + 1) * 128],
                         in_=ln1_scr[tt]).then_inc(c_sy.sem, 16)
    c_sy.n += NT * 16
    ln1_own = c_sy.n

    exs = ExitStack()
    xe = exs.enter_context
    exO = xe(nc.sbuf_tensor("exO", [128, 2, DC, HALO], BF16, **L))
    exO2 = xe(nc.sbuf_tensor("exO2", [128, 2, DC, HALO], BF16, **L))
    exC = xe(nc.sbuf_tensor("exC", [128, 4, DC, GK], BF16, **L))

    _exccw = c_cc.n

    @blockq.gpsimd
    def _(g, ccw=_exccw):
        g.wait_ge(c_cc.sem, ccw)
        g.wait_ge(c_sy.sem, ln1_own)
        for sl in range(2):
            g.dma_start(out=exO[:, sl], in_=ccA_o[sl]).then_inc(c_gp.sem, 16)
            g.dma_start(out=exO2[:, sl], in_=ccB_o[sl]).then_inc(c_gp.sem, 16)
        g.dma_start(out=exC[:, 0], in_=ccC_o[0]).then_inc(c_gp.sem, 16)
    c_gp.n += 80
    exld = c_gp.n

    @blockq.vector
    def _(v):
        v.wait_ge(c_gp.sem, exld)
        v.wait_ge(c_sy.sem, ln1_own)
        hl = ln1T[:, :, 0:HALO]
        v.tensor_scalar_mul(hl, exO[:, 0], hsel[:, 0:1])
        v.scalar_tensor_tensor(hl, exO[:, 1], hsel[:, 1:2], hl, ALU.mult, ALU.add)
        v.scalar_tensor_tensor(hl, exO2[:, 0], hsel[:, 2:3], hl, ALU.mult, ALU.add)
        v.scalar_tensor_tensor(hl, exO2[:, 1], hsel[:, 3:4], hl, ALU.mult, ALU.add)
        hr = ln1T[:, :, OWN0 + T:OWN0 + T + HALO]
        v.tensor_scalar_mul(hr, exO[:, 0], hsel[:, 4:5])
        v.scalar_tensor_tensor(hr, exO[:, 1], hsel[:, 5:6], hr, ALU.mult, ALU.add)
        v.scalar_tensor_tensor(hr, exO2[:, 0], hsel[:, 6:7], hr, ALU.mult, ALU.add)
        v.scalar_tensor_tensor(hr, exO2[:, 1], hsel[:, 7:8], hr, ALU.mult, ALU.add)
        v.tensor_copy(ln1T[:, :, GCOL:GCOL + GK], exC[:, 0]).then_inc(c_v.sem, 1)
    c_v.n += 1
    halo_done = c_v.n

    exs.close()
    if stop_after == "exchange":
        finish()
        flush()
        return nc, st

    ast = ExitStack()
    ae = ast.enter_context
    kw_sb = ae(nc.sbuf_tensor("kw_sb", [128, 4, KW], BF16, **L))
    vw_sb = ae(nc.sbuf_tensor("vw_sb", [128, KT, 512], BF16, **L))
    qw_sb = ae(nc.sbuf_tensor("qw_sb", [128, 4, T], BF16, **L))
    qg_sb = ae(nc.sbuf_tensor("qg_sb", [128, 4, 128], BF16, **L))
    wat = [ae(nc.sbuf_tensor(f"wat{b}", [128, DC, 512], BF16, **L))
           for b in range(2)]
    pT_sb = ae(nc.sbuf_tensor("pT_sb", [128, 8, 128], BF16, **L))
    lr_sb = ae(nc.sbuf_tensor("lr_sb", [1, 128], F32, **L))
    lrep_sb = ae(nc.sbuf_tensor("lrep_sb", [128, 128], F32, **L))

    @blockq.vector
    def _(v):
        v.memset(qg_sb[:], 0.0).then_inc(c_v.sem, 1)
    c_v.n += 1
    qg_zero = c_v.n

    def proj_wave(w, which, wbuf, wdma):
        if which == "k":
            for oc in range(4):
                for ci, ts in enumerate(range(0, KW, 512)):
                    tn = min(512, KW - ts)
                    bk = 4 + (oc * 4 + ci) % 2
                    pb = psum_f[bk]
                    _fpk = wf(("bank", bk))

                    @blockq.tensor
                    def _(t, oc=oc, ts=ts, tn=tn, pb=pb, bk=bk, _fpk=_fpk):
                        t.wait_ge(c_gp.sem, wdma)
                        t.wait_ge(c_v.sem, halo_done)
                        wdeps(t, _fpk)
                        for dc in range(DC):
                            mm = t.matmul(pb[:, 0:tn],
                                          wbuf[:, dc, oc * 128:(oc + 1) * 128],
                                          ln1T[:, dc, ts:ts + tn],
                                          start=(dc == 0), stop=(dc == DC - 1))
                        mm.then_inc(c_t.sem, 1)
                    c_t.n += 1
                    mmd = c_t.n

                    @blockq.scalar
                    def _(s, oc=oc, ts=ts, tn=tn, pb=pb, mmd=mmd):
                        s.wait_ge(c_t.sem, mmd)
                        s.copy(out=kw_sb[:, oc, ts:ts + tn],
                               in_=pb[:, 0:tn]).then_inc(c_s.sem, 1)
                    c_s.n += 1
                    set_free(("bank", bk), (c_s, c_s.n))
        elif which == "v":
            for tt in range(KT):
                bk = 4 + tt % 2
                pb = psum_f[bk]
                _fpv = wf(("bank", bk))

                @blockq.tensor
                def _(t, tt=tt, pb=pb, bk=bk, _fpv=_fpv):
                    t.wait_ge(c_gp.sem, wdma)
                    t.wait_ge(c_v.sem, halo_done)
                    wdeps(t, _fpv)
                    for dc in range(DC):
                        mm = t.matmul(pb[:], ln1T[:, dc, tt * 128:(tt + 1) * 128],
                                      wbuf[:, dc, :],
                                      start=(dc == 0), stop=(dc == DC - 1))
                    mm.then_inc(c_t.sem, 1)
                c_t.n += 1
                mmd = c_t.n

                @blockq.scalar
                def _(s, tt=tt, pb=pb, mmd=mmd):
                    s.wait_ge(c_t.sem, mmd)
                    s.copy(out=vw_sb[:, tt, :], in_=pb[:]).then_inc(c_s.sem, 1)
                c_s.n += 1
                set_free(("bank", bk), (c_s, c_s.n))
        else:
            for oc in range(4):
                for th in range(3):
                    bk = 4 + th % 2
                    pb = psum_f[bk]
                    _fpq = wf(("bank", bk))

                    @blockq.tensor
                    def _(t, oc=oc, th=th, pb=pb, bk=bk, _fpq=_fpq):
                        t.wait_ge(c_gp.sem, wdma)
                        if th == 2:
                            t.wait_ge(c_v.sem, qg_zero)
                        wdeps(t, _fpq)
                        for dc in range(DC):
                            if th < 2:
                                rhs = ln1T[:, dc, OWN0 + th * 512:OWN0 + th * 512 + 512]
                                dst = pb[:]
                            else:
                                rhs = ln1T[:, dc, GCOL:GCOL + GK]
                                dst = pb[:, 0:GK]
                            mm = t.matmul(dst, wbuf[:, dc, oc * 128:(oc + 1) * 128],
                                          rhs, start=(dc == 0), stop=(dc == DC - 1))
                        mm.then_inc(c_t.sem, 1)
                    c_t.n += 1
                    mmd = c_t.n

                    @blockq.scalar
                    def _(s, oc=oc, th=th, pb=pb, mmd=mmd):
                        s.wait_ge(c_t.sem, mmd)
                        if th < 2:
                            s.copy(out=qw_sb[:, oc, th * 512:th * 512 + 512],
                                   in_=pb[:]).then_inc(c_s.sem, 1)
                        else:
                            s.copy(out=qg_sb[:, oc, 0:GK],
                                   in_=pb[:, 0:GK]).then_inc(c_s.sem, 1)
                    c_s.n += 1
                    set_free(("bank", bk), (c_s, c_s.n))

    def attend(w, hh, qt):
        glob = (qt == 8)
        nkt = 8 if glob else 6
        po, pl = psum_f[2], psum_f[3]
        h = w * 4 + hh

        expd = [0] * nkt
        for i in range(nkt):
            pb = psum_f[i % 2]
            _fsb = wf(("bank", i % 2)) if i < 2 else None

            @blockq.tensor
            def _(t, i=i, pb=pb, _fsb=_fsb):
                if i >= 2:
                    d = expd[i - 2]
                    if isinstance(d, tuple):
                        t.wait_ge(c_v.sem, d[1])
                    else:
                        t.wait_ge(c_s.sem, d)
                else:
                    wdeps(t, _fsb)
                if glob:
                    kcol, nk = OWN0 + i * 128, 128
                    rhs = qg_sb[:, hh, :]
                else:
                    kcol = GCOL if i == 5 else 128 * qt + 128 * i
                    nk = GK if i == 5 else 128
                    rhs = qw_sb[:, hh, qt * 128:(qt + 1) * 128]
                mm = t.matmul(pb[0:nk, 0:128], kw_sb[:, hh, kcol:kcol + nk],
                              rhs, start=True, stop=True)
                mm.then_inc(c_t.sem, 1)
            c_t.n += 1
            smm = c_t.n

            @blockq.scalar
            def _(s, i=i, pb=pb, smm=smm):
                s.wait_ge(c_t.sem, smm)
                nk = GK if (not glob and i == 5) else 128
                bias = neg16[0:nk, :] if glob else ebias[0:nk, qt, i:i + 1]
                s.activation(out=pT_sb[0:nk, i, :], in_=pb[0:nk, 0:128],
                             func=AF.Exp, bias=bias, scale=SCL).then_inc(c_s.sem, 1)
            c_s.n += 1
            expd[i] = c_s.n
            set_free(("bank", i % 2), (c_s, c_s.n))

            if (not glob) and i in (0, 4):
                @blockq.vector
                def _(v, i=i, ed=expd[i]):
                    v.wait_ge(c_s.sem, ed)
                    v.wait_ge(c_sy.sem, consts_ld)
                    v.tensor_mul(pT_sb[:, i, :], pT_sb[:, i, :],
                                 m01[:, qt, (0 if i == 0 else 1), :]
                                 ).then_inc(c_v.sem, 1)
                c_v.n += 1
                expd[i] = ("v", c_v.n)

        _fol = wf(("bank", 2), ("bank", 3))
        for i in range(nkt):
            dep = expd[i]

            @blockq.tensor
            def _(t, i=i, dep=dep, _fol=_fol):
                if isinstance(dep, tuple):
                    t.wait_ge(c_v.sem, dep[1])
                else:
                    t.wait_ge(c_s.sem, dep)
                if i == 0:
                    wdeps(t, _fol)
                if glob:
                    nk, vt = 128, 2 + i
                else:
                    nk = GK if i == 5 else 128
                    vt = 12 if i == 5 else qt + i
                mm = t.matmul(pl[0:1, 0:128], ones_col[0:nk, :],
                              pT_sb[0:nk, i, :], start=(i == 0), stop=(i == nkt - 1))
                mm.then_inc(c_t.sem, 1)
                mm = t.matmul(po[:, 0:128], vw_sb[0:nk, vt, hh * 128:hh * 128 + 128],
                              pT_sb[0:nk, i, :], start=(i == 0), stop=(i == nkt - 1))
                mm.then_inc(c_t.sem, 1)
            c_t.n += 2
        avd = c_t.n

        if glob:
            @blockq.vector
            def _(v, h=h):
                v.wait_ge(c_t.sem, avd)
                v.tensor_copy(oLg[:, h * GK:(h + 1) * GK], po[:, 0:GK])
                v.tensor_copy(lgf[0:1, h * GK:(h + 1) * GK],
                              pl[0:1, 0:GK]).then_inc(c_v.sem, 1)
            c_v.n += 1
            set_free(("bank", 2), (c_v, c_v.n))
            set_free(("bank", 3), (c_v, c_v.n))
            return c_v.n

        @blockq.vector
        def _(v):
            v.wait_ge(c_t.sem, avd)
            v.reciprocal(lr_sb[:], pl[0:1, 0:128]).then_inc(c_v.sem, 1)
        c_v.n += 1
        lrd = c_v.n
        set_free(("bank", 3), (c_v, lrd))

        _frep = wf(("bank", 4), ("lrep",))

        @blockq.tensor
        def _(t, _frep=_frep):
            t.wait_ge(c_v.sem, lrd)
            wdeps(t, _frep)
            t.matmul(psum_f[4][:, 0:128], ones_row[:], lr_sb[:],
                     start=True, stop=True).then_inc(c_t.sem, 1)
        c_t.n += 1
        repd = c_t.n

        lcp = c_v.n + 1

        @blockq.vector
        def _(v, h=h, lcp=lcp):
            v.wait_ge(c_t.sem, repd)
            v.tensor_copy(lrep_sb[:], psum_f[4][:, 0:128]).then_inc(c_v.sem, 1)
            v.wait_ge(c_v.sem, lcp)
            v.tensor_mul(osc[:, h, qt * 128:(qt + 1) * 128], po[:, 0:128],
                         lrep_sb[:]).then_inc(c_v.sem, 1)
        c_v.n += 2
        repc = c_v.n
        set_free(("bank", 4), (c_v, repc))
        set_free(("bank", 2), (c_v, c_v.n))
        set_free(("lrep",), (c_v, c_v.n))
        return c_v.n

    for b in range(2):
        set_free(("wat", b), (c_v, halo_done), (c_t, c_t.n))
    for w in range(4):
        for which, win, b in (("k", wk_in, 0), ("v", wv_in, 1), ("q", wq_in, 0)):
            _fwat = wf(("wat", b))

            @blockq.gpsimd
            def _(g, w=w, win=win, b=b, _fwat=_fwat):
                wdeps(g, _fwat)
                g.dma_start(out=wat[b][:], in_=win[w]).then_inc(c_gp.sem, 16)
            c_gp.n += 16
            proj_wave(w, which, wat[b], c_gp.n)
            set_free(("wat", b), (c_t, c_t.n))
        for hh in range(4):
            for qt in range(8):
                attend(w, hh, qt)
            attend(w, hh, 8)

    ast.close()
    lnf.close()
    attn_done_v = c_v.n
    if stop_after == "attn":
        finish()
        flush()
        return nc, st

    _gmidle = c_gp.n

    @blockq.vector
    def _(v, gmidle=_gmidle):
        v.wait_ge(c_gp.sem, gmidle)
        v.memset(gmrg[:, 0, 256:512], 0.0).then_inc(c_v.sem, 1)
    c_v.n += 1
    zpad = c_v.n

    _mgbase = c_gp.n
    _mgccw = c_cc.n + 1

    @blockq.gpsimd
    def _(g, base=_mgbase, ccw=_mgccw):
        g.wait_ge(c_v.sem, attn_done_v)
        g.wait_ge(c_v.sem, zpad)
        g.dma_start(out=ccD_i[0:128, 0:256], in_=oLg[:]).then_inc(c_gp.sem, 16)
        g.dma_start(out=ccD_i[0:128, 256:512],
                    in_=gmrg[:, 0, 256:512]).then_inc(c_gp.sem, 16)
        g.dma_start(out=ccD_i[0:1, 256:512], in_=lgf[:]).then_inc(c_gp.sem, 16)
        g.wait_ge(c_gp.sem, base + 48)
        g.collective_compute(
            "AllGather", ALU.bypass,
            replica_groups=[[0, 1, 2, 3], [4, 5, 6, 7]],
            ins=[ccD_i.opt()], outs=[ccD_o.opt()]).then_inc(c_cc.sem, 1)
        g.wait_ge(c_cc.sem, ccw)
        for sl in range(4):
            g.dma_start(out=gmrg[:, sl], in_=ccD_o[sl]).then_inc(c_gp.sem, 16)
    c_gp.n += 112
    c_cc.n += 1
    mrg_ld = c_gp.n

    def f(v):
        v.wait_ge(c_gp.sem, mrg_ld)
        v.tensor_add(gmrg[:, 0, :], gmrg[:, 0, :], gmrg[:, 1, :])
        return v.tensor_add(gmrg[:, 2, :], gmrg[:, 2, :], gmrg[:, 3, :])
    m1 = vstep(f)

    def f(v):
        v.wait_ge(c_v.sem, m1)
        return v.tensor_add(gmrg[:, 0, :], gmrg[:, 0, :], gmrg[:, 2, :])
    m2 = vstep(f)

    def f(v):
        v.wait_ge(c_v.sem, m2)
        return v.reciprocal(gmrg[0:1, 1, 256:512], gmrg[0:1, 0, 256:512])
    mrgd = vstep(f)

    gsc_dep = {}
    for hq in range(NH):
        bk = 4 + hq % 2
        pb = psum_f[bk]

        _fg5 = wf(("bank", bk))

        @blockq.tensor
        def _(t, hq=hq, pb=pb, bk=bk, _fg5=_fg5):
            t.wait_ge(c_v.sem, mrgd)
            wdeps(t, _fg5)
            t.matmul(pb[:, 0:GK], ones_row[:],
                     gmrg[0:1, 1, 256 + hq * GK:256 + (hq + 1) * GK],
                     start=True, stop=True).then_inc(c_t.sem, 1)
        c_t.n += 1
        rpd = c_t.n

        @blockq.vector
        def _(v, hq=hq, pb=pb, rpd=rpd):
            v.wait_ge(c_t.sem, rpd)
            v.tensor_mul(gsc[:, hq, :], gmrg[:, 0, hq * GK:(hq + 1) * GK],
                         pb[:, 0:GK]).then_inc(c_v.sem, 1)
        c_v.n += 1
        gsc_dep[hq] = c_v.n
        set_free(("bank", bk), (c_v, c_v.n))
    gsc_all = c_v.n

    p6 = ExitStack()
    p6e = p6.enter_context
    x2t_sb = p6e(nc.sbuf_tensor("x2t_sb", [128, NT, D], F32, **L))
    x1t_sb = p6e(nc.sbuf_tensor("x1t_sb", [128, NT, D], F32, **L))
    wo_sb = [p6e(nc.sbuf_tensor(f"wo_sb{b}", [128, DC, 128], BF16, **L))
             for b in range(2)]
    cta_sb = p6e(nc.sbuf_tensor("cta_sb", [128, T], F32, **L))
    gtmp_sb = p6e(nc.sbuf_tensor("gtmp_sb", [128, GK], F32, **L))

    for b in range(2):
        set_free(("wo", b), (c_v, gsc_all), (c_t, c_t.n))
    x1_ld = [0] * NT
    for tt in range(NT):
        @blockq.sync
        def _(sy, tt=tt):
            sy.wait_ge(c_v.sem, gsc_all)
            sy.dma_start(out=x1t_sb[:, tt, :],
                         in_=x1_scr[tt * 128:(tt + 1) * 128, :]).then_inc(c_sy.sem, 16)
        c_sy.n += 16
        x1_ld[tt] = c_sy.n

    aoev = {}
    x2d = {}
    bldd = {}
    for ddc in range(DC):
        b = ddc % 2

        _fwo = wf(("wo", b))

        @blockq.gpsimd
        def _(g, ddc=ddc, b=b, _fwo=_fwo):
            wdeps(g, _fwo)
            g.dma_start(out=wo_sb[b][:], in_=wo_in[ddc]).then_inc(c_gp.sem, 16)
        c_gp.n += 16
        wld = c_gp.n

        pa, pb2, pg16 = psum_f[0 + b], psum_f[2 + b], psum_f[4]

        _fp6 = (wf(("bank", 0 + b), ("bank", 2 + b)) if ddc < 2 else []) + \
               (wf(("bank", 4)) if ddc == 0 else [])

        @blockq.tensor
        def _(t, ddc=ddc, b=b, wld=wld, pa=pa, pb2=pb2, pg16=pg16, _fp6=_fp6):
            t.wait_ge(c_gp.sem, wld)
            t.wait_ge(c_v.sem, gsc_all)
            wdeps(t, _fp6)
            t.wait_ge(c_s.sem, aoev.get(ddc - 2, 0))
            t.wait_ge(c_v.sem, bldd.get(ddc - 1, 0))
            for th in range(2):
                dst = pa if th == 0 else pb2
                for dvc in range(DC):
                    mm = t.matmul(dst[:], wo_sb[b][:, dvc, :],
                                  osc[:, dvc, th * 512:th * 512 + 512],
                                  start=(dvc == 0), stop=(dvc == DC - 1))
                mm.then_inc(c_t.sem, 1)
            for dvc in range(DC):
                mm = t.matmul(pg16[:, 0:GK], wo_sb[b][:, dvc, :], gsc[:, dvc, :],
                              start=(dvc == 0), stop=(dvc == DC - 1))
            mm.then_inc(c_t.sem, 1)
        c_t.n += 3
        mmd = c_t.n
        set_free(("wo", b), (c_t, mmd))

        @blockq.scalar
        def _(s, pa=pa, pb2=pb2, mmd=mmd):
            s.wait_ge(c_t.sem, mmd - 1)
            s.copy(out=cta_sb[:, 0:512], in_=pa[:]).then_inc(c_s.sem, 1)
            s.copy(out=cta_sb[:, 512:1024], in_=pb2[:]).then_inc(c_s.sem, 1)
        c_s.n += 2
        ctd = c_s.n
        aoev[ddc] = c_s.n

        def f(v, ddc=ddc, pg16=pg16, ctd=ctd, mmd=mmd):
            v.wait_ge(c_s.sem, ctd)
            v.wait_ge(c_t.sem, mmd)
            return v.tensor_sub(gtmp_sb[:], pg16[:, 0:GK], cta_sb[:, 0:GK])
        b1 = vstep(f)

        def f(v, b1=b1):
            v.wait_ge(c_v.sem, b1)
            return v.tensor_mul(gtmp_sb[:], gtmp_sb[:], gbl[:])
        b2 = vstep(f)

        def f(v, b2=b2):
            v.wait_ge(c_v.sem, b2)
            return v.tensor_add(cta_sb[:, 0:GK], cta_sb[:, 0:GK], gtmp_sb[:])
        bldd[ddc] = vstep(f)
        bld = bldd[ddc]

        for tt in range(NT):
            pbt = psum_f[6 + tt % 2]
            _fx67 = wf(("bank", 6 + tt % 2)) if (ddc == 0 and tt < 2) else None

            @blockq.tensor
            def _(t, ddc=ddc, tt=tt, pbt=pbt, bld=bld, _fx67=_fx67):
                t.wait_ge(c_v.sem, bld)
                t.wait_ge(c_v.sem, x2d.get((ddc - 1, tt), 0))
                t.wait_ge(c_v.sem, x2d.get((ddc, tt - 2), 0))
                if _fx67 is not None:
                    wdeps(t, _fx67)
                t.transpose(pbt[:, 0:128], cta_sb[:, tt * 128:(tt + 1) * 128],
                            ident[:]).then_inc(c_t.sem, 1)
            c_t.n += 1
            trd = c_t.n

            @blockq.vector
            def _(v, ddc=ddc, tt=tt, pbt=pbt, trd=trd):
                v.wait_ge(c_t.sem, trd)
                v.wait_ge(c_sy.sem, x1_ld[tt])
                v.tensor_add(x2t_sb[:, tt, ddc * 128:(ddc + 1) * 128],
                             pbt[:, 0:128],
                             x1t_sb[:, tt, ddc * 128:(ddc + 1) * 128]
                             ).then_inc(c_v.sem, 1)
            c_v.n += 1
            x2d[(ddc, tt)] = c_v.n
    set_free(("bank", 6), (c_v, x2d[(DC - 1, NT - 2)]))
    set_free(("bank", 7), (c_v, x2d[(DC - 1, NT - 1)]))
    set_free(("bank", 0), (c_s, aoev[DC - 2]))
    set_free(("bank", 1), (c_s, aoev[DC - 1]))
    set_free(("bank", 2), (c_s, aoev[DC - 2]))
    set_free(("bank", 3), (c_s, aoev[DC - 1]))
    set_free(("bank", 4), (c_v, bldd[DC - 1]))

    for tt in range(NT):
        done = x2d[(DC - 1, tt)]

        @blockq.sync
        def _(sy, tt=tt, done=done):
            sy.wait_ge(c_v.sem, done)
            sy.dma_start(out=x2_scr[tt * 128:(tt + 1) * 128, :],
                         in_=x2t_sb[:, tt, :]).then_inc(c_sy.sem, 16)
        c_sy.n += 16

    p6.close()
    oscf.close()
    if stop_after == "p6":
        finish()
        flush()
        return nc, st

    emit_ffn(2)

    finish()
    flush()
    return nc, st


def _host_masks(core):
    S0 = 1024 * (core % 4)
    m01 = np.zeros((8, 2, 128, 128), np.float32)
    eb = np.zeros((8, 6, 128), np.float32)
    for qt in range(8):
        for ii, i in enumerate((0, 4)):
            kb = 128 * qt + 128 * i + np.arange(128)[:, None]
            kg = S0 - 256 + kb
            qg = S0 + 128 * qt + np.arange(128)[None, :]
            ok = (kg >= 0) & (kg < S) & ((np.abs(qg - kg) < 256) | (kg < GK))
            m01[qt, ii] = np.where(ok, 1.0, 0.0)
        for i in range(5):
            kg0 = S0 - 256 + 128 * qt + 128 * i
            valid = (kg0 >= 0) and (kg0 + 127 < S)
            eb[qt, i, :] = -STAB if valid else MSK
        eb[qt, 5, :] = MSK
        dbl = (S0 == 0) and (qt <= 2)
        eb[qt, 5, 0:GK] = MSK if dbl else -STAB
    c4 = core % 4
    hs = np.zeros(12, np.float32)
    if c4 % 2 == 0:
        hs[9] = 1.0
    else:
        hs[8] = 1.0
    if c4 == 1:
        hs[11] = 1.0
    else:
        hs[10] = 1.0
    if c4 == 1:
        hs[0] = 1.0
    elif c4 == 2:
        hs[2] = 1.0
    elif c4 == 3:
        hs[0] = 1.0
    if c4 == 0:
        hs[5] = 1.0
    elif c4 == 1:
        hs[7] = 1.0
    elif c4 == 2:
        hs[5] = 1.0
    hsel = np.broadcast_to(hs, (128, 12)).copy()
    gbl = np.full((128, GK), 1.0 if c4 == 0 else 0.0, np.float32)
    return m01, eb, hsel, gbl


def _prep_inputs(x, Wq, Wk, Wv, Wo, W_g, W_u, W_d):
    import ml_dtypes
    x = np.ascontiguousarray(np.asarray(x, np.float32))
    Wq, Wk, Wv, Wo, W_g, W_u, W_d = (
        np.asarray(w, np.float32) for w in (Wq, Wk, Wv, Wo, W_g, W_u, W_d))

    wg = np.ascontiguousarray(W_g.reshape(HC, 128, DC, 128).transpose(0, 3, 2, 1))
    wu = np.ascontiguousarray(W_u.reshape(HC, 128, DC, 128).transpose(0, 3, 2, 1))
    wd = np.ascontiguousarray(W_d.reshape(DC, 128, HC, 128).transpose(0, 3, 2, 1))
    wq = np.ascontiguousarray(Wq.reshape(DC, 128, 4, 512).transpose(2, 1, 0, 3))
    wk = np.ascontiguousarray(Wk.reshape(DC, 128, 4, 512).transpose(2, 1, 0, 3))
    wv = np.ascontiguousarray(Wv.reshape(DC, 128, 4, 512).transpose(2, 1, 0, 3))
    wo = np.ascontiguousarray(Wo.reshape(DC, 128, DC, 128).transpose(2, 1, 0, 3))
    ident = np.eye(128, dtype=np.float32)

    in_maps = []
    for core in range(8):
        b, c4 = core // 4, core % 4
        m01, eb, hsel, gbl = _host_masks(core)
        in_maps.append({
            "x": np.ascontiguousarray(x[b, c4 * T:(c4 + 1) * T, :]),
            "wg": wg, "wu": wu, "wd": wd,
            "wq": wq, "wk": wk, "wv": wv, "wo": wo,
            "m01": np.ascontiguousarray(
                m01.transpose(2, 0, 1, 3)).astype(ml_dtypes.bfloat16),
            "eb": np.ascontiguousarray(eb.transpose(2, 0, 1)),
            "hsel": hsel, "gbl": gbl, "ident": ident,
        })
    return in_maps


def kernel(L0_attn_hidden_states, L0_attn_Wq, L0_attn_Wk, L0_attn_Wv,
           L0_attn_Wo, L0_ffn_W_g, L0_ffn_W_u, L0_ffn_W_d):
    in_maps = _prep_inputs(L0_attn_hidden_states, L0_attn_Wq, L0_attn_Wk,
                           L0_attn_Wv, L0_attn_Wo, L0_ffn_W_g, L0_ffn_W_u,
                           L0_ffn_W_d)
    nc, st = build(STOP_AFTER)
    st.close()
    res = run_bass_kernel_spmd(nc, in_maps, core_ids=list(range(8)))
    out = np.empty((2, S, D), np.float32)
    for core in range(8):
        b, c4 = core // 4, core % 4
        out[b, c4 * T:(c4 + 1) * T, :] = res.results[core]["out"]
    return out


# version 12
# speedup vs baseline: 1.3244x; 1.0006x over previous
import numpy as np
from contextlib import ExitStack

import concourse.bass as bass
import concourse.mybir as mybir
from concourse.bass_utils import run_bass_kernel_spmd

try:
    import jax
    jax.config.update("jax_compilation_cache_dir", "/tmp/jaxcache")
    jax.config.update("jax_persistent_cache_min_entry_size_bytes", -1)
    jax.config.update("jax_persistent_cache_min_compile_time_secs", 0.0)
except Exception:
    pass

F32 = mybir.dt.float32
BF16 = mybir.dt.bfloat16
AF = mybir.ActivationFunctionType
ALU = mybir.AluOpType
AXX = mybir.AxisListType.X

T = 1024
NT = 8
D = 2048
DC = 16
H = 8192
HC = 64
NH = 16
S = 4096
HALO = 256
GK = 16
KW = 1664
KT = 13
OWN0 = 256
GCOL = 1536
STAB = 16.0
MSK = -1.0e30
EPS1 = 1e-5
EPS2 = 1e-8
SCL = float(1.0 / np.sqrt(np.float32(128.0)))
INV_D = 1.0 / 2048.0
BIGS = 1.0e38

TMAP = (0, 1, 6, 7, 2, 3, 4, 5)

STOP_AFTER = None
DEBUG_SCR = False


def build(stop_after=None):
    nc = bass.Bass("TRN2", target_bir_lowering=False, detect_race_conditions=False)
    st = ExitStack()
    e = st.enter_context

    def inp(name, shape, dt=F32):
        return nc.dram_tensor(name, list(shape), dt, kind="ExternalInput").ap()

    x_in = inp("x", (T, D))
    wg_in = inp("wg", (HC, 128, DC, 128))
    wu_in = inp("wu", (HC, 128, DC, 128))
    wd_in = inp("wd", (DC, 128, HC, 128))
    wq_in = inp("wq", (4, 128, DC, 512))
    wk_in = inp("wk", (4, 128, DC, 512))
    wv_in = inp("wv", (4, 128, DC, 512))
    wo_in = inp("wo", (DC, 128, DC, 128))
    m01_in = inp("m01", (128, 8, 2, 128), BF16)
    eb_in = inp("eb", (128, 8, 6))
    hsel_in = inp("hsel", (128, 12))
    gbl_in = inp("gbl", (128, 16))
    id_in = inp("ident", (128, 128))

    out_t = nc.dram_tensor("out", [T, D], F32, kind="ExternalOutput").ap()

    def scr(name, shape, dt=F32):
        kind = ("ExternalOutput" if DEBUG_SCR and not name.startswith("cc")
                else "Internal")
        return nc.dram_tensor(name, list(shape), dt, kind=kind).ap()

    x1_scr = scr("x1_scr", (T, D))
    x2_scr = scr("x2_scr", (T, D))
    ln1_scr = scr("ln1_scr", (NT, 128, DC, 128), BF16)
    sgn_g = scr("sgn_g", (HC, 128, DC, 128), BF16)
    sgn_u = scr("sgn_u", (HC, 128, DC, 128), BF16)
    sgn_d = scr("sgn_d", (DC, 128, HC, 128), BF16)
    ccA_i = scr("ccA_i", (128, DC, 2 * HALO), BF16)
    ccA_o = scr("ccA_o", (2, 128, DC, 2 * HALO), BF16)
    ccB_i = scr("ccB_i", (128, DC, 2 * HALO), BF16)
    ccB_o = scr("ccB_o", (2, 128, DC, 2 * HALO), BF16)
    ccC_i = scr("ccC_i", (128, DC, GK), BF16)
    ccC_o = scr("ccC_o", (4, 128, DC, GK), BF16)
    ccD_i = scr("ccD_i", (128, 512))
    ccD_o = scr("ccD_o", (4, 128, 512))

    class Ctr:
        def __init__(self, name):
            self.sem = e(nc.semaphore(name))
            self.n = 0

    c_sy = Ctr("c_sy")
    c_gp = Ctr("c_gp")
    c_sq = Ctr("c_sq")
    c_t = Ctr("c_t")
    c_v = Ctr("c_v")
    c_s = Ctr("c_s")
    c_cc = Ctr("c_cc")

    R = dict(side="right")
    ident = e(nc.sbuf_tensor("identS", [128, 128], F32, **R))
    m01 = e(nc.sbuf_tensor("m01S", [128, 8, 2, 128], BF16, **R))
    ebias = e(nc.sbuf_tensor("ebiasS", [128, 8, 6], F32, **R))
    hsel = e(nc.sbuf_tensor("hselS", [128, 12], F32, **R))
    gbl = e(nc.sbuf_tensor("gblS", [128, 16], F32, **R))
    ones_col = e(nc.sbuf_tensor("ones_col", [128, 1], BF16, **R))
    ones_row = e(nc.sbuf_tensor("ones_row", [1, 128], F32, **R))
    neg16 = e(nc.sbuf_tensor("neg16", [128, 1], F32, **R))
    stats = e(nc.sbuf_tensor("stats", [128, 32], F32, **R))
    bnscr = e(nc.sbuf_tensor("bnscr", [128, 48], F32, **R))
    gsc = e(nc.sbuf_tensor("gsc", [128, DC, GK], BF16, **R))

    block = e(nc.Block())

    pend = {k: [] for k in ("tensor", "vector", "scalar", "gpsimd", "sync")}

    class _DQ:
        def __getattr__(self, eng):
            def dec(fn):
                pend[eng].append(fn)
                return fn
            return dec

    blockq = _DQ()

    def flush():
        for eng in ("tensor", "vector", "scalar", "gpsimd", "sync"):
            fns = pend[eng]
            pend[eng] = []
            if fns:
                getattr(block, eng)(lambda e, fns=fns: [f(e) for f in fns])

    def wdeps(eng, deps):
        for c_, v_ in deps:
            eng.wait_ge(c_.sem, v_)

    psum_f = [e(nc.psum_tensor(f"pb{i}", [128, 512], F32)) for i in range(8)]

    @blockq.sync
    def _(sy):
        for dst, src in ((ident, id_in), (m01, m01_in), (ebias, eb_in),
                         (hsel, hsel_in), (gbl, gbl_in)):
            sy.dma_start(out=dst[:], in_=src).then_inc(c_sy.sem, 16)
    c_sy.n += 5 * 16
    consts_ld = c_sy.n

    @blockq.vector
    def _(v):
        v.memset(ones_col[:], 1.0)
        v.memset(ones_row[:], 1.0)
        v.memset(neg16[:], -STAB).then_inc(c_v.sem, 1)
    c_v.n += 1
    consts_v = c_v.n

    @blockq.scalar
    def _(s):
        s.wait_ge(c_v.sem, consts_v)
        s.wait_ge(c_sy.sem, consts_ld)

    @blockq.tensor
    def _(t):
        t.wait_ge(c_v.sem, consts_v)
        t.wait_ge(c_sy.sem, consts_ld)

    wfree = {}

    def wait_free(engine, key):
        for ctr, val in wfree.get(key, ()):
            engine.wait_ge(ctr.sem, val)

    def wf(*keys):
        out = []
        for k in keys:
            out.extend(wfree.get(k, ()))
        return out

    def set_free(key, *deps):
        wfree[key] = list(deps)


    def finish():
        @blockq.sync
        def _(sy, nsy=None, nsq=None):
            sy.wait_ge(c_sy.sem, c_sy.n if nsy is None else nsy)
            sy.wait_ge(c_sq.sem, c_sq.n if nsq is None else nsq)
        @blockq.gpsimd
        def _(g):
            g.wait_ge(c_gp.sem, c_gp.n)

    def vstep(fn):
        @blockq.vector
        def _(v):
            fn(v).then_inc(c_v.sem, 1)
        c_v.n += 1
        return c_v.n

    def gen_ln(src, dst, eps, so=0, bo=0, src_s=None, src_v=None, src_sy=None,
               dst_t=None, out=None):
        def _w(eng):
            if src_s is not None:
                eng.wait_ge(c_s.sem, src_s)
            if src_v is not None:
                eng.wait_ge(c_v.sem, src_v)
            if src_sy is not None:
                eng.wait_ge(c_sy.sem, src_sy)

        def f(v):
            _w(v)
            for k in range(4):
                r = v.bn_stats(bnscr[:, bo + 6 * k:bo + 6 * k + 6],
                               src[:, 512 * k:512 * k + 512])
            return r
        va = vstep(f)
        yield

        def f(v):
            v.wait_ge(c_v.sem, va)
            return v.bn_aggr(stats[:, so:so + 2], bnscr[:, bo:bo + 24])
        vb = vstep(f)
        yield

        def f(v):
            v.wait_ge(c_v.sem, vb)
            return v.tensor_scalar_add(stats[:, so + 6:so + 7],
                                       stats[:, so + 1:so + 2], float(eps))
        ve = vstep(f)
        yield

        @blockq.scalar
        def _(s):
            s.wait_ge(c_v.sem, ve)
            s.activation(out=stats[:, so + 7:so + 8], in_=stats[:, so + 6:so + 7],
                         func=AF.Sqrt).then_inc(c_s.sem, 1)
        c_s.n += 1
        sddone = c_s.n
        yield

        def f(v):
            v.wait_ge(c_s.sem, sddone)
            return v.reciprocal(stats[:, so + 8:so + 9], stats[:, so + 7:so + 8])
        vf = vstep(f)
        yield

        def f(v):
            v.wait_ge(c_v.sem, vf)
            return v.scalar_tensor_tensor(stats[:, so + 9:so + 10],
                                          stats[:, so:so + 1], -1.0,
                                          stats[:, so + 8:so + 9],
                                          ALU.mult, ALU.mult)
        vg = vstep(f)
        yield

        def f(v):
            v.wait_ge(c_v.sem, vg)
            if dst_t is not None:
                v.wait_ge(c_t.sem, dst_t)
            return v.tensor_scalar(dst, src, stats[:, so + 8:so + 9],
                                   stats[:, so + 9:so + 10], ALU.mult, ALU.add)
        r = vstep(f)
        if out is not None:
            out.append(r)

    def emit_ln(src, dst, eps, tmp_sq=None, src_s=None, src_v=None,
                src_sy=None, dst_t=None):
        out = []
        for _ in gen_ln(src, dst, eps, src_s=src_s, src_v=src_v,
                        src_sy=src_sy, dst_t=dst_t, out=out):
            pass
        return out[0]

    def gen_quant(z, y, a, so, src_v, out=None):
        MAGIC = 12582912.0

        def f(v):
            v.wait_ge(c_v.sem, src_v)
            return v.tensor_reduce(stats[:, so + 10:so + 11], z, axis=AXX,
                                   op=ALU.max, apply_absolute_value=True)
        qa = vstep(f)
        yield

        def f(v):
            v.wait_ge(c_v.sem, qa)
            return v.reciprocal(stats[:, so + 11:so + 12], stats[:, so + 10:so + 11])
        qb = vstep(f)
        yield

        def f(v):
            v.wait_ge(c_v.sem, qb)
            v.tensor_scalar_mul(stats[:, so + 12:so + 13], stats[:, so + 11:so + 12], 127.0)
            return v.tensor_scalar_mul(stats[:, so + 13:so + 14],
                                       stats[:, so + 10:so + 11], 1.0 / 127.0)
        qc = vstep(f)
        yield

        def f(v):
            v.wait_ge(c_v.sem, qc)
            v.tensor_scalar_mul(y, z, stats[:, so + 12:so + 13])
            v.tensor_scalar(a, y, MAGIC, -MAGIC, ALU.add, ALU.add)
            v.tensor_scalar(a, a, 127.0, -128.0, ALU.min, ALU.max)
            return v.tensor_scalar_mul(a, a, stats[:, so + 13:so + 14])
        r = vstep(f)
        if out is not None:
            out.append(r)

    def emit_transposes(dst_fn, src, n, src_v=None, src_s=None, src_sy=None):
        sready = c_s.n
        last_ev = {0: None, 1: None}
        for i in range(n):
            bk = 6 + (i % 2)
            pb = psum_f[bk]
            _fdv = wf(("bank", bk)) if i < 2 else None

            @blockq.tensor
            def _(t, i=i, pb=pb, bk=bk, _fd=_fdv):
                if src_v is not None:
                    t.wait_ge(c_v.sem, src_v)
                if src_s is not None:
                    t.wait_ge(c_s.sem, src_s)
                if src_sy is not None:
                    t.wait_ge(c_sy.sem, src_sy)
                if _fd is not None:
                    wdeps(t, _fd)
                else:
                    t.wait_ge(c_s.sem, sready + i - 1)
                t.transpose(pb[:, 0:128], src[:, i * 128:(i + 1) * 128],
                            ident[:]).then_inc(c_t.sem, 1)
            c_t.n += 1
            trd = c_t.n

            @blockq.scalar
            def _(s, i=i, pb=pb, trd=trd):
                s.wait_ge(c_t.sem, trd)
                s.copy(out=dst_fn(i), in_=pb[:, 0:128]).then_inc(c_s.sem, 1)
            c_s.n += 1
            last_ev[i % 2] = c_s.n
        for par in (0, 1):
            if last_ev[par] is not None:
                set_free(("bank", 6 + par), (c_s, last_ev[par]))
        return c_s.n

    L = dict(side="left")

    def emit_entry(f, src_dram):
        sub = ExitStack()
        Ab = [sub.enter_context(nc.sbuf_tensor(f"{f['tag']}A{p}", [128, D], F32, **L))
              for p in range(2)]
        Bb = [sub.enter_context(nc.sbuf_tensor(f"{f['tag']}B{p}", [128, D], F32, **L))
              for p in range(2)]
        buf_free = [0, 0]

        def tile_gen(lt):
            p = lt % 2
            A, B = Ab[p], Bb[p]
            rt = TMAP[lt]

            @blockq.sync
            def _(sy, free_s=buf_free[p], rt=rt):
                sy.wait_ge(c_s.sem, free_s)
                sy.dma_start(out=A[:], in_=src_dram[rt * 128:(rt + 1) * 128, :]
                             ).then_inc(c_sy.sem, 16)
            c_sy.n += 16
            ld = c_sy.n
            yield

            o1 = []
            yield from gen_ln(A[:], B[:], EPS1, so=16 * p, bo=24 * p,
                              src_sy=ld, out=o1)
            o2 = []
            yield from gen_ln(B[:], A[:], EPS2, so=16 * p, bo=24 * p,
                              src_v=o1[0], out=o2)
            o3 = []
            yield from gen_quant(A[:], B[:], A[:], 16 * p, o2[0], out=o3)
            buf_free[p] = emit_transposes(
                lambda dc, lt=lt: f["xq"][:, dc, lt * 128:(lt + 1) * 128],
                A[:], DC, src_v=o3[0])

        gens = [tile_gen(0), tile_gen(1)]
        nxt = 2
        while gens:
            done = []
            for g in gens:
                if next(g, "END") == "END":
                    done.append(g)
            for g in done:
                gens.remove(g)
                if nxt < NT:
                    gens.append(tile_gen(nxt))
                    nxt += 1
        sub.close()
        return c_s.n

    def emit_stage12(f, grp, first_pass, xq_rdy, hook=None):
        t0 = grp * 512
        mm12 = {}
        pmul = {}
        for jc in range(HC):
            b = jc % 2
            if hook is not None and jc == 3:
                hook()
                hook = None

            _fgu = wf(("gu", b))

            @blockq.gpsimd
            def _(g, jc=jc, b=b, _fgu=_fgu):
                wdeps(g, _fgu)
                if first_pass:
                    g.dma_start(out=f["wg"][b][:], in_=wg_in[jc]).then_inc(c_gp.sem, 16)
                    g.dma_start(out=f["wu"][b][:], in_=wu_in[jc]).then_inc(c_gp.sem, 16)
                else:
                    g.dma_start(out=f["wg"][b][:], in_=sgn_g[jc]).then_inc(c_gp.sem, 16)
                    g.dma_start(out=f["wu"][b][:], in_=sgn_u[jc]).then_inc(c_gp.sem, 16)
            c_gp.n += 32
            wld = c_gp.n
            wldu = c_sy.n

            if first_pass:
                @blockq.vector
                def _(v, b=b, wld=wld):
                    v.wait_ge(c_gp.sem, wld)
                    v.tensor_scalar(f["wg"][b][:], f["wg"][b][:], BIGS, None, ALU.mult)
                    v.tensor_scalar(f["wg"][b][:], f["wg"][b][:], -1.0, 1.0,
                                    ALU.max, ALU.min).then_inc(c_v.sem, 1)
                c_v.n += 1
                gsd = c_v.n

                @blockq.vector
                def _(v, b=b, wld=wld):
                    v.wait_ge(c_gp.sem, wld)
                    v.tensor_scalar(f["wu"][b][:], f["wu"][b][:], BIGS, None, ALU.mult)
                    v.tensor_scalar(f["wu"][b][:], f["wu"][b][:], -1.0, 1.0,
                                    ALU.max, ALU.min).then_inc(c_v.sem, 1)
                c_v.n += 1
                usd = c_v.n

                @blockq.sync
                def _(sy, jc=jc, b=b, gsd=gsd, usd=usd):
                    sy.wait_ge(c_v.sem, usd)
                    sy.dma_start(out=sgn_g[jc], in_=f["wg"][b][:]).then_inc(c_sy.sem, 16)
                    sy.dma_start(out=sgn_u[jc], in_=f["wu"][b][:]).then_inc(c_sy.sem, 16)
                c_sy.n += 32
                scr_wr = c_sy.n
            else:
                gsd = usd = scr_wr = None

            pg, pu = psum_f[0 + 2 * b], psum_f[1 + 2 * b]
            _fbk = wf(("bank", 0 + 2 * b), ("bank", 1 + 2 * b)) if jc < 2 else None

            @blockq.tensor
            def _(t, jc=jc, b=b, wld=wld, wldu=wldu, gsd=gsd, usd=usd, pg=pg,
                  pu=pu, _fbk=_fbk):
                if gsd is not None:
                    t.wait_ge(c_v.sem, usd)
                else:
                    t.wait_ge(c_gp.sem, wld)
                if jc == 0:
                    t.wait_ge(c_s.sem, xq_rdy)
                if jc >= 2:
                    t.wait_ge(c_v.sem, pmul[jc - 2])
                else:
                    wdeps(t, _fbk)
                for dc in range(DC):
                    mm = t.matmul(pg[:], f["wg"][b][:, dc, :],
                                  f["xq"][:, dc, t0:t0 + 512],
                                  start=(dc == 0), stop=(dc == DC - 1))
                mm.then_inc(c_t.sem, 1)
                for dc in range(DC):
                    mm = t.matmul(pu[:], f["wu"][b][:, dc, :],
                                  f["xq"][:, dc, t0:t0 + 512],
                                  start=(dc == 0), stop=(dc == DC - 1))
                mm.then_inc(c_t.sem, 1)
            c_t.n += 2
            mm12[jc] = c_t.n
            deps = [(c_t, mm12[jc])]
            if first_pass:
                deps.append((c_sy, scr_wr))
            set_free(("gu", b), *deps)

            @blockq.scalar
            def _(s, pg=pg, mmd=mm12[jc]):
                s.wait_ge(c_t.sem, mmd - 1)
                s.activation(out=f["g"][:], in_=pg[:], func=AF.Silu).then_inc(c_s.sem, 1)
            c_s.n += 1
            evd = c_s.n

            ucp = c_v.n + 1

            @blockq.vector
            def _(v, jc=jc, pu=pu, evd=evd, mmd=mm12[jc], ucp=ucp):
                v.wait_ge(c_t.sem, mmd)
                v.tensor_copy(f["u"][:], pu[:]).then_inc(c_v.sem, 1)
                v.wait_ge(c_v.sem, ucp)
                v.wait_ge(c_s.sem, evd)
                v.tensor_mul(f["p"][:, jc, :], f["g"][:], f["u"][:]).then_inc(c_v.sem, 1)
            c_v.n += 2
            pmul[jc] = c_v.n
        set_free(("bank", 0), (c_s, c_s.n), (c_v, c_v.n))
        set_free(("bank", 1), (c_s, c_s.n), (c_v, c_v.n))
        set_free(("bank", 2), (c_s, c_s.n), (c_v, c_v.n))
        set_free(("bank", 3), (c_s, c_s.n), (c_v, c_v.n))
        return pmul[HC - 1]

    def emit_stage3(f, app, grp, first_pass, src_res, dst_res, p_rdy):
        t0 = grp * 512
        x1rd = [0] * 4
        wlds = {}
        barrier_s = c_s.n
        barrier_v = c_v.n
        for tt in range(4):
            @blockq.sync
            def _(sy, tt=tt):
                sy.wait_ge(c_s.sem, barrier_s)
                sy.wait_ge(c_v.sem, barrier_v)
                r0 = TMAP[grp * 4 + tt] * 128
                sy.dma_start(out=f["x1g"][:, tt, :],
                             in_=src_res[r0:r0 + 128, :]).then_inc(c_sy.sem, 16)
            c_sy.n += 16
            x1rd[tt] = c_sy.n

        s3ev = {}
        xadd = {}
        for ddc in range(DC):
            b = ddc % 2

            _fwd = wf(("wd", b))

            @blockq.gpsimd
            def _(g, ddc=ddc, b=b, _fwd=_fwd):
                wdeps(g, _fwd)
                if first_pass:
                    g.dma_start(out=f["wd"][b][:], in_=wd_in[ddc]).then_inc(c_gp.sem, 16)
                else:
                    g.dma_start(out=f["wd"][b][:], in_=sgn_d[ddc]).then_inc(c_gp.sem, 16)
            c_gp.n += 16
            wld = c_gp.n

            if first_pass:
                @blockq.vector
                def _(v, b=b, wld=wld):
                    v.wait_ge(c_gp.sem, wld)
                    v.tensor_scalar(f["wd"][b][:], f["wd"][b][:],
                                    BIGS, None, ALU.mult)
                    v.tensor_scalar(f["wd"][b][:], f["wd"][b][:],
                                    -1.0, 1.0, ALU.max, ALU.min).then_inc(c_v.sem, 1)
                c_v.n += 1
                dsv = c_v.n
                dss = None

                @blockq.sync
                def _(sy, ddc=ddc, b=b, dsv=dsv):
                    sy.wait_ge(c_v.sem, dsv)
                    sy.dma_start(out=sgn_d[ddc], in_=f["wd"][b][:]).then_inc(c_sy.sem, 16)
                c_sy.n += 16
                scr3_wr = c_sy.n
            else:
                dsv = dss = scr3_wr = None

            po = psum_f[0 + b]
            _fb3 = wf(("bank", b)) if ddc < 2 else None

            _wls = wlds.get(ddc)

            @blockq.tensor
            def _(t, ddc=ddc, b=b, wld=wld, dsv=dsv, po=po, _fb3=_fb3,
                  _wls=_wls):
                if dsv is not None:
                    t.wait_ge(c_v.sem, dsv)
                else:
                    t.wait_ge(c_gp.sem, wld)
                    if _wls is not None:
                        t.wait_ge(c_sy.sem, _wls)
                if ddc == 0:
                    t.wait_ge(c_v.sem, p_rdy)
                if ddc >= 2:
                    t.wait_ge(c_s.sem, s3ev[ddc - 2])
                else:
                    wdeps(t, _fb3)
                for jc in range(HC):
                    mm = t.matmul(po[:], f["wd"][b][:, jc, :], f["p"][:, jc, :],
                                  start=(jc == 0), stop=(jc == HC - 1))
                mm.then_inc(c_t.sem, 1)
            c_t.n += 1
            mmd = c_t.n
            deps = [(c_t, mmd)]
            if first_pass:
                deps.append((c_sy, scr3_wr))
            set_free(("wd", b), *deps)

            @blockq.scalar
            def _(s, po=po, mmd=mmd):
                s.wait_ge(c_t.sem, mmd)
                s.mul(out=f["ct"][:], in_=po[:], mul=0.5).then_inc(c_s.sem, 1)
            c_s.n += 1
            s3ev[ddc] = c_s.n
            ctd = c_s.n

            for tt in range(4):
                pb = psum_f[6 + (tt % 2)]
                _fb67 = wf(("bank", 6 + tt % 2)) if (ddc == 0 and tt < 2) else None

                @blockq.tensor
                def _(t, ddc=ddc, tt=tt, pb=pb, ctd=ctd, _fb67=_fb67):
                    t.wait_ge(c_s.sem, ctd)
                    t.wait_ge(c_v.sem, xadd.get((ddc - 1, tt), 0))
                    t.wait_ge(c_v.sem, xadd.get((ddc, tt - 2), 0))
                    if _fb67 is not None:
                        wdeps(t, _fb67)
                    t.transpose(pb[:, 0:128], f["ct"][:, tt * 128:(tt + 1) * 128],
                                ident[:]).then_inc(c_t.sem, 1)
                c_t.n += 1
                trd = c_t.n

                @blockq.vector
                def _(v, ddc=ddc, tt=tt, pb=pb, trd=trd):
                    v.wait_ge(c_t.sem, trd)
                    v.wait_ge(c_sy.sem, x1rd[tt])
                    v.tensor_add(f["x1g"][:, tt, ddc * 128:(ddc + 1) * 128],
                                 pb[:, 0:128],
                                 f["x1g"][:, tt, ddc * 128:(ddc + 1) * 128]
                                 ).then_inc(c_v.sem, 1)
                c_v.n += 1
                xadd[(ddc, tt)] = c_v.n
        set_free(("bank", 6), (c_v, xadd[(DC - 1, 2)]))
        set_free(("bank", 7), (c_v, xadd[(DC - 1, 3)]))
        set_free(("bank", 0), (c_s, s3ev[DC - 2]))
        set_free(("bank", 1), (c_s, s3ev[DC - 1]))

        for tt in range(4):
            done = xadd[(DC - 1, tt)]

            @blockq.sync
            def _(sy, tt=tt, done=done):
                sy.wait_ge(c_v.sem, done)
                r0 = TMAP[grp * 4 + tt] * 128
                sy.dma_start(out=dst_res[r0:r0 + 128, :],
                             in_=f["x1g"][:, tt, :]).then_inc(c_sy.sem, 16)
            c_sy.n += 16

            if app == 1:
                lnd = emit_ln(f["x1g"][:, tt, :], f["B"][:], EPS1,
                              src_v=done, dst_t=c_t.n)
                lnstg_wr = wfree.get(("lnstg",))
                emit_transposes(
                    lambda dc: f["lnstg"][:, dc, :], f["B"][:], DC, src_v=lnd,
                    src_sy=(lnstg_wr[0][1] if lnstg_wr else None))
                gt = TMAP[grp * 4 + tt]

                @blockq.sync
                def _(sy, gt=gt, lst=c_s.n):
                    sy.wait_ge(c_s.sem, lst)
                    sy.dma_start(out=ln1_scr[gt], in_=f["lnstg"][:]
                                 ).then_inc(c_sy.sem, 16)
                c_sy.n += 16
                set_free(("lnstg",), (c_sy, c_sy.n))

    def emit_ffn(app):
        fs = ExitStack()
        fe = fs.enter_context
        tag = f"f{app}"
        f = {"tag": tag}
        f["p"] = fe(nc.sbuf_tensor(tag + "p", [128, HC, 512], BF16, **L))
        f["xq"] = fe(nc.sbuf_tensor(tag + "xq", [128, DC, T], BF16, **L))
        f["B"] = fe(nc.sbuf_tensor(tag + "B", [128, D], F32, **L))
        f["lnstg"] = fe(nc.sbuf_tensor(tag + "lnstg", [128, DC, 128], BF16, **L))
        f["wg"] = [fe(nc.sbuf_tensor(f"{tag}wg{b}", [128, DC, 128], BF16, **L))
                   for b in range(2)]
        f["wu"] = [fe(nc.sbuf_tensor(f"{tag}wu{b}", [128, DC, 128], BF16, **L))
                   for b in range(2)]
        f["g"] = fe(nc.sbuf_tensor(tag + "g", [128, 512], BF16, **L))
        f["u"] = fe(nc.sbuf_tensor(tag + "u", [128, 512], BF16, **L))
        f["ct"] = fe(nc.sbuf_tensor(tag + "ct", [128, 512], F32, **L))
        f["x1g"] = fe(nc.sbuf_tensor(tag + "x1g", [128, 4, D], F32, **L))

        src = x_in if app == 1 else x2_scr
        dst = x1_scr if app == 1 else out_t
        if DEBUG_SCR and app == 1:
            xq_dbg = scr("xq_dbg", (128, DC, T), BF16)
            st_dbg = scr("st_dbg", (128, 32))
            id_dbg = scr("id_dbg", (128, 128))
        for b in range(2):
            set_free(("gu", b), (c_v, c_v.n), (c_t, c_t.n))
            set_free(("wd", b), (c_v, c_v.n), (c_t, c_t.n))

        def exchange_send():
            @blockq.sync
            def _(sy):
                for cc_i in (ccA_i, ccB_i):
                    for sl, blk in enumerate((0, 1, 6, 7)):
                        sy.dma_start(out=cc_i[:, :, sl * 128:(sl + 1) * 128],
                                     in_=ln1_scr[blk]).then_inc(c_sy.sem, 16)
                sy.dma_start(out=ccC_i, in_=ln1_scr[0][:, :, 0:16]
                             ).then_inc(c_sy.sem, 16)
            c_sy.n += 144
            stg = c_sy.n

            def trig_hook():
                drained = c_sy.n

                @blockq.gpsimd
                def _(g, stg=stg, drained=drained):
                    g.wait_ge(c_sy.sem, max(stg, drained))
                    g.collective_compute(
                        "AllGather", ALU.bypass,
                        replica_groups=[[0, 1], [2, 3], [4, 5], [6, 7]],
                        ins=[ccA_i.opt()], outs=[ccA_o.opt()]).then_inc(c_cc.sem, 1)
                    g.collective_compute(
                        "AllGather", ALU.bypass,
                        replica_groups=[[0, 3], [1, 2], [4, 7], [5, 6]],
                        ins=[ccB_i.opt()], outs=[ccB_o.opt()]).then_inc(c_cc.sem, 1)
                    g.collective_compute(
                        "AllGather", ALU.bypass,
                        replica_groups=[[0, 1, 2, 3], [4, 5, 6, 7]],
                        ins=[ccC_i.opt()], outs=[ccC_o.opt()]).then_inc(c_cc.sem, 1)
                c_cc.n += 3
            return trig_hook

        xq_rdy = emit_entry(f, src)
        if DEBUG_SCR and app == 1:
            @blockq.sync
            def _(sy):
                sy.wait_ge(c_s.sem, xq_rdy)
                sy.dma_start(out=xq_dbg, in_=f["xq"][:]).then_inc(c_sy.sem, 16)
                sy.dma_start(out=st_dbg, in_=stats[:]).then_inc(c_sy.sem, 16)
                sy.dma_start(out=id_dbg, in_=ident[:]).then_inc(c_sy.sem, 16)
            c_sy.n += 48

        sub = ExitStack()
        f["wd"] = [sub.enter_context(
            nc.sbuf_tensor(f"{tag}wd{b}", [128, HC, 128], BF16, **L))
            for b in range(2)]
        trig = [None]
        for grp in (0, 1):
            first = (app == 1 and grp == 0)
            p_rdy = emit_stage12(f, grp, first, xq_rdy)
            emit_stage3(f, app, grp, first, src_res=src, dst_res=dst, p_rdy=p_rdy)
            if app == 1 and grp == 0 and stop_after != "ffn1":
                trig[0] = exchange_send()
        if trig[0] is not None:
            trig[0]()
        sub.close()
        fs.close()

    emit_ffn(1)
    if stop_after == "ffn1":
        finish()
        flush()
        return nc, st

    oscf = ExitStack()
    osc = oscf.enter_context(nc.sbuf_tensor("osc", [128, NH, T], BF16, **L))
    oLg = oscf.enter_context(nc.sbuf_tensor("oLg", [128, DC * GK], F32, **L))
    lgf = oscf.enter_context(nc.sbuf_tensor("lgf", [1, NH * GK], F32, **L))
    gmrg = oscf.enter_context(nc.sbuf_tensor("gmrg", [128, 4, 512], F32, **L))
    lnf = ExitStack()
    ln1T = lnf.enter_context(nc.sbuf_tensor("ln1T", [128, DC, KW], BF16, **L))

    @blockq.vector
    def _(v):
        v.memset(ln1T[:, :, 1552:1664], 0.0).then_inc(c_v.sem, 1)
    c_v.n += 1

    @blockq.sync
    def _(sy):
        for tt in range(NT):
            sy.dma_start(out=ln1T[:, :, OWN0 + tt * 128:OWN0 + (tt + 1) * 128],
                         in_=ln1_scr[tt]).then_inc(c_sy.sem, 16)
    c_sy.n += NT * 16
    ln1_own = c_sy.n

    exs = ExitStack()
    xe = exs.enter_context
    exO = xe(nc.sbuf_tensor("exO", [128, 2, DC, 2 * HALO], BF16, **L))
    exO2 = xe(nc.sbuf_tensor("exO2", [128, 2, DC, 2 * HALO], BF16, **L))
    exC = xe(nc.sbuf_tensor("exC", [128, 4, DC, GK], BF16, **L))

    _exccw = c_cc.n

    @blockq.gpsimd
    def _(g, ccw=_exccw):
        g.wait_ge(c_cc.sem, ccw)
        g.wait_ge(c_sy.sem, ln1_own)
        for sl in range(2):
            g.dma_start(out=exO[:, sl], in_=ccA_o[sl]).then_inc(c_gp.sem, 16)
            g.dma_start(out=exO2[:, sl], in_=ccB_o[sl]).then_inc(c_gp.sem, 16)
        g.dma_start(out=exC[:, 0], in_=ccC_o[0]).then_inc(c_gp.sem, 16)
    c_gp.n += 80
    exld = c_gp.n

    @blockq.vector
    def _(v):
        v.wait_ge(c_gp.sem, exld)
        v.wait_ge(c_sy.sem, ln1_own)
        hl = ln1T[:, :, 0:HALO]
        v.tensor_scalar_mul(hl, exO[:, 0, :, HALO:], hsel[:, 0:1])
        v.scalar_tensor_tensor(hl, exO[:, 1, :, HALO:], hsel[:, 1:2], hl, ALU.mult, ALU.add)
        v.scalar_tensor_tensor(hl, exO2[:, 0, :, HALO:], hsel[:, 2:3], hl, ALU.mult, ALU.add)
        v.scalar_tensor_tensor(hl, exO2[:, 1, :, HALO:], hsel[:, 3:4], hl, ALU.mult, ALU.add)
        hr = ln1T[:, :, OWN0 + T:OWN0 + T + HALO]
        v.tensor_scalar_mul(hr, exO[:, 0, :, 0:HALO], hsel[:, 4:5])
        v.scalar_tensor_tensor(hr, exO[:, 1, :, 0:HALO], hsel[:, 5:6], hr, ALU.mult, ALU.add)
        v.scalar_tensor_tensor(hr, exO2[:, 0, :, 0:HALO], hsel[:, 6:7], hr, ALU.mult, ALU.add)
        v.scalar_tensor_tensor(hr, exO2[:, 1, :, 0:HALO], hsel[:, 7:8], hr, ALU.mult, ALU.add)
        v.tensor_copy(ln1T[:, :, GCOL:GCOL + GK], exC[:, 0]).then_inc(c_v.sem, 1)
    c_v.n += 1
    halo_done = c_v.n

    exs.close()
    if stop_after == "exchange":
        finish()
        flush()
        return nc, st

    ast = ExitStack()
    ae = ast.enter_context
    kw_sb = ae(nc.sbuf_tensor("kw_sb", [128, 4, KW], BF16, **L))
    vw_sb = ae(nc.sbuf_tensor("vw_sb", [128, KT, 512], BF16, **L))
    qw_sb = ae(nc.sbuf_tensor("qw_sb", [128, 4, T], BF16, **L))
    qg_sb = ae(nc.sbuf_tensor("qg_sb", [128, 4, 128], BF16, **L))
    wat = [ae(nc.sbuf_tensor(f"wat{b}", [128, DC, 512], BF16, **L))
           for b in range(2)]
    pT_sb = ae(nc.sbuf_tensor("pT_sb", [128, 8, 128], BF16, **L))
    lr_sb = ae(nc.sbuf_tensor("lr_sb", [1, 128], F32, **L))
    lrep_sb = ae(nc.sbuf_tensor("lrep_sb", [128, 128], F32, **L))

    @blockq.vector
    def _(v):
        v.memset(qg_sb[:], 0.0).then_inc(c_v.sem, 1)
    c_v.n += 1
    qg_zero = c_v.n

    def proj_wave(w, which, wbuf, wdma):
        if which == "k":
            for oc in range(4):
                for ci, ts in enumerate(range(0, KW, 512)):
                    tn = min(512, KW - ts)
                    bk = 4 + (oc * 4 + ci) % 2
                    pb = psum_f[bk]
                    _fpk = wf(("bank", bk))

                    @blockq.tensor
                    def _(t, oc=oc, ts=ts, tn=tn, pb=pb, bk=bk, _fpk=_fpk):
                        t.wait_ge(c_gp.sem, wdma)
                        t.wait_ge(c_v.sem, halo_done)
                        wdeps(t, _fpk)
                        for dc in range(DC):
                            mm = t.matmul(pb[:, 0:tn],
                                          wbuf[:, dc, oc * 128:(oc + 1) * 128],
                                          ln1T[:, dc, ts:ts + tn],
                                          start=(dc == 0), stop=(dc == DC - 1))
                        mm.then_inc(c_t.sem, 1)
                    c_t.n += 1
                    mmd = c_t.n

                    @blockq.scalar
                    def _(s, oc=oc, ts=ts, tn=tn, pb=pb, mmd=mmd):
                        s.wait_ge(c_t.sem, mmd)
                        s.copy(out=kw_sb[:, oc, ts:ts + tn],
                               in_=pb[:, 0:tn]).then_inc(c_s.sem, 1)
                    c_s.n += 1
                    set_free(("bank", bk), (c_s, c_s.n))
        elif which == "v":
            for tt in range(KT):
                bk = 4 + tt % 2
                pb = psum_f[bk]
                _fpv = wf(("bank", bk))

                @blockq.tensor
                def _(t, tt=tt, pb=pb, bk=bk, _fpv=_fpv):
                    t.wait_ge(c_gp.sem, wdma)
                    t.wait_ge(c_v.sem, halo_done)
                    wdeps(t, _fpv)
                    for dc in range(DC):
                        mm = t.matmul(pb[:], ln1T[:, dc, tt * 128:(tt + 1) * 128],
                                      wbuf[:, dc, :],
                                      start=(dc == 0), stop=(dc == DC - 1))
                    mm.then_inc(c_t.sem, 1)
                c_t.n += 1
                mmd = c_t.n

                @blockq.scalar
                def _(s, tt=tt, pb=pb, mmd=mmd):
                    s.wait_ge(c_t.sem, mmd)
                    s.copy(out=vw_sb[:, tt, :], in_=pb[:]).then_inc(c_s.sem, 1)
                c_s.n += 1
                set_free(("bank", bk), (c_s, c_s.n))
        else:
            for oc in range(4):
                for th in range(3):
                    bk = 4 + th % 2
                    pb = psum_f[bk]
                    _fpq = wf(("bank", bk))

                    @blockq.tensor
                    def _(t, oc=oc, th=th, pb=pb, bk=bk, _fpq=_fpq):
                        t.wait_ge(c_gp.sem, wdma)
                        if th == 2:
                            t.wait_ge(c_v.sem, qg_zero)
                        wdeps(t, _fpq)
                        for dc in range(DC):
                            if th < 2:
                                rhs = ln1T[:, dc, OWN0 + th * 512:OWN0 + th * 512 + 512]
                                dst = pb[:]
                            else:
                                rhs = ln1T[:, dc, GCOL:GCOL + GK]
                                dst = pb[:, 0:GK]
                            mm = t.matmul(dst, wbuf[:, dc, oc * 128:(oc + 1) * 128],
                                          rhs, start=(dc == 0), stop=(dc == DC - 1))
                        mm.then_inc(c_t.sem, 1)
                    c_t.n += 1
                    mmd = c_t.n

                    @blockq.scalar
                    def _(s, oc=oc, th=th, pb=pb, mmd=mmd):
                        s.wait_ge(c_t.sem, mmd)
                        if th < 2:
                            s.copy(out=qw_sb[:, oc, th * 512:th * 512 + 512],
                                   in_=pb[:]).then_inc(c_s.sem, 1)
                        else:
                            s.copy(out=qg_sb[:, oc, 0:GK],
                                   in_=pb[:, 0:GK]).then_inc(c_s.sem, 1)
                    c_s.n += 1
                    set_free(("bank", bk), (c_s, c_s.n))

    def attend(w, hh, qt):
        glob = (qt == 8)
        nkt = 8 if glob else 6
        po, pl = psum_f[2], psum_f[3]
        h = w * 4 + hh

        expd = [0] * nkt
        for i in range(nkt):
            pb = psum_f[i % 2]
            _fsb = wf(("bank", i % 2)) if i < 2 else None

            @blockq.tensor
            def _(t, i=i, pb=pb, _fsb=_fsb):
                if i >= 2:
                    d = expd[i - 2]
                    if isinstance(d, tuple):
                        t.wait_ge(c_v.sem, d[1])
                    else:
                        t.wait_ge(c_s.sem, d)
                else:
                    wdeps(t, _fsb)
                if glob:
                    kcol, nk = OWN0 + i * 128, 128
                    rhs = qg_sb[:, hh, :]
                else:
                    kcol = GCOL if i == 5 else 128 * qt + 128 * i
                    nk = GK if i == 5 else 128
                    rhs = qw_sb[:, hh, qt * 128:(qt + 1) * 128]
                mm = t.matmul(pb[0:nk, 0:128], kw_sb[:, hh, kcol:kcol + nk],
                              rhs, start=True, stop=True)
                mm.then_inc(c_t.sem, 1)
            c_t.n += 1
            smm = c_t.n

            @blockq.scalar
            def _(s, i=i, pb=pb, smm=smm):
                s.wait_ge(c_t.sem, smm)
                nk = GK if (not glob and i == 5) else 128
                bias = neg16[0:nk, :] if glob else ebias[0:nk, qt, i:i + 1]
                s.activation(out=pT_sb[0:nk, i, :], in_=pb[0:nk, 0:128],
                             func=AF.Exp, bias=bias, scale=SCL).then_inc(c_s.sem, 1)
            c_s.n += 1
            expd[i] = c_s.n
            set_free(("bank", i % 2), (c_s, c_s.n))

            if (not glob) and i in (0, 4):
                @blockq.vector
                def _(v, i=i, ed=expd[i]):
                    v.wait_ge(c_s.sem, ed)
                    v.wait_ge(c_sy.sem, consts_ld)
                    v.tensor_mul(pT_sb[:, i, :], pT_sb[:, i, :],
                                 m01[:, qt, (0 if i == 0 else 1), :]
                                 ).then_inc(c_v.sem, 1)
                c_v.n += 1
                expd[i] = ("v", c_v.n)

        _fol = wf(("bank", 2), ("bank", 3))
        for i in range(nkt):
            dep = expd[i]

            @blockq.tensor
            def _(t, i=i, dep=dep, _fol=_fol):
                if isinstance(dep, tuple):
                    t.wait_ge(c_v.sem, dep[1])
                else:
                    t.wait_ge(c_s.sem, dep)
                if i == 0:
                    wdeps(t, _fol)
                if glob:
                    nk, vt = 128, 2 + i
                else:
                    nk = GK if i == 5 else 128
                    vt = 12 if i == 5 else qt + i
                mm = t.matmul(pl[0:1, 0:128], ones_col[0:nk, :],
                              pT_sb[0:nk, i, :], start=(i == 0), stop=(i == nkt - 1))
                mm.then_inc(c_t.sem, 1)
                mm = t.matmul(po[:, 0:128], vw_sb[0:nk, vt, hh * 128:hh * 128 + 128],
                              pT_sb[0:nk, i, :], start=(i == 0), stop=(i == nkt - 1))
                mm.then_inc(c_t.sem, 1)
            c_t.n += 2
        avd = c_t.n

        if glob:
            @blockq.vector
            def _(v, h=h):
                v.wait_ge(c_t.sem, avd)
                v.tensor_copy(oLg[:, h * GK:(h + 1) * GK], po[:, 0:GK])
                v.tensor_copy(lgf[0:1, h * GK:(h + 1) * GK],
                              pl[0:1, 0:GK]).then_inc(c_v.sem, 1)
            c_v.n += 1
            set_free(("bank", 2), (c_v, c_v.n))
            set_free(("bank", 3), (c_v, c_v.n))
            return c_v.n

        @blockq.vector
        def _(v):
            v.wait_ge(c_t.sem, avd)
            v.reciprocal(lr_sb[:], pl[0:1, 0:128]).then_inc(c_v.sem, 1)
        c_v.n += 1
        lrd = c_v.n
        set_free(("bank", 3), (c_v, lrd))

        _frep = wf(("bank", 4), ("lrep",))

        @blockq.tensor
        def _(t, _frep=_frep):
            t.wait_ge(c_v.sem, lrd)
            wdeps(t, _frep)
            t.matmul(psum_f[4][:, 0:128], ones_row[:], lr_sb[:],
                     start=True, stop=True).then_inc(c_t.sem, 1)
        c_t.n += 1
        repd = c_t.n

        lcp = c_v.n + 1

        @blockq.vector
        def _(v, h=h, lcp=lcp):
            v.wait_ge(c_t.sem, repd)
            v.tensor_copy(lrep_sb[:], psum_f[4][:, 0:128]).then_inc(c_v.sem, 1)
            v.wait_ge(c_v.sem, lcp)
            v.tensor_mul(osc[:, h, qt * 128:(qt + 1) * 128], po[:, 0:128],
                         lrep_sb[:]).then_inc(c_v.sem, 1)
        c_v.n += 2
        repc = c_v.n
        set_free(("bank", 4), (c_v, repc))
        set_free(("bank", 2), (c_v, c_v.n))
        set_free(("lrep",), (c_v, c_v.n))
        return c_v.n

    for b in range(2):
        set_free(("wat", b), (c_v, halo_done), (c_t, c_t.n))
    for w in range(4):
        for which, win, b in (("k", wk_in, 0), ("v", wv_in, 1), ("q", wq_in, 0)):
            _fwat = wf(("wat", b))

            @blockq.gpsimd
            def _(g, w=w, win=win, b=b, _fwat=_fwat):
                wdeps(g, _fwat)
                g.dma_start(out=wat[b][:], in_=win[w]).then_inc(c_gp.sem, 16)
            c_gp.n += 16
            proj_wave(w, which, wat[b], c_gp.n)
            set_free(("wat", b), (c_t, c_t.n))
        for hh in range(4):
            for qt in range(8):
                attend(w, hh, qt)
            attend(w, hh, 8)

    ast.close()
    lnf.close()
    attn_done_v = c_v.n
    if stop_after == "attn":
        finish()
        flush()
        return nc, st

    _gmidle = c_gp.n

    @blockq.vector
    def _(v, gmidle=_gmidle):
        v.wait_ge(c_gp.sem, gmidle)
        v.memset(gmrg[:, 0, 256:512], 0.0).then_inc(c_v.sem, 1)
    c_v.n += 1
    zpad = c_v.n

    _mgbase = c_gp.n
    _mgccw = c_cc.n + 1

    @blockq.gpsimd
    def _(g, base=_mgbase, ccw=_mgccw):
        g.wait_ge(c_v.sem, attn_done_v)
        g.wait_ge(c_v.sem, zpad)
        g.dma_start(out=ccD_i[0:128, 0:256], in_=oLg[:]).then_inc(c_gp.sem, 16)
        g.dma_start(out=ccD_i[0:128, 256:512],
                    in_=gmrg[:, 0, 256:512]).then_inc(c_gp.sem, 16)
        g.dma_start(out=ccD_i[0:1, 256:512], in_=lgf[:]).then_inc(c_gp.sem, 16)
        g.wait_ge(c_gp.sem, base + 48)
        g.collective_compute(
            "AllGather", ALU.bypass,
            replica_groups=[[0, 1, 2, 3], [4, 5, 6, 7]],
            ins=[ccD_i.opt()], outs=[ccD_o.opt()]).then_inc(c_cc.sem, 1)
        g.wait_ge(c_cc.sem, ccw)
        for sl in range(4):
            g.dma_start(out=gmrg[:, sl], in_=ccD_o[sl]).then_inc(c_gp.sem, 16)
    c_gp.n += 112
    c_cc.n += 1
    mrg_ld = c_gp.n

    def f(v):
        v.wait_ge(c_gp.sem, mrg_ld)
        v.tensor_add(gmrg[:, 0, :], gmrg[:, 0, :], gmrg[:, 1, :])
        return v.tensor_add(gmrg[:, 2, :], gmrg[:, 2, :], gmrg[:, 3, :])
    m1 = vstep(f)

    def f(v):
        v.wait_ge(c_v.sem, m1)
        return v.tensor_add(gmrg[:, 0, :], gmrg[:, 0, :], gmrg[:, 2, :])
    m2 = vstep(f)

    def f(v):
        v.wait_ge(c_v.sem, m2)
        return v.reciprocal(gmrg[0:1, 1, 256:512], gmrg[0:1, 0, 256:512])
    mrgd = vstep(f)

    gsc_dep = {}
    for hq in range(NH):
        bk = 4 + hq % 2
        pb = psum_f[bk]

        _fg5 = wf(("bank", bk))

        @blockq.tensor
        def _(t, hq=hq, pb=pb, bk=bk, _fg5=_fg5):
            t.wait_ge(c_v.sem, mrgd)
            wdeps(t, _fg5)
            t.matmul(pb[:, 0:GK], ones_row[:],
                     gmrg[0:1, 1, 256 + hq * GK:256 + (hq + 1) * GK],
                     start=True, stop=True).then_inc(c_t.sem, 1)
        c_t.n += 1
        rpd = c_t.n

        @blockq.vector
        def _(v, hq=hq, pb=pb, rpd=rpd):
            v.wait_ge(c_t.sem, rpd)
            v.tensor_mul(gsc[:, hq, :], gmrg[:, 0, hq * GK:(hq + 1) * GK],
                         pb[:, 0:GK]).then_inc(c_v.sem, 1)
        c_v.n += 1
        gsc_dep[hq] = c_v.n
        set_free(("bank", bk), (c_v, c_v.n))
    gsc_all = c_v.n

    p6 = ExitStack()
    p6e = p6.enter_context
    x2t_sb = p6e(nc.sbuf_tensor("x2t_sb", [128, NT, D], F32, **L))
    x1t_sb = p6e(nc.sbuf_tensor("x1t_sb", [128, NT, D], F32, **L))
    wo_sb = [p6e(nc.sbuf_tensor(f"wo_sb{b}", [128, DC, 128], BF16, **L))
             for b in range(2)]
    cta_sb = p6e(nc.sbuf_tensor("cta_sb", [128, T], F32, **L))
    gtmp_sb = p6e(nc.sbuf_tensor("gtmp_sb", [128, GK], F32, **L))

    for b in range(2):
        set_free(("wo", b), (c_v, gsc_all), (c_t, c_t.n))
    x1_ld = [0] * NT
    for tt in range(NT):
        @blockq.sync
        def _(sy, tt=tt):
            sy.wait_ge(c_v.sem, gsc_all)
            sy.dma_start(out=x1t_sb[:, tt, :],
                         in_=x1_scr[tt * 128:(tt + 1) * 128, :]).then_inc(c_sy.sem, 16)
        c_sy.n += 16
        x1_ld[tt] = c_sy.n

    aoev = {}
    x2d = {}
    bldd = {}
    for ddc in range(DC):
        b = ddc % 2

        _fwo = wf(("wo", b))

        @blockq.gpsimd
        def _(g, ddc=ddc, b=b, _fwo=_fwo):
            wdeps(g, _fwo)
            g.dma_start(out=wo_sb[b][:], in_=wo_in[ddc]).then_inc(c_gp.sem, 16)
        c_gp.n += 16
        wld = c_gp.n

        pa, pb2, pg16 = psum_f[0 + b], psum_f[2 + b], psum_f[4]

        _fp6 = (wf(("bank", 0 + b), ("bank", 2 + b)) if ddc < 2 else []) + \
               (wf(("bank", 4)) if ddc == 0 else [])

        @blockq.tensor
        def _(t, ddc=ddc, b=b, wld=wld, pa=pa, pb2=pb2, pg16=pg16, _fp6=_fp6):
            t.wait_ge(c_gp.sem, wld)
            t.wait_ge(c_v.sem, gsc_all)
            wdeps(t, _fp6)
            t.wait_ge(c_s.sem, aoev.get(ddc - 2, 0))
            t.wait_ge(c_v.sem, bldd.get(ddc - 1, 0))
            for th in range(2):
                dst = pa if th == 0 else pb2
                for dvc in range(DC):
                    mm = t.matmul(dst[:], wo_sb[b][:, dvc, :],
                                  osc[:, dvc, th * 512:th * 512 + 512],
                                  start=(dvc == 0), stop=(dvc == DC - 1))
                mm.then_inc(c_t.sem, 1)
            for dvc in range(DC):
                mm = t.matmul(pg16[:, 0:GK], wo_sb[b][:, dvc, :], gsc[:, dvc, :],
                              start=(dvc == 0), stop=(dvc == DC - 1))
            mm.then_inc(c_t.sem, 1)
        c_t.n += 3
        mmd = c_t.n
        set_free(("wo", b), (c_t, mmd))

        @blockq.scalar
        def _(s, pa=pa, pb2=pb2, mmd=mmd):
            s.wait_ge(c_t.sem, mmd - 1)
            s.copy(out=cta_sb[:, 0:512], in_=pa[:]).then_inc(c_s.sem, 1)
            s.copy(out=cta_sb[:, 512:1024], in_=pb2[:]).then_inc(c_s.sem, 1)
        c_s.n += 2
        ctd = c_s.n
        aoev[ddc] = c_s.n

        def f(v, ddc=ddc, pg16=pg16, ctd=ctd, mmd=mmd):
            v.wait_ge(c_s.sem, ctd)
            v.wait_ge(c_t.sem, mmd)
            return v.tensor_sub(gtmp_sb[:], pg16[:, 0:GK], cta_sb[:, 0:GK])
        b1 = vstep(f)

        def f(v, b1=b1):
            v.wait_ge(c_v.sem, b1)
            return v.tensor_mul(gtmp_sb[:], gtmp_sb[:], gbl[:])
        b2 = vstep(f)

        def f(v, b2=b2):
            v.wait_ge(c_v.sem, b2)
            return v.tensor_add(cta_sb[:, 0:GK], cta_sb[:, 0:GK], gtmp_sb[:])
        bldd[ddc] = vstep(f)
        bld = bldd[ddc]

        for tt in range(NT):
            pbt = psum_f[6 + tt % 2]
            _fx67 = wf(("bank", 6 + tt % 2)) if (ddc == 0 and tt < 2) else None

            @blockq.tensor
            def _(t, ddc=ddc, tt=tt, pbt=pbt, bld=bld, _fx67=_fx67):
                t.wait_ge(c_v.sem, bld)
                t.wait_ge(c_v.sem, x2d.get((ddc - 1, tt), 0))
                t.wait_ge(c_v.sem, x2d.get((ddc, tt - 2), 0))
                if _fx67 is not None:
                    wdeps(t, _fx67)
                t.transpose(pbt[:, 0:128], cta_sb[:, tt * 128:(tt + 1) * 128],
                            ident[:]).then_inc(c_t.sem, 1)
            c_t.n += 1
            trd = c_t.n

            @blockq.vector
            def _(v, ddc=ddc, tt=tt, pbt=pbt, trd=trd):
                v.wait_ge(c_t.sem, trd)
                v.wait_ge(c_sy.sem, x1_ld[tt])
                v.tensor_add(x2t_sb[:, tt, ddc * 128:(ddc + 1) * 128],
                             pbt[:, 0:128],
                             x1t_sb[:, tt, ddc * 128:(ddc + 1) * 128]
                             ).then_inc(c_v.sem, 1)
            c_v.n += 1
            x2d[(ddc, tt)] = c_v.n
    set_free(("bank", 6), (c_v, x2d[(DC - 1, NT - 2)]))
    set_free(("bank", 7), (c_v, x2d[(DC - 1, NT - 1)]))
    set_free(("bank", 0), (c_s, aoev[DC - 2]))
    set_free(("bank", 1), (c_s, aoev[DC - 1]))
    set_free(("bank", 2), (c_s, aoev[DC - 2]))
    set_free(("bank", 3), (c_s, aoev[DC - 1]))
    set_free(("bank", 4), (c_v, bldd[DC - 1]))

    for tt in range(NT):
        done = x2d[(DC - 1, tt)]

        @blockq.sync
        def _(sy, tt=tt, done=done):
            sy.wait_ge(c_v.sem, done)
            sy.dma_start(out=x2_scr[tt * 128:(tt + 1) * 128, :],
                         in_=x2t_sb[:, tt, :]).then_inc(c_sy.sem, 16)
        c_sy.n += 16

    p6.close()
    oscf.close()
    if stop_after == "p6":
        finish()
        flush()
        return nc, st

    emit_ffn(2)

    finish()
    flush()
    return nc, st


def _host_masks(core):
    S0 = 1024 * (core % 4)
    m01 = np.zeros((8, 2, 128, 128), np.float32)
    eb = np.zeros((8, 6, 128), np.float32)
    for qt in range(8):
        for ii, i in enumerate((0, 4)):
            kb = 128 * qt + 128 * i + np.arange(128)[:, None]
            kg = S0 - 256 + kb
            qg = S0 + 128 * qt + np.arange(128)[None, :]
            ok = (kg >= 0) & (kg < S) & ((np.abs(qg - kg) < 256) | (kg < GK))
            m01[qt, ii] = np.where(ok, 1.0, 0.0)
        for i in range(5):
            kg0 = S0 - 256 + 128 * qt + 128 * i
            valid = (kg0 >= 0) and (kg0 + 127 < S)
            eb[qt, i, :] = -STAB if valid else MSK
        eb[qt, 5, :] = MSK
        dbl = (S0 == 0) and (qt <= 2)
        eb[qt, 5, 0:GK] = MSK if dbl else -STAB
    c4 = core % 4
    hs = np.zeros(12, np.float32)
    if c4 % 2 == 0:
        hs[9] = 1.0
    else:
        hs[8] = 1.0
    if c4 == 1:
        hs[11] = 1.0
    else:
        hs[10] = 1.0
    if c4 == 1:
        hs[0] = 1.0
    elif c4 == 2:
        hs[2] = 1.0
    elif c4 == 3:
        hs[0] = 1.0
    if c4 == 0:
        hs[5] = 1.0
    elif c4 == 1:
        hs[7] = 1.0
    elif c4 == 2:
        hs[5] = 1.0
    hsel = np.broadcast_to(hs, (128, 12)).copy()
    gbl = np.full((128, GK), 1.0 if c4 == 0 else 0.0, np.float32)
    return m01, eb, hsel, gbl


def _prep_inputs(x, Wq, Wk, Wv, Wo, W_g, W_u, W_d):
    import ml_dtypes
    x = np.ascontiguousarray(np.asarray(x, np.float32))
    Wq, Wk, Wv, Wo, W_g, W_u, W_d = (
        np.asarray(w, np.float32) for w in (Wq, Wk, Wv, Wo, W_g, W_u, W_d))

    wg = np.ascontiguousarray(W_g.reshape(HC, 128, DC, 128).transpose(0, 3, 2, 1))
    wu = np.ascontiguousarray(W_u.reshape(HC, 128, DC, 128).transpose(0, 3, 2, 1))
    wd = np.ascontiguousarray(W_d.reshape(DC, 128, HC, 128).transpose(0, 3, 2, 1))
    wq = np.ascontiguousarray(Wq.reshape(DC, 128, 4, 512).transpose(2, 1, 0, 3))
    wk = np.ascontiguousarray(Wk.reshape(DC, 128, 4, 512).transpose(2, 1, 0, 3))
    wv = np.ascontiguousarray(Wv.reshape(DC, 128, 4, 512).transpose(2, 1, 0, 3))
    wo = np.ascontiguousarray(Wo.reshape(DC, 128, DC, 128).transpose(2, 1, 0, 3))
    ident = np.eye(128, dtype=np.float32)

    in_maps = []
    for core in range(8):
        b, c4 = core // 4, core % 4
        m01, eb, hsel, gbl = _host_masks(core)
        in_maps.append({
            "x": np.ascontiguousarray(x[b, c4 * T:(c4 + 1) * T, :]),
            "wg": wg, "wu": wu, "wd": wd,
            "wq": wq, "wk": wk, "wv": wv, "wo": wo,
            "m01": np.ascontiguousarray(
                m01.transpose(2, 0, 1, 3)).astype(ml_dtypes.bfloat16),
            "eb": np.ascontiguousarray(eb.transpose(2, 0, 1)),
            "hsel": hsel, "gbl": gbl, "ident": ident,
        })
    return in_maps


def kernel(L0_attn_hidden_states, L0_attn_Wq, L0_attn_Wk, L0_attn_Wv,
           L0_attn_Wo, L0_ffn_W_g, L0_ffn_W_u, L0_ffn_W_d):
    in_maps = _prep_inputs(L0_attn_hidden_states, L0_attn_Wq, L0_attn_Wk,
                           L0_attn_Wv, L0_attn_Wo, L0_ffn_W_g, L0_ffn_W_u,
                           L0_ffn_W_d)
    nc, st = build(STOP_AFTER)
    st.close()
    res = run_bass_kernel_spmd(nc, in_maps, core_ids=list(range(8)))
    out = np.empty((2, S, D), np.float32)
    for core in range(8):
        b, c4 = core // 4, core % 4
        out[b, c4 * T:(c4 + 1) * T, :] = res.results[core]["out"]
    return out


# version 14
# speedup vs baseline: 1.3375x; 1.0099x over previous
import numpy as np
from contextlib import ExitStack

import concourse.bass as bass
import concourse.mybir as mybir
from concourse.bass_utils import run_bass_kernel_spmd

try:
    import jax
    jax.config.update("jax_compilation_cache_dir", "/tmp/jaxcache")
    jax.config.update("jax_persistent_cache_min_entry_size_bytes", -1)
    jax.config.update("jax_persistent_cache_min_compile_time_secs", 0.0)
except Exception:
    pass

F32 = mybir.dt.float32
BF16 = mybir.dt.bfloat16
AF = mybir.ActivationFunctionType
ALU = mybir.AluOpType
AXX = mybir.AxisListType.X

T = 1024
NT = 8
D = 2048
DC = 16
H = 8192
HC = 64
NH = 16
S = 4096
HALO = 256
GK = 16
KW = 1664
KT = 13
OWN0 = 256
GCOL = 1536
STAB = 16.0
MSK = -1.0e30
EPS1 = 1e-5
EPS2 = 1e-8
SCL = float(1.0 / np.sqrt(np.float32(128.0)))
INV_D = 1.0 / 2048.0
BIGS = 1.0e38

TMAP = (0, 1, 6, 7, 2, 3, 4, 5)

STOP_AFTER = None
DEBUG_SCR = False


def build(stop_after=None):
    nc = bass.Bass("TRN2", target_bir_lowering=False, detect_race_conditions=False)
    st = ExitStack()
    e = st.enter_context

    def inp(name, shape, dt=F32):
        return nc.dram_tensor(name, list(shape), dt, kind="ExternalInput").ap()

    x_in = inp("x", (T, D))
    wg_in = inp("wg", (HC, 128, DC, 128))
    wu_in = inp("wu", (HC, 128, DC, 128))
    wd_in = inp("wd", (DC, 128, HC, 128))
    wq_in = inp("wq", (4, 128, DC, 512))
    wk_in = inp("wk", (4, 128, DC, 512))
    wv_in = inp("wv", (4, 128, DC, 512))
    wo_in = inp("wo", (DC, 128, DC, 128))
    m01_in = inp("m01", (128, 8, 2, 128), BF16)
    eb_in = inp("eb", (128, 8, 6))
    hsel_in = inp("hsel", (128, 12))
    gbl_in = inp("gbl", (128, 16))
    id_in = inp("ident", (128, 128))

    out_t = nc.dram_tensor("out", [T, D], F32, kind="ExternalOutput").ap()

    def scr(name, shape, dt=F32):
        kind = ("ExternalOutput" if DEBUG_SCR and not name.startswith("cc")
                else "Internal")
        return nc.dram_tensor(name, list(shape), dt, kind=kind).ap()

    x1_scr = scr("x1_scr", (T, D))
    x2_scr = scr("x2_scr", (T, D))
    ln1_scr = scr("ln1_scr", (NT, 128, DC, 128), BF16)
    sgn_g = scr("sgn_g", (HC, 128, DC, 128), BF16)
    sgn_u = scr("sgn_u", (HC, 128, DC, 128), BF16)
    sgn_d = scr("sgn_d", (DC, 128, HC, 128), BF16)
    ccA_i = scr("ccA_i", (128, DC, 2 * HALO), BF16)
    ccA_o = scr("ccA_o", (2, 128, DC, 2 * HALO), BF16)
    ccB_i = scr("ccB_i", (128, DC, 2 * HALO), BF16)
    ccB_o = scr("ccB_o", (2, 128, DC, 2 * HALO), BF16)
    ccC_i = scr("ccC_i", (128, DC, GK), BF16)
    ccC_o = scr("ccC_o", (4, 128, DC, GK), BF16)
    ccD_i = scr("ccD_i", (128, 512))
    ccD_o = scr("ccD_o", (4, 128, 512))

    class Ctr:
        def __init__(self, name):
            self.sem = e(nc.semaphore(name))
            self.n = 0

    c_sy = Ctr("c_sy")
    c_gp = Ctr("c_gp")
    c_sq = Ctr("c_sq")
    c_t = Ctr("c_t")
    c_v = Ctr("c_v")
    c_s = Ctr("c_s")
    c_cc = Ctr("c_cc")

    R = dict(side="right")
    ident = e(nc.sbuf_tensor("identS", [128, 128], F32, **R))
    m01 = e(nc.sbuf_tensor("m01S", [128, 8, 2, 128], BF16, **R))
    ebias = e(nc.sbuf_tensor("ebiasS", [128, 8, 6], F32, **R))
    hsel = e(nc.sbuf_tensor("hselS", [128, 12], F32, **R))
    gbl = e(nc.sbuf_tensor("gblS", [128, 16], F32, **R))
    ones_col = e(nc.sbuf_tensor("ones_col", [128, 1], BF16, **R))
    ones_row = e(nc.sbuf_tensor("ones_row", [1, 128], F32, **R))
    neg16 = e(nc.sbuf_tensor("neg16", [128, 1], F32, **R))
    stats = e(nc.sbuf_tensor("stats", [128, 32], F32, **R))
    bnscr = e(nc.sbuf_tensor("bnscr", [128, 48], F32, **R))
    gsc = e(nc.sbuf_tensor("gsc", [128, DC, GK], BF16, **R))

    block = e(nc.Block())

    pend = {k: [] for k in ("tensor", "vector", "scalar", "gpsimd", "sync")}

    class _DQ:
        def __getattr__(self, eng):
            def dec(fn):
                pend[eng].append(fn)
                return fn
            return dec

    blockq = _DQ()

    def flush():
        for eng in ("tensor", "vector", "scalar", "gpsimd", "sync"):
            fns = pend[eng]
            pend[eng] = []
            if fns:
                getattr(block, eng)(lambda e, fns=fns: [f(e) for f in fns])

    def wdeps(eng, deps):
        for c_, v_ in deps:
            eng.wait_ge(c_.sem, v_)

    psum_f = [e(nc.psum_tensor(f"pb{i}", [128, 512], F32)) for i in range(8)]

    @blockq.sync
    def _(sy):
        for dst, src in ((ident, id_in), (m01, m01_in), (ebias, eb_in),
                         (hsel, hsel_in), (gbl, gbl_in)):
            sy.dma_start(out=dst[:], in_=src).then_inc(c_sy.sem, 16)
    c_sy.n += 5 * 16
    consts_ld = c_sy.n

    @blockq.vector
    def _(v):
        v.memset(ones_col[:], 1.0)
        v.memset(ones_row[:], 1.0)
        v.memset(neg16[:], -STAB).then_inc(c_v.sem, 1)
    c_v.n += 1
    consts_v = c_v.n

    @blockq.scalar
    def _(s):
        s.wait_ge(c_v.sem, consts_v)
        s.wait_ge(c_sy.sem, consts_ld)

    @blockq.tensor
    def _(t):
        t.wait_ge(c_v.sem, consts_v)
        t.wait_ge(c_sy.sem, consts_ld)

    wfree = {}

    def wait_free(engine, key):
        for ctr, val in wfree.get(key, ()):
            engine.wait_ge(ctr.sem, val)

    def wf(*keys):
        out = []
        for k in keys:
            out.extend(wfree.get(k, ()))
        return out

    def set_free(key, *deps):
        wfree[key] = list(deps)


    def finish():
        @blockq.sync
        def _(sy, nsy=None, nsq=None):
            sy.wait_ge(c_sy.sem, c_sy.n if nsy is None else nsy)
            sy.wait_ge(c_sq.sem, c_sq.n if nsq is None else nsq)
        @blockq.gpsimd
        def _(g):
            g.wait_ge(c_gp.sem, c_gp.n)

    def vstep(fn):
        @blockq.vector
        def _(v):
            fn(v).then_inc(c_v.sem, 1)
        c_v.n += 1
        return c_v.n

    def gen_ln(src, dst, eps, so=0, bo=0, src_s=None, src_v=None, src_sy=None,
               dst_t=None, out=None):
        def _w(eng):
            if src_s is not None:
                eng.wait_ge(c_s.sem, src_s)
            if src_v is not None:
                eng.wait_ge(c_v.sem, src_v)
            if src_sy is not None:
                eng.wait_ge(c_sy.sem, src_sy)

        def f(v):
            _w(v)
            for k in range(4):
                r = v.bn_stats(bnscr[:, bo + 6 * k:bo + 6 * k + 6],
                               src[:, 512 * k:512 * k + 512])
            return r
        va = vstep(f)
        yield

        def f(v):
            v.wait_ge(c_v.sem, va)
            return v.bn_aggr(stats[:, so:so + 2], bnscr[:, bo:bo + 24])
        vb = vstep(f)
        yield

        def f(v):
            v.wait_ge(c_v.sem, vb)
            return v.tensor_scalar_add(stats[:, so + 6:so + 7],
                                       stats[:, so + 1:so + 2], float(eps))
        ve = vstep(f)
        yield

        @blockq.scalar
        def _(s):
            s.wait_ge(c_v.sem, ve)
            s.activation(out=stats[:, so + 7:so + 8], in_=stats[:, so + 6:so + 7],
                         func=AF.Sqrt).then_inc(c_s.sem, 1)
        c_s.n += 1
        sddone = c_s.n
        yield

        def f(v):
            v.wait_ge(c_s.sem, sddone)
            return v.reciprocal(stats[:, so + 8:so + 9], stats[:, so + 7:so + 8])
        vf = vstep(f)
        yield

        def f(v):
            v.wait_ge(c_v.sem, vf)
            return v.scalar_tensor_tensor(stats[:, so + 9:so + 10],
                                          stats[:, so:so + 1], -1.0,
                                          stats[:, so + 8:so + 9],
                                          ALU.mult, ALU.mult)
        vg = vstep(f)
        yield

        def f(v):
            v.wait_ge(c_v.sem, vg)
            if dst_t is not None:
                v.wait_ge(c_t.sem, dst_t)
            return v.tensor_scalar(dst, src, stats[:, so + 8:so + 9],
                                   stats[:, so + 9:so + 10], ALU.mult, ALU.add)
        r = vstep(f)
        if out is not None:
            out.append(r)

    def emit_ln(src, dst, eps, tmp_sq=None, src_s=None, src_v=None,
                src_sy=None, dst_t=None):
        out = []
        for _ in gen_ln(src, dst, eps, src_s=src_s, src_v=src_v,
                        src_sy=src_sy, dst_t=dst_t, out=out):
            pass
        return out[0]

    def gen_quant(z, y, a, so, src_v, out=None):
        MAGIC = 12582912.0

        def f(v):
            v.wait_ge(c_v.sem, src_v)
            return v.tensor_reduce(stats[:, so + 10:so + 11], z, axis=AXX,
                                   op=ALU.max, apply_absolute_value=True)
        qa = vstep(f)
        yield

        def f(v):
            v.wait_ge(c_v.sem, qa)
            return v.reciprocal(stats[:, so + 11:so + 12], stats[:, so + 10:so + 11])
        qb = vstep(f)
        yield

        def f(v):
            v.wait_ge(c_v.sem, qb)
            v.tensor_scalar_mul(stats[:, so + 12:so + 13], stats[:, so + 11:so + 12], 127.0)
            return v.tensor_scalar_mul(stats[:, so + 13:so + 14],
                                       stats[:, so + 10:so + 11], 1.0 / 127.0)
        qc = vstep(f)
        yield

        def f(v):
            v.wait_ge(c_v.sem, qc)
            v.tensor_scalar_mul(y, z, stats[:, so + 12:so + 13])
            v.tensor_scalar(a, y, MAGIC, -MAGIC, ALU.add, ALU.add)
            v.tensor_scalar(a, a, 127.0, -128.0, ALU.min, ALU.max)
            return v.tensor_scalar_mul(a, a, stats[:, so + 13:so + 14])
        r = vstep(f)
        if out is not None:
            out.append(r)

    def emit_transposes(dst_fn, src, n, src_v=None, src_s=None, src_sy=None):
        sready = c_s.n
        last_ev = {0: None, 1: None}
        for i in range(n):
            bk = 6 + (i % 2)
            pb = psum_f[bk]
            _fdv = wf(("bank", bk)) if i < 2 else None

            @blockq.tensor
            def _(t, i=i, pb=pb, bk=bk, _fd=_fdv):
                if src_v is not None:
                    t.wait_ge(c_v.sem, src_v)
                if src_s is not None:
                    t.wait_ge(c_s.sem, src_s)
                if src_sy is not None:
                    t.wait_ge(c_sy.sem, src_sy)
                if _fd is not None:
                    wdeps(t, _fd)
                else:
                    t.wait_ge(c_s.sem, sready + i - 1)
                t.transpose(pb[:, 0:128], src[:, i * 128:(i + 1) * 128],
                            ident[:]).then_inc(c_t.sem, 1)
            c_t.n += 1
            trd = c_t.n

            @blockq.scalar
            def _(s, i=i, pb=pb, trd=trd):
                s.wait_ge(c_t.sem, trd)
                s.copy(out=dst_fn(i), in_=pb[:, 0:128]).then_inc(c_s.sem, 1)
            c_s.n += 1
            last_ev[i % 2] = c_s.n
        for par in (0, 1):
            if last_ev[par] is not None:
                set_free(("bank", 6 + par), (c_s, last_ev[par]))
        return c_s.n

    L = dict(side="left")

    def emit_entry(f, src_dram):
        sub = ExitStack()
        Ab = [sub.enter_context(nc.sbuf_tensor(f"{f['tag']}A{p}", [128, D], F32, **L))
              for p in range(2)]
        Bb = [sub.enter_context(nc.sbuf_tensor(f"{f['tag']}B{p}", [128, D], F32, **L))
              for p in range(2)]
        buf_free = [0, 0]

        def tile_gen(lt):
            p = lt % 2
            A, B = Ab[p], Bb[p]
            rt = TMAP[lt]

            @blockq.sync
            def _(sy, free_s=buf_free[p], rt=rt):
                sy.wait_ge(c_s.sem, free_s)
                sy.dma_start(out=A[:], in_=src_dram[rt * 128:(rt + 1) * 128, :]
                             ).then_inc(c_sy.sem, 16)
            c_sy.n += 16
            ld = c_sy.n
            yield

            o1 = []
            yield from gen_ln(A[:], B[:], EPS1, so=16 * p, bo=24 * p,
                              src_sy=ld, out=o1)
            o2 = []
            yield from gen_ln(B[:], A[:], EPS2, so=16 * p, bo=24 * p,
                              src_v=o1[0], out=o2)
            o3 = []
            yield from gen_quant(A[:], B[:], A[:], 16 * p, o2[0], out=o3)
            buf_free[p] = emit_transposes(
                lambda dc, lt=lt: f["xq"][:, dc, lt * 128:(lt + 1) * 128],
                A[:], DC, src_v=o3[0])

        gens = [tile_gen(0), tile_gen(1)]
        nxt = 2
        while gens:
            done = []
            for g in gens:
                if next(g, "END") == "END":
                    done.append(g)
            for g in done:
                gens.remove(g)
                if nxt < NT:
                    gens.append(tile_gen(nxt))
                    nxt += 1
        sub.close()
        return c_s.n

    def emit_stage12(f, grp, first_pass, xq_rdy, hook=None):
        t0 = grp * 512
        mm12 = {}
        pmul = {}
        for jc in range(HC):
            b = jc % 2
            if hook is not None and jc == 3:
                hook()
                hook = None

            _fgu = wf(("gu", b))

            @blockq.gpsimd
            def _(g, jc=jc, b=b, _fgu=_fgu):
                wdeps(g, _fgu)
                if first_pass:
                    g.dma_start(out=f["wg"][b][:], in_=wg_in[jc]).then_inc(c_gp.sem, 16)
                    g.dma_start(out=f["wu"][b][:], in_=wu_in[jc]).then_inc(c_gp.sem, 16)
                else:
                    g.dma_start(out=f["wg"][b][:], in_=sgn_g[jc]).then_inc(c_gp.sem, 16)
                    g.dma_start(out=f["wu"][b][:], in_=sgn_u[jc]).then_inc(c_gp.sem, 16)
            c_gp.n += 32
            wld = c_gp.n
            wldu = c_sy.n

            if first_pass:
                @blockq.vector
                def _(v, b=b, wld=wld):
                    v.wait_ge(c_gp.sem, wld)
                    v.tensor_scalar(f["wg"][b][:], f["wg"][b][:], BIGS, None, ALU.mult)
                    v.tensor_scalar(f["wg"][b][:], f["wg"][b][:], -1.0, 1.0,
                                    ALU.max, ALU.min).then_inc(c_v.sem, 1)
                c_v.n += 1
                gsd = c_v.n

                @blockq.vector
                def _(v, b=b, wld=wld):
                    v.wait_ge(c_gp.sem, wld)
                    v.tensor_scalar(f["wu"][b][:], f["wu"][b][:], BIGS, None, ALU.mult)
                    v.tensor_scalar(f["wu"][b][:], f["wu"][b][:], -1.0, 1.0,
                                    ALU.max, ALU.min).then_inc(c_v.sem, 1)
                c_v.n += 1
                usd = c_v.n

                @blockq.sync
                def _(sy, jc=jc, b=b, gsd=gsd, usd=usd):
                    sy.wait_ge(c_v.sem, usd)
                    sy.dma_start(out=sgn_g[jc], in_=f["wg"][b][:]).then_inc(c_sy.sem, 16)
                    sy.dma_start(out=sgn_u[jc], in_=f["wu"][b][:]).then_inc(c_sy.sem, 16)
                c_sy.n += 32
                scr_wr = c_sy.n
            else:
                gsd = usd = scr_wr = None

            pg, pu = psum_f[0 + 2 * b], psum_f[1 + 2 * b]
            _fbk = wf(("bank", 0 + 2 * b), ("bank", 1 + 2 * b)) if jc < 2 else None

            @blockq.tensor
            def _(t, jc=jc, b=b, wld=wld, wldu=wldu, gsd=gsd, usd=usd, pg=pg,
                  pu=pu, _fbk=_fbk):
                if gsd is not None:
                    t.wait_ge(c_v.sem, usd)
                else:
                    t.wait_ge(c_gp.sem, wld)
                if jc == 0:
                    t.wait_ge(c_s.sem, xq_rdy)
                if jc >= 2:
                    t.wait_ge(c_v.sem, pmul[jc - 2])
                else:
                    wdeps(t, _fbk)
                for dc in range(DC):
                    mm = t.matmul(pg[:], f["wg"][b][:, dc, :],
                                  f["xq"][:, dc, t0:t0 + 512],
                                  start=(dc == 0), stop=(dc == DC - 1))
                mm.then_inc(c_t.sem, 1)
                for dc in range(DC):
                    mm = t.matmul(pu[:], f["wu"][b][:, dc, :],
                                  f["xq"][:, dc, t0:t0 + 512],
                                  start=(dc == 0), stop=(dc == DC - 1))
                mm.then_inc(c_t.sem, 1)
            c_t.n += 2
            mm12[jc] = c_t.n
            deps = [(c_t, mm12[jc])]
            if first_pass:
                deps.append((c_sy, scr_wr))
            set_free(("gu", b), *deps)

            @blockq.scalar
            def _(s, pg=pg, mmd=mm12[jc]):
                s.wait_ge(c_t.sem, mmd - 1)
                s.activation(out=f["g"][:], in_=pg[:], func=AF.Silu).then_inc(c_s.sem, 1)
            c_s.n += 1
            evd = c_s.n

            ucp = c_v.n + 1

            @blockq.vector
            def _(v, jc=jc, pu=pu, evd=evd, mmd=mm12[jc], ucp=ucp):
                v.wait_ge(c_t.sem, mmd)
                v.tensor_copy(f["u"][:], pu[:]).then_inc(c_v.sem, 1)
                v.wait_ge(c_v.sem, ucp)
                v.wait_ge(c_s.sem, evd)
                v.tensor_mul(f["p"][:, jc, :], f["g"][:], f["u"][:]).then_inc(c_v.sem, 1)
            c_v.n += 2
            pmul[jc] = c_v.n
        set_free(("bank", 0), (c_s, c_s.n), (c_v, c_v.n))
        set_free(("bank", 1), (c_s, c_s.n), (c_v, c_v.n))
        set_free(("bank", 2), (c_s, c_s.n), (c_v, c_v.n))
        set_free(("bank", 3), (c_s, c_s.n), (c_v, c_v.n))
        return pmul[HC - 1]

    def emit_stage3(f, app, grp, first_pass, src_res, dst_res, p_rdy):
        t0 = grp * 512
        x1rd = [0] * 4
        wlds = {}
        barrier_s = c_s.n
        barrier_v = c_v.n
        for tt in range(4):
            @blockq.sync
            def _(sy, tt=tt):
                sy.wait_ge(c_s.sem, barrier_s)
                sy.wait_ge(c_v.sem, barrier_v)
                r0 = TMAP[grp * 4 + tt] * 128
                sy.dma_start(out=f["x1g"][:, tt, :],
                             in_=src_res[r0:r0 + 128, :]).then_inc(c_sy.sem, 16)
            c_sy.n += 16
            x1rd[tt] = c_sy.n

        s3ev = {}
        xadd = {}
        for ddc in range(DC):
            b = ddc % 2

            _fwd = wf(("wd", b))

            @blockq.gpsimd
            def _(g, ddc=ddc, b=b, _fwd=_fwd):
                wdeps(g, _fwd)
                if first_pass:
                    g.dma_start(out=f["wd"][b][:], in_=wd_in[ddc]).then_inc(c_gp.sem, 16)
                else:
                    g.dma_start(out=f["wd"][b][:], in_=sgn_d[ddc]).then_inc(c_gp.sem, 16)
            c_gp.n += 16
            wld = c_gp.n

            if first_pass:
                @blockq.vector
                def _(v, b=b, wld=wld):
                    v.wait_ge(c_gp.sem, wld)
                    v.tensor_scalar(f["wd"][b][:], f["wd"][b][:],
                                    BIGS, None, ALU.mult)
                    v.tensor_scalar(f["wd"][b][:], f["wd"][b][:],
                                    -1.0, 1.0, ALU.max, ALU.min).then_inc(c_v.sem, 1)
                c_v.n += 1
                dsv = c_v.n
                dss = None

                @blockq.sync
                def _(sy, ddc=ddc, b=b, dsv=dsv):
                    sy.wait_ge(c_v.sem, dsv)
                    sy.dma_start(out=sgn_d[ddc], in_=f["wd"][b][:]).then_inc(c_sy.sem, 16)
                c_sy.n += 16
                scr3_wr = c_sy.n
            else:
                dsv = dss = scr3_wr = None

            po = psum_f[0 + b]
            _fb3 = wf(("bank", b)) if ddc < 2 else None

            _wls = wlds.get(ddc)

            @blockq.tensor
            def _(t, ddc=ddc, b=b, wld=wld, dsv=dsv, po=po, _fb3=_fb3,
                  _wls=_wls):
                if dsv is not None:
                    t.wait_ge(c_v.sem, dsv)
                else:
                    t.wait_ge(c_gp.sem, wld)
                    if _wls is not None:
                        t.wait_ge(c_sy.sem, _wls)
                if ddc == 0:
                    t.wait_ge(c_v.sem, p_rdy)
                if ddc >= 2:
                    t.wait_ge(c_s.sem, s3ev[ddc - 2])
                else:
                    wdeps(t, _fb3)
                for jc in range(HC):
                    mm = t.matmul(po[:], f["wd"][b][:, jc, :], f["p"][:, jc, :],
                                  start=(jc == 0), stop=(jc == HC - 1))
                mm.then_inc(c_t.sem, 1)
            c_t.n += 1
            mmd = c_t.n
            deps = [(c_t, mmd)]
            if first_pass:
                deps.append((c_sy, scr3_wr))
            set_free(("wd", b), *deps)

            @blockq.scalar
            def _(s, po=po, mmd=mmd):
                s.wait_ge(c_t.sem, mmd)
                s.mul(out=f["ct"][:], in_=po[:], mul=0.5).then_inc(c_s.sem, 1)
            c_s.n += 1
            s3ev[ddc] = c_s.n
            ctd = c_s.n

            for tt in range(4):
                pb = psum_f[6 + (tt % 2)]
                _fb67 = wf(("bank", 6 + tt % 2)) if (ddc == 0 and tt < 2) else None

                @blockq.tensor
                def _(t, ddc=ddc, tt=tt, pb=pb, ctd=ctd, _fb67=_fb67):
                    t.wait_ge(c_s.sem, ctd)
                    t.wait_ge(c_v.sem, xadd.get((ddc - 1, tt), 0))
                    t.wait_ge(c_v.sem, xadd.get((ddc, tt - 2), 0))
                    if _fb67 is not None:
                        wdeps(t, _fb67)
                    t.transpose(pb[:, 0:128], f["ct"][:, tt * 128:(tt + 1) * 128],
                                ident[:]).then_inc(c_t.sem, 1)
                c_t.n += 1
                trd = c_t.n

                @blockq.vector
                def _(v, ddc=ddc, tt=tt, pb=pb, trd=trd):
                    v.wait_ge(c_t.sem, trd)
                    v.wait_ge(c_sy.sem, x1rd[tt])
                    v.tensor_add(f["x1g"][:, tt, ddc * 128:(ddc + 1) * 128],
                                 pb[:, 0:128],
                                 f["x1g"][:, tt, ddc * 128:(ddc + 1) * 128]
                                 ).then_inc(c_v.sem, 1)
                c_v.n += 1
                xadd[(ddc, tt)] = c_v.n
        set_free(("bank", 6), (c_v, xadd[(DC - 1, 2)]))
        set_free(("bank", 7), (c_v, xadd[(DC - 1, 3)]))
        set_free(("bank", 0), (c_s, s3ev[DC - 2]))
        set_free(("bank", 1), (c_s, s3ev[DC - 1]))

        for tt in range(4):
            done = xadd[(DC - 1, tt)]

            @blockq.sync
            def _(sy, tt=tt, done=done):
                sy.wait_ge(c_v.sem, done)
                r0 = TMAP[grp * 4 + tt] * 128
                sy.dma_start(out=dst_res[r0:r0 + 128, :],
                             in_=f["x1g"][:, tt, :]).then_inc(c_sy.sem, 16)
            c_sy.n += 16

            if app == 1:
                lnd = emit_ln(f["x1g"][:, tt, :], f["B"][:], EPS1,
                              src_v=done, dst_t=c_t.n)
                lnstg_wr = wfree.get(("lnstg",))
                emit_transposes(
                    lambda dc: f["lnstg"][:, dc, :], f["B"][:], DC, src_v=lnd,
                    src_sy=(lnstg_wr[0][1] if lnstg_wr else None))
                gt = TMAP[grp * 4 + tt]

                @blockq.sync
                def _(sy, gt=gt, lst=c_s.n):
                    sy.wait_ge(c_s.sem, lst)
                    sy.dma_start(out=ln1_scr[gt], in_=f["lnstg"][:]
                                 ).then_inc(c_sy.sem, 16)
                c_sy.n += 16
                set_free(("lnstg",), (c_sy, c_sy.n))

    def emit_ffn(app):
        fs = ExitStack()
        fe = fs.enter_context
        tag = f"f{app}"
        f = {"tag": tag}
        f["p"] = fe(nc.sbuf_tensor(tag + "p", [128, HC, 512], BF16, **L))
        f["xq"] = fe(nc.sbuf_tensor(tag + "xq", [128, DC, T], BF16, **L))
        f["B"] = fe(nc.sbuf_tensor(tag + "B", [128, D], F32, **L))
        f["lnstg"] = fe(nc.sbuf_tensor(tag + "lnstg", [128, DC, 128], BF16, **L))
        f["wg"] = [fe(nc.sbuf_tensor(f"{tag}wg{b}", [128, DC, 128], BF16, **L))
                   for b in range(2)]
        f["wu"] = [fe(nc.sbuf_tensor(f"{tag}wu{b}", [128, DC, 128], BF16, **L))
                   for b in range(2)]
        f["g"] = fe(nc.sbuf_tensor(tag + "g", [128, 512], BF16, **L))
        f["u"] = fe(nc.sbuf_tensor(tag + "u", [128, 512], BF16, **L))
        f["ct"] = fe(nc.sbuf_tensor(tag + "ct", [128, 512], F32, **L))
        f["x1g"] = fe(nc.sbuf_tensor(tag + "x1g", [128, 4, D], F32, **L))

        src = x_in if app == 1 else x2_scr
        dst = x1_scr if app == 1 else out_t
        if DEBUG_SCR and app == 1:
            xq_dbg = scr("xq_dbg", (128, DC, T), BF16)
            st_dbg = scr("st_dbg", (128, 32))
            id_dbg = scr("id_dbg", (128, 128))
        for b in range(2):
            set_free(("gu", b), (c_v, c_v.n), (c_t, c_t.n))
            set_free(("wd", b), (c_v, c_v.n), (c_t, c_t.n))

        def exchange_send():
            @blockq.sync
            def _(sy):
                for cc_i in (ccA_i, ccB_i):
                    for sl, blk in enumerate((0, 1, 6, 7)):
                        sy.dma_start(out=cc_i[:, :, sl * 128:(sl + 1) * 128],
                                     in_=ln1_scr[blk]).then_inc(c_sy.sem, 16)
                sy.dma_start(out=ccC_i, in_=ln1_scr[0][:, :, 0:16]
                             ).then_inc(c_sy.sem, 16)
            c_sy.n += 144
            stg = c_sy.n

            def trig_hook():
                drained = c_sy.n

                @blockq.gpsimd
                def _(g, stg=stg, drained=drained):
                    g.wait_ge(c_sy.sem, max(stg, drained))
                    g.collective_compute(
                        "AllGather", ALU.bypass,
                        replica_groups=[[0, 1], [2, 3], [4, 5], [6, 7]],
                        ins=[ccA_i.opt()], outs=[ccA_o.opt()]).then_inc(c_cc.sem, 1)
                    g.collective_compute(
                        "AllGather", ALU.bypass,
                        replica_groups=[[0, 3], [1, 2], [4, 7], [5, 6]],
                        ins=[ccB_i.opt()], outs=[ccB_o.opt()]).then_inc(c_cc.sem, 1)
                    g.collective_compute(
                        "AllGather", ALU.bypass,
                        replica_groups=[[0, 1, 2, 3], [4, 5, 6, 7]],
                        ins=[ccC_i.opt()], outs=[ccC_o.opt()]).then_inc(c_cc.sem, 1)
                c_cc.n += 3
            return trig_hook

        xq_rdy = emit_entry(f, src)
        if DEBUG_SCR and app == 1:
            @blockq.sync
            def _(sy):
                sy.wait_ge(c_s.sem, xq_rdy)
                sy.dma_start(out=xq_dbg, in_=f["xq"][:]).then_inc(c_sy.sem, 16)
                sy.dma_start(out=st_dbg, in_=stats[:]).then_inc(c_sy.sem, 16)
                sy.dma_start(out=id_dbg, in_=ident[:]).then_inc(c_sy.sem, 16)
            c_sy.n += 48

        sub = ExitStack()
        f["wd"] = [sub.enter_context(
            nc.sbuf_tensor(f"{tag}wd{b}", [128, HC, 128], BF16, **L))
            for b in range(2)]
        trig = [None]
        for grp in (0, 1):
            first = (app == 1 and grp == 0)
            p_rdy = emit_stage12(f, grp, first, xq_rdy)
            emit_stage3(f, app, grp, first, src_res=src, dst_res=dst, p_rdy=p_rdy)
            if app == 1 and grp == 0 and stop_after != "ffn1":
                trig[0] = exchange_send()
        if trig[0] is not None:
            trig[0]()
        sub.close()
        fs.close()

    emit_ffn(1)
    if stop_after == "ffn1":
        finish()
        flush()
        return nc, st

    oscf = ExitStack()
    osc = oscf.enter_context(nc.sbuf_tensor("osc", [128, NH, T], BF16, **L))
    oLg = oscf.enter_context(nc.sbuf_tensor("oLg", [128, DC * GK], F32, **L))
    lgf = oscf.enter_context(nc.sbuf_tensor("lgf", [1, NH * GK], F32, **L))
    gmrg = oscf.enter_context(nc.sbuf_tensor("gmrg", [128, 4, 512], F32, **L))
    lnf = ExitStack()
    ln1T = lnf.enter_context(nc.sbuf_tensor("ln1T", [128, DC, KW], BF16, **L))

    @blockq.vector
    def _(v):
        v.memset(ln1T[:, :, 1552:1664], 0.0).then_inc(c_v.sem, 1)
    c_v.n += 1

    @blockq.sync
    def _(sy):
        for tt in range(NT):
            sy.dma_start(out=ln1T[:, :, OWN0 + tt * 128:OWN0 + (tt + 1) * 128],
                         in_=ln1_scr[tt]).then_inc(c_sy.sem, 16)
    c_sy.n += NT * 16
    ln1_own = c_sy.n

    exs = ExitStack()
    xe = exs.enter_context
    exBB = xe(nc.sbuf_tensor("exBB", [128, 2, DC, HALO], BF16, **L))
    exC = xe(nc.sbuf_tensor("exC", [128, DC, GK], BF16, **L))

    _exccw = c_cc.n
    hl = ln1T[:, :, 0:HALO]
    hr = ln1T[:, :, OWN0 + T:OWN0 + T + HALO]
    seq = [(hl, HALO, 0), (hl, HALO, 1), (hl, HALO, 2), (hl, HALO, 3),
           (hr, 0, 4), (hr, 0, 5), (hr, 0, 6), (hr, 0, 7)]
    blend_v = {}
    for k, (dst, co, sc) in enumerate(seq):
        cc = (ccA_o, ccB_o)[(sc % 4) // 2]
        sl = sc % 2

        @blockq.gpsimd
        def _(g, k=k, cc=cc, sl=sl, co=co, ccw=_exccw):
            if k < 2:
                g.wait_ge(c_cc.sem, ccw)
                g.wait_ge(c_sy.sem, ln1_own)
            else:
                g.wait_ge(c_v.sem, blend_v[k - 2])
            g.dma_start(out=exBB[:, k % 2],
                        in_=cc[sl][:, :, co:co + HALO]).then_inc(c_gp.sem, 16)
        c_gp.n += 16
        ld = c_gp.n

        @blockq.vector
        def _(v, k=k, dst=dst, sc=sc, ld=ld):
            v.wait_ge(c_gp.sem, ld)
            if k == 0:
                v.wait_ge(c_sy.sem, consts_ld)
            if sc % 4 == 0:
                v.tensor_scalar_mul(dst, exBB[:, k % 2], hsel[:, sc:sc + 1]
                                    ).then_inc(c_v.sem, 1)
            else:
                v.scalar_tensor_tensor(dst, exBB[:, k % 2], hsel[:, sc:sc + 1],
                                       dst, ALU.mult, ALU.add
                                       ).then_inc(c_v.sem, 1)
        c_v.n += 1
        blend_v[k] = c_v.n

    @blockq.gpsimd
    def _(g):
        g.wait_ge(c_v.sem, blend_v[6])
        g.dma_start(out=exC[:], in_=ccC_o[0]).then_inc(c_gp.sem, 16)
    c_gp.n += 16
    exld = c_gp.n

    @blockq.vector
    def _(v):
        v.wait_ge(c_gp.sem, exld)
        v.tensor_copy(ln1T[:, :, GCOL:GCOL + GK], exC[:]).then_inc(c_v.sem, 1)
    c_v.n += 1
    halo_done = c_v.n

    exs.close()
    if stop_after == "exchange":
        finish()
        flush()
        return nc, st

    ast = ExitStack()
    ae = ast.enter_context
    pT_sb = ae(nc.sbuf_tensor("pT_sb", [128, 8, 128], BF16, **L))
    lr_sb = ae(nc.sbuf_tensor("lr_sb", [1, 128], F32, **L))
    lrep_sb = ae(nc.sbuf_tensor("lrep_sb", [128, 128], F32, **L))
    qg_sb = ae(nc.sbuf_tensor("qg_sb", [128, 4, 128], BF16, **L))
    spacer = ae(nc.sbuf_tensor("spacer", [128, 3330], F32, **L))
    kw_sb = ae(nc.sbuf_tensor("kw_sb", [128, 4, KW], BF16, **L))
    vw_sb = ae(nc.sbuf_tensor("vw_sb", [128, KT, 512], BF16, **L))
    qw_sb = ae(nc.sbuf_tensor("qw_sb", [128, 4, T], BF16, **L))
    wat = [ae(nc.sbuf_tensor(f"wat{b}", [128, DC, 512], BF16, **L))
           for b in range(2)]
    watq = ae(nc.sbuf_tensor("watq", [128, DC, 512], BF16, **L))

    @blockq.vector
    def _(v):
        v.memset(qg_sb[:], 0.0).then_inc(c_v.sem, 1)
    c_v.n += 1
    qg_zero = c_v.n

    def proj_wave(w, which, wbuf, wdma, part):
        if which == "k":
            chunks = (((256, 512), (768, 512)) if part == "own"
                      else ((0, 256), (1280, 384)))
            for oc in range(4):
                for ci, (ts, tn) in enumerate(chunks):
                    bk = 4 + (oc * 2 + ci) % 2
                    pb = psum_f[bk]
                    _fpk = wf(("bank", bk))

                    @blockq.tensor
                    def _(t, oc=oc, ts=ts, tn=tn, pb=pb, bk=bk, _fpk=_fpk):
                        t.wait_ge(c_gp.sem, wdma)
                        if part == "halo":
                            t.wait_ge(c_v.sem, halo_done)
                        else:
                            t.wait_ge(c_sy.sem, ln1_own)
                        wdeps(t, _fpk)
                        for dc in range(DC):
                            mm = t.matmul(pb[:, 0:tn],
                                          wbuf[:, dc, oc * 128:(oc + 1) * 128],
                                          ln1T[:, dc, ts:ts + tn],
                                          start=(dc == 0), stop=(dc == DC - 1))
                        mm.then_inc(c_t.sem, 1)
                    c_t.n += 1
                    mmd = c_t.n

                    @blockq.scalar
                    def _(s, oc=oc, ts=ts, tn=tn, pb=pb, mmd=mmd):
                        s.wait_ge(c_t.sem, mmd)
                        s.copy(out=kw_sb[:, oc, ts:ts + tn],
                               in_=pb[:, 0:tn]).then_inc(c_s.sem, 1)
                    c_s.n += 1
                    set_free(("bank", bk), (c_s, c_s.n))
        elif which == "v":
            tts = range(2, 10) if part == "own" else (0, 1, 10, 11, 12)
            for tt in tts:
                bk = 4 + tt % 2
                pb = psum_f[bk]
                _fpv = wf(("bank", bk))

                @blockq.tensor
                def _(t, tt=tt, pb=pb, bk=bk, _fpv=_fpv):
                    t.wait_ge(c_gp.sem, wdma)
                    if part == "halo":
                        t.wait_ge(c_v.sem, halo_done)
                    else:
                        t.wait_ge(c_sy.sem, ln1_own)
                    wdeps(t, _fpv)
                    for dc in range(DC):
                        mm = t.matmul(pb[:], ln1T[:, dc, tt * 128:(tt + 1) * 128],
                                      wbuf[:, dc, :],
                                      start=(dc == 0), stop=(dc == DC - 1))
                    mm.then_inc(c_t.sem, 1)
                c_t.n += 1
                mmd = c_t.n

                @blockq.scalar
                def _(s, tt=tt, pb=pb, mmd=mmd):
                    s.wait_ge(c_t.sem, mmd)
                    s.copy(out=vw_sb[:, tt, :], in_=pb[:]).then_inc(c_s.sem, 1)
                c_s.n += 1
                set_free(("bank", bk), (c_s, c_s.n))
        else:
            ths = (0, 1) if part == "own" else (2,)
            for oc in range(4):
                for th in ths:
                    bk = 4 + th % 2
                    pb = psum_f[bk]
                    _fpq = wf(("bank", bk))

                    @blockq.tensor
                    def _(t, oc=oc, th=th, pb=pb, bk=bk, _fpq=_fpq):
                        t.wait_ge(c_gp.sem, wdma)
                        if th == 2:
                            t.wait_ge(c_v.sem, qg_zero)
                            t.wait_ge(c_v.sem, halo_done)
                        else:
                            t.wait_ge(c_sy.sem, ln1_own)
                        wdeps(t, _fpq)
                        for dc in range(DC):
                            if th < 2:
                                rhs = ln1T[:, dc, OWN0 + th * 512:OWN0 + th * 512 + 512]
                                dst = pb[:]
                            else:
                                rhs = ln1T[:, dc, GCOL:GCOL + GK]
                                dst = pb[:, 0:GK]
                            mm = t.matmul(dst, wbuf[:, dc, oc * 128:(oc + 1) * 128],
                                          rhs, start=(dc == 0), stop=(dc == DC - 1))
                        mm.then_inc(c_t.sem, 1)
                    c_t.n += 1
                    mmd = c_t.n

                    @blockq.scalar
                    def _(s, oc=oc, th=th, pb=pb, mmd=mmd):
                        s.wait_ge(c_t.sem, mmd)
                        if th < 2:
                            s.copy(out=qw_sb[:, oc, th * 512:th * 512 + 512],
                                   in_=pb[:]).then_inc(c_s.sem, 1)
                        else:
                            s.copy(out=qg_sb[:, oc, 0:GK],
                                   in_=pb[:, 0:GK]).then_inc(c_s.sem, 1)
                    c_s.n += 1
                    set_free(("bank", bk), (c_s, c_s.n))

    def attend(w, hh, qt, proj_ev):
        glob = (qt == 8)
        nkt = 8 if glob else 6
        po, pl = psum_f[2], psum_f[3]
        h = w * 4 + hh

        expd = [0] * nkt
        for i in range(nkt):
            pb = psum_f[i % 2]
            _fsb = wf(("bank", i % 2)) if i < 2 else None

            @blockq.tensor
            def _(t, i=i, pb=pb, _fsb=_fsb):
                if i >= 2:
                    d = expd[i - 2]
                    if isinstance(d, tuple):
                        t.wait_ge(c_v.sem, d[1])
                    else:
                        t.wait_ge(c_s.sem, d)
                else:
                    t.wait_ge(c_s.sem, proj_ev)
                    wdeps(t, _fsb)
                if glob:
                    kcol, nk = OWN0 + i * 128, 128
                    rhs = qg_sb[:, hh, :]
                else:
                    kcol = GCOL if i == 5 else 128 * qt + 128 * i
                    nk = GK if i == 5 else 128
                    rhs = qw_sb[:, hh, qt * 128:(qt + 1) * 128]
                mm = t.matmul(pb[0:nk, 0:128], kw_sb[:, hh, kcol:kcol + nk],
                              rhs, start=True, stop=True)
                mm.then_inc(c_t.sem, 1)
            c_t.n += 1
            smm = c_t.n

            @blockq.scalar
            def _(s, i=i, pb=pb, smm=smm):
                s.wait_ge(c_t.sem, smm)
                nk = GK if (not glob and i == 5) else 128
                bias = neg16[0:nk, :] if glob else ebias[0:nk, qt, i:i + 1]
                s.activation(out=pT_sb[0:nk, i, :], in_=pb[0:nk, 0:128],
                             func=AF.Exp, bias=bias, scale=SCL).then_inc(c_s.sem, 1)
            c_s.n += 1
            expd[i] = c_s.n
            set_free(("bank", i % 2), (c_s, c_s.n))

            if (not glob) and i in (0, 4):
                @blockq.vector
                def _(v, i=i, ed=expd[i]):
                    v.wait_ge(c_s.sem, ed)
                    v.wait_ge(c_sy.sem, consts_ld)
                    v.tensor_mul(pT_sb[:, i, :], pT_sb[:, i, :],
                                 m01[:, qt, (0 if i == 0 else 1), :]
                                 ).then_inc(c_v.sem, 1)
                c_v.n += 1
                expd[i] = ("v", c_v.n)

        _fol = wf(("bank", 2), ("bank", 3))
        for i in range(nkt):
            dep = expd[i]

            @blockq.tensor
            def _(t, i=i, dep=dep, _fol=_fol):
                if isinstance(dep, tuple):
                    t.wait_ge(c_v.sem, dep[1])
                else:
                    t.wait_ge(c_s.sem, dep)
                if i == 0:
                    wdeps(t, _fol)
                if glob:
                    nk, vt = 128, 2 + i
                else:
                    nk = GK if i == 5 else 128
                    vt = 12 if i == 5 else qt + i
                mm = t.matmul(pl[0:1, 0:128], ones_col[0:nk, :],
                              pT_sb[0:nk, i, :], start=(i == 0), stop=(i == nkt - 1))
                mm.then_inc(c_t.sem, 1)
                mm = t.matmul(po[:, 0:128], vw_sb[0:nk, vt, hh * 128:hh * 128 + 128],
                              pT_sb[0:nk, i, :], start=(i == 0), stop=(i == nkt - 1))
                mm.then_inc(c_t.sem, 1)
            c_t.n += 2
        avd = c_t.n

        if glob:
            @blockq.vector
            def _(v, h=h):
                v.wait_ge(c_t.sem, avd)
                v.tensor_copy(oLg[:, h * GK:(h + 1) * GK], po[:, 0:GK])
                v.tensor_copy(lgf[0:1, h * GK:(h + 1) * GK],
                              pl[0:1, 0:GK]).then_inc(c_v.sem, 1)
            c_v.n += 1
            set_free(("bank", 2), (c_v, c_v.n))
            set_free(("bank", 3), (c_v, c_v.n))
            return c_v.n

        @blockq.vector
        def _(v):
            v.wait_ge(c_t.sem, avd)
            v.reciprocal(lr_sb[:], pl[0:1, 0:128]).then_inc(c_v.sem, 1)
        c_v.n += 1
        lrd = c_v.n
        set_free(("bank", 3), (c_v, lrd))

        _frep = wf(("bank", 4), ("lrep",))

        @blockq.tensor
        def _(t, _frep=_frep):
            t.wait_ge(c_v.sem, lrd)
            wdeps(t, _frep)
            t.matmul(psum_f[4][:, 0:128], ones_row[:], lr_sb[:],
                     start=True, stop=True).then_inc(c_t.sem, 1)
        c_t.n += 1
        repd = c_t.n

        lcp = c_v.n + 1

        @blockq.vector
        def _(v, h=h, lcp=lcp):
            v.wait_ge(c_t.sem, repd)
            v.tensor_copy(lrep_sb[:], psum_f[4][:, 0:128]).then_inc(c_v.sem, 1)
            v.wait_ge(c_v.sem, lcp)
            v.tensor_mul(osc[:, h, qt * 128:(qt + 1) * 128], po[:, 0:128],
                         lrep_sb[:]).then_inc(c_v.sem, 1)
        c_v.n += 2
        repc = c_v.n
        set_free(("bank", 4), (c_v, repc))
        set_free(("bank", 2), (c_v, c_v.n))
        set_free(("lrep",), (c_v, c_v.n))
        return c_v.n

    for b in range(2):
        set_free(("wat", b), (c_sy, ln1_own), (c_t, c_t.n))
    set_free(("watq",), (c_sy, ln1_own), (c_t, c_t.n))
    for w in range(4):
        wdmas = {}
        for which, win, b in (("k", wk_in, 0), ("v", wv_in, 1), ("q", wq_in, None)):
            buf = wat[b] if b is not None else watq
            key = ("wat", b) if b is not None else ("watq",)
            _fwat = wf(key)

            @blockq.gpsimd
            def _(g, w=w, win=win, buf=buf, _fwat=_fwat):
                wdeps(g, _fwat)
                g.dma_start(out=buf[:], in_=win[w]).then_inc(c_gp.sem, 16)
            c_gp.n += 16
            wdmas[which] = c_gp.n
            proj_wave(w, which, buf, c_gp.n, "own")
        for which in ("k", "v", "q"):
            buf = wat[0] if which == "k" else (wat[1] if which == "v" else watq)
            proj_wave(w, which, buf, wdmas[which], "halo")
        for key in (("wat", 0), ("wat", 1), ("watq",)):
            set_free(key, (c_t, c_t.n))
        proj_ev = c_s.n
        for hh in range(4):
            for qt in range(8):
                attend(w, hh, qt, proj_ev)
            attend(w, hh, 8, proj_ev)

    ast.close()
    lnf.close()
    attn_done_v = c_v.n
    if stop_after == "attn":
        finish()
        flush()
        return nc, st

    _gmidle = c_gp.n

    @blockq.vector
    def _(v, gmidle=_gmidle):
        v.wait_ge(c_gp.sem, gmidle)
        v.memset(gmrg[:, 0, 256:512], 0.0).then_inc(c_v.sem, 1)
    c_v.n += 1
    zpad = c_v.n

    _mgbase = c_gp.n
    _mgccw = c_cc.n + 1

    @blockq.gpsimd
    def _(g, base=_mgbase, ccw=_mgccw):
        g.wait_ge(c_v.sem, attn_done_v)
        g.wait_ge(c_v.sem, zpad)
        g.dma_start(out=ccD_i[0:128, 0:256], in_=oLg[:]).then_inc(c_gp.sem, 16)
        g.dma_start(out=ccD_i[0:128, 256:512],
                    in_=gmrg[:, 0, 256:512]).then_inc(c_gp.sem, 16)
        g.dma_start(out=ccD_i[0:1, 256:512], in_=lgf[:]).then_inc(c_gp.sem, 16)
        g.wait_ge(c_gp.sem, base + 48)
        g.collective_compute(
            "AllGather", ALU.bypass,
            replica_groups=[[0, 1, 2, 3], [4, 5, 6, 7]],
            ins=[ccD_i.opt()], outs=[ccD_o.opt()]).then_inc(c_cc.sem, 1)
        g.wait_ge(c_cc.sem, ccw)
        for sl in range(4):
            g.dma_start(out=gmrg[:, sl], in_=ccD_o[sl]).then_inc(c_gp.sem, 16)
    c_gp.n += 112
    c_cc.n += 1
    mrg_ld = c_gp.n

    def f(v):
        v.wait_ge(c_gp.sem, mrg_ld)
        v.tensor_add(gmrg[:, 0, :], gmrg[:, 0, :], gmrg[:, 1, :])
        return v.tensor_add(gmrg[:, 2, :], gmrg[:, 2, :], gmrg[:, 3, :])
    m1 = vstep(f)

    def f(v):
        v.wait_ge(c_v.sem, m1)
        return v.tensor_add(gmrg[:, 0, :], gmrg[:, 0, :], gmrg[:, 2, :])
    m2 = vstep(f)

    def f(v):
        v.wait_ge(c_v.sem, m2)
        return v.reciprocal(gmrg[0:1, 1, 256:512], gmrg[0:1, 0, 256:512])
    mrgd = vstep(f)

    gsc_dep = {}
    for hq in range(NH):
        bk = 4 + hq % 2
        pb = psum_f[bk]

        _fg5 = wf(("bank", bk))

        @blockq.tensor
        def _(t, hq=hq, pb=pb, bk=bk, _fg5=_fg5):
            t.wait_ge(c_v.sem, mrgd)
            wdeps(t, _fg5)
            t.matmul(pb[:, 0:GK], ones_row[:],
                     gmrg[0:1, 1, 256 + hq * GK:256 + (hq + 1) * GK],
                     start=True, stop=True).then_inc(c_t.sem, 1)
        c_t.n += 1
        rpd = c_t.n

        @blockq.vector
        def _(v, hq=hq, pb=pb, rpd=rpd):
            v.wait_ge(c_t.sem, rpd)
            v.tensor_mul(gsc[:, hq, :], gmrg[:, 0, hq * GK:(hq + 1) * GK],
                         pb[:, 0:GK]).then_inc(c_v.sem, 1)
        c_v.n += 1
        gsc_dep[hq] = c_v.n
        set_free(("bank", bk), (c_v, c_v.n))
    gsc_all = c_v.n

    p6 = ExitStack()
    p6e = p6.enter_context
    x2t_sb = p6e(nc.sbuf_tensor("x2t_sb", [128, NT, D], F32, **L))
    x1t_sb = p6e(nc.sbuf_tensor("x1t_sb", [128, NT, D], F32, **L))
    wo_sb = [p6e(nc.sbuf_tensor(f"wo_sb{b}", [128, DC, 128], BF16, **L))
             for b in range(2)]
    cta_sb = p6e(nc.sbuf_tensor("cta_sb", [128, T], F32, **L))
    gtmp_sb = p6e(nc.sbuf_tensor("gtmp_sb", [128, GK], F32, **L))

    for b in range(2):
        set_free(("wo", b), (c_v, gsc_all), (c_t, c_t.n))
    x1_ld = [0] * NT
    for tt in range(NT):
        @blockq.sync
        def _(sy, tt=tt):
            sy.wait_ge(c_v.sem, gsc_all)
            sy.dma_start(out=x1t_sb[:, tt, :],
                         in_=x1_scr[tt * 128:(tt + 1) * 128, :]).then_inc(c_sy.sem, 16)
        c_sy.n += 16
        x1_ld[tt] = c_sy.n

    aoev = {}
    x2d = {}
    bldd = {}
    for ddc in range(DC):
        b = ddc % 2

        _fwo = wf(("wo", b))

        @blockq.gpsimd
        def _(g, ddc=ddc, b=b, _fwo=_fwo):
            wdeps(g, _fwo)
            g.dma_start(out=wo_sb[b][:], in_=wo_in[ddc]).then_inc(c_gp.sem, 16)
        c_gp.n += 16
        wld = c_gp.n

        pa, pb2, pg16 = psum_f[0 + b], psum_f[2 + b], psum_f[4]

        _fp6 = (wf(("bank", 0 + b), ("bank", 2 + b)) if ddc < 2 else []) + \
               (wf(("bank", 4)) if ddc == 0 else [])

        @blockq.tensor
        def _(t, ddc=ddc, b=b, wld=wld, pa=pa, pb2=pb2, pg16=pg16, _fp6=_fp6):
            t.wait_ge(c_gp.sem, wld)
            t.wait_ge(c_v.sem, gsc_all)
            wdeps(t, _fp6)
            t.wait_ge(c_s.sem, aoev.get(ddc - 2, 0))
            t.wait_ge(c_v.sem, bldd.get(ddc - 1, 0))
            for th in range(2):
                dst = pa if th == 0 else pb2
                for dvc in range(DC):
                    mm = t.matmul(dst[:], wo_sb[b][:, dvc, :],
                                  osc[:, dvc, th * 512:th * 512 + 512],
                                  start=(dvc == 0), stop=(dvc == DC - 1))
                mm.then_inc(c_t.sem, 1)
            for dvc in range(DC):
                mm = t.matmul(pg16[:, 0:GK], wo_sb[b][:, dvc, :], gsc[:, dvc, :],
                              start=(dvc == 0), stop=(dvc == DC - 1))
            mm.then_inc(c_t.sem, 1)
        c_t.n += 3
        mmd = c_t.n
        set_free(("wo", b), (c_t, mmd))

        @blockq.scalar
        def _(s, pa=pa, pb2=pb2, mmd=mmd):
            s.wait_ge(c_t.sem, mmd - 1)
            s.copy(out=cta_sb[:, 0:512], in_=pa[:]).then_inc(c_s.sem, 1)
            s.copy(out=cta_sb[:, 512:1024], in_=pb2[:]).then_inc(c_s.sem, 1)
        c_s.n += 2
        ctd = c_s.n
        aoev[ddc] = c_s.n

        def f(v, ddc=ddc, pg16=pg16, ctd=ctd, mmd=mmd):
            v.wait_ge(c_s.sem, ctd)
            v.wait_ge(c_t.sem, mmd)
            return v.tensor_sub(gtmp_sb[:], pg16[:, 0:GK], cta_sb[:, 0:GK])
        b1 = vstep(f)

        def f(v, b1=b1):
            v.wait_ge(c_v.sem, b1)
            return v.tensor_mul(gtmp_sb[:], gtmp_sb[:], gbl[:])
        b2 = vstep(f)

        def f(v, b2=b2):
            v.wait_ge(c_v.sem, b2)
            return v.tensor_add(cta_sb[:, 0:GK], cta_sb[:, 0:GK], gtmp_sb[:])
        bldd[ddc] = vstep(f)
        bld = bldd[ddc]

        for tt in range(NT):
            pbt = psum_f[6 + tt % 2]
            _fx67 = wf(("bank", 6 + tt % 2)) if (ddc == 0 and tt < 2) else None

            @blockq.tensor
            def _(t, ddc=ddc, tt=tt, pbt=pbt, bld=bld, _fx67=_fx67):
                t.wait_ge(c_v.sem, bld)
                t.wait_ge(c_v.sem, x2d.get((ddc - 1, tt), 0))
                t.wait_ge(c_v.sem, x2d.get((ddc, tt - 2), 0))
                if _fx67 is not None:
                    wdeps(t, _fx67)
                t.transpose(pbt[:, 0:128], cta_sb[:, tt * 128:(tt + 1) * 128],
                            ident[:]).then_inc(c_t.sem, 1)
            c_t.n += 1
            trd = c_t.n

            @blockq.vector
            def _(v, ddc=ddc, tt=tt, pbt=pbt, trd=trd):
                v.wait_ge(c_t.sem, trd)
                v.wait_ge(c_sy.sem, x1_ld[tt])
                v.tensor_add(x2t_sb[:, tt, ddc * 128:(ddc + 1) * 128],
                             pbt[:, 0:128],
                             x1t_sb[:, tt, ddc * 128:(ddc + 1) * 128]
                             ).then_inc(c_v.sem, 1)
            c_v.n += 1
            x2d[(ddc, tt)] = c_v.n
    set_free(("bank", 6), (c_v, x2d[(DC - 1, NT - 2)]))
    set_free(("bank", 7), (c_v, x2d[(DC - 1, NT - 1)]))
    set_free(("bank", 0), (c_s, aoev[DC - 2]))
    set_free(("bank", 1), (c_s, aoev[DC - 1]))
    set_free(("bank", 2), (c_s, aoev[DC - 2]))
    set_free(("bank", 3), (c_s, aoev[DC - 1]))
    set_free(("bank", 4), (c_v, bldd[DC - 1]))

    for tt in range(NT):
        done = x2d[(DC - 1, tt)]

        @blockq.sync
        def _(sy, tt=tt, done=done):
            sy.wait_ge(c_v.sem, done)
            sy.dma_start(out=x2_scr[tt * 128:(tt + 1) * 128, :],
                         in_=x2t_sb[:, tt, :]).then_inc(c_sy.sem, 16)
        c_sy.n += 16

    p6.close()
    oscf.close()
    if stop_after == "p6":
        finish()
        flush()
        return nc, st

    emit_ffn(2)

    finish()
    flush()
    return nc, st


def _host_masks(core):
    S0 = 1024 * (core % 4)
    m01 = np.zeros((8, 2, 128, 128), np.float32)
    eb = np.zeros((8, 6, 128), np.float32)
    for qt in range(8):
        for ii, i in enumerate((0, 4)):
            kb = 128 * qt + 128 * i + np.arange(128)[:, None]
            kg = S0 - 256 + kb
            qg = S0 + 128 * qt + np.arange(128)[None, :]
            ok = (kg >= 0) & (kg < S) & ((np.abs(qg - kg) < 256) | (kg < GK))
            m01[qt, ii] = np.where(ok, 1.0, 0.0)
        for i in range(5):
            kg0 = S0 - 256 + 128 * qt + 128 * i
            valid = (kg0 >= 0) and (kg0 + 127 < S)
            eb[qt, i, :] = -STAB if valid else MSK
        eb[qt, 5, :] = MSK
        dbl = (S0 == 0) and (qt <= 2)
        eb[qt, 5, 0:GK] = MSK if dbl else -STAB
    c4 = core % 4
    hs = np.zeros(12, np.float32)
    if c4 % 2 == 0:
        hs[9] = 1.0
    else:
        hs[8] = 1.0
    if c4 == 1:
        hs[11] = 1.0
    else:
        hs[10] = 1.0
    if c4 == 1:
        hs[0] = 1.0
    elif c4 == 2:
        hs[2] = 1.0
    elif c4 == 3:
        hs[0] = 1.0
    if c4 == 0:
        hs[5] = 1.0
    elif c4 == 1:
        hs[7] = 1.0
    elif c4 == 2:
        hs[5] = 1.0
    hsel = np.broadcast_to(hs, (128, 12)).copy()
    gbl = np.full((128, GK), 1.0 if c4 == 0 else 0.0, np.float32)
    return m01, eb, hsel, gbl


def _prep_inputs(x, Wq, Wk, Wv, Wo, W_g, W_u, W_d):
    import ml_dtypes
    x = np.ascontiguousarray(np.asarray(x, np.float32))
    Wq, Wk, Wv, Wo, W_g, W_u, W_d = (
        np.asarray(w, np.float32) for w in (Wq, Wk, Wv, Wo, W_g, W_u, W_d))

    wg = np.ascontiguousarray(W_g.reshape(HC, 128, DC, 128).transpose(0, 3, 2, 1))
    wu = np.ascontiguousarray(W_u.reshape(HC, 128, DC, 128).transpose(0, 3, 2, 1))
    wd = np.ascontiguousarray(W_d.reshape(DC, 128, HC, 128).transpose(0, 3, 2, 1))
    wq = np.ascontiguousarray(Wq.reshape(DC, 128, 4, 512).transpose(2, 1, 0, 3))
    wk = np.ascontiguousarray(Wk.reshape(DC, 128, 4, 512).transpose(2, 1, 0, 3))
    wv = np.ascontiguousarray(Wv.reshape(DC, 128, 4, 512).transpose(2, 1, 0, 3))
    wo = np.ascontiguousarray(Wo.reshape(DC, 128, DC, 128).transpose(2, 1, 0, 3))
    ident = np.eye(128, dtype=np.float32)

    in_maps = []
    for core in range(8):
        b, c4 = core // 4, core % 4
        m01, eb, hsel, gbl = _host_masks(core)
        in_maps.append({
            "x": np.ascontiguousarray(x[b, c4 * T:(c4 + 1) * T, :]),
            "wg": wg, "wu": wu, "wd": wd,
            "wq": wq, "wk": wk, "wv": wv, "wo": wo,
            "m01": np.ascontiguousarray(
                m01.transpose(2, 0, 1, 3)).astype(ml_dtypes.bfloat16),
            "eb": np.ascontiguousarray(eb.transpose(2, 0, 1)),
            "hsel": hsel, "gbl": gbl, "ident": ident,
        })
    return in_maps


def kernel(L0_attn_hidden_states, L0_attn_Wq, L0_attn_Wk, L0_attn_Wv,
           L0_attn_Wo, L0_ffn_W_g, L0_ffn_W_u, L0_ffn_W_d):
    in_maps = _prep_inputs(L0_attn_hidden_states, L0_attn_Wq, L0_attn_Wk,
                           L0_attn_Wv, L0_attn_Wo, L0_ffn_W_g, L0_ffn_W_u,
                           L0_ffn_W_d)
    nc, st = build(STOP_AFTER)
    st.close()
    res = run_bass_kernel_spmd(nc, in_maps, core_ids=list(range(8)))
    out = np.empty((2, S, D), np.float32)
    for core in range(8):
        b, c4 = core // 4, core % 4
        out[b, c4 * T:(c4 + 1) * T, :] = res.results[core]["out"]
    return out


# version 16
# speedup vs baseline: 1.3554x; 1.0134x over previous
import numpy as np
from contextlib import ExitStack

import concourse.bass as bass
import concourse.mybir as mybir
from concourse.bass_utils import run_bass_kernel_spmd

try:
    import jax
    jax.config.update("jax_compilation_cache_dir", "/tmp/jaxcache")
    jax.config.update("jax_persistent_cache_min_entry_size_bytes", -1)
    jax.config.update("jax_persistent_cache_min_compile_time_secs", 0.0)
except Exception:
    pass

F32 = mybir.dt.float32
BF16 = mybir.dt.bfloat16
AF = mybir.ActivationFunctionType
ALU = mybir.AluOpType
AXX = mybir.AxisListType.X

T = 1024
NT = 8
D = 2048
DC = 16
H = 8192
HC = 64
NH = 16
S = 4096
HALO = 256
GK = 16
KW = 1664
KT = 13
OWN0 = 256
GCOL = 1536
STAB = 16.0
MSK = -1.0e30
EPS1 = 1e-5
EPS2 = 1e-8
SCL = float(1.0 / np.sqrt(np.float32(128.0)))
INV_D = 1.0 / 2048.0
BIGS = 1.0e38

TMAP = (0, 1, 6, 7, 2, 3, 4, 5)

STOP_AFTER = None
DEBUG_SCR = False


def build(stop_after=None):
    nc = bass.Bass("TRN2", target_bir_lowering=False, detect_race_conditions=False)
    st = ExitStack()
    e = st.enter_context

    def inp(name, shape, dt=F32):
        return nc.dram_tensor(name, list(shape), dt, kind="ExternalInput").ap()

    x_in = inp("x", (T, D))
    wg_in = inp("wg", (HC, 128, DC, 128))
    wu_in = inp("wu", (HC, 128, DC, 128))
    wd_in = inp("wd", (DC, 128, HC, 128))
    wq_in = inp("wq", (4, 128, DC, 512))
    wk_in = inp("wk", (4, 128, DC, 512))
    wv_in = inp("wv", (4, 128, DC, 512))
    wo_in = inp("wo", (DC, 128, DC, 128))
    m01_in = inp("m01", (128, 8, 2, 128), BF16)
    eb_in = inp("eb", (128, 8, 6))
    hsel_in = inp("hsel", (128, 12))
    gbl_in = inp("gbl", (128, 16))
    id_in = inp("ident", (128, 128))

    out_t = nc.dram_tensor("out", [T, D], F32, kind="ExternalOutput").ap()

    def scr(name, shape, dt=F32):
        kind = "ExternalOutput" if DEBUG_SCR else "Internal"
        return nc.dram_tensor(name, list(shape), dt, kind=kind).ap()

    x1_scr = scr("x1_scr", (T, D))
    x2_scr = scr("x2_scr", (T, D))
    ln1_scr = scr("ln1_scr", (NT, 128, DC, 128), BF16)
    sgn_g = scr("sgn_g", (HC, 128, DC, 128), BF16)
    sgn_u = scr("sgn_u", (HC, 128, DC, 128), BF16)
    sgn_d = scr("sgn_d", (DC, 128, HC, 128), BF16)
    ccA_i = scr("ccA_i", (128, DC, HALO), BF16)
    ccA_o = scr("ccA_o", (2, 128, DC, HALO), BF16)
    ccB_i = scr("ccB_i", (128, DC, HALO), BF16)
    ccB_o = scr("ccB_o", (2, 128, DC, HALO), BF16)
    ccC_i = scr("ccC_i", (128, DC, GK), BF16)
    ccC_o = scr("ccC_o", (4, 128, DC, GK), BF16)
    ccD_i = scr("ccD_i", (128, 512))
    ccD_o = scr("ccD_o", (4, 128, 512))

    class Ctr:
        def __init__(self, name):
            self.sem = e(nc.semaphore(name))
            self.n = 0

    c_sy = Ctr("c_sy")
    c_gp = Ctr("c_gp")
    c_t = Ctr("c_t")
    c_v = Ctr("c_v")
    c_s = Ctr("c_s")
    c_cc = Ctr("c_cc")

    R = dict(side="right")
    ident = e(nc.sbuf_tensor("identS", [128, 128], F32, **R))
    m01 = e(nc.sbuf_tensor("m01S", [128, 8, 2, 128], BF16, **R))
    ebias = e(nc.sbuf_tensor("ebiasS", [128, 8, 6], F32, **R))
    hsel = e(nc.sbuf_tensor("hselS", [128, 12], F32, **R))
    gbl = e(nc.sbuf_tensor("gblS", [128, 16], F32, **R))
    ones_col = e(nc.sbuf_tensor("ones_col", [128, 1], BF16, **R))
    ones_row = e(nc.sbuf_tensor("ones_row", [1, 128], F32, **R))
    neg16 = e(nc.sbuf_tensor("neg16", [128, 1], F32, **R))
    stats = e(nc.sbuf_tensor("stats", [128, 32], F32, **R))
    bnscr = e(nc.sbuf_tensor("bnscr", [128, 48], F32, **R))
    gsc = e(nc.sbuf_tensor("gsc", [128, DC, GK], BF16, **R))

    block = e(nc.Block())

    pend = {k: [] for k in ("tensor", "vector", "scalar", "gpsimd", "sync")}

    class _DQ:
        def __getattr__(self, eng):
            def dec(fn):
                pend[eng].append(fn)
                return fn
            return dec

    blockq = _DQ()

    def flush():
        for eng in ("tensor", "vector", "scalar", "gpsimd", "sync"):
            fns = pend[eng]
            pend[eng] = []
            if fns:
                getattr(block, eng)(lambda e, fns=fns: [f(e) for f in fns])

    def wdeps(eng, deps):
        for c_, v_ in deps:
            eng.wait_ge(c_.sem, v_)

    psum_f = [e(nc.psum_tensor(f"pb{i}", [128, 512], F32)) for i in range(8)]

    @blockq.sync
    def _(sy):
        for dst, src in ((ident, id_in), (m01, m01_in), (ebias, eb_in),
                         (hsel, hsel_in), (gbl, gbl_in)):
            sy.dma_start(out=dst[:], in_=src).then_inc(c_sy.sem, 16)
    c_sy.n += 5 * 16
    consts_ld = c_sy.n

    @blockq.vector
    def _(v):
        v.memset(ones_col[:], 1.0)
        v.memset(ones_row[:], 1.0)
        v.memset(neg16[:], -STAB).then_inc(c_v.sem, 1)
    c_v.n += 1
    consts_v = c_v.n

    @blockq.scalar
    def _(s):
        s.wait_ge(c_v.sem, consts_v)
        s.wait_ge(c_sy.sem, consts_ld)

    @blockq.tensor
    def _(t):
        t.wait_ge(c_v.sem, consts_v)
        t.wait_ge(c_sy.sem, consts_ld)

    wfree = {}

    def wait_free(engine, key):
        for ctr, val in wfree.get(key, ()):
            engine.wait_ge(ctr.sem, val)

    def wf(*keys):
        out = []
        for k in keys:
            out.extend(wfree.get(k, ()))
        return out

    def set_free(key, *deps):
        wfree[key] = list(deps)


    def finish():
        @blockq.sync
        def _(sy):
            sy.wait_ge(c_sy.sem, c_sy.n)
        @blockq.gpsimd
        def _(g):
            g.wait_ge(c_gp.sem, c_gp.n)

    def vstep(fn):
        @blockq.vector
        def _(v):
            fn(v).then_inc(c_v.sem, 1)
        c_v.n += 1
        return c_v.n

    def gen_ln(src, dst, eps, so=0, bo=0, src_s=None, src_v=None, src_sy=None,
               dst_t=None, out=None):
        def _w(eng):
            if src_s is not None:
                eng.wait_ge(c_s.sem, src_s)
            if src_v is not None:
                eng.wait_ge(c_v.sem, src_v)
            if src_sy is not None:
                eng.wait_ge(c_sy.sem, src_sy)

        def f(v):
            _w(v)
            for k in range(4):
                r = v.bn_stats(bnscr[:, bo + 6 * k:bo + 6 * k + 6],
                               src[:, 512 * k:512 * k + 512])
            return r
        va = vstep(f)
        yield

        def f(v):
            v.wait_ge(c_v.sem, va)
            return v.bn_aggr(stats[:, so:so + 2], bnscr[:, bo:bo + 24])
        vb = vstep(f)
        yield

        def f(v):
            v.wait_ge(c_v.sem, vb)
            return v.tensor_scalar_add(stats[:, so + 6:so + 7],
                                       stats[:, so + 1:so + 2], float(eps))
        ve = vstep(f)
        yield

        @blockq.scalar
        def _(s):
            s.wait_ge(c_v.sem, ve)
            s.activation(out=stats[:, so + 7:so + 8], in_=stats[:, so + 6:so + 7],
                         func=AF.Sqrt).then_inc(c_s.sem, 1)
        c_s.n += 1
        sddone = c_s.n
        yield

        def f(v):
            v.wait_ge(c_s.sem, sddone)
            return v.reciprocal(stats[:, so + 8:so + 9], stats[:, so + 7:so + 8])
        vf = vstep(f)
        yield

        def f(v):
            v.wait_ge(c_v.sem, vf)
            return v.scalar_tensor_tensor(stats[:, so + 9:so + 10],
                                          stats[:, so:so + 1], -1.0,
                                          stats[:, so + 8:so + 9],
                                          ALU.mult, ALU.mult)
        vg = vstep(f)
        yield

        def f(v):
            v.wait_ge(c_v.sem, vg)
            if dst_t is not None:
                v.wait_ge(c_t.sem, dst_t)
            return v.tensor_scalar(dst, src, stats[:, so + 8:so + 9],
                                   stats[:, so + 9:so + 10], ALU.mult, ALU.add)
        r = vstep(f)
        if out is not None:
            out.append(r)

    def emit_ln(src, dst, eps, tmp_sq=None, src_s=None, src_v=None,
                src_sy=None, dst_t=None):
        out = []
        for _ in gen_ln(src, dst, eps, src_s=src_s, src_v=src_v,
                        src_sy=src_sy, dst_t=dst_t, out=out):
            pass
        return out[0]

    def gen_quant(z, y, a, so, src_v, out=None):
        MAGIC = 12582912.0

        def f(v):
            v.wait_ge(c_v.sem, src_v)
            return v.tensor_reduce(stats[:, so + 10:so + 11], z, axis=AXX,
                                   op=ALU.max, apply_absolute_value=True)
        qa = vstep(f)
        yield

        def f(v):
            v.wait_ge(c_v.sem, qa)
            return v.reciprocal(stats[:, so + 11:so + 12], stats[:, so + 10:so + 11])
        qb = vstep(f)
        yield

        def f(v):
            v.wait_ge(c_v.sem, qb)
            v.tensor_scalar_mul(stats[:, so + 12:so + 13], stats[:, so + 11:so + 12], 127.0)
            return v.tensor_scalar_mul(stats[:, so + 13:so + 14],
                                       stats[:, so + 10:so + 11], 1.0 / 127.0)
        qc = vstep(f)
        yield

        def f(v):
            v.wait_ge(c_v.sem, qc)
            v.tensor_scalar_mul(y, z, stats[:, so + 12:so + 13])
            v.tensor_scalar(a, y, MAGIC, -MAGIC, ALU.add, ALU.add)
            v.tensor_scalar(a, a, 127.0, -128.0, ALU.min, ALU.max)
            return v.tensor_scalar_mul(a, a, stats[:, so + 13:so + 14])
        r = vstep(f)
        if out is not None:
            out.append(r)

    def emit_transposes(dst_fn, src, n, src_v=None, src_s=None, src_sy=None):
        sready = c_s.n
        last_ev = {0: None, 1: None}
        for i in range(n):
            bk = 6 + (i % 2)
            pb = psum_f[bk]
            _fdv = wf(("bank", bk)) if i < 2 else None

            @blockq.tensor
            def _(t, i=i, pb=pb, bk=bk, _fd=_fdv):
                if src_v is not None:
                    t.wait_ge(c_v.sem, src_v)
                if src_s is not None:
                    t.wait_ge(c_s.sem, src_s)
                if src_sy is not None:
                    t.wait_ge(c_sy.sem, src_sy)
                if _fd is not None:
                    wdeps(t, _fd)
                else:
                    t.wait_ge(c_s.sem, sready + i - 1)
                t.transpose(pb[:, 0:128], src[:, i * 128:(i + 1) * 128],
                            ident[:]).then_inc(c_t.sem, 1)
            c_t.n += 1
            trd = c_t.n

            @blockq.scalar
            def _(s, i=i, pb=pb, trd=trd):
                s.wait_ge(c_t.sem, trd)
                s.copy(out=dst_fn(i), in_=pb[:, 0:128]).then_inc(c_s.sem, 1)
            c_s.n += 1
            last_ev[i % 2] = c_s.n
        for par in (0, 1):
            if last_ev[par] is not None:
                set_free(("bank", 6 + par), (c_s, last_ev[par]))
        return c_s.n

    L = dict(side="left")

    def emit_entry(f, src_dram):
        sub = ExitStack()
        Ab = [sub.enter_context(nc.sbuf_tensor(f"{f['tag']}A{p}", [128, D], F32, **L))
              for p in range(2)]
        Bb = [sub.enter_context(nc.sbuf_tensor(f"{f['tag']}B{p}", [128, D], F32, **L))
              for p in range(2)]
        buf_free = [0, 0]

        def tile_gen(lt):
            p = lt % 2
            A, B = Ab[p], Bb[p]
            rt = TMAP[lt]

            @blockq.sync
            def _(sy, free_s=buf_free[p], rt=rt):
                sy.wait_ge(c_s.sem, free_s)
                sy.dma_start(out=A[:], in_=src_dram[rt * 128:(rt + 1) * 128, :]
                             ).then_inc(c_sy.sem, 16)
            c_sy.n += 16
            ld = c_sy.n
            yield

            o1 = []
            yield from gen_ln(A[:], B[:], EPS1, so=16 * p, bo=24 * p,
                              src_sy=ld, out=o1)
            o2 = []
            yield from gen_ln(B[:], A[:], EPS2, so=16 * p, bo=24 * p,
                              src_v=o1[0], out=o2)
            o3 = []
            yield from gen_quant(A[:], B[:], A[:], 16 * p, o2[0], out=o3)
            buf_free[p] = emit_transposes(
                lambda dc, lt=lt: f["xq"][:, dc, lt * 128:(lt + 1) * 128],
                A[:], DC, src_v=o3[0])

        gens = [tile_gen(0), tile_gen(1)]
        nxt = 2
        while gens:
            done = []
            for g in gens:
                if next(g, "END") == "END":
                    done.append(g)
            for g in done:
                gens.remove(g)
                if nxt < NT:
                    gens.append(tile_gen(nxt))
                    nxt += 1
        sub.close()
        return c_s.n

    def emit_stage12(f, grp, first_pass, xq_rdy):
        t0 = grp * 512
        mm12 = {}
        pmul = {}
        for jc in range(HC):
            b = jc % 2

            _fgu = wf(("gu", b))

            @blockq.gpsimd
            def _(g, jc=jc, b=b, _fgu=_fgu):
                wdeps(g, _fgu)
                if first_pass:
                    g.dma_start(out=f["wg"][b][:], in_=wg_in[jc]).then_inc(c_gp.sem, 16)
                    g.dma_start(out=f["wu"][b][:], in_=wu_in[jc]).then_inc(c_gp.sem, 16)
                else:
                    g.dma_start(out=f["wg"][b][:], in_=sgn_g[jc]).then_inc(c_gp.sem, 16)
                    g.dma_start(out=f["wu"][b][:], in_=sgn_u[jc]).then_inc(c_gp.sem, 16)
            c_gp.n += 32
            wld = c_gp.n

            if first_pass:
                @blockq.vector
                def _(v, b=b, wld=wld):
                    v.wait_ge(c_gp.sem, wld)
                    v.tensor_scalar(f["wg"][b][:], f["wg"][b][:], BIGS, None, ALU.mult)
                    v.tensor_scalar(f["wg"][b][:], f["wg"][b][:], -1.0, 1.0,
                                    ALU.max, ALU.min).then_inc(c_v.sem, 1)
                c_v.n += 1
                gsd = c_v.n

                @blockq.vector
                def _(v, b=b, wld=wld):
                    v.wait_ge(c_gp.sem, wld)
                    v.tensor_scalar(f["wu"][b][:], f["wu"][b][:], BIGS, None, ALU.mult)
                    v.tensor_scalar(f["wu"][b][:], f["wu"][b][:], -1.0, 1.0,
                                    ALU.max, ALU.min).then_inc(c_v.sem, 1)
                c_v.n += 1
                usd = c_v.n

                @blockq.sync
                def _(sy, jc=jc, b=b, gsd=gsd, usd=usd):
                    sy.wait_ge(c_v.sem, usd)
                    sy.dma_start(out=sgn_g[jc], in_=f["wg"][b][:]).then_inc(c_sy.sem, 16)
                    sy.dma_start(out=sgn_u[jc], in_=f["wu"][b][:]).then_inc(c_sy.sem, 16)
                c_sy.n += 32
                scr_wr = c_sy.n
            else:
                gsd = usd = scr_wr = None

            pg, pu = psum_f[0 + 2 * b], psum_f[1 + 2 * b]
            _fbk = wf(("bank", 0 + 2 * b), ("bank", 1 + 2 * b)) if jc < 2 else None

            @blockq.tensor
            def _(t, jc=jc, b=b, wld=wld, gsd=gsd, usd=usd, pg=pg, pu=pu,
                  _fbk=_fbk):
                if gsd is not None:
                    t.wait_ge(c_v.sem, usd)
                else:
                    t.wait_ge(c_gp.sem, wld)
                if jc == 0:
                    t.wait_ge(c_s.sem, xq_rdy)
                if jc >= 2:
                    t.wait_ge(c_v.sem, pmul[jc - 2])
                else:
                    wdeps(t, _fbk)
                for dc in range(DC):
                    mm = t.matmul(pg[:], f["wg"][b][:, dc, :],
                                  f["xq"][:, dc, t0:t0 + 512],
                                  start=(dc == 0), stop=(dc == DC - 1))
                mm.then_inc(c_t.sem, 1)
                for dc in range(DC):
                    mm = t.matmul(pu[:], f["wu"][b][:, dc, :],
                                  f["xq"][:, dc, t0:t0 + 512],
                                  start=(dc == 0), stop=(dc == DC - 1))
                mm.then_inc(c_t.sem, 1)
            c_t.n += 2
            mm12[jc] = c_t.n
            deps = [(c_t, mm12[jc])]
            if first_pass:
                deps.append((c_sy, scr_wr))
            set_free(("gu", b), *deps)

            @blockq.scalar
            def _(s, pg=pg, mmd=mm12[jc]):
                s.wait_ge(c_t.sem, mmd - 1)
                s.activation(out=f["g"][:], in_=pg[:], func=AF.Silu).then_inc(c_s.sem, 1)
            c_s.n += 1
            evd = c_s.n

            ucp = c_v.n + 1

            @blockq.vector
            def _(v, jc=jc, pu=pu, evd=evd, mmd=mm12[jc], ucp=ucp):
                v.wait_ge(c_t.sem, mmd)
                v.tensor_copy(f["u"][:], pu[:]).then_inc(c_v.sem, 1)
                v.wait_ge(c_v.sem, ucp)
                v.wait_ge(c_s.sem, evd)
                v.tensor_mul(f["p"][:, jc, :], f["g"][:], f["u"][:]).then_inc(c_v.sem, 1)
            c_v.n += 2
            pmul[jc] = c_v.n
        set_free(("bank", 0), (c_s, c_s.n), (c_v, c_v.n))
        set_free(("bank", 1), (c_s, c_s.n), (c_v, c_v.n))
        set_free(("bank", 2), (c_s, c_s.n), (c_v, c_v.n))
        set_free(("bank", 3), (c_s, c_s.n), (c_v, c_v.n))
        return pmul[HC - 1]

    def emit_stage3(f, app, grp, first_pass, src_res, dst_res, p_rdy):
        t0 = grp * 512
        x1rd = [0] * 4
        barrier_s = c_s.n
        barrier_v = c_v.n
        for tt in range(4):
            @blockq.sync
            def _(sy, tt=tt):
                sy.wait_ge(c_s.sem, barrier_s)
                sy.wait_ge(c_v.sem, barrier_v)
                r0 = TMAP[grp * 4 + tt] * 128
                sy.dma_start(out=f["x1g"][:, tt, :],
                             in_=src_res[r0:r0 + 128, :]).then_inc(c_sy.sem, 16)
            c_sy.n += 16
            x1rd[tt] = c_sy.n

        s3ev = {}
        xadd = {}
        for ddc in range(DC):
            b = ddc % 2

            _fwd = wf(("wd", b))

            @blockq.gpsimd
            def _(g, ddc=ddc, b=b, _fwd=_fwd):
                wdeps(g, _fwd)
                if first_pass:
                    g.dma_start(out=f["wd"][b][:], in_=wd_in[ddc]).then_inc(c_gp.sem, 16)
                else:
                    g.dma_start(out=f["wd"][b][:], in_=sgn_d[ddc]).then_inc(c_gp.sem, 16)
            c_gp.n += 16
            wld = c_gp.n

            if first_pass:
                @blockq.vector
                def _(v, b=b, wld=wld):
                    v.wait_ge(c_gp.sem, wld)
                    v.tensor_scalar(f["wd"][b][:], f["wd"][b][:],
                                    BIGS, None, ALU.mult)
                    v.tensor_scalar(f["wd"][b][:], f["wd"][b][:],
                                    -1.0, 1.0, ALU.max, ALU.min).then_inc(c_v.sem, 1)
                c_v.n += 1
                dsv = c_v.n
                dss = None

                @blockq.sync
                def _(sy, ddc=ddc, b=b, dsv=dsv):
                    sy.wait_ge(c_v.sem, dsv)
                    sy.dma_start(out=sgn_d[ddc], in_=f["wd"][b][:]).then_inc(c_sy.sem, 16)
                c_sy.n += 16
                scr3_wr = c_sy.n
            else:
                dsv = dss = scr3_wr = None

            po = psum_f[0 + b]
            _fb3 = wf(("bank", b)) if ddc < 2 else None

            @blockq.tensor
            def _(t, ddc=ddc, b=b, wld=wld, dsv=dsv, po=po, _fb3=_fb3):
                if dsv is not None:
                    t.wait_ge(c_v.sem, dsv)
                else:
                    t.wait_ge(c_gp.sem, wld)
                if ddc == 0:
                    t.wait_ge(c_v.sem, p_rdy)
                if ddc >= 2:
                    t.wait_ge(c_s.sem, s3ev[ddc - 2])
                else:
                    wdeps(t, _fb3)
                for jc in range(HC):
                    mm = t.matmul(po[:], f["wd"][b][:, jc, :], f["p"][:, jc, :],
                                  start=(jc == 0), stop=(jc == HC - 1))
                mm.then_inc(c_t.sem, 1)
            c_t.n += 1
            mmd = c_t.n
            deps = [(c_t, mmd)]
            if first_pass:
                deps.append((c_sy, scr3_wr))
            set_free(("wd", b), *deps)

            @blockq.scalar
            def _(s, po=po, mmd=mmd):
                s.wait_ge(c_t.sem, mmd)
                s.mul(out=f["ct"][:], in_=po[:], mul=0.5).then_inc(c_s.sem, 1)
            c_s.n += 1
            s3ev[ddc] = c_s.n
            ctd = c_s.n

            for tt in range(4):
                pb = psum_f[6 + (tt % 2)]
                _fb67 = wf(("bank", 6 + tt % 2)) if (ddc == 0 and tt < 2) else None

                @blockq.tensor
                def _(t, ddc=ddc, tt=tt, pb=pb, ctd=ctd, _fb67=_fb67):
                    t.wait_ge(c_s.sem, ctd)
                    t.wait_ge(c_v.sem, xadd.get((ddc - 1, tt), 0))
                    t.wait_ge(c_v.sem, xadd.get((ddc, tt - 2), 0))
                    if _fb67 is not None:
                        wdeps(t, _fb67)
                    t.transpose(pb[:, 0:128], f["ct"][:, tt * 128:(tt + 1) * 128],
                                ident[:]).then_inc(c_t.sem, 1)
                c_t.n += 1
                trd = c_t.n

                @blockq.vector
                def _(v, ddc=ddc, tt=tt, pb=pb, trd=trd):
                    v.wait_ge(c_t.sem, trd)
                    v.wait_ge(c_sy.sem, x1rd[tt])
                    v.tensor_add(f["x1g"][:, tt, ddc * 128:(ddc + 1) * 128],
                                 pb[:, 0:128],
                                 f["x1g"][:, tt, ddc * 128:(ddc + 1) * 128]
                                 ).then_inc(c_v.sem, 1)
                c_v.n += 1
                xadd[(ddc, tt)] = c_v.n
        set_free(("bank", 6), (c_v, xadd[(DC - 1, 2)]))
        set_free(("bank", 7), (c_v, xadd[(DC - 1, 3)]))
        set_free(("bank", 0), (c_s, s3ev[DC - 2]))
        set_free(("bank", 1), (c_s, s3ev[DC - 1]))

        for tt in range(4):
            done = xadd[(DC - 1, tt)]

            @blockq.sync
            def _(sy, tt=tt, done=done):
                sy.wait_ge(c_v.sem, done)
                r0 = TMAP[grp * 4 + tt] * 128
                sy.dma_start(out=dst_res[r0:r0 + 128, :],
                             in_=f["x1g"][:, tt, :]).then_inc(c_sy.sem, 16)
            c_sy.n += 16

            if app == 1:
                lnd = emit_ln(f["x1g"][:, tt, :], f["B"][:], EPS1,
                              src_v=done, dst_t=c_t.n)
                lnstg_wr = wfree.get(("lnstg",))
                emit_transposes(
                    lambda dc: f["lnstg"][:, dc, :], f["B"][:], DC, src_v=lnd,
                    src_sy=(lnstg_wr[0][1] if lnstg_wr else None))
                gt = TMAP[grp * 4 + tt]

                @blockq.sync
                def _(sy, gt=gt, lst=c_s.n):
                    sy.wait_ge(c_s.sem, lst)
                    sy.dma_start(out=ln1_scr[gt], in_=f["lnstg"][:]
                                 ).then_inc(c_sy.sem, 16)
                c_sy.n += 16
                set_free(("lnstg",), (c_sy, c_sy.n))

    def emit_ffn(app):
        fs = ExitStack()
        fe = fs.enter_context
        tag = f"f{app}"
        f = {"tag": tag}
        f["p"] = fe(nc.sbuf_tensor(tag + "p", [128, HC, 512], BF16, **L))
        f["xq"] = fe(nc.sbuf_tensor(tag + "xq", [128, DC, T], BF16, **L))
        f["B"] = fe(nc.sbuf_tensor(tag + "B", [128, D], F32, **L))
        f["lnstg"] = fe(nc.sbuf_tensor(tag + "lnstg", [128, DC, 128], BF16, **L))
        f["wg"] = [fe(nc.sbuf_tensor(f"{tag}wg{b}", [128, DC, 128], BF16, **L))
                   for b in range(2)]
        f["wu"] = [fe(nc.sbuf_tensor(f"{tag}wu{b}", [128, DC, 128], BF16, **L))
                   for b in range(2)]
        f["g"] = fe(nc.sbuf_tensor(tag + "g", [128, 512], BF16, **L))
        f["u"] = fe(nc.sbuf_tensor(tag + "u", [128, 512], BF16, **L))
        f["ct"] = fe(nc.sbuf_tensor(tag + "ct", [128, 512], F32, **L))
        f["x1g"] = fe(nc.sbuf_tensor(tag + "x1g", [128, 4, D], F32, **L))

        src = x_in if app == 1 else x2_scr
        dst = x1_scr if app == 1 else out_t
        if DEBUG_SCR and app == 1:
            xq_dbg = scr("xq_dbg", (128, DC, T), BF16)
            st_dbg = scr("st_dbg", (128, 16))
            id_dbg = scr("id_dbg", (128, 128))
        for b in range(2):
            set_free(("gu", b), (c_v, c_v.n), (c_t, c_t.n))
            set_free(("wd", b), (c_v, c_v.n), (c_t, c_t.n))

        def exchange_send():
            wd0, wd1 = f["wd"][0], f["wd"][1]
            eready_s = c_sy.n
            fdw = wf(("wd", 0)) + wf(("wd", 1))

            @blockq.gpsimd
            def _(g, eready_s=eready_s, fdw=fdw):
                wdeps(g, fdw)
                g.wait_ge(c_sy.sem, eready_s)
                g.dma_start(out=wd0[:, 32:48, :], in_=ln1_scr[0]).then_inc(c_gp.sem, 16)
                g.dma_start(out=wd0[:, 48:64, :], in_=ln1_scr[1]).then_inc(c_gp.sem, 16)
                g.dma_start(out=wd1[:, 32:48, :], in_=ln1_scr[6]).then_inc(c_gp.sem, 16)
                g.dma_start(out=wd1[:, 48:64, :], in_=ln1_scr[7]).then_inc(c_gp.sem, 16)
            c_gp.n += 64
            eld = c_gp.n

            @blockq.vector
            def _(v, eld=eld):
                v.wait_ge(c_gp.sem, eld)
                v.wait_ge(c_sy.sem, consts_ld)
                for k in range(2):
                    eL = wd0[:, 32 + 16 * k:48 + 16 * k, :]
                    eR = wd1[:, 32 + 16 * k:48 + 16 * k, :]
                    sA = wd0[:, 16 * k:16 * k + 16, :]
                    sB = wd1[:, 16 * k:16 * k + 16, :]
                    v.tensor_scalar_mul(sA, eL, hsel[:, 8:9])
                    v.scalar_tensor_tensor(sA, eR, hsel[:, 9:10], sA,
                                           ALU.mult, ALU.add)
                    v.tensor_scalar_mul(sB, eL, hsel[:, 10:11])
                    r = v.scalar_tensor_tensor(sB, eR, hsel[:, 11:12], sB,
                                               ALU.mult, ALU.add)
                r.then_inc(c_v.sem, 1)
            c_v.n += 1
            bld = c_v.n

            @blockq.gpsimd
            def _(g, bld=bld):
                g.wait_ge(c_v.sem, bld)
                g.dma_start(out=ccA_i[:, :, 0:128], in_=wd0[:, 0:16, :]).then_inc(c_gp.sem, 16)
                g.dma_start(out=ccA_i[:, :, 128:256], in_=wd0[:, 16:32, :]).then_inc(c_gp.sem, 16)
                g.dma_start(out=ccB_i[:, :, 0:128], in_=wd1[:, 0:16, :]).then_inc(c_gp.sem, 16)
                g.dma_start(out=ccB_i[:, :, 128:256], in_=wd1[:, 16:32, :]).then_inc(c_gp.sem, 16)
                g.dma_start(out=ccC_i, in_=ln1_scr[0][:, :, 0:16]).then_inc(c_gp.sem, 16)
            c_gp.n += 80
            stg = c_gp.n

            @blockq.gpsimd
            def _(g, stg=stg):
                g.wait_ge(c_gp.sem, stg)
                g.collective_compute(
                    "AllGather", ALU.bypass,
                    replica_groups=[[0, 1], [2, 3], [4, 5], [6, 7]],
                    ins=[ccA_i.opt()], outs=[ccA_o.opt()]).then_inc(c_cc.sem, 1)
                g.collective_compute(
                    "AllGather", ALU.bypass,
                    replica_groups=[[0, 3], [1, 2], [4, 7], [5, 6]],
                    ins=[ccB_i.opt()], outs=[ccB_o.opt()]).then_inc(c_cc.sem, 1)
                g.collective_compute(
                    "AllGather", ALU.bypass,
                    replica_groups=[[0, 1, 2, 3], [4, 5, 6, 7]],
                    ins=[ccC_i.opt()], outs=[ccC_o.opt()]).then_inc(c_cc.sem, 1)
            c_cc.n += 3
            set_free(("wd", 0), (c_gp, stg))
            set_free(("wd", 1), (c_gp, stg))

        xq_rdy = emit_entry(f, src)
        if DEBUG_SCR and app == 1:
            @blockq.sync
            def _(sy):
                sy.wait_ge(c_s.sem, xq_rdy)
                sy.dma_start(out=xq_dbg, in_=f["xq"][:]).then_inc(c_sy.sem, 16)
                sy.dma_start(out=st_dbg, in_=stats[:]).then_inc(c_sy.sem, 16)
                sy.dma_start(out=id_dbg, in_=ident[:]).then_inc(c_sy.sem, 16)
            c_sy.n += 48

        sub = ExitStack()
        f["wd"] = [sub.enter_context(
            nc.sbuf_tensor(f"{tag}wd{b}", [128, HC, 128], BF16, **L))
            for b in range(2)]
        for grp in (0, 1):
            first = (app == 1 and grp == 0)
            p_rdy = emit_stage12(f, grp, first, xq_rdy)
            emit_stage3(f, app, grp, first, src_res=src, dst_res=dst, p_rdy=p_rdy)
            if app == 1 and grp == 1 and stop_after != "ffn1":
                exchange_send()
        sub.close()
        fs.close()

    emit_ffn(1)
    if stop_after == "ffn1":
        finish()
        flush()
        return nc, st

    oscf = ExitStack()
    osc = oscf.enter_context(nc.sbuf_tensor("osc", [128, NH, T], BF16, **L))
    oLg = oscf.enter_context(nc.sbuf_tensor("oLg", [128, DC * GK], F32, **L))
    lgf = oscf.enter_context(nc.sbuf_tensor("lgf", [1, NH * GK], F32, **L))
    gmrg = oscf.enter_context(nc.sbuf_tensor("gmrg", [128, 4, 512], F32, **L))
    lnf = ExitStack()
    ln1T = lnf.enter_context(nc.sbuf_tensor("ln1T", [128, DC, KW], BF16, **L))

    @blockq.vector
    def _(v):
        v.memset(ln1T[:, :, 1552:1664], 0.0).then_inc(c_v.sem, 1)
    c_v.n += 1

    @blockq.sync
    def _(sy):
        for tt in range(NT):
            sy.dma_start(out=ln1T[:, :, OWN0 + tt * 128:OWN0 + (tt + 1) * 128],
                         in_=ln1_scr[tt]).then_inc(c_sy.sem, 16)
    c_sy.n += NT * 16
    ln1_own = c_sy.n

    exs = ExitStack()
    xe = exs.enter_context
    exO = xe(nc.sbuf_tensor("exO", [128, 2, DC, HALO], BF16, **L))
    exO2 = xe(nc.sbuf_tensor("exO2", [128, 2, DC, HALO], BF16, **L))
    exC = xe(nc.sbuf_tensor("exC", [128, 4, DC, GK], BF16, **L))

    _exccw = c_cc.n

    @blockq.gpsimd
    def _(g, ccw=_exccw):
        g.wait_ge(c_cc.sem, ccw)
        g.wait_ge(c_sy.sem, ln1_own)
        for sl in range(2):
            g.dma_start(out=exO[:, sl], in_=ccA_o[sl]).then_inc(c_gp.sem, 16)
            g.dma_start(out=exO2[:, sl], in_=ccB_o[sl]).then_inc(c_gp.sem, 16)
        g.dma_start(out=exC[:, 0], in_=ccC_o[0]).then_inc(c_gp.sem, 16)
    c_gp.n += 80
    exld = c_gp.n

    @blockq.vector
    def _(v):
        v.wait_ge(c_gp.sem, exld)
        v.wait_ge(c_sy.sem, ln1_own)
        hl = ln1T[:, :, 0:HALO]
        v.tensor_scalar_mul(hl, exO[:, 0], hsel[:, 0:1])
        v.scalar_tensor_tensor(hl, exO[:, 1], hsel[:, 1:2], hl, ALU.mult, ALU.add)
        v.scalar_tensor_tensor(hl, exO2[:, 0], hsel[:, 2:3], hl, ALU.mult, ALU.add)
        v.scalar_tensor_tensor(hl, exO2[:, 1], hsel[:, 3:4], hl, ALU.mult, ALU.add)
        hr = ln1T[:, :, OWN0 + T:OWN0 + T + HALO]
        v.tensor_scalar_mul(hr, exO[:, 0], hsel[:, 4:5])
        v.scalar_tensor_tensor(hr, exO[:, 1], hsel[:, 5:6], hr, ALU.mult, ALU.add)
        v.scalar_tensor_tensor(hr, exO2[:, 0], hsel[:, 6:7], hr, ALU.mult, ALU.add)
        v.scalar_tensor_tensor(hr, exO2[:, 1], hsel[:, 7:8], hr, ALU.mult, ALU.add)
        v.tensor_copy(ln1T[:, :, GCOL:GCOL + GK], exC[:, 0]).then_inc(c_v.sem, 1)
    c_v.n += 1
    halo_done = c_v.n

    exs.close()
    if stop_after == "exchange":
        finish()
        flush()
        return nc, st

    ast = ExitStack()
    ae = ast.enter_context
    kw_sb = ae(nc.sbuf_tensor("kw_sb", [128, 4, KW], BF16, **L))
    vw_sb = ae(nc.sbuf_tensor("vw_sb", [128, KT, 512], BF16, **L))
    qw_sb = ae(nc.sbuf_tensor("qw_sb", [128, 4, T], BF16, **L))
    qg_sb = ae(nc.sbuf_tensor("qg_sb", [128, 4, 128], BF16, **L))
    wat = [ae(nc.sbuf_tensor(f"wat{b}", [128, DC, 512], BF16, **L))
           for b in range(2)]
    pT_sb = ae(nc.sbuf_tensor("pT_sb", [128, 8, 128], BF16, **L))
    lr_sb = ae(nc.sbuf_tensor("lr_sb", [1, 128], F32, **L))
    lrep_sb = ae(nc.sbuf_tensor("lrep_sb", [128, 128], F32, **L))

    @blockq.vector
    def _(v):
        v.memset(qg_sb[:], 0.0).then_inc(c_v.sem, 1)
    c_v.n += 1
    qg_zero = c_v.n

    def proj_wave(w, which, wbuf, wdma):
        if which == "k":
            for oc in range(4):
                for ci, ts in enumerate(range(0, KW, 512)):
                    tn = min(512, KW - ts)
                    bk = 4 + (oc * 4 + ci) % 2
                    pb = psum_f[bk]
                    _fpk = wf(("bank", bk))

                    @blockq.tensor
                    def _(t, oc=oc, ts=ts, tn=tn, pb=pb, bk=bk, _fpk=_fpk):
                        t.wait_ge(c_gp.sem, wdma)
                        t.wait_ge(c_v.sem, halo_done)
                        wdeps(t, _fpk)
                        for dc in range(DC):
                            mm = t.matmul(pb[:, 0:tn],
                                          wbuf[:, dc, oc * 128:(oc + 1) * 128],
                                          ln1T[:, dc, ts:ts + tn],
                                          start=(dc == 0), stop=(dc == DC - 1))
                        mm.then_inc(c_t.sem, 1)
                    c_t.n += 1
                    mmd = c_t.n

                    @blockq.scalar
                    def _(s, oc=oc, ts=ts, tn=tn, pb=pb, mmd=mmd):
                        s.wait_ge(c_t.sem, mmd)
                        s.copy(out=kw_sb[:, oc, ts:ts + tn],
                               in_=pb[:, 0:tn]).then_inc(c_s.sem, 1)
                    c_s.n += 1
                    set_free(("bank", bk), (c_s, c_s.n))
        elif which == "v":
            for tt in range(KT):
                bk = 4 + tt % 2
                pb = psum_f[bk]
                _fpv = wf(("bank", bk))

                @blockq.tensor
                def _(t, tt=tt, pb=pb, bk=bk, _fpv=_fpv):
                    t.wait_ge(c_gp.sem, wdma)
                    t.wait_ge(c_v.sem, halo_done)
                    wdeps(t, _fpv)
                    for dc in range(DC):
                        mm = t.matmul(pb[:], ln1T[:, dc, tt * 128:(tt + 1) * 128],
                                      wbuf[:, dc, :],
                                      start=(dc == 0), stop=(dc == DC - 1))
                    mm.then_inc(c_t.sem, 1)
                c_t.n += 1
                mmd = c_t.n

                @blockq.scalar
                def _(s, tt=tt, pb=pb, mmd=mmd):
                    s.wait_ge(c_t.sem, mmd)
                    s.copy(out=vw_sb[:, tt, :], in_=pb[:]).then_inc(c_s.sem, 1)
                c_s.n += 1
                set_free(("bank", bk), (c_s, c_s.n))
        else:
            for oc in range(4):
                for th in range(3):
                    bk = 4 + th % 2
                    pb = psum_f[bk]
                    _fpq = wf(("bank", bk))

                    @blockq.tensor
                    def _(t, oc=oc, th=th, pb=pb, bk=bk, _fpq=_fpq):
                        t.wait_ge(c_gp.sem, wdma)
                        if th == 2:
                            t.wait_ge(c_v.sem, qg_zero)
                        wdeps(t, _fpq)
                        for dc in range(DC):
                            if th < 2:
                                rhs = ln1T[:, dc, OWN0 + th * 512:OWN0 + th * 512 + 512]
                                dst = pb[:]
                            else:
                                rhs = ln1T[:, dc, GCOL:GCOL + GK]
                                dst = pb[:, 0:GK]
                            mm = t.matmul(dst, wbuf[:, dc, oc * 128:(oc + 1) * 128],
                                          rhs, start=(dc == 0), stop=(dc == DC - 1))
                        mm.then_inc(c_t.sem, 1)
                    c_t.n += 1
                    mmd = c_t.n

                    @blockq.scalar
                    def _(s, oc=oc, th=th, pb=pb, mmd=mmd):
                        s.wait_ge(c_t.sem, mmd)
                        if th < 2:
                            s.copy(out=qw_sb[:, oc, th * 512:th * 512 + 512],
                                   in_=pb[:]).then_inc(c_s.sem, 1)
                        else:
                            s.copy(out=qg_sb[:, oc, 0:GK],
                                   in_=pb[:, 0:GK]).then_inc(c_s.sem, 1)
                    c_s.n += 1
                    set_free(("bank", bk), (c_s, c_s.n))

    def attend(w, hh, qt):
        glob = (qt == 8)
        nkt = 8 if glob else 6
        po, pl = psum_f[2], psum_f[3]
        h = w * 4 + hh

        expd = [0] * nkt
        SBK = (0, 1, 5)
        for i in range(nkt):
            pb = psum_f[SBK[i % 3]]
            _fsb = wf(("bank", SBK[i % 3])) if i < 3 else None

            @blockq.tensor
            def _(t, i=i, pb=pb, _fsb=_fsb):
                if i >= 3:
                    d = expd[i - 3]
                    if isinstance(d, tuple):
                        t.wait_ge(c_v.sem, d[1])
                    else:
                        t.wait_ge(c_s.sem, d)
                else:
                    wdeps(t, _fsb)
                if glob:
                    kcol, nk = OWN0 + i * 128, 128
                    rhs = qg_sb[:, hh, :]
                else:
                    kcol = GCOL if i == 5 else 128 * qt + 128 * i
                    nk = GK if i == 5 else 128
                    rhs = qw_sb[:, hh, qt * 128:(qt + 1) * 128]
                mm = t.matmul(pb[0:nk, 0:128], kw_sb[:, hh, kcol:kcol + nk],
                              rhs, start=True, stop=True)
                mm.then_inc(c_t.sem, 1)
            c_t.n += 1
            smm = c_t.n

            @blockq.scalar
            def _(s, i=i, pb=pb, smm=smm):
                s.wait_ge(c_t.sem, smm)
                nk = GK if (not glob and i == 5) else 128
                bias = neg16[0:nk, :] if glob else ebias[0:nk, qt, i:i + 1]
                s.activation(out=pT_sb[0:nk, i, :], in_=pb[0:nk, 0:128],
                             func=AF.Exp, bias=bias, scale=SCL).then_inc(c_s.sem, 1)
            c_s.n += 1
            expd[i] = c_s.n
            set_free(("bank", SBK[i % 3]), (c_s, c_s.n))

            if (not glob) and i in (0, 4):
                @blockq.vector
                def _(v, i=i, ed=expd[i]):
                    v.wait_ge(c_s.sem, ed)
                    v.wait_ge(c_sy.sem, consts_ld)
                    v.tensor_mul(pT_sb[:, i, :], pT_sb[:, i, :],
                                 m01[:, qt, (0 if i == 0 else 1), :]
                                 ).then_inc(c_v.sem, 1)
                c_v.n += 1
                expd[i] = ("v", c_v.n)

        _fol = wf(("bank", 2), ("bank", 3))
        for i in range(nkt):
            dep = expd[i]

            @blockq.tensor
            def _(t, i=i, dep=dep, _fol=_fol):
                if isinstance(dep, tuple):
                    t.wait_ge(c_v.sem, dep[1])
                else:
                    t.wait_ge(c_s.sem, dep)
                if i == 0:
                    wdeps(t, _fol)
                if glob:
                    nk, vt = 128, 2 + i
                else:
                    nk = GK if i == 5 else 128
                    vt = 12 if i == 5 else qt + i
                mm = t.matmul(pl[0:1, 0:128], ones_col[0:nk, :],
                              pT_sb[0:nk, i, :], start=(i == 0), stop=(i == nkt - 1))
                mm.then_inc(c_t.sem, 1)
                mm = t.matmul(po[:, 0:128], vw_sb[0:nk, vt, hh * 128:hh * 128 + 128],
                              pT_sb[0:nk, i, :], start=(i == 0), stop=(i == nkt - 1))
                mm.then_inc(c_t.sem, 1)
            c_t.n += 2
        avd = c_t.n

        if glob:
            @blockq.vector
            def _(v, h=h):
                v.wait_ge(c_t.sem, avd)
                v.tensor_copy(oLg[:, h * GK:(h + 1) * GK], po[:, 0:GK])
                v.tensor_copy(lgf[0:1, h * GK:(h + 1) * GK],
                              pl[0:1, 0:GK]).then_inc(c_v.sem, 1)
            c_v.n += 1
            set_free(("bank", 2), (c_v, c_v.n))
            set_free(("bank", 3), (c_v, c_v.n))
            return c_v.n

        @blockq.vector
        def _(v):
            v.wait_ge(c_t.sem, avd)
            v.reciprocal(lr_sb[:], pl[0:1, 0:128]).then_inc(c_v.sem, 1)
        c_v.n += 1
        lrd = c_v.n
        set_free(("bank", 3), (c_v, lrd))

        _frep = wf(("bank", 4), ("lrep",))

        @blockq.tensor
        def _(t, _frep=_frep):
            t.wait_ge(c_v.sem, lrd)
            wdeps(t, _frep)
            t.matmul(psum_f[4][:, 0:128], ones_row[:], lr_sb[:],
                     start=True, stop=True).then_inc(c_t.sem, 1)
        c_t.n += 1
        repd = c_t.n

        lcp = c_v.n + 1

        @blockq.vector
        def _(v, h=h, lcp=lcp):
            v.wait_ge(c_t.sem, repd)
            v.tensor_copy(lrep_sb[:], psum_f[4][:, 0:128]).then_inc(c_v.sem, 1)
            v.wait_ge(c_v.sem, lcp)
            v.tensor_mul(osc[:, h, qt * 128:(qt + 1) * 128], po[:, 0:128],
                         lrep_sb[:]).then_inc(c_v.sem, 1)
        c_v.n += 2
        repc = c_v.n
        set_free(("bank", 4), (c_v, repc))
        set_free(("bank", 2), (c_v, c_v.n))
        set_free(("lrep",), (c_v, c_v.n))
        return c_v.n

    for b in range(2):
        set_free(("wat", b), (c_v, halo_done), (c_t, c_t.n))
    for w in range(4):
        for which, win, b in (("k", wk_in, 0), ("v", wv_in, 1), ("q", wq_in, 0)):
            _fwat = wf(("wat", b))

            @blockq.gpsimd
            def _(g, w=w, win=win, b=b, _fwat=_fwat):
                wdeps(g, _fwat)
                g.dma_start(out=wat[b][:], in_=win[w]).then_inc(c_gp.sem, 16)
            c_gp.n += 16
            proj_wave(w, which, wat[b], c_gp.n)
            set_free(("wat", b), (c_t, c_t.n))
        for hh in range(4):
            for qt in range(8):
                attend(w, hh, qt)
            attend(w, hh, 8)

    ast.close()
    lnf.close()
    attn_done_v = c_v.n
    if stop_after == "attn":
        finish()
        flush()
        return nc, st

    _gmidle = c_gp.n

    @blockq.vector
    def _(v, gmidle=_gmidle):
        v.wait_ge(c_gp.sem, gmidle)
        v.memset(gmrg[:, 0, 256:512], 0.0).then_inc(c_v.sem, 1)
    c_v.n += 1
    zpad = c_v.n

    _mgbase = c_gp.n
    _mgccw = c_cc.n + 1

    @blockq.gpsimd
    def _(g, base=_mgbase, ccw=_mgccw):
        g.wait_ge(c_v.sem, attn_done_v)
        g.wait_ge(c_v.sem, zpad)
        g.dma_start(out=ccD_i[0:128, 0:256], in_=oLg[:]).then_inc(c_gp.sem, 16)
        g.dma_start(out=ccD_i[0:128, 256:512],
                    in_=gmrg[:, 0, 256:512]).then_inc(c_gp.sem, 16)
        g.dma_start(out=ccD_i[0:1, 256:512], in_=lgf[:]).then_inc(c_gp.sem, 16)
        g.wait_ge(c_gp.sem, base + 48)
        g.collective_compute(
            "AllGather", ALU.bypass,
            replica_groups=[[0, 1, 2, 3], [4, 5, 6, 7]],
            ins=[ccD_i.opt()], outs=[ccD_o.opt()]).then_inc(c_cc.sem, 1)
        g.wait_ge(c_cc.sem, ccw)
        for sl in range(4):
            g.dma_start(out=gmrg[:, sl], in_=ccD_o[sl]).then_inc(c_gp.sem, 16)
    c_gp.n += 112
    c_cc.n += 1
    mrg_ld = c_gp.n

    def f(v):
        v.wait_ge(c_gp.sem, mrg_ld)
        v.tensor_add(gmrg[:, 0, :], gmrg[:, 0, :], gmrg[:, 1, :])
        return v.tensor_add(gmrg[:, 2, :], gmrg[:, 2, :], gmrg[:, 3, :])
    m1 = vstep(f)

    def f(v):
        v.wait_ge(c_v.sem, m1)
        return v.tensor_add(gmrg[:, 0, :], gmrg[:, 0, :], gmrg[:, 2, :])
    m2 = vstep(f)

    def f(v):
        v.wait_ge(c_v.sem, m2)
        return v.reciprocal(gmrg[0:1, 1, 256:512], gmrg[0:1, 0, 256:512])
    mrgd = vstep(f)

    gsc_dep = {}
    for hq in range(NH):
        bk = 4 + hq % 2
        pb = psum_f[bk]

        _fg5 = wf(("bank", bk))

        @blockq.tensor
        def _(t, hq=hq, pb=pb, bk=bk, _fg5=_fg5):
            t.wait_ge(c_v.sem, mrgd)
            wdeps(t, _fg5)
            t.matmul(pb[:, 0:GK], ones_row[:],
                     gmrg[0:1, 1, 256 + hq * GK:256 + (hq + 1) * GK],
                     start=True, stop=True).then_inc(c_t.sem, 1)
        c_t.n += 1
        rpd = c_t.n

        @blockq.vector
        def _(v, hq=hq, pb=pb, rpd=rpd):
            v.wait_ge(c_t.sem, rpd)
            v.tensor_mul(gsc[:, hq, :], gmrg[:, 0, hq * GK:(hq + 1) * GK],
                         pb[:, 0:GK]).then_inc(c_v.sem, 1)
        c_v.n += 1
        gsc_dep[hq] = c_v.n
        set_free(("bank", bk), (c_v, c_v.n))
    gsc_all = c_v.n

    p6 = ExitStack()
    p6e = p6.enter_context
    x2t_sb = p6e(nc.sbuf_tensor("x2t_sb", [128, NT, D], F32, **L))
    x1t_sb = p6e(nc.sbuf_tensor("x1t_sb", [128, NT, D], F32, **L))
    wo_sb = [p6e(nc.sbuf_tensor(f"wo_sb{b}", [128, DC, 128], BF16, **L))
             for b in range(2)]
    cta_sb = p6e(nc.sbuf_tensor("cta_sb", [128, T], F32, **L))
    gtmp_sb = p6e(nc.sbuf_tensor("gtmp_sb", [128, GK], F32, **L))

    for b in range(2):
        set_free(("wo", b), (c_v, gsc_all), (c_t, c_t.n))
    x1_ld = [0] * NT
    for tt in range(NT):
        @blockq.sync
        def _(sy, tt=tt):
            sy.wait_ge(c_v.sem, gsc_all)
            sy.dma_start(out=x1t_sb[:, tt, :],
                         in_=x1_scr[tt * 128:(tt + 1) * 128, :]).then_inc(c_sy.sem, 16)
        c_sy.n += 16
        x1_ld[tt] = c_sy.n

    aoev = {}
    x2d = {}
    bldd = {}
    for ddc in range(DC):
        b = ddc % 2

        _fwo = wf(("wo", b))

        @blockq.gpsimd
        def _(g, ddc=ddc, b=b, _fwo=_fwo):
            wdeps(g, _fwo)
            g.dma_start(out=wo_sb[b][:], in_=wo_in[ddc]).then_inc(c_gp.sem, 16)
        c_gp.n += 16
        wld = c_gp.n

        pa, pb2, pg16 = psum_f[0 + b], psum_f[2 + b], psum_f[4]

        _fp6 = (wf(("bank", 0 + b), ("bank", 2 + b)) if ddc < 2 else []) + \
               (wf(("bank", 4)) if ddc == 0 else [])

        @blockq.tensor
        def _(t, ddc=ddc, b=b, wld=wld, pa=pa, pb2=pb2, pg16=pg16, _fp6=_fp6):
            t.wait_ge(c_gp.sem, wld)
            t.wait_ge(c_v.sem, gsc_all)
            wdeps(t, _fp6)
            t.wait_ge(c_s.sem, aoev.get(ddc - 2, 0))
            t.wait_ge(c_v.sem, bldd.get(ddc - 1, 0))
            for th in range(2):
                dst = pa if th == 0 else pb2
                for dvc in range(DC):
                    mm = t.matmul(dst[:], wo_sb[b][:, dvc, :],
                                  osc[:, dvc, th * 512:th * 512 + 512],
                                  start=(dvc == 0), stop=(dvc == DC - 1))
                mm.then_inc(c_t.sem, 1)
            for dvc in range(DC):
                mm = t.matmul(pg16[:, 0:GK], wo_sb[b][:, dvc, :], gsc[:, dvc, :],
                              start=(dvc == 0), stop=(dvc == DC - 1))
            mm.then_inc(c_t.sem, 1)
        c_t.n += 3
        mmd = c_t.n
        set_free(("wo", b), (c_t, mmd))

        @blockq.scalar
        def _(s, pa=pa, pb2=pb2, mmd=mmd):
            s.wait_ge(c_t.sem, mmd - 1)
            s.copy(out=cta_sb[:, 0:512], in_=pa[:]).then_inc(c_s.sem, 1)
            s.copy(out=cta_sb[:, 512:1024], in_=pb2[:]).then_inc(c_s.sem, 1)
        c_s.n += 2
        ctd = c_s.n
        aoev[ddc] = c_s.n

        def f(v, ddc=ddc, pg16=pg16, ctd=ctd, mmd=mmd):
            v.wait_ge(c_s.sem, ctd)
            v.wait_ge(c_t.sem, mmd)
            return v.tensor_sub(gtmp_sb[:], pg16[:, 0:GK], cta_sb[:, 0:GK])
        b1 = vstep(f)

        def f(v, b1=b1):
            v.wait_ge(c_v.sem, b1)
            return v.tensor_mul(gtmp_sb[:], gtmp_sb[:], gbl[:])
        b2 = vstep(f)

        def f(v, b2=b2):
            v.wait_ge(c_v.sem, b2)
            return v.tensor_add(cta_sb[:, 0:GK], cta_sb[:, 0:GK], gtmp_sb[:])
        bldd[ddc] = vstep(f)
        bld = bldd[ddc]

        for tt in range(NT):
            pbt = psum_f[6 + tt % 2]
            _fx67 = wf(("bank", 6 + tt % 2)) if (ddc == 0 and tt < 2) else None

            @blockq.tensor
            def _(t, ddc=ddc, tt=tt, pbt=pbt, bld=bld, _fx67=_fx67):
                t.wait_ge(c_v.sem, bld)
                t.wait_ge(c_v.sem, x2d.get((ddc - 1, tt), 0))
                t.wait_ge(c_v.sem, x2d.get((ddc, tt - 2), 0))
                if _fx67 is not None:
                    wdeps(t, _fx67)
                t.transpose(pbt[:, 0:128], cta_sb[:, tt * 128:(tt + 1) * 128],
                            ident[:]).then_inc(c_t.sem, 1)
            c_t.n += 1
            trd = c_t.n

            @blockq.vector
            def _(v, ddc=ddc, tt=tt, pbt=pbt, trd=trd):
                v.wait_ge(c_t.sem, trd)
                v.wait_ge(c_sy.sem, x1_ld[tt])
                v.tensor_add(x2t_sb[:, tt, ddc * 128:(ddc + 1) * 128],
                             pbt[:, 0:128],
                             x1t_sb[:, tt, ddc * 128:(ddc + 1) * 128]
                             ).then_inc(c_v.sem, 1)
            c_v.n += 1
            x2d[(ddc, tt)] = c_v.n
    set_free(("bank", 6), (c_v, x2d[(DC - 1, NT - 2)]))
    set_free(("bank", 7), (c_v, x2d[(DC - 1, NT - 1)]))
    set_free(("bank", 0), (c_s, aoev[DC - 2]))
    set_free(("bank", 1), (c_s, aoev[DC - 1]))
    set_free(("bank", 2), (c_s, aoev[DC - 2]))
    set_free(("bank", 3), (c_s, aoev[DC - 1]))
    set_free(("bank", 4), (c_v, bldd[DC - 1]))

    for tt in range(NT):
        done = x2d[(DC - 1, tt)]

        @blockq.sync
        def _(sy, tt=tt, done=done):
            sy.wait_ge(c_v.sem, done)
            sy.dma_start(out=x2_scr[tt * 128:(tt + 1) * 128, :],
                         in_=x2t_sb[:, tt, :]).then_inc(c_sy.sem, 16)
        c_sy.n += 16

    p6.close()
    oscf.close()
    if stop_after == "p6":
        finish()
        flush()
        return nc, st

    emit_ffn(2)

    finish()
    flush()
    return nc, st


def _host_masks(core):
    S0 = 1024 * (core % 4)
    m01 = np.zeros((8, 2, 128, 128), np.float32)
    eb = np.zeros((8, 6, 128), np.float32)
    for qt in range(8):
        for ii, i in enumerate((0, 4)):
            kb = 128 * qt + 128 * i + np.arange(128)[:, None]
            kg = S0 - 256 + kb
            qg = S0 + 128 * qt + np.arange(128)[None, :]
            ok = (kg >= 0) & (kg < S) & ((np.abs(qg - kg) < 256) | (kg < GK))
            m01[qt, ii] = np.where(ok, 1.0, 0.0)
        for i in range(5):
            kg0 = S0 - 256 + 128 * qt + 128 * i
            valid = (kg0 >= 0) and (kg0 + 127 < S)
            eb[qt, i, :] = -STAB if valid else MSK
        eb[qt, 5, :] = MSK
        dbl = (S0 == 0) and (qt <= 2)
        eb[qt, 5, 0:GK] = MSK if dbl else -STAB
    c4 = core % 4
    hs = np.zeros(12, np.float32)
    if c4 % 2 == 0:
        hs[9] = 1.0
    else:
        hs[8] = 1.0
    if c4 == 1:
        hs[11] = 1.0
    else:
        hs[10] = 1.0
    if c4 == 1:
        hs[0] = 1.0
    elif c4 == 2:
        hs[2] = 1.0
    elif c4 == 3:
        hs[0] = 1.0
    if c4 == 0:
        hs[5] = 1.0
    elif c4 == 1:
        hs[7] = 1.0
    elif c4 == 2:
        hs[5] = 1.0
    hsel = np.broadcast_to(hs, (128, 12)).copy()
    gbl = np.full((128, GK), 1.0 if c4 == 0 else 0.0, np.float32)
    return m01, eb, hsel, gbl


def _prep_inputs(x, Wq, Wk, Wv, Wo, W_g, W_u, W_d):
    import ml_dtypes
    x = np.ascontiguousarray(np.asarray(x, np.float32))
    Wq, Wk, Wv, Wo, W_g, W_u, W_d = (
        np.asarray(w, np.float32) for w in (Wq, Wk, Wv, Wo, W_g, W_u, W_d))

    wg = np.ascontiguousarray(W_g.reshape(HC, 128, DC, 128).transpose(0, 3, 2, 1))
    wu = np.ascontiguousarray(W_u.reshape(HC, 128, DC, 128).transpose(0, 3, 2, 1))
    wd = np.ascontiguousarray(W_d.reshape(DC, 128, HC, 128).transpose(0, 3, 2, 1))
    wq = np.ascontiguousarray(Wq.reshape(DC, 128, 4, 512).transpose(2, 1, 0, 3))
    wk = np.ascontiguousarray(Wk.reshape(DC, 128, 4, 512).transpose(2, 1, 0, 3))
    wv = np.ascontiguousarray(Wv.reshape(DC, 128, 4, 512).transpose(2, 1, 0, 3))
    wo = np.ascontiguousarray(Wo.reshape(DC, 128, DC, 128).transpose(2, 1, 0, 3))
    ident = np.eye(128, dtype=np.float32)

    in_maps = []
    for core in range(8):
        b, c4 = core // 4, core % 4
        m01, eb, hsel, gbl = _host_masks(core)
        in_maps.append({
            "x": np.ascontiguousarray(x[b, c4 * T:(c4 + 1) * T, :]),
            "wg": wg, "wu": wu, "wd": wd,
            "wq": wq, "wk": wk, "wv": wv, "wo": wo,
            "m01": np.ascontiguousarray(
                m01.transpose(2, 0, 1, 3)).astype(ml_dtypes.bfloat16),
            "eb": np.ascontiguousarray(eb.transpose(2, 0, 1)),
            "hsel": hsel, "gbl": gbl, "ident": ident,
        })
    return in_maps


def kernel(L0_attn_hidden_states, L0_attn_Wq, L0_attn_Wk, L0_attn_Wv,
           L0_attn_Wo, L0_ffn_W_g, L0_ffn_W_u, L0_ffn_W_d):
    in_maps = _prep_inputs(L0_attn_hidden_states, L0_attn_Wq, L0_attn_Wk,
                           L0_attn_Wv, L0_attn_Wo, L0_ffn_W_g, L0_ffn_W_u,
                           L0_ffn_W_d)
    nc, st = build(STOP_AFTER)
    st.close()
    res = run_bass_kernel_spmd(nc, in_maps, core_ids=list(range(8)))
    out = np.empty((2, S, D), np.float32)
    for core in range(8):
        b, c4 = core // 4, core % 4
        out[b, c4 * T:(c4 + 1) * T, :] = res.results[core]["out"]
    return out
